# version 08
# baseline (speedup 1.0000x reference)
import numpy as np
import concourse.bass as bass
import concourse.mybir as mybir
from concourse.bass_utils import run_bass_kernel_spmd

F32 = mybir.dt.float32
BF16 = mybir.dt.bfloat16
AF = mybir.ActivationFunctionType
ALU = mybir.AluOpType
AX = mybir.AxisListType

D, DCH = 1024, 8
CAP = 512
S = 4096
T = 2048
H, HD = 16, 64
FF = 2048
NE = 8
TBL = 512
EPS = 1e-5
ISC = 1.0 / 32.0
KB = 1024

ENGINES = ("tensor", "scalar", "vector", "gpsimd", "sync")
DBG = False
DBGKIND = "Internal"
XBAR = 3


class Planner:
    def __init__(self, nc, sems):
        self.nc = nc
        self.ops = {e: [] for e in ENGINES}
        self.sems = sems
        self.counts = {k: 0 for k in sems}
        self.pending = {e: [] for e in ENGINES}

    SELFSEM = {"vector": "DV", "scalar": "AC"}

    def op(self, eng, fn, waits=(), inc=None, k=1):
        w = self.pending[eng] + [x for x in waits if x is not None and x[1]]
        self.pending[eng] = []
        ss = self.SELFSEM.get(eng) or ("PL" if eng == "gpsimd" and inc == "PL" else None)
        if ss is not None and self.counts[ss] > 0:
            w.append((ss, self.counts[ss]))
        self.ops[eng].append((fn, w, inc, k))
        if inc is not None:
            self.counts[inc] += k
            return self.counts[inc]
        return None

    def n(self, sem):
        return self.counts[sem]

    def barrier(self):
        snap = [(s, c) for s, c in self.counts.items() if c > 0]
        for e in ENGINES:
            self.pending[e] = list(snap)

    def emit(self, block):
        for engname in ENGINES:
            def make_body(ops=self.ops[engname], sems=self.sems):
                def body(eng):
                    for fn, waits, inc, k in ops:
                        for s, v in waits:
                            eng.wait_ge(sems[s], v)
                        ins = fn(eng)
                        if inc is not None:
                            ins.then_inc(sems[inc], k)
                return body
            getattr(block, engname)(make_body())


class Arena:

    def __init__(self, master, origin, base, limit, tag):
        self.m, self.origin = master, origin
        self.off, self.limit, self.tag = base, limit, tag

    def al(self, name, shape, dtype):
        nbytes = int(np.prod(shape[1:])) * (2 if dtype == BF16 else 4)
        nbytes = (nbytes + 31) // 32 * 32
        o = self.off - self.origin
        view = self.m if dtype == F32 else self.m.bitcast(BF16)
        es = 4 if dtype == F32 else 2
        t = view[0:shape[0], o // es: o // es + int(np.prod(shape[1:]))]
        if len(shape) > 2:
            raise ValueError("arena views are 2-D")
        self.off += nbytes
        assert self.off <= self.limit, f"{self.tag}/{name}: off {self.off} > {self.limit}"
        return t


def build_program():
    nc = bass.Bass("TRN2", target_bir_lowering=False)
    xT = nc.dram_tensor("xT", [D, S], F32, kind="ExternalInput").ap()
    wq = nc.dram_tensor("wq", [D, D], F32, kind="ExternalInput").ap()
    wk = nc.dram_tensor("wk", [D, D], F32, kind="ExternalInput").ap()
    wv = nc.dram_tensor("wv", [D, D], F32, kind="ExternalInput").ap()
    wo = nc.dram_tensor("wo", [D, D], F32, kind="ExternalInput").ap()
    bo = nc.dram_tensor("bo", [128, DCH], F32, kind="ExternalInput").ap()
    gw = nc.dram_tensor("gw", [D, NE], F32, kind="ExternalInput").ap()
    sg = nc.dram_tensor("sg", [D, FF], F32, kind="ExternalInput").ap()
    su = nc.dram_tensor("su", [D, FF], F32, kind="ExternalInput").ap()
    sd = nc.dram_tensor("sd", [FF, D], F32, kind="ExternalInput").ap()
    rg = nc.dram_tensor("rg", [NE, D, FF], F32, kind="ExternalInput").ap()
    ru = nc.dram_tensor("ru", [NE, D, FF], F32, kind="ExternalInput").ap()
    rd = nc.dram_tensor("rd", [NE, FF, D], F32, kind="ExternalInput").ap()
    yT = nc.dram_tensor("yT", [D, T], F32, kind="ExternalOutput").ap()
    xh_h = nc.dram_tensor("xh_h", [D, S], BF16, kind="Internal").ap()
    y1_h = nc.dram_tensor("y1_h", [D, T], F32, kind="Internal").ap()
    if DBG:
        dbg_ot = nc.dram_tensor("dbg_ot", [128, DCH * T], BF16, kind=DBGKIND).ap()
        dbg_ares = nc.dram_tensor("dbg_ares", [128, DCH * T], F32, kind=DBGKIND).ap()
        dbg_x2 = nc.dram_tensor("dbg_x2", [128, DCH * T], BF16, kind=DBGKIND).ap()
        dbg_slot = nc.dram_tensor("dbg_slot", [1, T], F32, kind=DBGKIND).ap()
        dbg_sc = nc.dram_tensor("dbg_sc", [1, T], BF16, kind=DBGKIND).ap()
        dbg_ffn = nc.dram_tensor("dbg_ffn", [128, DCH * T], F32, kind=DBGKIND).ap()
        dbg_xe = nc.dram_tensor("dbg_xe", [128, DCH * CAP], BF16, kind=DBGKIND).ap()
        dbg_dnb = nc.dram_tensor("dbg_dnb", [128, 4 * KB], BF16, kind=DBGKIND).ap()
        dbg_xtok = nc.dram_tensor("dbg_xtok", [128, 16 * KB], BF16, kind=DBGKIND).ap()
        dbg_shr = nc.dram_tensor("dbg_shr", [128, DCH * T], F32, kind=DBGKIND).ap()
        dbg_pt = nc.dram_tensor("dbg_pt", [128, 4 * TBL], BF16, kind=DBGKIND).ap()
        dbg_sbc = nc.dram_tensor("dbg_sbc", [128, T], F32, kind=DBGKIND).ap()

    sems = {}
    for name in ("PE", "AC", "DV", "PL", "SO", "C0", "W0", "WO", "WG", "WU", "WD",
                 "MS", "SC", "Y1S", "C1", "XA0", "XA1", "XA2", "XA3", "XB0", "XB1",
                 "SH0", "SH1", "XT0", "XT1", "XF0", "XF1", "XG0", "XG1", "YL0", "YL1", "SR"):
        sems[name] = nc.semaphore(name).__enter__()
    P = Planner(nc, sems)

    GB = (nc.sbuf_base + 31) // 32 * 32
    TOP = nc.sbuf_top
    GSZ = 44 * KB
    GMST = nc.alloc_sbuf_tensor_at("GMST", [128, GSZ // 4], F32, offset=GB).ap()
    GA = Arena(GMST, GB, GB, GB + GSZ, "g")

    ones_b = GA.al("ones_b", [1, 128], BF16)
    ones_f = GA.al("ones_f", [1, 128], F32)
    col1_f = GA.al("col1_f", [128, 1], F32)
    identF = GA.al("identF", [128, 128], F32)
    gw_sb = GA.al("gw_sb", [128, DCH * NE], BF16)
    bo_sb = GA.al("bo_sb", [128, DCH], F32)
    iotaw = GA.al("iotaw", [128, NE], F32)
    epsr = GA.al("epsr", [1, 1], F32)
    ar2 = GA.al("ar2", [1, T], F32)
    br2 = GA.al("br2", [1, T], F32)
    scrow = GA.al("scrow", [1, T], BF16)
    identB = GA.al("identB", [128, 128], BF16)
    LS = GA.al("LS", [128, 128], BF16)
    ones2 = GA.al("ones2", [128, 128], BF16)
    ecaps = GA.al("ecaps", [128, NE], F32)
    iota512 = GA.al("iota512", [128, CAP], F32)
    iotacol = GA.al("iotacol", [128, 1], F32)
    mbf = GA.al("mbf", [128, 16 * NE], BF16)
    rkc = GA.al("rkc", [128, 8], F32)
    slotrow = GA.al("slotrow", [1, T], F32)
    sct2b = GA.al("sct2b", [128, 16], F32)
    P1 = (GA.off + 127) // 128 * 128
    NF32 = (TOP - P1) // 4 // 32 * 32
    MSC = nc.alloc_sbuf_tensor_at("MSC", [128, NF32], F32, offset=P1).ap()

    MB = MSC.bitcast(BF16)
    OT = MB[:, 0:DCH * T]
    x2b = MB[:, DCH * T: 2 * DCH * T]
    facc = MSC[:, 0:DCH * KB]

    PS = nc.alloc_psum_tensor("PS", [128, 4096], F32)
    BK = lambda b: PS[:, b * 512:(b + 1) * 512]

    P.op("gpsimd", lambda g: g.iota(iotaw[:], [[-1, NE]], base=NE, channel_multiplier=0,
                                    allow_small_or_imprecise_dtypes=True), inc="PL")
    P.op("gpsimd", lambda g: g.iota(identF[:], [[1, 128]], base=0, channel_multiplier=-1,
                                    allow_small_or_imprecise_dtypes=True), inc="PL")
    P.op("gpsimd", lambda g: g.iota(LS[:], [[1, 128]], base=0, channel_multiplier=-1,
                                    allow_small_or_imprecise_dtypes=True), inc="PL")
    P.op("gpsimd", lambda g: g.iota(ecaps[:], [[CAP, NE]], base=0, channel_multiplier=0,
                                    allow_small_or_imprecise_dtypes=True), inc="PL")
    P.op("gpsimd", lambda g: g.iota(iota512[:], [[1, CAP]], base=0, channel_multiplier=0,
                                    allow_small_or_imprecise_dtypes=True), inc="PL")
    P.op("gpsimd", lambda g: g.iota(iotacol[:], [[1, 1]], base=0, channel_multiplier=1,
                                    allow_small_or_imprecise_dtypes=True), inc="PL")
    ident_pl = P.n("PL")
    P.op("vector", lambda v: v.memset(ones_b[:], 1.0), inc="DV")
    P.op("vector", lambda v: v.memset(ones_f[:], 1.0), inc="DV")
    P.op("vector", lambda v: v.memset(col1_f[:], 1.0), inc="DV")
    P.op("vector", lambda v: v.memset(epsr[:], EPS), inc="DV")
    w_ident = P.op("vector", lambda v: v.tensor_single_scalar(identF[:], identF[:], 0.0, ALU.is_equal),
                   waits=[("PL", ident_pl)], inc="DV")
    P.op("vector", lambda v: v.tensor_copy(identB[:], identF[:]), inc="DV")
    P.op("vector", lambda v: v.tensor_single_scalar(LS[:], LS[:], 0.0, ALU.is_gt), inc="DV")
    P.op("vector", lambda v: v.memset(ones2[:], 1.0), inc="DV")
    init_dv = P.n("DV")
    P.barrier()
    P.op("sync", lambda s: s.dma_start(out=bo_sb[:], in_=bo), inc="C0", k=16)
    P.op("gpsimd", lambda g: g.dma_start(out=gw_sb[:].rearrange("p (c e) -> p c e", c=DCH),
                                         in_=gw.rearrange("(c p) e -> p c e", p=128)), inc="C1", k=16)

    def row_math(sumr, sqr, ar, br, tmp, width, c0=0, first_waits=None):
        sl = lambda r: r[0:1, c0:c0 + width]
        P.op("vector", lambda v: v.tensor_scalar_mul(sl(sumr), sl(sumr), 1.0 / D),
             waits=([("PE", P.n("PE"))] if first_waits is None else first_waits), inc="DV")
        P.op("vector", lambda v: v.tensor_scalar_mul(sl(sqr), sl(sqr), 1.0 / D), inc="DV")
        P.op("vector", lambda v: v.tensor_mul(sl(tmp), sl(sumr), sl(sumr)), inc="DV")
        r3 = P.op("vector", lambda v: v.tensor_sub(sl(sqr), sl(sqr), sl(tmp)), inc="DV")
        r4 = P.op("scalar", lambda s: s.activation(sl(tmp), sl(sqr), AF.Sqrt, bias=epsr[0:1, 0:1]),
                  waits=[("DV", r3)], inc="AC")
        P.op("vector", lambda v: v.reciprocal(sl(ar), sl(tmp)),
             waits=[("AC", r4)], inc="DV")
        return P.op("vector", lambda v: v.scalar_tensor_tensor(
            sl(br), sl(sumr), -1.0, sl(ar), ALU.mult, ALU.mult), inc="DV")

    A = Arena(MSC, P1, P1, TOP, "A")
    xfA = [A.al(f"xf{i}", [128, DCH * TBL], F32) for i in range(4)]
    sqA = [A.al(f"sq{i}", [128, TBL], F32) for i in range(2)]
    nscA = [A.al(f"nsc{i}", [128, TBL], F32) for i in range(2)]
    ssA = A.al("ssum", [1, S], F32)
    sqrA = A.al("ssq", [1, S], F32)
    a1r = A.al("a1r", [1, S], F32)
    b1r = A.al("b1r", [1, S], F32)
    xhA = [A.al(f"xh{i}", [128, DCH * TBL], BF16) for i in range(2)]

    NTBS = S // TBL
    mm2_id, cp_id, nrm_id, st_id = {}, {}, {}, {}
    for tb in range(NTBS):
        buf = xfA[tb % 4]
        sm = f"XA{tb % 4}"
        ltb = (sm, P.op("sync", lambda s, buf=buf, tb=tb: s.dma_start(
            out=buf[:].rearrange("p (c t) -> p c t", c=DCH),
            in_=xT[:, tb * TBL:(tb + 1) * TBL].rearrange("(c p) t -> p c t", p=128)),
            waits=[("DV", nrm_id.get(tb - 4))], inc=sm, k=16))
        pb = (tb % 2) * 2
        for chn in range(DCH):
            i = tb * DCH + chn
            sqb = sqA[i % 2]
            sq_ac = P.op("scalar", lambda s, buf=buf, chn=chn, sqb=sqb: s.activation(
                sqb[:], buf[:, chn * TBL:(chn + 1) * TBL], AF.Square),
                waits=[ltb, ("PE", mm2_id.get(i - 2))], inc="AC")
            P.op("tensor", lambda t, buf=buf, chn=chn, pb=pb: t.matmul(
                BK(pb)[0:1, :], col1_f[:], buf[:, chn * TBL:(chn + 1) * TBL],
                start=(chn == 0), stop=(chn == DCH - 1)),
                waits=[ltb] + ([("DV", cp_id.get(tb - 2))] if chn == 0 else []),
                inc="PE")
            mm2_id[i] = P.op("tensor", lambda t, sqb=sqb, pb=pb, chn=chn: t.matmul(
                BK(pb + 1)[0:1, :], col1_f[:], sqb[:], start=(chn == 0), stop=(chn == DCH - 1)),
                waits=[("AC", sq_ac)], inc="PE")
        P.op("vector", lambda v, tb=tb, pb=pb: v.tensor_copy(
            ssA[0:1, tb * TBL:(tb + 1) * TBL], BK(pb)[0:1, :]),
            waits=[("PE", mm2_id[tb * DCH + DCH - 1])], inc="DV")
        cp_id[tb] = P.op("vector", lambda v, tb=tb, pb=pb: v.tensor_copy(
            sqrA[0:1, tb * TBL:(tb + 1) * TBL], BK(pb + 1)[0:1, :]), inc="DV")

        rmtb = row_math(ssA, sqrA, a1r, b1r, b1r, TBL, c0=tb * TBL, first_waits=[])

        pbb = 4 + (tb % 2) * 2
        P.op("tensor", lambda t, tb=tb, pbb=pbb: t.matmul(
            BK(pbb), ones_f[:], a1r[0:1, tb * TBL:(tb + 1) * TBL], start=True, stop=True),
            waits=[("DV", rmtb), ("DV", nrm_id.get(tb - 2))], inc="PE")
        bcb = P.op("tensor", lambda t, tb=tb, pbb=pbb: t.matmul(
            BK(pbb + 1), ones_f[:], b1r[0:1, tb * TBL:(tb + 1) * TBL], start=True, stop=True), inc="PE")

        xo = xhA[tb % 2]
        for chn in range(DCH):
            P.op("vector", lambda v, buf=buf, chn=chn, pbb=pbb, sl=chn % 2: v.tensor_mul(
                nscA[sl][:], buf[:, chn * TBL:(chn + 1) * TBL], BK(pbb)),
                waits=([("PE", bcb), st_id.get(tb - 2)] if chn == 0 else []), inc="DV")
            nrm_id[tb] = P.op("vector", lambda v, xo=xo, chn=chn, pbb=pbb, sl=chn % 2: v.tensor_add(
                xo[:, chn * TBL:(chn + 1) * TBL], nscA[sl][:], BK(pbb + 1)), inc="DV")
        sms = f"SH{tb % 2}"
        st_id[tb] = (sms, P.op("sync", lambda s, xo=xo, tb=tb: s.dma_start(
            out=xh_h[:, tb * TBL:(tb + 1) * TBL].rearrange("(c p) t -> p c t", p=128),
            in_=xo[:].rearrange("p (c t) -> p c t", c=DCH)),
            waits=[("DV", nrm_id[tb])], inc=sms, k=16))
    P.barrier()

    for hg in range(2):
        B = Arena(MSC, P1, P1 + 32 * KB, TOP, "B")
        Qc = B.al("Qc", [128, 4 * T], BF16)
        Kc = B.al("Kc", [128, 4 * S], BF16)
        Vt = B.al("Vt", [128, 32 * 520], BF16)
        wq_sb = B.al("wq", [128, DCH * TBL], BF16)
        wk_sb = B.al("wk", [128, DCH * TBL], BF16)
        wv_sb = B.al("wv", [128, DCH * TBL], BF16)
        xh_tb = [B.al(f"xh{i}", [128, DCH * TBL], BF16) for i in range(2)]
        At = [B.al(f"At{i}", [128, 1024], BF16) for i in range(3)]
        bc_sb = B.al("bc_sb", [64, TBL], F32)
        rsrow = B.al("rsrow", [1, TBL], F32)
        rsrow2 = B.al("rsrow2", [1, TBL], F32)

        wls = {}
        for nm, wsrc, wdst in (("q", wq, wq_sb), ("k", wk, wk_sb), ("v", wv, wv_sb)):
            P.op("gpsimd", lambda g, wsrc=wsrc, wdst=wdst, hg=hg: g.dma_start(
                out=wdst[:].rearrange("p (c o) -> p c o", c=DCH),
                in_=wsrc[:, hg * TBL:(hg + 1) * TBL].rearrange("(c p) o -> p c o", p=128)),
                inc="W0", k=16)
        for nm in ("q", "k", "v"):
            wls[nm] = ("W0", P.n("W0"))
        for tt in range(32):
            P.op("vector", lambda v, tt=tt: v.memset(
                Vt[:, tt * 520:(tt + 1) * 520].rearrange("p (h c) -> p h c", h=8)[:, :, 64:65], 1.0), inc="DV")
        vones_dv = P.n("DV")

        kq_ev, v_ev, pe_tb = [], [], {}
        for tb in range(NTBS):
            buf = xh_tb[tb % 2]
            smt = f"XT{tb % 2}"
            ltb = (smt, P.op("sync", lambda s, buf=buf, tb=tb: s.dma_start(
                out=buf[:].rearrange("p (c t) -> p c t", c=DCH),
                in_=xh_h[:, tb * TBL:(tb + 1) * TBL].rearrange("(c p) t -> p c t", p=128)),
                waits=[("PE", pe_tb.get(tb - 2))], inc=smt, k=16))
            for kind in (("k", "q") if tb < 4 else ("k",)):
                wsb, dst, dstw = (wk_sb, Kc, S) if kind == "k" else (wq_sb, Qc, T)
                for och in range(4):
                    seq = len(kq_ev)
                    pb = seq % 2
                    mmlast = None
                    for chn in range(DCH):
                        w0 = ([ltb, wls[kind]] +
                              ([("DV", kq_ev[seq - 2])] if seq >= 2 else [])) if chn == 0 else []
                        mmlast = P.op("tensor", lambda t, wsb=wsb, chn=chn, och=och, buf=buf, pb=pb: t.matmul(
                            BK(pb), wsb[:, chn * TBL + och * 128: chn * TBL + och * 128 + 128],
                            buf[:, chn * TBL:(chn + 1) * TBL],
                            start=(chn == 0), stop=(chn == DCH - 1)),
                            waits=w0, inc="PE")
                    kq_ev.append(P.op("vector", lambda v, dst=dst, dstw=dstw, och=och, tb=tb, pb=pb: v.tensor_copy(
                        dst[:, och * dstw + tb * TBL: och * dstw + (tb + 1) * TBL], BK(pb)),
                        waits=[("PE", mmlast)], inc="DV"))
            for ttl in range(4):
                tt = tb * 4 + ttl
                sv = len(v_ev)
                pb = 2 + sv % 2
                mmlast = None
                for chn in range(DCH):
                    w0 = ([ltb, wls["v"]] +
                          ([("AC", v_ev[sv - 2])] if sv >= 2 else [])) if chn == 0 else []
                    mmlast = P.op("tensor", lambda t, chn=chn, buf=buf, ttl=ttl, pb=pb, wv_sb=wv_sb: t.matmul(
                        BK(pb), buf[:, chn * TBL + ttl * 128: chn * TBL + ttl * 128 + 128],
                        wv_sb[:, chn * TBL:(chn + 1) * TBL],
                        start=(chn == 0), stop=(chn == DCH - 1)),
                        waits=w0, inc="PE")
                v_ev.append(P.op("scalar", lambda s, tt=tt, pb=pb, Vt=Vt: s.copy(
                    Vt[:, tt * 520:(tt + 1) * 520].rearrange("p (h c) -> p h c", h=8)[:, :, 0:64],
                    BK(pb).rearrange("p (h c) -> p h c", h=8)),
                    waits=[("PE", mmlast)], inc="AC"))
            pe_tb[tb] = P.n("PE")
        b_dv, b_ac = P.n("DV"), P.n("AC")

        prev_exp = None
        prev_av = None
        prev_ot = None
        for hp in range(4):
            och = hp
            for qb in range(4):
                qk_id, ex_id, av_id = {}, {}, {}
                for kk in range(34):
                    if kk < 32:
                        kb, sl = kk, kk % 2
                        for hx in range(2):
                            rb = hx * 64
                            wts = []
                            if hx == 0:
                                if kb == 0 and hp == 0 and qb == 0:
                                    wts += [("DV", b_dv), ("AC", b_ac)]
                                if kb >= 2:
                                    wts.append(("AC", ex_id[kb - 2]))
                                elif prev_exp is not None:
                                    wts.append(("AC", prev_exp[kb]))
                            qk_id[(kb, hx)] = P.op("tensor", lambda t, och=och, rb=rb, kb=kb, hx=hx, qb=qb, sl=sl, Kc=Kc, Qc=Qc: t.matmul(
                                PS[:, sl * 1024 + hx * 512: sl * 1024 + (hx + 1) * 512],
                                Kc[rb:rb + 64, och * S + kb * 128: och * S + (kb + 1) * 128],
                                Qc[rb:rb + 64, och * T + qb * TBL: och * T + (qb + 1) * TBL],
                                start=True, stop=True, tile_position=(rb, 0)),
                                waits=wts, inc="PE")
                        ewts = [("PE", qk_id[(kb, 1)])]
                        if kb >= 3:
                            ewts.append(("PE", av_id[kb - 3]))
                        elif prev_av is not None:
                            ewts.append(("PE", prev_av))
                        ex_id[kb] = P.op("scalar", lambda s, sl=sl, a3=kb % 3, At=At: s.activation(
                            At[a3][:], PS[:, sl * 1024: (sl + 1) * 1024], AF.Exp, scale=ISC),
                            waits=ewts, inc="AC")
                    if kk >= 2:
                        kb = kk - 2
                        avlast = None
                        for hx in range(2):
                            hh = 2 * hp + hx
                            avw = [("AC", ex_id[kb])] if hx == 0 else []
                            if kb == 0 and hx == 0:
                                avw.append(("DV", vones_dv))
                                if prev_ot is not None:
                                    avw.append(("DV", prev_ot))
                            avlast = P.op("tensor", lambda t, hh=hh, kb=kb, hx=hx, a3=kb % 3, Vt=Vt, At=At: t.matmul(
                                PS[0:65, 2048 + hx * 512: 2048 + (hx + 1) * 512],
                                Vt[:, kb * 520 + hh * 65: kb * 520 + (hh + 1) * 65],
                                At[a3][:, hx * 512:(hx + 1) * 512],
                                start=(kb == 0), stop=(kb == 31)),
                                waits=avw, inc="PE")
                        av_id[kb] = avlast
                prev_exp = {0: ex_id[30], 1: ex_id[31]}
                prev_av = av_id[31]
                for hx in range(2):
                    h = hg * 8 + 2 * hp + hx
                    oc2, rb2 = h // 2, (h % 2) * 64
                    P.op("scalar", lambda s, hx=hx, rsrow=rsrow: s.activation(
                        rsrow[0:1, :], PS[64:65, 2048 + hx * 512: 2048 + (hx + 1) * 512], AF.Ln),
                        waits=[("PE", av_id[31])], inc="AC")
                    rc = P.op("scalar", lambda s, rsrow=rsrow, rsrow2=rsrow2: s.activation(
                        rsrow2[0:1, :], rsrow[0:1, :], AF.Exp, scale=-1.0), inc="AC")
                    bcm = P.op("tensor", lambda t, rsrow2=rsrow2: t.matmul(
                        PS[0:64, 3072:3584], ones_f[0:1, 0:64], rsrow2[0:1, :], start=True, stop=True),
                        waits=[("AC", rc)], inc="PE")
                    bce = P.op("scalar", lambda s, bc_sb=bc_sb: s.copy(bc_sb[:], PS[0:64, 3072:3584]),
                               waits=[("PE", bcm), ("DV", prev_ot)], inc="AC")
                    prev_ot = P.op("vector", lambda v, oc2=oc2, rb2=rb2, qb=qb, hx=hx, bc_sb=bc_sb: v.tensor_mul(
                        OT[rb2:rb2 + 64, oc2 * T + qb * TBL: oc2 * T + (qb + 1) * TBL],
                        PS[0:64, 2048 + hx * 512: 2048 + (hx + 1) * 512], bc_sb[:]),
                        waits=[("AC", bce)], inc="DV")
        P.barrier()

    if DBG:
        P.op("sync", lambda s: s.dma_start(out=dbg_ot, in_=OT[:]), inc="SO", k=16)

    Dz = Arena(MSC, P1, P1 + 64 * KB, TOP, "D")
    ares = Dz.al("ares", [128, DCH * T], F32)
    wo_sb = Dz.al("wo", [128, DCH * KB], BF16)
    xftl = [Dz.al(f"xf{i}", [128, TBL], F32) for i in range(2)]
    x1tl = [Dz.al(f"x1{i}", [128, TBL], BF16) for i in range(2)]
    sqD = [Dz.al(f"sq{i}", [128, TBL], F32) for i in range(2)]
    ssD = MSC[0:1, 0:T]
    sqrD = MSC[0:1, 2 * KB: 2 * KB + T]
    lg_sb = Dz.al("lg", [128, 16 * NE], F32)
    mxt = Dz.al("mxt", [128, 16], F32)
    sct = Dz.al("sct", [128, 16], F32)
    mkt = Dz.al("mkt", [128, NE], F32)
    wkt = Dz.al("wkt", [128, NE], F32)
    sct2 = Dz.al("sct2", [16, 128], F32)
    sct3 = Dz.al("sct3", [16, 128], F32)

    wol = ("WO", P.op("gpsimd", lambda g: g.dma_start(
        out=wo_sb[:].rearrange("p (c o) -> p c o", c=DCH),
        in_=wo.rearrange("(c p) o -> p c o", p=128)), inc="WO", k=16))

    d1_dv = {}
    for och in range(DCH):
        for tb in range(4):
            i = och * 4 + tb
            pb = i % 2
            smg = f"XG{i % 2}"
            lx = (smg, P.op("sync", lambda s, och=och, tb=tb, sl=i % 2: s.dma_start(
                out=x1tl[sl][:], in_=xh_h[och * 128:(och + 1) * 128, tb * TBL:(tb + 1) * TBL]),
                waits=[("DV", d1_dv.get(i - 2))], inc=smg, k=16))
            mmlast = None
            for cin in range(DCH):
                wts = []
                if cin == 0:
                    wts.append(wol)
                    if i >= 2:
                        wts.append(("DV", d1_dv[i - 2]))
                mmlast = P.op("tensor", lambda t, och=och, tb=tb, cin=cin, pb=pb: t.matmul(
                    BK(pb), wo_sb[:, cin * KB + och * 128: cin * KB + och * 128 + 128],
                    OT[:, cin * T + tb * TBL: cin * T + (tb + 1) * TBL],
                    start=(cin == 0), stop=(cin == DCH - 1)),
                    waits=wts, inc="PE")
            d1_dv[i] = P.op("vector", lambda v, och=och, tb=tb, pb=pb, sl=i % 2: v.scalar_tensor_tensor(
                ares[:, och * T + tb * TBL: och * T + (tb + 1) * TBL],
                BK(pb), bo_sb[:, och:och + 1], x1tl[sl][:], ALU.add, ALU.add),
                waits=[("PE", mmlast), lx], inc="DV")

    def ln_stats(buf, cstride, colbase, width, sqr_ring, ssr, sqrr):
        ntb = width // TBL
        mm_l, cp_l = {}, {}
        for tb in range(ntb):
            for chn in range(DCH):
                i = tb * DCH + chn
                sqb = sqr_ring[i % 2]
                col = chn * cstride + colbase + tb * TBL
                sq_ac = P.op("scalar", lambda s, sqb=sqb, col=col, buf=buf: s.activation(
                    sqb[:], buf[:, col:col + TBL], AF.Square),
                    waits=[("PE", mm_l.get(i - 2))], inc="AC")
                pb = (tb % 2) * 2
                P.op("tensor", lambda t, buf=buf, col=col, pb=pb, chn=chn: t.matmul(
                    BK(pb)[0:1, :], col1_f[:], buf[:, col:col + TBL],
                    start=(chn == 0), stop=(chn == DCH - 1)),
                    waits=([("DV", cp_l.get(tb - 2))] if chn == 0 else []), inc="PE")
                mm_l[i] = P.op("tensor", lambda t, sqb=sqb, pb=pb, chn=chn: t.matmul(
                    BK(pb + 1)[0:1, :], col1_f[:], sqb[:],
                    start=(chn == 0), stop=(chn == DCH - 1)),
                    waits=[("AC", sq_ac)], inc="PE")
            pb = (tb % 2) * 2
            P.op("vector", lambda v, tb=tb, pb=pb: v.tensor_copy(
                ssr[0:1, tb * TBL:(tb + 1) * TBL], BK(pb)[0:1, :]),
                waits=[("PE", mm_l[tb * DCH + DCH - 1])], inc="DV")
            cp_l[tb] = P.op("vector", lambda v, tb=tb, pb=pb: v.tensor_copy(
                sqrr[0:1, tb * TBL:(tb + 1) * TBL], BK(pb + 1)[0:1, :]), inc="DV")

    P.barrier()
    ln_stats(ares, T, 0, T, sqD, ssD, sqrD)
    rm2 = row_math(ssD, sqrD, ar2, br2, br2, T)

    d4_dv = {}
    for tb in range(4):
        pb = 4 + (tb % 2) * 2
        P.op("tensor", lambda t, tb=tb, pb=pb: t.matmul(
            BK(pb), ones_f[:], ar2[0:1, tb * TBL:(tb + 1) * TBL], start=True, stop=True),
            waits=[("DV", rm2) if tb == 0 else ("DV", d4_dv.get((tb - 2, DCH - 1)))], inc="PE")
        bcb = P.op("tensor", lambda t, tb=tb, pb=pb: t.matmul(
            BK(pb + 1), ones_f[:], br2[0:1, tb * TBL:(tb + 1) * TBL], start=True, stop=True), inc="PE")
        for och in range(DCH):
            i = tb * DCH + och
            smf = f"XF{i % 2}"
            lx = (smf, P.op("sync", lambda s, och=och, tb=tb, sl=i % 2: s.dma_start(
                out=xftl[sl][:], in_=xT[och * 128:(och + 1) * 128, tb * TBL:(tb + 1) * TBL]),
                waits=[("DV", d4_dv.get((tb, och - 2)) or d4_dv.get((tb - 1, DCH - 2 + och)))], inc=smf, k=16))
            P.op("vector", lambda v, och=och, tb=tb, pb=pb, sl=i % 2: v.tensor_mul(
                sqD[sl][:], ares[:, och * T + tb * TBL: och * T + (tb + 1) * TBL], BK(pb)),
                waits=[("PE", bcb)], inc="DV")
            P.op("vector", lambda v, pb=pb, sl=i % 2: v.tensor_add(sqD[sl][:], sqD[sl][:], BK(pb + 1)), inc="DV")
            d4_dv[(tb, och)] = P.op("vector", lambda v, och=och, tb=tb, sl=i % 2: v.tensor_add(
                ares[:, och * T + tb * TBL: och * T + (tb + 1) * TBL], sqD[sl][:], xftl[sl][:]),
                waits=[lx], inc="DV")
    for och in range(DCH):
        P.op("sync", lambda s, och=och: s.dma_start(
            out=y1_h[och * 128:(och + 1) * 128, :], in_=ares[:, och * T:(och + 1) * T]),
            waits=[("DV", d4_dv[(3, och)])], inc="Y1S", k=16)

    P.barrier()
    ln_stats(ares, T, 0, T, sqD, ssD, sqrD)
    rm3 = row_math(ssD, sqrD, ar2, br2, br2, T)

    d7_dv = {}
    for tb in range(4):
        pb = 4 + (tb % 2) * 2
        P.op("tensor", lambda t, tb=tb, pb=pb: t.matmul(
            BK(pb), ones_f[:], ar2[0:1, tb * TBL:(tb + 1) * TBL], start=True, stop=True),
            waits=[("DV", rm3) if tb == 0 else ("DV", d7_dv.get((tb - 2, DCH - 1)))], inc="PE")
        bcb = P.op("tensor", lambda t, tb=tb, pb=pb: t.matmul(
            BK(pb + 1), ones_f[:], br2[0:1, tb * TBL:(tb + 1) * TBL], start=True, stop=True), inc="PE")
        for och in range(DCH):
            i = tb * DCH + och
            P.op("vector", lambda v, och=och, tb=tb, pb=pb, sl=i % 2: v.tensor_mul(
                sqD[sl][:], ares[:, och * T + tb * TBL: och * T + (tb + 1) * TBL], BK(pb)),
                waits=[("PE", bcb)], inc="DV")
            d7_dv[(tb, och)] = P.op("vector", lambda v, och=och, tb=tb, pb=pb, sl=i % 2: v.tensor_add(
                x2b[:, och * T + tb * TBL: och * T + (tb + 1) * TBL], sqD[sl][:], BK(pb + 1)), inc="DV")
    x2_done = P.n("DV")

    P.barrier()
    rk_dv = {}
    for ttl in range(16):
        mmlast = None
        for chn in range(DCH):
            wts = []
            if chn == 0 and ttl >= 2:
                wts.append(("DV", rk_dv[ttl - 2]))
            mmlast = P.op("tensor", lambda t, ttl=ttl, chn=chn: t.matmul(
                PS[0:128, 2048 + (ttl % 2) * 512: 2048 + (ttl % 2) * 512 + NE],
                x2b[:, chn * T + ttl * 128: chn * T + (ttl + 1) * 128],
                gw_sb[:, chn * NE:(chn + 1) * NE],
                start=(chn == 0), stop=(chn == DCH - 1)),
                waits=wts, inc="PE")
        P.op("vector", lambda v, ttl=ttl: v.tensor_copy(
            lg_sb[:, ttl * NE:(ttl + 1) * NE], PS[0:128, 2048 + (ttl % 2) * 512: 2048 + (ttl % 2) * 512 + NE]),
            waits=[("PE", mmlast)], inc="DV")
        P.op("vector", lambda v, ttl=ttl: v.tensor_reduce(
            mxt[:, ttl:ttl + 1], lg_sb[:, ttl * NE:(ttl + 1) * NE], AX.X, ALU.max), inc="DV")
        P.op("vector", lambda v, ttl=ttl: v.tensor_single_scalar(
            mkt[:], lg_sb[:, ttl * NE:(ttl + 1) * NE], mxt[:, ttl:ttl + 1], ALU.is_equal), inc="DV")
        P.op("vector", lambda v: v.tensor_mul(wkt[:], mkt[:], iotaw[:]), inc="DV")
        P.op("vector", lambda v, ttl=ttl: v.tensor_reduce(sct[:, ttl:ttl + 1], wkt[:], AX.X, ALU.max), inc="DV")
        P.op("vector", lambda v, ttl=ttl: v.tensor_single_scalar(
            mkt[:], wkt[:], sct[:, ttl:ttl + 1], ALU.is_equal), inc="DV")
        mb_dv = P.op("vector", lambda v, ttl=ttl: v.tensor_copy(
            mbf[:, ttl * NE:(ttl + 1) * NE], mkt[:]), inc="DV")
        rlast = None
        for j in range(ttl + 1):
            lt = ones2 if j < ttl else LS
            rlast = P.op("tensor", lambda t, j=j, lt=lt, ttl=ttl: t.matmul(
                PS[0:128, 3584:3584 + NE], lt[:], mbf[:, j * NE:(j + 1) * NE],
                start=(j == 0), stop=(j == ttl)),
                waits=[("DV", mb_dv), ("DV", rk_dv.get(ttl - 1))] if j == 0 else [], inc="PE")
        P.op("vector", lambda v: v.tensor_mul(wkt[:], mkt[:], PS[0:128, 3584:3584 + NE]),
             waits=[("PE", rlast)], inc="DV")
        P.op("vector", lambda v: v.tensor_reduce(rkc[:, 0:1], wkt[:], AX.X, ALU.add), inc="DV")
        P.op("vector", lambda v: v.tensor_mul(wkt[:], mkt[:], ecaps[:]), inc="DV")
        P.op("vector", lambda v: v.tensor_reduce(rkc[:, 1:2], wkt[:], AX.X, ALU.add), inc="DV")
        P.op("vector", lambda v: v.tensor_add(rkc[:, 2:3], rkc[:, 0:1], rkc[:, 1:2]), inc="DV")
        P.op("vector", lambda v: v.tensor_single_scalar(rkc[:, 3:4], rkc[:, 0:1], float(CAP) - 0.5, ALU.is_ge), inc="DV")
        rk_dv[ttl] = P.op("vector", lambda v, ttl=ttl: v.scalar_tensor_tensor(
            sct2b[:, ttl:ttl + 1], rkc[:, 3:4], 8192.0, rkc[:, 2:3], ALU.mult, ALU.add), inc="DV")
    sg_ac = P.op("scalar", lambda s: s.activation(sct[:], mxt[:], AF.Sigmoid),
                 waits=[("DV", P.n("DV"))], inc="AC")
    trs = P.op("tensor", lambda t: t.transpose(PS[0:16, 3072:3200], sct[:], identF[:]),
               waits=[("AC", sg_ac), ("DV", w_ident)], inc="PE")
    sc_dv = P.op("vector", lambda v: v.tensor_copy(sct2[:], PS[0:16, 3072:3200]),
                 waits=[("PE", trs)], inc="DV")
    sc_xl = ("SC", P.op("gpsimd", lambda g: g.dma_start(out=scrow[0:1, :], in_=sct2[:]),
                 waits=[("DV", sc_dv)], inc="SC", k=16))
    trs2 = P.op("tensor", lambda t: t.transpose(PS[0:16, 3200:3328], sct2b[:], identF[:]),
                waits=[("DV", rk_dv[15])], inc="PE")
    sc3_dv = P.op("vector", lambda v: v.tensor_copy(sct3[:], PS[0:16, 3200:3328]),
                  waits=[("PE", trs2)], inc="DV")
    sr_xl = ("SR", P.op("sync", lambda s: s.dma_start(out=slotrow[0:1, :], in_=sct3[:]),
        waits=[("DV", sc3_dv)], inc="SR", k=16))
    if DBG:
        P.op("sync", lambda s: s.dma_start(out=dbg_x2, in_=x2b[:]), waits=[("DV", x2_done)], inc="SO", k=16)
        P.op("sync", lambda s: s.dma_start(out=dbg_slot, in_=slotrow[0:1, :]), waits=[sr_xl], inc="SO", k=16)
        P.op("sync", lambda s: s.dma_start(out=dbg_sc, in_=scrow[0:1, :]), waits=[sc_xl], inc="SO", k=16)
        P.op("sync", lambda s: s.dma_start(out=dbg_ares, in_=ares[:]), inc="SO", k=16)

    P.barrier()
    MB2 = MSC.bitcast(BF16)
    PSB = PS.bitcast(BF16)
    X_tok = MB2[:, 0:16 * KB]
    xe = MB2[:, 16 * KB:16 * KB + DCH * CAP]
    Pg = MB2[:, 20 * KB:20 * KB + 16 * CAP]
    dnb = MB2[:, 20 * KB:20 * KB + 4 * KB]
    PTt = MB2[:, 24 * KB:24 * KB + 4 * TBL]
    slot_bc = MSC[:, 14 * KB:14 * KB + T]
    ssE = MSC[0:1, 8 * KB:8 * KB + T]
    sqrE = MSC[0:1, 10 * KB:10 * KB + T]
    y1tl = [MSC[:, 12 * KB + i * TBL:12 * KB + (i + 1) * TBL] for i in range(2)]
    sqE = [MSC[:, 12 * KB + KB + i * TBL:12 * KB + KB + (i + 1) * TBL] for i in range(2)]
    faccv = MSC[:, 16 * KB:16 * KB + DCH * T]

    E = Arena(MSC, P1, P1 + 128 * KB, TOP, "E")
    wg_sb = E.al("wg", [128, DCH * TBL], BF16)
    wu_sb = E.al("wu", [128, DCH * TBL], BF16)
    wd_sb = E.al("wd", [128, 8 * KB], BF16)

    gs = E.al("gs", [128, 8 * TBL], BF16)
    um = [E.al(f"um{i}", [128, TBL], BF16) for i in range(2)]

    last_g = last_u = last_d = None
    acc_dv = {}
    bank_rd = {}

    npass = 0
    for fh in range(2):
        for fq in range(2):
            wgl = ("WG", P.op("gpsimd", lambda g, fh=fh, fq=fq: g.dma_start(
                out=wg_sb[:].rearrange("p (c f) -> p c f", c=DCH),
                in_=sg[:, fh * KB + fq * TBL: fh * KB + (fq + 1) * TBL].rearrange("(c p) f -> p c f", p=128)),
                waits=[("PE", last_g)], inc="WG", k=16))
            wul = ("WU", P.op("gpsimd", lambda g, fh=fh, fq=fq: g.dma_start(
                out=wu_sb[:].rearrange("p (c f) -> p c f", c=DCH),
                in_=su[:, fh * KB + fq * TBL: fh * KB + (fq + 1) * TBL].rearrange("(c p) f -> p c f", p=128)),
                waits=[("PE", last_u)], inc="WU", k=16))
            if fq == 0:
                wdl = ("WD", P.op("gpsimd", lambda g, fh=fh: g.dma_start(
                    out=wd_sb[:].rearrange("p (c dd) -> p c dd", c=8),
                    in_=sd[fh * KB:(fh + 1) * KB, :].rearrange("(c p) dd -> p c dd", p=128)),
                    waits=[("PE", last_d)], inc="WD", k=16))
            for tb in range(4):
                gsil, um_dv, hdone = {}, {}, {}
                for fc in range(4):
                    pb = fc % 2
                    mmlast = None
                    for chn in range(DCH):
                        wts = []
                        if chn == 0:
                            wts.append(wgl)
                            wts.append(bank_rd.get(pb))
                            if fc < 2:
                                wts.append(("PE", last_d))
                        mmlast = P.op("tensor", lambda t, chn=chn, fc=fc, fq=fq, tb=tb, pb=pb: t.matmul(
                            BK(pb), wg_sb[:, chn * TBL + fc * 128: chn * TBL + fc * 128 + 128],
                            x2b[:, chn * T + tb * TBL: chn * T + (tb + 1) * TBL],
                            start=(chn == 0), stop=(chn == DCH - 1)),
                            waits=wts, inc="PE")
                    last_g = mmlast
                    gsil[fc] = P.op("scalar", lambda st, fc=fc, pb=pb: st.activation(
                        gs[:, fc * TBL:(fc + 1) * TBL], BK(pb), AF.Silu),
                        waits=[("PE", mmlast)], inc="AC")
                    bank_rd[pb] = ("AC", gsil[fc])
                for fc in range(4):
                    pb = fc % 2
                    mmlast = None
                    for chn in range(DCH):
                        wts = []
                        if chn == 0:
                            wts.append(wul)
                            wts.append(bank_rd.get(pb))
                        mmlast = P.op("tensor", lambda t, chn=chn, fc=fc, tb=tb, pb=pb: t.matmul(
                            BK(pb), wu_sb[:, chn * TBL + fc * 128: chn * TBL + fc * 128 + 128],
                            x2b[:, chn * T + tb * TBL: chn * T + (tb + 1) * TBL],
                            start=(chn == 0), stop=(chn == DCH - 1)),
                            waits=wts, inc="PE")
                    last_u = mmlast
                    uw = [("PE", mmlast), ("PL", hdone.get(fc - 2) or P.n("PL"))]
                    um_dv[fc] = P.op("vector", lambda v, fc=fc, pb=pb: v.tensor_copy(
                        um[fc % 2][:], BK(pb)), waits=uw, inc="DV")
                    bank_rd[pb] = ("DV", um_dv[fc])
                    hdone[fc] = P.op("gpsimd", lambda g, fc=fc: g.tensor_mul(
                        gs[:, fc * TBL:(fc + 1) * TBL], gs[:, fc * TBL:(fc + 1) * TBL], um[fc % 2][:]),
                        waits=[("DV", um_dv[fc]), ("AC", gsil[fc])], inc="PL")
                for dch in range(DCH):
                    pb = 2 + dch % 2
                    mmlast = None
                    for fc in range(4):
                        wts = []
                        if fc == 0:
                            wts.append(wdl)
                            wts.append(("PL", hdone[3]))
                            wts.append(bank_rd.get(pb))
                        mmlast = P.op("tensor", lambda t, dch=dch, fc=fc, fq=fq, pb=pb: t.matmul(
                            BK(pb), wd_sb[:, (fq * 4 + fc) * KB + dch * 128: (fq * 4 + fc) * KB + dch * 128 + 128],
                            gs[:, fc * TBL:(fc + 1) * TBL],
                            start=(fc == 0), stop=(fc == 3)),
                            waits=wts, inc="PE")
                    last_d = mmlast
                    if npass == 0:
                        acc_dv[(tb, dch)] = P.op("vector", lambda v, dch=dch, tb=tb, pb=pb: v.tensor_copy(
                            faccv[:, dch * T + tb * TBL: dch * T + (tb + 1) * TBL], BK(pb)),
                            waits=[("PE", mmlast)], inc="DV")
                    else:
                        acc_dv[(tb, dch)] = P.op("vector", lambda v, dch=dch, tb=tb, pb=pb: v.tensor_add(
                            faccv[:, dch * T + tb * TBL: dch * T + (tb + 1) * TBL],
                            faccv[:, dch * T + tb * TBL: dch * T + (tb + 1) * TBL], BK(pb)),
                            waits=[("PE", mmlast)], inc="DV")
                    bank_rd[pb] = ("DV", acc_dv[(tb, dch)])
            npass += 1

    if XBAR & 1:
        P.barrier()
    if DBG:
        P.op("sync", lambda s: s.dma_start(out=dbg_shr, in_=faccv), waits=[("DV", P.n("DV"))], inc="SO", k=16)
        for _e2 in ENGINES:
            P.pending[_e2].append(("SO", P.n("SO")))
    shared_pe = P.n("PE")
    cv_dv = {}
    for tb in range(4):
        pb = 6 + tb % 2
        bcs = P.op("tensor", lambda t, tb=tb, pb=pb: t.matmul(
            BK(pb), ones_b[0:1, :], scrow[0:1, tb * TBL:(tb + 1) * TBL],
            start=True, stop=True),
            waits=([sc_xl, ("PE", shared_pe)] if tb == 0 else []) +
                  [bank_rd.get(pb)], inc="PE")
        for chn in range(DCH):
            cv_dv[(tb, chn)] = P.op("vector", lambda v, chn=chn, tb=tb, pb=pb: v.tensor_mul(
                x2b[:, chn * T + tb * TBL: chn * T + (tb + 1) * TBL],
                x2b[:, chn * T + tb * TBL: chn * T + (tb + 1) * TBL], BK(pb)),
                waits=[("PE", bcs)], inc="DV")
            bank_rd[pb] = ("DV", cv_dv[(tb, chn)])
    P.barrier()
    bank_rd = {}
    tr_dv = {}
    for tt in range(16):
        for dch in range(DCH):
            j = tt * DCH + dch
            sl = j % 2
            po = 6144 + 1024 * sl
            trp = P.op("tensor", lambda t, tt=tt, dch=dch, po=po: t.transpose(
                PSB[:, po:po + 128], x2b[:, dch * T + tt * 128: dch * T + (tt + 1) * 128], identB[:]),
                waits=[("DV", tr_dv.get(j - 2))], inc="PE")
            tr_dv[j] = P.op("vector", lambda v, tt=tt, dch=dch, po=po: v.tensor_copy(
                X_tok[:, tt * KB + dch * 128: tt * KB + dch * 128 + 128], PSB[:, po:po + 128]),
                waits=[("PE", trp)], inc="DV")
    if XBAR & 2:
        P.barrier()
    sb_ac = {}
    for tb in range(4):
        pb = 4 + tb % 2
        bcm = P.op("tensor", lambda t, tb=tb, pb=pb: t.matmul(
            BK(pb), ones_f[0:1, :], slotrow[0:1, tb * TBL:(tb + 1) * TBL], start=True, stop=True),
            waits=[sr_xl if tb == 0 else None, ("AC", sb_ac.get(tb - 2))], inc="PE")
        sb_ac[tb] = P.op("scalar", lambda st, tb=tb, pb=pb: st.copy(
            slot_bc[:, tb * TBL:(tb + 1) * TBL], BK(pb)),
            waits=[("PE", bcm)], inc="AC")
        bank_rd[pb] = ("AC", sb_ac[tb])
    P.barrier()

    last_scatter = None
    sc_fa = {}
    for e in range(NE):
        pg_dv = {}
        for tt in range(16):
            pg_dv[tt] = P.op("vector", lambda v, tt=tt, e=e: v.tensor_scalar(
                Pg[:, tt * CAP:(tt + 1) * CAP], iota512[:], sct2b[:, tt:tt + 1], -float(e * CAP),
                ALU.subtract, ALU.is_equal),
                waits=[("PE", last_scatter), ("SR", P.n("SR"))] if tt == 0 else [], inc="DV")
        pg_done = pg_dv[15]
        if XBAR & 4:
            P.barrier()
        ge_dv = {}
        for dh in range(2):
            mml = {}
            for dc in range(4):
                dch = dh * 4 + dc
                mmlast = None
                for tt in range(16):
                    wts = []
                    if tt == 0:
                        wts.append(("DV", pg_done))
                        wts.append(bank_rd.get(dc))
                    mmlast = P.op("tensor", lambda t, tt=tt, dch=dch, dc=dc: t.matmul(
                        BK(dc), X_tok[:, tt * KB + dch * 128: tt * KB + dch * 128 + 128],
                        Pg[:, tt * CAP:(tt + 1) * CAP],
                        start=(tt == 0), stop=(tt == 15)),
                        waits=wts, inc="PE")
                mml[dc] = mmlast
            for dc in range(4):
                dch = dh * 4 + dc
                ge_dv[dc] = P.op("vector", lambda v, dch=dch, dc=dc: v.tensor_copy(
                    xe[:, dch * CAP:(dch + 1) * CAP], BK(dc)),
                    waits=[("PE", mml[dc])], inc="DV")
                bank_rd[dc] = ("DV", ge_dv[dc])
        xe_done = P.n("DV")
        if XBAR & 8:
            P.barrier()
        if DBG and e == 0:
            P.op("sync", lambda s: s.dma_start(out=dbg_xe, in_=xe), waits=[("DV", xe_done)], inc="SO", k=16)
            P.op("sync", lambda s: s.dma_start(out=dbg_xtok, in_=X_tok), inc="SO", k=16)
            for _e2 in ENGINES:
                P.pending[_e2].append(("SO", P.n("SO")))
        dn_dv = {}
        for fh in range(2):
            gsil, um_dv, hdone = {}, {}, {}
            for fq in range(2):
                wgl = ("WG", P.op("gpsimd", lambda g, e=e, fh=fh, fq=fq: g.dma_start(
                    out=wg_sb[:].rearrange("p (c f) -> p c f", c=DCH),
                    in_=rg[e][:, fh * KB + fq * TBL: fh * KB + (fq + 1) * TBL].rearrange("(c p) f -> p c f", p=128)),
                    waits=[("PE", last_g)], inc="WG", k=16))
                wul = ("WU", P.op("gpsimd", lambda g, e=e, fh=fh, fq=fq: g.dma_start(
                    out=wu_sb[:].rearrange("p (c f) -> p c f", c=DCH),
                    in_=ru[e][:, fh * KB + fq * TBL: fh * KB + (fq + 1) * TBL].rearrange("(c p) f -> p c f", p=128)),
                    waits=[("PE", last_u)], inc="WU", k=16))
                if fq == 0:
                    wdl = ("WD", P.op("gpsimd", lambda g, e=e, fh=fh: g.dma_start(
                        out=wd_sb[:].rearrange("p (c dd) -> p c dd", c=8),
                        in_=rd[e][fh * KB:(fh + 1) * KB, :].rearrange("(c p) dd -> p c dd", p=128)),
                        waits=[("PE", last_d)], inc="WD", k=16))
                for fc in range(4):
                    fcl = fq * 4 + fc
                    pb = fc % 2
                    mmlast = None
                    for chn in range(DCH):
                        wts = []
                        if chn == 0:
                            wts.append(wgl)
                            wts.append(bank_rd.get(pb))
                            if fcl < 2:
                                wts.append(("PE", last_d))
                                wts.append(("DV", xe_done))
                        mmlast = P.op("tensor", lambda t, chn=chn, fcl=fcl, fc=fc, pb=pb: t.matmul(
                            BK(pb), wg_sb[:, chn * TBL + fc * 128: chn * TBL + fc * 128 + 128],
                            xe[:, chn * CAP:(chn + 1) * CAP],
                            start=(chn == 0), stop=(chn == DCH - 1)),
                            waits=wts, inc="PE")
                    last_g = mmlast
                    gsil[fcl] = P.op("scalar", lambda st, fcl=fcl, pb=pb: st.activation(
                        gs[:, fcl * TBL:(fcl + 1) * TBL], BK(pb), AF.Silu),
                        waits=[("PE", mmlast)], inc="AC")
                    bank_rd[pb] = ("AC", gsil[fcl])
                for fc in range(4):
                    fcl = fq * 4 + fc
                    pb = fc % 2
                    mmlast = None
                    for chn in range(DCH):
                        wts = []
                        if chn == 0:
                            wts.append(wul)
                            wts.append(bank_rd.get(pb))
                        mmlast = P.op("tensor", lambda t, chn=chn, fc=fc, pb=pb: t.matmul(
                            BK(pb), wu_sb[:, chn * TBL + fc * 128: chn * TBL + fc * 128 + 128],
                            xe[:, chn * CAP:(chn + 1) * CAP],
                            start=(chn == 0), stop=(chn == DCH - 1)),
                            waits=wts, inc="PE")
                    last_u = mmlast
                    uw = [("PE", mmlast), ("PL", hdone.get(fcl - 2) or P.n("PL"))]
                    um_dv[fcl] = P.op("vector", lambda v, fcl=fcl, pb=pb: v.tensor_copy(
                        um[fcl % 2][:], BK(pb)), waits=uw, inc="DV")
                    bank_rd[pb] = ("DV", um_dv[fcl])
                    hdone[fcl] = P.op("gpsimd", lambda g, fcl=fcl: g.tensor_mul(
                        gs[:, fcl * TBL:(fcl + 1) * TBL], gs[:, fcl * TBL:(fcl + 1) * TBL], um[fcl % 2][:]),
                        waits=[("DV", um_dv[fcl]), ("AC", gsil[fcl])], inc="PL")
            if XBAR & 16:
                P.barrier()
            for stile in range(4):
                for dblk in range(2):
                    gidx = fh * 8 + stile * 2 + dblk
                    pb = 2 + gidx % 2
                    mmlast = None
                    for fcl in range(8):
                        wts = []
                        if fcl == 0:
                            wts.append(wdl)
                            wts.append(("PL", hdone[7]))
                            wts.append(bank_rd.get(pb))
                        mmlast = P.op("tensor", lambda t, stile=stile, dblk=dblk, fcl=fcl, pb=pb: t.matmul(
                            BK(pb), gs[:, fcl * TBL + stile * 128: fcl * TBL + stile * 128 + 128],
                            wd_sb[:, fcl * KB + dblk * TBL: fcl * KB + (dblk + 1) * TBL],
                            start=(fcl == 0), stop=(fcl == 7)),
                            waits=wts, inc="PE")
                    last_d = mmlast
                    if fh == 0:
                        dn_dv[gidx] = P.op("vector", lambda v, stile=stile, dblk=dblk, pb=pb: v.tensor_copy(
                            dnb[:, stile * KB + dblk * TBL: stile * KB + (dblk + 1) * TBL], BK(pb)),
                            waits=[("PE", mmlast)], inc="DV")
                        bank_rd[pb] = ("DV", dn_dv[gidx])
                    else:
                        dn_dv[gidx] = P.op("vector", lambda v, stile=stile, dblk=dblk, pb=pb: v.tensor_add(
                            dnb[:, stile * KB + dblk * TBL: stile * KB + (dblk + 1) * TBL],
                            dnb[:, stile * KB + dblk * TBL: stile * KB + (dblk + 1) * TBL], BK(pb)),
                            waits=[("PE", mmlast)], inc="DV")
                        bank_rd[pb] = ("DV", dn_dv[gidx])
        dn_done = P.n("DV")
        if XBAR & 32:
            P.barrier()
        if DBG and e == 0:
            P.op("sync", lambda s: s.dma_start(out=dbg_dnb, in_=dnb), waits=[("DV", dn_done)], inc="SO", k=16)
            for _e2 in ENGINES:
                P.pending[_e2].append(("SO", P.n("SO")))
        if XBAR & 64:
            P.barrier()
        for tblk in range(4):
            pt_dv = {}
            for stile in range(4):
                pt_dv[stile] = P.op("vector", lambda v, stile=stile, tblk=tblk, e=e: v.tensor_scalar(
                    PTt[:, stile * TBL:(stile + 1) * TBL],
                    slot_bc[:, tblk * TBL:(tblk + 1) * TBL], iotacol[:],
                    float(e * CAP + stile * 128), ALU.subtract, ALU.is_equal),
                    waits=[("DV", dn_done)] if stile == 0 and tblk == 0 else [], inc="DV")
            if DBG and e == 0 and tblk == 0:
                P.op("sync", lambda s: s.dma_start(out=dbg_pt, in_=PTt), waits=[("DV", pt_dv[3])], inc="SO", k=16)
                P.op("sync", lambda s: s.dma_start(out=dbg_sbc, in_=slot_bc), inc="SO", k=16)
                for _e2 in ENGINES:
                    P.pending[_e2].append(("SO", P.n("SO")))
            for dch in range(DCH):
                pb = 4 + dch % 2
                mmlast = None
                for stile in range(4):
                    wts = []
                    if stile == 0:
                        wts.append(("DV", pt_dv[3]))
                        wts.append(bank_rd.get(pb))
                    mmlast = P.op("tensor", lambda t, stile=stile, dch=dch, tblk=tblk, pb=pb: t.matmul(
                        BK(pb), dnb[:, stile * KB + dch * 128: stile * KB + dch * 128 + 128],
                        PTt[:, stile * TBL:(stile + 1) * TBL],
                        start=(stile == 0), stop=(stile == 3)),
                        waits=wts, inc="PE")
                last_scatter = mmlast
                sc_fa[(tblk, dch)] = P.op("vector", lambda v, dch=dch, tblk=tblk, pb=pb: v.tensor_add(
                    faccv[:, dch * T + tblk * TBL: dch * T + (tblk + 1) * TBL],
                    faccv[:, dch * T + tblk * TBL: dch * T + (tblk + 1) * TBL], BK(pb)),
                    waits=[("PE", mmlast)], inc="DV")
                bank_rd[pb] = ("DV", sc_fa[(tblk, dch)])

    P.barrier()
    if DBG:
        P.op("sync", lambda s: s.dma_start(out=dbg_ffn, in_=faccv), inc="SO", k=16)
        for _e in ENGINES:
            P.pending[_e].append(("SO", P.n("SO")))
    ln_stats(faccv, T, 0, T, sqE, ssE, sqrE)
    rm4 = row_math(ssE, sqrE, ar2, br2, br2, T)
    fin_dv = {}
    for tb in range(4):
        pb = 4 + (tb % 2) * 2
        P.op("tensor", lambda t, tb=tb, pb=pb: t.matmul(
            BK(pb), ones_f[:], ar2[0:1, tb * TBL:(tb + 1) * TBL], start=True, stop=True),
            waits=[("DV", rm4) if tb == 0 else ("DV", fin_dv.get((tb - 2, DCH - 1)))], inc="PE")
        bcb = P.op("tensor", lambda t, tb=tb, pb=pb: t.matmul(
            BK(pb + 1), ones_f[:], br2[0:1, tb * TBL:(tb + 1) * TBL], start=True, stop=True), inc="PE")
        for och in range(DCH):
            i = tb * DCH + och
            sml = f"YL{i % 2}"
            ly = (sml, P.op("sync", lambda s, och=och, tb=tb, sl=i % 2: s.dma_start(
                out=y1tl[sl][:], in_=y1_h[och * 128:(och + 1) * 128, tb * TBL:(tb + 1) * TBL]),
                waits=[("DV", fin_dv.get((tb, och - 2)) or fin_dv.get((tb - 1, DCH - 2 + och)))], inc=sml, k=16))
            P.op("vector", lambda v, och=och, tb=tb, pb=pb, sl=i % 2: v.tensor_mul(
                sqE[sl][:], faccv[:, och * T + tb * TBL: och * T + (tb + 1) * TBL], BK(pb)),
                waits=[("PE", bcb)], inc="DV")
            P.op("vector", lambda v, pb=pb, sl=i % 2: v.tensor_add(sqE[sl][:], sqE[sl][:], BK(pb + 1)), inc="DV")
            fin_dv[(tb, och)] = P.op("vector", lambda v, och=och, tb=tb, sl=i % 2: v.tensor_add(
                faccv[:, och * T + tb * TBL: och * T + (tb + 1) * TBL], sqE[sl][:], y1tl[sl][:]),
                waits=[ly], inc="DV")
    for och in range(DCH):
        P.op("sync", lambda s, och=och: s.dma_start(
            out=yT[och * 128:(och + 1) * 128, :], in_=faccv[:, och * T:(och + 1) * T]),
            waits=[("DV", fin_dv[(3, och)])], inc="SO", k=16)

    P.op("sync", lambda s: s.nop(), waits=[("SO", P.n("SO"))])

    with nc.Block() as block:
        P.emit(block)
    return nc


_NC_CACHE = {}


def _get_nc():
    if "nc" not in _NC_CACHE:
        _NC_CACHE["nc"] = build_program()
    return _NC_CACHE["nc"]


def make_in_maps(L0_attn_x, L0_attn_W_qkv, L0_attn_W_out, L0_attn_b_out, L0_ffn_gate_w,
                 L0_ffn_shared_gate, L0_ffn_shared_up, L0_ffn_shared_down,
                 L0_ffn_r_gate, L0_ffn_r_up, L0_ffn_r_down):
    x = np.asarray(L0_attn_x)
    Wr = np.asarray(L0_attn_W_qkv).reshape(H, 3, HD, D)
    wq_h = np.ascontiguousarray(Wr[:, 0].reshape(H * HD, D).T)
    wk_h = np.ascontiguousarray(Wr[:, 1].reshape(H * HD, D).T)
    wv_h = np.ascontiguousarray(Wr[:, 2].reshape(H * HD, D).T)
    wo_h = np.ascontiguousarray(np.asarray(L0_attn_W_out).T)
    bo_h = np.ascontiguousarray(np.asarray(L0_attn_b_out).reshape(DCH, 128).T)
    gw_h = np.ascontiguousarray(np.asarray(L0_ffn_gate_w).T)
    sg_h = np.ascontiguousarray(np.asarray(L0_ffn_shared_gate).T)
    su_h = np.ascontiguousarray(np.asarray(L0_ffn_shared_up).T)
    sd_h = np.ascontiguousarray(np.asarray(L0_ffn_shared_down).T)
    rg_h = np.ascontiguousarray(np.transpose(np.asarray(L0_ffn_r_gate), (0, 2, 1)))
    ru_h = np.ascontiguousarray(np.transpose(np.asarray(L0_ffn_r_up), (0, 2, 1)))
    rd_h = np.ascontiguousarray(np.transpose(np.asarray(L0_ffn_r_down), (0, 2, 1)))
    in_maps = []
    for c in range(8):
        b, h = c // 2, c % 2
        xb = x[b]
        xcore = np.concatenate([xb[h * T:(h + 1) * T], xb[(1 - h) * T:(2 - h) * T]], axis=0)
        in_maps.append({
            "xT": np.ascontiguousarray(xcore.T),
            "wq": wq_h, "wk": wk_h, "wv": wv_h, "wo": wo_h, "bo": bo_h, "gw": gw_h,
            "sg": sg_h, "su": su_h, "sd": sd_h, "rg": rg_h, "ru": ru_h, "rd": rd_h,
        })
    return in_maps


def kernel(**inputs):
    in_maps = make_in_maps(**inputs)
    nc = _get_nc()
    res = run_bass_kernel_spmd(nc, in_maps, core_ids=list(range(8)))
    out = np.empty((4, S, D), np.float32)
    for c in range(8):
        b, h = c // 2, c % 2
        out[b, h * T:(h + 1) * T, :] = res.results[c]["yT"].T
    return out



# version 09
# speedup vs baseline: 1.2364x; 1.2364x over previous
import numpy as np
import concourse.bass as bass
import concourse.mybir as mybir
from concourse.bass_utils import run_bass_kernel_spmd

F32 = mybir.dt.float32
BF16 = mybir.dt.bfloat16
AF = mybir.ActivationFunctionType
ALU = mybir.AluOpType
AX = mybir.AxisListType

D, DCH = 1024, 8
CAP = 512
S = 4096
T = 2048
H, HD = 16, 64
FF = 2048
NE = 8
TBL = 512
EPS = 1e-5
ISC = 1.0 / 32.0
KB = 1024

ENGINES = ("tensor", "scalar", "vector", "gpsimd", "sync")
DBG = False
DBGKIND = "Internal"
XBAR = 3


class Planner:
    def __init__(self, nc, sems):
        self.nc = nc
        self.ops = {e: [] for e in ENGINES}
        self.sems = sems
        self.counts = {k: 0 for k in sems}
        self.pending = {e: [] for e in ENGINES}

    SELFSEM = {"vector": "DV", "scalar": "AC"}

    def op(self, eng, fn, waits=(), inc=None, k=1):
        w = self.pending[eng] + [x for x in waits if x is not None and x[1]]
        self.pending[eng] = []
        ss = self.SELFSEM.get(eng) or ("PL" if eng == "gpsimd" and inc == "PL" else None)
        if ss is not None and self.counts[ss] > 0:
            w.append((ss, self.counts[ss]))
        self.ops[eng].append((fn, w, inc, k))
        if inc is not None:
            self.counts[inc] += k
            return self.counts[inc]
        return None

    def n(self, sem):
        return self.counts[sem]

    def barrier(self):
        snap = [(s, c) for s, c in self.counts.items() if c > 0]
        for e in ENGINES:
            self.pending[e] = list(snap)

    def emit(self, block):
        for engname in ENGINES:
            def make_body(ops=self.ops[engname], sems=self.sems):
                def body(eng):
                    for fn, waits, inc, k in ops:
                        for s, v in waits:
                            eng.wait_ge(sems[s], v)
                        ins = fn(eng)
                        if inc is not None:
                            ins.then_inc(sems[inc], k)
                return body
            getattr(block, engname)(make_body())


class Arena:

    def __init__(self, master, origin, base, limit, tag):
        self.m, self.origin = master, origin
        self.off, self.limit, self.tag = base, limit, tag

    def al(self, name, shape, dtype):
        nbytes = int(np.prod(shape[1:])) * (2 if dtype == BF16 else 4)
        nbytes = (nbytes + 31) // 32 * 32
        o = self.off - self.origin
        view = self.m if dtype == F32 else self.m.bitcast(BF16)
        es = 4 if dtype == F32 else 2
        t = view[0:shape[0], o // es: o // es + int(np.prod(shape[1:]))]
        if len(shape) > 2:
            raise ValueError("arena views are 2-D")
        self.off += nbytes
        assert self.off <= self.limit, f"{self.tag}/{name}: off {self.off} > {self.limit}"
        return t


def build_program():
    nc = bass.Bass("TRN2", target_bir_lowering=False)
    xT = nc.dram_tensor("xT", [D, S], F32, kind="ExternalInput").ap()
    wq = nc.dram_tensor("wq", [D, D], F32, kind="ExternalInput").ap()
    wk = nc.dram_tensor("wk", [D, D], F32, kind="ExternalInput").ap()
    wv = nc.dram_tensor("wv", [D, D], F32, kind="ExternalInput").ap()
    wo = nc.dram_tensor("wo", [D, D], F32, kind="ExternalInput").ap()
    bo = nc.dram_tensor("bo", [128, DCH], F32, kind="ExternalInput").ap()
    gw = nc.dram_tensor("gw", [D, NE], F32, kind="ExternalInput").ap()
    sg = nc.dram_tensor("sg", [D, FF], F32, kind="ExternalInput").ap()
    su = nc.dram_tensor("su", [D, FF], F32, kind="ExternalInput").ap()
    sd = nc.dram_tensor("sd", [FF, D], F32, kind="ExternalInput").ap()
    rg = nc.dram_tensor("rg", [NE, D, FF], F32, kind="ExternalInput").ap()
    ru = nc.dram_tensor("ru", [NE, D, FF], F32, kind="ExternalInput").ap()
    rd = nc.dram_tensor("rd", [NE, FF, D], F32, kind="ExternalInput").ap()
    yT = nc.dram_tensor("yT", [D, T], F32, kind="ExternalOutput").ap()
    xh_h = nc.dram_tensor("xh_h", [D, S], BF16, kind="Internal").ap()
    y1_h = nc.dram_tensor("y1_h", [D, T], F32, kind="Internal").ap()
    if DBG:
        dbg_ot = nc.dram_tensor("dbg_ot", [128, DCH * T], BF16, kind=DBGKIND).ap()
        dbg_ares = nc.dram_tensor("dbg_ares", [128, DCH * T], F32, kind=DBGKIND).ap()
        dbg_x2 = nc.dram_tensor("dbg_x2", [128, DCH * T], BF16, kind=DBGKIND).ap()
        dbg_slot = nc.dram_tensor("dbg_slot", [1, T], F32, kind=DBGKIND).ap()
        dbg_sc = nc.dram_tensor("dbg_sc", [1, T], BF16, kind=DBGKIND).ap()
        dbg_ffn = nc.dram_tensor("dbg_ffn", [128, DCH * T], F32, kind=DBGKIND).ap()
        dbg_xe = nc.dram_tensor("dbg_xe", [128, DCH * CAP], BF16, kind=DBGKIND).ap()
        dbg_dnb = nc.dram_tensor("dbg_dnb", [128, 4 * KB], BF16, kind=DBGKIND).ap()
        dbg_xtok = nc.dram_tensor("dbg_xtok", [128, 16 * KB], BF16, kind=DBGKIND).ap()
        dbg_shr = nc.dram_tensor("dbg_shr", [128, DCH * T], F32, kind=DBGKIND).ap()
        dbg_pt = nc.dram_tensor("dbg_pt", [128, 4 * TBL], BF16, kind=DBGKIND).ap()
        dbg_sbc = nc.dram_tensor("dbg_sbc", [128, T], F32, kind=DBGKIND).ap()

    sems = {}
    for name in ("PE", "AC", "DV", "PL", "SO", "C0", "W0", "WO", "WG", "WU", "WD",
                 "MS", "SC", "Y1S", "C1", "XA0", "XA1", "XA2", "XA3", "XB0", "XB1",
                 "SH0", "SH1", "XT0", "XT1", "XF0", "XF1", "XG0", "XG1", "YL0", "YL1", "SR"):
        sems[name] = nc.semaphore(name).__enter__()
    P = Planner(nc, sems)

    GB = (nc.sbuf_base + 31) // 32 * 32
    TOP = nc.sbuf_top
    GSZ = 44 * KB
    GMST = nc.alloc_sbuf_tensor_at("GMST", [128, GSZ // 4], F32, offset=GB).ap()
    GA = Arena(GMST, GB, GB, GB + GSZ, "g")

    ones_b = GA.al("ones_b", [1, 128], BF16)
    ones_f = GA.al("ones_f", [1, 128], F32)
    col1_f = GA.al("col1_f", [128, 1], F32)
    identF = GA.al("identF", [128, 128], F32)
    gw_sb = GA.al("gw_sb", [128, DCH * NE], BF16)
    bo_sb = GA.al("bo_sb", [128, DCH], F32)
    iotaw = GA.al("iotaw", [128, NE], F32)
    epsr = GA.al("epsr", [1, 1], F32)
    ar2 = GA.al("ar2", [1, T], F32)
    br2 = GA.al("br2", [1, T], F32)
    scrow = GA.al("scrow", [1, T], BF16)
    identB = GA.al("identB", [128, 128], BF16)
    LS = GA.al("LS", [128, 128], BF16)
    ones2 = GA.al("ones2", [128, 128], BF16)
    ecaps = GA.al("ecaps", [128, NE], F32)
    iota512 = GA.al("iota512", [128, CAP], F32)
    iotacol = GA.al("iotacol", [128, 1], F32)
    mbf = GA.al("mbf", [128, 16 * NE], BF16)
    rkc = GA.al("rkc", [128, 8], F32)
    slotrow = GA.al("slotrow", [1, T], F32)
    sct2b = GA.al("sct2b", [128, 16], F32)
    P1 = (GA.off + 127) // 128 * 128
    NF32 = (TOP - P1) // 4 // 32 * 32
    MSC = nc.alloc_sbuf_tensor_at("MSC", [128, NF32], F32, offset=P1).ap()

    MB = MSC.bitcast(BF16)
    OT = MB[:, 0:DCH * T]
    x2b = MB[:, DCH * T: 2 * DCH * T]
    facc = MSC[:, 0:DCH * KB]

    PS = nc.alloc_psum_tensor("PS", [128, 4096], F32)
    BK = lambda b: PS[:, b * 512:(b + 1) * 512]

    P.op("gpsimd", lambda g: g.iota(iotaw[:], [[-1, NE]], base=NE, channel_multiplier=0,
                                    allow_small_or_imprecise_dtypes=True), inc="PL")
    P.op("gpsimd", lambda g: g.iota(identF[:], [[1, 128]], base=0, channel_multiplier=-1,
                                    allow_small_or_imprecise_dtypes=True), inc="PL")
    P.op("gpsimd", lambda g: g.iota(LS[:], [[1, 128]], base=0, channel_multiplier=-1,
                                    allow_small_or_imprecise_dtypes=True), inc="PL")
    P.op("gpsimd", lambda g: g.iota(ecaps[:], [[CAP, NE]], base=0, channel_multiplier=0,
                                    allow_small_or_imprecise_dtypes=True), inc="PL")
    P.op("gpsimd", lambda g: g.iota(iota512[:], [[1, CAP]], base=0, channel_multiplier=0,
                                    allow_small_or_imprecise_dtypes=True), inc="PL")
    P.op("gpsimd", lambda g: g.iota(iotacol[:], [[1, 1]], base=0, channel_multiplier=1,
                                    allow_small_or_imprecise_dtypes=True), inc="PL")
    ident_pl = P.n("PL")
    P.op("vector", lambda v: v.memset(ones_b[:], 1.0), inc="DV")
    P.op("vector", lambda v: v.memset(ones_f[:], 1.0), inc="DV")
    P.op("vector", lambda v: v.memset(col1_f[:], 1.0), inc="DV")
    P.op("vector", lambda v: v.memset(epsr[:], EPS), inc="DV")
    w_ident = P.op("vector", lambda v: v.tensor_single_scalar(identF[:], identF[:], 0.0, ALU.is_equal),
                   waits=[("PL", ident_pl)], inc="DV")
    P.op("vector", lambda v: v.tensor_copy(identB[:], identF[:]), inc="DV")
    P.op("vector", lambda v: v.tensor_single_scalar(LS[:], LS[:], 0.0, ALU.is_gt), inc="DV")
    P.op("vector", lambda v: v.memset(ones2[:], 1.0), inc="DV")
    init_dv = P.n("DV")
    P.barrier()
    P.op("sync", lambda s: s.dma_start(out=bo_sb[:], in_=bo), inc="C0", k=16)
    P.op("gpsimd", lambda g: g.dma_start(out=gw_sb[:].rearrange("p (c e) -> p c e", c=DCH),
                                         in_=gw.rearrange("(c p) e -> p c e", p=128)), inc="C1", k=16)

    def row_math(sumr, sqr, ar, br, tmp, width, c0=0, first_waits=None):
        sl = lambda r: r[0:1, c0:c0 + width]
        P.op("vector", lambda v: v.tensor_scalar_mul(sl(sumr), sl(sumr), 1.0 / D),
             waits=([("PE", P.n("PE"))] if first_waits is None else first_waits), inc="DV")
        P.op("vector", lambda v: v.tensor_scalar_mul(sl(sqr), sl(sqr), 1.0 / D), inc="DV")
        P.op("vector", lambda v: v.tensor_mul(sl(tmp), sl(sumr), sl(sumr)), inc="DV")
        r3 = P.op("vector", lambda v: v.tensor_sub(sl(sqr), sl(sqr), sl(tmp)), inc="DV")
        r4 = P.op("scalar", lambda s: s.activation(sl(tmp), sl(sqr), AF.Sqrt, bias=epsr[0:1, 0:1]),
                  waits=[("DV", r3)], inc="AC")
        P.op("vector", lambda v: v.reciprocal(sl(ar), sl(tmp)),
             waits=[("AC", r4)], inc="DV")
        return P.op("vector", lambda v: v.scalar_tensor_tensor(
            sl(br), sl(sumr), -1.0, sl(ar), ALU.mult, ALU.mult), inc="DV")

    A = Arena(MSC, P1, P1, TOP, "A")
    xfA = [A.al(f"xf{i}", [128, DCH * TBL], F32) for i in range(4)]
    sqA = [A.al(f"sq{i}", [128, TBL], F32) for i in range(2)]
    nscA = [A.al(f"nsc{i}", [128, TBL], F32) for i in range(2)]
    ssA = A.al("ssum", [1, S], F32)
    sqrA = A.al("ssq", [1, S], F32)
    a1r = A.al("a1r", [1, S], F32)
    b1r = A.al("b1r", [1, S], F32)
    xhA = [A.al(f"xh{i}", [128, DCH * TBL], BF16) for i in range(2)]

    NTBS = S // TBL
    mm2_id, cp_id, rm_id, nrm_id, st_id = {}, {}, {}, {}, {}
    for it in range(NTBS + 1):
        if it < NTBS:
            tb = it
            buf = xfA[tb % 4]
            sm = f"XA{tb % 4}"
            ltb = (sm, P.op("sync", lambda s, buf=buf, tb=tb: s.dma_start(
                out=buf[:].rearrange("p (c t) -> p c t", c=DCH),
                in_=xT[:, tb * TBL:(tb + 1) * TBL].rearrange("(c p) t -> p c t", p=128)),
                waits=[("DV", nrm_id.get(tb - 4))], inc=sm, k=16))
            pb = (tb % 2) * 2
            for chn in range(DCH):
                i = tb * DCH + chn
                sqb = sqA[i % 2]
                sq_ac = P.op("scalar", lambda s, buf=buf, chn=chn, sqb=sqb: s.activation(
                    sqb[:], buf[:, chn * TBL:(chn + 1) * TBL], AF.Square),
                    waits=[ltb, ("PE", mm2_id.get(i - 2))], inc="AC")
                P.op("tensor", lambda t, buf=buf, chn=chn, pb=pb: t.matmul(
                    BK(pb)[0:1, :], col1_f[:], buf[:, chn * TBL:(chn + 1) * TBL],
                    start=(chn == 0), stop=(chn == DCH - 1)),
                    waits=[ltb] + ([("DV", cp_id.get(tb - 2))] if chn == 0 else []),
                    inc="PE")
                mm2_id[i] = P.op("tensor", lambda t, sqb=sqb, pb=pb, chn=chn: t.matmul(
                    BK(pb + 1)[0:1, :], col1_f[:], sqb[:], start=(chn == 0), stop=(chn == DCH - 1)),
                    waits=[("AC", sq_ac)], inc="PE")
            P.op("vector", lambda v, tb=tb, pb=pb: v.tensor_copy(
                ssA[0:1, tb * TBL:(tb + 1) * TBL], BK(pb)[0:1, :]),
                waits=[("PE", mm2_id[tb * DCH + DCH - 1])], inc="DV")
            cp_id[tb] = P.op("vector", lambda v, tb=tb, pb=pb: v.tensor_copy(
                sqrA[0:1, tb * TBL:(tb + 1) * TBL], BK(pb + 1)[0:1, :]), inc="DV")
            rm_id[tb] = row_math(ssA, sqrA, a1r, b1r, b1r, TBL, c0=tb * TBL, first_waits=[])
        if it >= 1:
            tb = it - 1
            buf = xfA[tb % 4]
            pbb = 4 + (tb % 2) * 2
            P.op("tensor", lambda t, tb=tb, pbb=pbb: t.matmul(
                BK(pbb), ones_f[:], a1r[0:1, tb * TBL:(tb + 1) * TBL], start=True, stop=True),
                waits=[("DV", rm_id[tb]), ("DV", nrm_id.get(tb - 2))], inc="PE")
            bcb = P.op("tensor", lambda t, tb=tb, pbb=pbb: t.matmul(
                BK(pbb + 1), ones_f[:], b1r[0:1, tb * TBL:(tb + 1) * TBL], start=True, stop=True), inc="PE")
            xo = xhA[tb % 2]
            for chn in range(DCH):
                P.op("vector", lambda v, buf=buf, chn=chn, pbb=pbb, sl=chn % 2: v.tensor_mul(
                    nscA[sl][:], buf[:, chn * TBL:(chn + 1) * TBL], BK(pbb)),
                    waits=([("PE", bcb), st_id.get(tb - 2)] if chn == 0 else []), inc="DV")
                nrm_id[tb] = P.op("vector", lambda v, xo=xo, chn=chn, pbb=pbb, sl=chn % 2: v.tensor_add(
                    xo[:, chn * TBL:(chn + 1) * TBL], nscA[sl][:], BK(pbb + 1)), inc="DV")
            sms = f"SH{tb % 2}"
            st_id[tb] = (sms, P.op("gpsimd", lambda g, xo=xo, tb=tb: g.dma_start(
                out=xh_h[:, tb * TBL:(tb + 1) * TBL].rearrange("(c p) t -> p c t", p=128),
                in_=xo[:].rearrange("p (c t) -> p c t", c=DCH)),
                waits=[("DV", nrm_id[tb])], inc=sms, k=16))
    P.barrier()

    for hg in range(2):
        B = Arena(MSC, P1, P1 + 32 * KB, TOP, "B")
        Qc = B.al("Qc", [128, 4 * T], BF16)
        Kc = B.al("Kc", [128, 4 * S], BF16)
        Vt = B.al("Vt", [128, 32 * 520], BF16)
        wq_sb = B.al("wq", [128, DCH * TBL], BF16)
        wk_sb = B.al("wk", [128, DCH * TBL], BF16)
        wv_sb = B.al("wv", [128, DCH * TBL], BF16)
        xh_tb = [B.al(f"xh{i}", [128, DCH * TBL], BF16) for i in range(2)]
        At = [B.al(f"At{i}", [128, 1024], BF16) for i in range(3)]
        bc_sb = B.al("bc_sb", [64, TBL], F32)
        rsrow = B.al("rsrow", [1, TBL], F32)
        rsrow2 = B.al("rsrow2", [1, TBL], F32)

        wls = {}
        for nm, wsrc, wdst in (("q", wq, wq_sb), ("k", wk, wk_sb), ("v", wv, wv_sb)):
            P.op("gpsimd", lambda g, wsrc=wsrc, wdst=wdst, hg=hg: g.dma_start(
                out=wdst[:].rearrange("p (c o) -> p c o", c=DCH),
                in_=wsrc[:, hg * TBL:(hg + 1) * TBL].rearrange("(c p) o -> p c o", p=128)),
                inc="W0", k=16)
        for nm in ("q", "k", "v"):
            wls[nm] = ("W0", P.n("W0"))
        for tt in range(32):
            P.op("vector", lambda v, tt=tt: v.memset(
                Vt[:, tt * 520:(tt + 1) * 520].rearrange("p (h c) -> p h c", h=8)[:, :, 64:65], 1.0), inc="DV")
        vones_dv = P.n("DV")

        kq_ev, v_ev, pe_tb = [], [], {}
        for tb in range(NTBS):
            buf = xh_tb[tb % 2]
            smt = f"XT{tb % 2}"
            ltb = (smt, P.op("sync", lambda s, buf=buf, tb=tb: s.dma_start(
                out=buf[:].rearrange("p (c t) -> p c t", c=DCH),
                in_=xh_h[:, tb * TBL:(tb + 1) * TBL].rearrange("(c p) t -> p c t", p=128)),
                waits=[("PE", pe_tb.get(tb - 2))], inc=smt, k=16))
            for kind in (("k", "q") if tb < 4 else ("k",)):
                wsb, dst, dstw = (wk_sb, Kc, S) if kind == "k" else (wq_sb, Qc, T)
                for och in range(4):
                    seq = len(kq_ev)
                    pb = seq % 2
                    mmlast = None
                    for chn in range(DCH):
                        w0 = ([ltb, wls[kind]] +
                              ([("DV", kq_ev[seq - 2])] if seq >= 2 else [])) if chn == 0 else []
                        mmlast = P.op("tensor", lambda t, wsb=wsb, chn=chn, och=och, buf=buf, pb=pb: t.matmul(
                            BK(pb), wsb[:, chn * TBL + och * 128: chn * TBL + och * 128 + 128],
                            buf[:, chn * TBL:(chn + 1) * TBL],
                            start=(chn == 0), stop=(chn == DCH - 1)),
                            waits=w0, inc="PE")
                    kq_ev.append(P.op("vector", lambda v, dst=dst, dstw=dstw, och=och, tb=tb, pb=pb: v.tensor_copy(
                        dst[:, och * dstw + tb * TBL: och * dstw + (tb + 1) * TBL], BK(pb)),
                        waits=[("PE", mmlast)], inc="DV"))
            for ttl in range(4):
                tt = tb * 4 + ttl
                sv = len(v_ev)
                pb = 2 + sv % 2
                mmlast = None
                for chn in range(DCH):
                    w0 = ([ltb, wls["v"]] +
                          ([("AC", v_ev[sv - 2])] if sv >= 2 else [])) if chn == 0 else []
                    mmlast = P.op("tensor", lambda t, chn=chn, buf=buf, ttl=ttl, pb=pb, wv_sb=wv_sb: t.matmul(
                        BK(pb), buf[:, chn * TBL + ttl * 128: chn * TBL + ttl * 128 + 128],
                        wv_sb[:, chn * TBL:(chn + 1) * TBL],
                        start=(chn == 0), stop=(chn == DCH - 1)),
                        waits=w0, inc="PE")
                v_ev.append(P.op("scalar", lambda s, tt=tt, pb=pb, Vt=Vt: s.copy(
                    Vt[:, tt * 520:(tt + 1) * 520].rearrange("p (h c) -> p h c", h=8)[:, :, 0:64],
                    BK(pb).rearrange("p (h c) -> p h c", h=8)),
                    waits=[("PE", mmlast)], inc="AC"))
            pe_tb[tb] = P.n("PE")
        b_dv, b_ac = P.n("DV"), P.n("AC")

        prev_exp = None
        prev_av = None
        prev_ot = None
        for hp in range(4):
            och = hp
            for qb in range(4):
                qk_id, ex_id, av_id = {}, {}, {}
                for kk in range(34):
                    if kk < 32:
                        kb, sl = kk, kk % 2
                        for hx in range(2):
                            rb = hx * 64
                            wts = []
                            if hx == 0:
                                if kb == 0 and hp == 0 and qb == 0:
                                    wts += [("DV", b_dv), ("AC", b_ac)]
                                if kb >= 2:
                                    wts.append(("AC", ex_id[kb - 2]))
                                elif prev_exp is not None:
                                    wts.append(("AC", prev_exp[kb]))
                            qk_id[(kb, hx)] = P.op("tensor", lambda t, och=och, rb=rb, kb=kb, hx=hx, qb=qb, sl=sl, Kc=Kc, Qc=Qc: t.matmul(
                                PS[:, sl * 1024 + hx * 512: sl * 1024 + (hx + 1) * 512],
                                Kc[rb:rb + 64, och * S + kb * 128: och * S + (kb + 1) * 128],
                                Qc[rb:rb + 64, och * T + qb * TBL: och * T + (qb + 1) * TBL],
                                start=True, stop=True, tile_position=(rb, 0)),
                                waits=wts, inc="PE")
                        ewts = [("PE", qk_id[(kb, 1)])]
                        if kb >= 3:
                            ewts.append(("PE", av_id[kb - 3]))
                        elif prev_av is not None:
                            ewts.append(("PE", prev_av))
                        ex_id[kb] = P.op("scalar", lambda s, sl=sl, a3=kb % 3, At=At: s.activation(
                            At[a3][:], PS[:, sl * 1024: (sl + 1) * 1024], AF.Exp, scale=ISC),
                            waits=ewts, inc="AC")
                    if kk >= 2:
                        kb = kk - 2
                        avlast = None
                        for hx in range(2):
                            hh = 2 * hp + hx
                            avw = [("AC", ex_id[kb])] if hx == 0 else []
                            if kb == 0 and hx == 0:
                                avw.append(("DV", vones_dv))
                                if prev_ot is not None:
                                    avw.append(("DV", prev_ot))
                            avlast = P.op("tensor", lambda t, hh=hh, kb=kb, hx=hx, a3=kb % 3, Vt=Vt, At=At: t.matmul(
                                PS[0:65, 2048 + hx * 512: 2048 + (hx + 1) * 512],
                                Vt[:, kb * 520 + hh * 65: kb * 520 + (hh + 1) * 65],
                                At[a3][:, hx * 512:(hx + 1) * 512],
                                start=(kb == 0), stop=(kb == 31)),
                                waits=avw, inc="PE")
                        av_id[kb] = avlast
                prev_exp = {0: ex_id[30], 1: ex_id[31]}
                prev_av = av_id[31]
                for hx in range(2):
                    h = hg * 8 + 2 * hp + hx
                    oc2, rb2 = h // 2, (h % 2) * 64
                    P.op("scalar", lambda s, hx=hx, rsrow=rsrow: s.activation(
                        rsrow[0:1, :], PS[64:65, 2048 + hx * 512: 2048 + (hx + 1) * 512], AF.Ln),
                        waits=[("PE", av_id[31])], inc="AC")
                    rc = P.op("scalar", lambda s, rsrow=rsrow, rsrow2=rsrow2: s.activation(
                        rsrow2[0:1, :], rsrow[0:1, :], AF.Exp, scale=-1.0), inc="AC")
                    bcm = P.op("tensor", lambda t, rsrow2=rsrow2: t.matmul(
                        PS[0:64, 3072:3584], ones_f[0:1, 0:64], rsrow2[0:1, :], start=True, stop=True),
                        waits=[("AC", rc)], inc="PE")
                    bce = P.op("scalar", lambda s, bc_sb=bc_sb: s.copy(bc_sb[:], PS[0:64, 3072:3584]),
                               waits=[("PE", bcm), ("DV", prev_ot)], inc="AC")
                    prev_ot = P.op("vector", lambda v, oc2=oc2, rb2=rb2, qb=qb, hx=hx, bc_sb=bc_sb: v.tensor_mul(
                        OT[rb2:rb2 + 64, oc2 * T + qb * TBL: oc2 * T + (qb + 1) * TBL],
                        PS[0:64, 2048 + hx * 512: 2048 + (hx + 1) * 512], bc_sb[:]),
                        waits=[("AC", bce)], inc="DV")
        P.barrier()

    if DBG:
        P.op("sync", lambda s: s.dma_start(out=dbg_ot, in_=OT[:]), inc="SO", k=16)

    Dz = Arena(MSC, P1, P1 + 64 * KB, TOP, "D")
    ares = Dz.al("ares", [128, DCH * T], F32)
    wo_sb = Dz.al("wo", [128, DCH * KB], BF16)
    xftl = [Dz.al(f"xf{i}", [128, TBL], F32) for i in range(2)]
    x1tl = [Dz.al(f"x1{i}", [128, TBL], BF16) for i in range(2)]
    sqD = [Dz.al(f"sq{i}", [128, TBL], F32) for i in range(2)]
    ssD = MSC[0:1, 0:T]
    sqrD = MSC[0:1, 2 * KB: 2 * KB + T]
    lg_sb = Dz.al("lg", [128, 16 * NE], F32)
    mxt = Dz.al("mxt", [128, 16], F32)
    sct = Dz.al("sct", [128, 16], F32)
    mkt = Dz.al("mkt", [128, NE], F32)
    wkt = Dz.al("wkt", [128, NE], F32)
    sct2 = Dz.al("sct2", [16, 128], F32)
    sct3 = Dz.al("sct3", [16, 128], F32)

    wol = ("WO", P.op("gpsimd", lambda g: g.dma_start(
        out=wo_sb[:].rearrange("p (c o) -> p c o", c=DCH),
        in_=wo.rearrange("(c p) o -> p c o", p=128)), inc="WO", k=16))

    d1_dv = {}
    for och in range(DCH):
        for tb in range(4):
            i = och * 4 + tb
            pb = i % 2
            smg = f"XG{i % 2}"
            lx = (smg, P.op("sync", lambda s, och=och, tb=tb, sl=i % 2: s.dma_start(
                out=x1tl[sl][:], in_=xh_h[och * 128:(och + 1) * 128, tb * TBL:(tb + 1) * TBL]),
                waits=[("DV", d1_dv.get(i - 2))], inc=smg, k=16))
            mmlast = None
            for cin in range(DCH):
                wts = []
                if cin == 0:
                    wts.append(wol)
                    if i >= 2:
                        wts.append(("DV", d1_dv[i - 2]))
                mmlast = P.op("tensor", lambda t, och=och, tb=tb, cin=cin, pb=pb: t.matmul(
                    BK(pb), wo_sb[:, cin * KB + och * 128: cin * KB + och * 128 + 128],
                    OT[:, cin * T + tb * TBL: cin * T + (tb + 1) * TBL],
                    start=(cin == 0), stop=(cin == DCH - 1)),
                    waits=wts, inc="PE")
            d1_dv[i] = P.op("vector", lambda v, och=och, tb=tb, pb=pb, sl=i % 2: v.scalar_tensor_tensor(
                ares[:, och * T + tb * TBL: och * T + (tb + 1) * TBL],
                BK(pb), bo_sb[:, och:och + 1], x1tl[sl][:], ALU.add, ALU.add),
                waits=[("PE", mmlast), lx], inc="DV")

    def ln_stats(buf, cstride, colbase, width, sqr_ring, ssr, sqrr):
        ntb = width // TBL
        mm_l, cp_l = {}, {}
        for tb in range(ntb):
            for chn in range(DCH):
                i = tb * DCH + chn
                sqb = sqr_ring[i % 2]
                col = chn * cstride + colbase + tb * TBL
                sq_ac = P.op("scalar", lambda s, sqb=sqb, col=col, buf=buf: s.activation(
                    sqb[:], buf[:, col:col + TBL], AF.Square),
                    waits=[("PE", mm_l.get(i - 2))], inc="AC")
                pb = (tb % 2) * 2
                P.op("tensor", lambda t, buf=buf, col=col, pb=pb, chn=chn: t.matmul(
                    BK(pb)[0:1, :], col1_f[:], buf[:, col:col + TBL],
                    start=(chn == 0), stop=(chn == DCH - 1)),
                    waits=([("DV", cp_l.get(tb - 2))] if chn == 0 else []), inc="PE")
                mm_l[i] = P.op("tensor", lambda t, sqb=sqb, pb=pb, chn=chn: t.matmul(
                    BK(pb + 1)[0:1, :], col1_f[:], sqb[:],
                    start=(chn == 0), stop=(chn == DCH - 1)),
                    waits=[("AC", sq_ac)], inc="PE")
            pb = (tb % 2) * 2
            P.op("vector", lambda v, tb=tb, pb=pb: v.tensor_copy(
                ssr[0:1, tb * TBL:(tb + 1) * TBL], BK(pb)[0:1, :]),
                waits=[("PE", mm_l[tb * DCH + DCH - 1])], inc="DV")
            cp_l[tb] = P.op("vector", lambda v, tb=tb, pb=pb: v.tensor_copy(
                sqrr[0:1, tb * TBL:(tb + 1) * TBL], BK(pb + 1)[0:1, :]), inc="DV")

    P.barrier()
    ln_stats(ares, T, 0, T, sqD, ssD, sqrD)
    rm2 = row_math(ssD, sqrD, ar2, br2, br2, T)

    d4_dv = {}
    for tb in range(4):
        pb = 4 + (tb % 2) * 2
        P.op("tensor", lambda t, tb=tb, pb=pb: t.matmul(
            BK(pb), ones_f[:], ar2[0:1, tb * TBL:(tb + 1) * TBL], start=True, stop=True),
            waits=[("DV", rm2) if tb == 0 else ("DV", d4_dv.get((tb - 2, DCH - 1)))], inc="PE")
        bcb = P.op("tensor", lambda t, tb=tb, pb=pb: t.matmul(
            BK(pb + 1), ones_f[:], br2[0:1, tb * TBL:(tb + 1) * TBL], start=True, stop=True), inc="PE")
        for och in range(DCH):
            i = tb * DCH + och
            smf = f"XF{i % 2}"
            lx = (smf, P.op("sync", lambda s, och=och, tb=tb, sl=i % 2: s.dma_start(
                out=xftl[sl][:], in_=xT[och * 128:(och + 1) * 128, tb * TBL:(tb + 1) * TBL]),
                waits=[("DV", d4_dv.get((tb, och - 2)) or d4_dv.get((tb - 1, DCH - 2 + och)))], inc=smf, k=16))
            P.op("vector", lambda v, och=och, tb=tb, pb=pb, sl=i % 2: v.tensor_mul(
                sqD[sl][:], ares[:, och * T + tb * TBL: och * T + (tb + 1) * TBL], BK(pb)),
                waits=[("PE", bcb)], inc="DV")
            P.op("vector", lambda v, pb=pb, sl=i % 2: v.tensor_add(sqD[sl][:], sqD[sl][:], BK(pb + 1)), inc="DV")
            d4_dv[(tb, och)] = P.op("vector", lambda v, och=och, tb=tb, sl=i % 2: v.tensor_add(
                ares[:, och * T + tb * TBL: och * T + (tb + 1) * TBL], sqD[sl][:], xftl[sl][:]),
                waits=[lx], inc="DV")
    for och in range(DCH):
        P.op("sync", lambda s, och=och: s.dma_start(
            out=y1_h[och * 128:(och + 1) * 128, :], in_=ares[:, och * T:(och + 1) * T]),
            waits=[("DV", d4_dv[(3, och)])], inc="Y1S", k=16)

    P.barrier()
    ln_stats(ares, T, 0, T, sqD, ssD, sqrD)
    rm3 = row_math(ssD, sqrD, ar2, br2, br2, T)

    d7_dv = {}
    for tb in range(4):
        pb = 4 + (tb % 2) * 2
        P.op("tensor", lambda t, tb=tb, pb=pb: t.matmul(
            BK(pb), ones_f[:], ar2[0:1, tb * TBL:(tb + 1) * TBL], start=True, stop=True),
            waits=[("DV", rm3) if tb == 0 else ("DV", d7_dv.get((tb - 2, DCH - 1)))], inc="PE")
        bcb = P.op("tensor", lambda t, tb=tb, pb=pb: t.matmul(
            BK(pb + 1), ones_f[:], br2[0:1, tb * TBL:(tb + 1) * TBL], start=True, stop=True), inc="PE")
        for och in range(DCH):
            i = tb * DCH + och
            P.op("vector", lambda v, och=och, tb=tb, pb=pb, sl=i % 2: v.tensor_mul(
                sqD[sl][:], ares[:, och * T + tb * TBL: och * T + (tb + 1) * TBL], BK(pb)),
                waits=[("PE", bcb)], inc="DV")
            d7_dv[(tb, och)] = P.op("vector", lambda v, och=och, tb=tb, pb=pb, sl=i % 2: v.tensor_add(
                x2b[:, och * T + tb * TBL: och * T + (tb + 1) * TBL], sqD[sl][:], BK(pb + 1)), inc="DV")
    x2_done = P.n("DV")

    P.barrier()
    rk_dv = {}
    for ttl in range(16):
        mmlast = None
        for chn in range(DCH):
            wts = []
            if chn == 0 and ttl >= 2:
                wts.append(("DV", rk_dv[ttl - 2]))
            mmlast = P.op("tensor", lambda t, ttl=ttl, chn=chn: t.matmul(
                PS[0:128, 2048 + (ttl % 2) * 512: 2048 + (ttl % 2) * 512 + NE],
                x2b[:, chn * T + ttl * 128: chn * T + (ttl + 1) * 128],
                gw_sb[:, chn * NE:(chn + 1) * NE],
                start=(chn == 0), stop=(chn == DCH - 1)),
                waits=wts, inc="PE")
        P.op("vector", lambda v, ttl=ttl: v.tensor_copy(
            lg_sb[:, ttl * NE:(ttl + 1) * NE], PS[0:128, 2048 + (ttl % 2) * 512: 2048 + (ttl % 2) * 512 + NE]),
            waits=[("PE", mmlast)], inc="DV")
        P.op("vector", lambda v, ttl=ttl: v.tensor_reduce(
            mxt[:, ttl:ttl + 1], lg_sb[:, ttl * NE:(ttl + 1) * NE], AX.X, ALU.max), inc="DV")
        P.op("vector", lambda v, ttl=ttl: v.tensor_single_scalar(
            mkt[:], lg_sb[:, ttl * NE:(ttl + 1) * NE], mxt[:, ttl:ttl + 1], ALU.is_equal), inc="DV")
        P.op("vector", lambda v: v.tensor_mul(wkt[:], mkt[:], iotaw[:]), inc="DV")
        P.op("vector", lambda v, ttl=ttl: v.tensor_reduce(sct[:, ttl:ttl + 1], wkt[:], AX.X, ALU.max), inc="DV")
        P.op("vector", lambda v, ttl=ttl: v.tensor_single_scalar(
            mkt[:], wkt[:], sct[:, ttl:ttl + 1], ALU.is_equal), inc="DV")
        mb_dv = P.op("vector", lambda v, ttl=ttl: v.tensor_copy(
            mbf[:, ttl * NE:(ttl + 1) * NE], mkt[:]), inc="DV")
        rlast = None
        for j in range(ttl + 1):
            lt = ones2 if j < ttl else LS
            rlast = P.op("tensor", lambda t, j=j, lt=lt, ttl=ttl: t.matmul(
                PS[0:128, 3584:3584 + NE], lt[:], mbf[:, j * NE:(j + 1) * NE],
                start=(j == 0), stop=(j == ttl)),
                waits=[("DV", mb_dv), ("DV", rk_dv.get(ttl - 1))] if j == 0 else [], inc="PE")
        P.op("vector", lambda v: v.tensor_mul(wkt[:], mkt[:], PS[0:128, 3584:3584 + NE]),
             waits=[("PE", rlast)], inc="DV")
        P.op("vector", lambda v: v.tensor_reduce(rkc[:, 0:1], wkt[:], AX.X, ALU.add), inc="DV")
        P.op("vector", lambda v: v.tensor_mul(wkt[:], mkt[:], ecaps[:]), inc="DV")
        P.op("vector", lambda v: v.tensor_reduce(rkc[:, 1:2], wkt[:], AX.X, ALU.add), inc="DV")
        P.op("vector", lambda v: v.tensor_add(rkc[:, 2:3], rkc[:, 0:1], rkc[:, 1:2]), inc="DV")
        P.op("vector", lambda v: v.tensor_single_scalar(rkc[:, 3:4], rkc[:, 0:1], float(CAP) - 0.5, ALU.is_ge), inc="DV")
        rk_dv[ttl] = P.op("vector", lambda v, ttl=ttl: v.scalar_tensor_tensor(
            sct2b[:, ttl:ttl + 1], rkc[:, 3:4], 8192.0, rkc[:, 2:3], ALU.mult, ALU.add), inc="DV")
    sg_ac = P.op("scalar", lambda s: s.activation(sct[:], mxt[:], AF.Sigmoid),
                 waits=[("DV", P.n("DV"))], inc="AC")
    trs = P.op("tensor", lambda t: t.transpose(PS[0:16, 3072:3200], sct[:], identF[:]),
               waits=[("AC", sg_ac), ("DV", w_ident)], inc="PE")
    sc_dv = P.op("vector", lambda v: v.tensor_copy(sct2[:], PS[0:16, 3072:3200]),
                 waits=[("PE", trs)], inc="DV")
    sc_xl = ("SC", P.op("gpsimd", lambda g: g.dma_start(out=scrow[0:1, :], in_=sct2[:]),
                 waits=[("DV", sc_dv)], inc="SC", k=16))
    trs2 = P.op("tensor", lambda t: t.transpose(PS[0:16, 3200:3328], sct2b[:], identF[:]),
                waits=[("DV", rk_dv[15])], inc="PE")
    sc3_dv = P.op("vector", lambda v: v.tensor_copy(sct3[:], PS[0:16, 3200:3328]),
                  waits=[("PE", trs2)], inc="DV")
    sr_xl = ("SR", P.op("sync", lambda s: s.dma_start(out=slotrow[0:1, :], in_=sct3[:]),
        waits=[("DV", sc3_dv)], inc="SR", k=16))
    if DBG:
        P.op("sync", lambda s: s.dma_start(out=dbg_x2, in_=x2b[:]), waits=[("DV", x2_done)], inc="SO", k=16)
        P.op("sync", lambda s: s.dma_start(out=dbg_slot, in_=slotrow[0:1, :]), waits=[sr_xl], inc="SO", k=16)
        P.op("sync", lambda s: s.dma_start(out=dbg_sc, in_=scrow[0:1, :]), waits=[sc_xl], inc="SO", k=16)
        P.op("sync", lambda s: s.dma_start(out=dbg_ares, in_=ares[:]), inc="SO", k=16)

    P.barrier()
    MB2 = MSC.bitcast(BF16)
    PSB = PS.bitcast(BF16)
    X_tok = MB2[:, 0:16 * KB]
    xe = MB2[:, 16 * KB:16 * KB + DCH * CAP]
    Pg = MB2[:, 20 * KB:20 * KB + 16 * CAP]
    dnb = MB2[:, 20 * KB:20 * KB + 4 * KB]
    PTt = MB2[:, 24 * KB:24 * KB + 4 * TBL]
    slot_bc = MSC[:, 14 * KB:14 * KB + T]
    ssE = MSC[0:1, 8 * KB:8 * KB + T]
    sqrE = MSC[0:1, 10 * KB:10 * KB + T]
    y1tl = [MSC[:, 12 * KB + i * TBL:12 * KB + (i + 1) * TBL] for i in range(2)]
    sqE = [MSC[:, 12 * KB + KB + i * TBL:12 * KB + KB + (i + 1) * TBL] for i in range(2)]
    faccv = MSC[:, 16 * KB:16 * KB + DCH * T]

    E = Arena(MSC, P1, P1 + 128 * KB, TOP, "E")
    wg_sb = E.al("wg", [128, DCH * TBL], BF16)
    wu_sb = E.al("wu", [128, DCH * TBL], BF16)
    wd_sb = E.al("wd", [128, 8 * KB], BF16)

    gs = E.al("gs", [128, 8 * TBL], BF16)
    um = [E.al(f"um{i}", [128, TBL], BF16) for i in range(2)]

    last_g = last_u = last_d = None
    acc_dv = {}
    bank_rd = {}

    npass = 0
    for fh in range(2):
        for fq in range(2):
            wgl = ("WG", P.op("gpsimd", lambda g, fh=fh, fq=fq: g.dma_start(
                out=wg_sb[:].rearrange("p (c f) -> p c f", c=DCH),
                in_=sg[:, fh * KB + fq * TBL: fh * KB + (fq + 1) * TBL].rearrange("(c p) f -> p c f", p=128)),
                waits=[("PE", last_g)], inc="WG", k=16))
            wul = ("WU", P.op("gpsimd", lambda g, fh=fh, fq=fq: g.dma_start(
                out=wu_sb[:].rearrange("p (c f) -> p c f", c=DCH),
                in_=su[:, fh * KB + fq * TBL: fh * KB + (fq + 1) * TBL].rearrange("(c p) f -> p c f", p=128)),
                waits=[("PE", last_u)], inc="WU", k=16))
            if fq == 0:
                wdl = ("WD", P.op("gpsimd", lambda g, fh=fh: g.dma_start(
                    out=wd_sb[:].rearrange("p (c dd) -> p c dd", c=8),
                    in_=sd[fh * KB:(fh + 1) * KB, :].rearrange("(c p) dd -> p c dd", p=128)),
                    waits=[("PE", last_d)], inc="WD", k=16))
            for tb in range(4):
                gsil, um_dv, hdone = {}, {}, {}
                for fc in range(4):
                    pb = fc % 2
                    mmlast = None
                    for chn in range(DCH):
                        wts = []
                        if chn == 0:
                            wts.append(wgl)
                            wts.append(bank_rd.get(pb))
                            if fc < 2:
                                wts.append(("PE", last_d))
                        mmlast = P.op("tensor", lambda t, chn=chn, fc=fc, fq=fq, tb=tb, pb=pb: t.matmul(
                            BK(pb), wg_sb[:, chn * TBL + fc * 128: chn * TBL + fc * 128 + 128],
                            x2b[:, chn * T + tb * TBL: chn * T + (tb + 1) * TBL],
                            start=(chn == 0), stop=(chn == DCH - 1)),
                            waits=wts, inc="PE")
                    last_g = mmlast
                    gsil[fc] = P.op("scalar", lambda st, fc=fc, pb=pb: st.activation(
                        gs[:, fc * TBL:(fc + 1) * TBL], BK(pb), AF.Silu),
                        waits=[("PE", mmlast)], inc="AC")
                    bank_rd[pb] = ("AC", gsil[fc])
                for fc in range(4):
                    pb = fc % 2
                    mmlast = None
                    for chn in range(DCH):
                        wts = []
                        if chn == 0:
                            wts.append(wul)
                            wts.append(bank_rd.get(pb))
                        mmlast = P.op("tensor", lambda t, chn=chn, fc=fc, tb=tb, pb=pb: t.matmul(
                            BK(pb), wu_sb[:, chn * TBL + fc * 128: chn * TBL + fc * 128 + 128],
                            x2b[:, chn * T + tb * TBL: chn * T + (tb + 1) * TBL],
                            start=(chn == 0), stop=(chn == DCH - 1)),
                            waits=wts, inc="PE")
                    last_u = mmlast
                    uw = [("PE", mmlast), ("PL", hdone.get(fc - 2) or P.n("PL"))]
                    um_dv[fc] = P.op("vector", lambda v, fc=fc, pb=pb: v.tensor_copy(
                        um[fc % 2][:], BK(pb)), waits=uw, inc="DV")
                    bank_rd[pb] = ("DV", um_dv[fc])
                    hdone[fc] = P.op("gpsimd", lambda g, fc=fc: g.tensor_mul(
                        gs[:, fc * TBL:(fc + 1) * TBL], gs[:, fc * TBL:(fc + 1) * TBL], um[fc % 2][:]),
                        waits=[("DV", um_dv[fc]), ("AC", gsil[fc])], inc="PL")
                for dch in range(DCH):
                    pb = 2 + dch % 2
                    mmlast = None
                    for fc in range(4):
                        wts = []
                        if fc == 0:
                            wts.append(wdl)
                            wts.append(("PL", hdone[3]))
                            wts.append(bank_rd.get(pb))
                        mmlast = P.op("tensor", lambda t, dch=dch, fc=fc, fq=fq, pb=pb: t.matmul(
                            BK(pb), wd_sb[:, (fq * 4 + fc) * KB + dch * 128: (fq * 4 + fc) * KB + dch * 128 + 128],
                            gs[:, fc * TBL:(fc + 1) * TBL],
                            start=(fc == 0), stop=(fc == 3)),
                            waits=wts, inc="PE")
                    last_d = mmlast
                    if npass == 0:
                        acc_dv[(tb, dch)] = P.op("vector", lambda v, dch=dch, tb=tb, pb=pb: v.tensor_copy(
                            faccv[:, dch * T + tb * TBL: dch * T + (tb + 1) * TBL], BK(pb)),
                            waits=[("PE", mmlast)], inc="DV")
                    else:
                        acc_dv[(tb, dch)] = P.op("vector", lambda v, dch=dch, tb=tb, pb=pb: v.tensor_add(
                            faccv[:, dch * T + tb * TBL: dch * T + (tb + 1) * TBL],
                            faccv[:, dch * T + tb * TBL: dch * T + (tb + 1) * TBL], BK(pb)),
                            waits=[("PE", mmlast)], inc="DV")
                    bank_rd[pb] = ("DV", acc_dv[(tb, dch)])
            npass += 1

    if XBAR & 1:
        P.barrier()
    if DBG:
        P.op("sync", lambda s: s.dma_start(out=dbg_shr, in_=faccv), waits=[("DV", P.n("DV"))], inc="SO", k=16)
        for _e2 in ENGINES:
            P.pending[_e2].append(("SO", P.n("SO")))
    shared_pe = P.n("PE")
    cv_dv = {}
    for tb in range(4):
        pb = 6 + tb % 2
        bcs = P.op("tensor", lambda t, tb=tb, pb=pb: t.matmul(
            BK(pb), ones_b[0:1, :], scrow[0:1, tb * TBL:(tb + 1) * TBL],
            start=True, stop=True),
            waits=([sc_xl, ("PE", shared_pe)] if tb == 0 else []) +
                  [bank_rd.get(pb)], inc="PE")
        for chn in range(DCH):
            cv_dv[(tb, chn)] = P.op("vector", lambda v, chn=chn, tb=tb, pb=pb: v.tensor_mul(
                x2b[:, chn * T + tb * TBL: chn * T + (tb + 1) * TBL],
                x2b[:, chn * T + tb * TBL: chn * T + (tb + 1) * TBL], BK(pb)),
                waits=[("PE", bcs)], inc="DV")
            bank_rd[pb] = ("DV", cv_dv[(tb, chn)])
    P.barrier()
    bank_rd = {}
    tr_dv = {}
    for tt in range(16):
        for dch in range(DCH):
            j = tt * DCH + dch
            sl = j % 2
            po = 6144 + 1024 * sl
            trp = P.op("tensor", lambda t, tt=tt, dch=dch, po=po: t.transpose(
                PSB[:, po:po + 128], x2b[:, dch * T + tt * 128: dch * T + (tt + 1) * 128], identB[:]),
                waits=[("DV", tr_dv.get(j - 2))], inc="PE")
            tr_dv[j] = P.op("vector", lambda v, tt=tt, dch=dch, po=po: v.tensor_copy(
                X_tok[:, tt * KB + dch * 128: tt * KB + dch * 128 + 128], PSB[:, po:po + 128]),
                waits=[("PE", trp)], inc="DV")
    if XBAR & 2:
        P.barrier()
    sb_ac = {}
    for tb in range(4):
        pb = 4 + tb % 2
        bcm = P.op("tensor", lambda t, tb=tb, pb=pb: t.matmul(
            BK(pb), ones_f[0:1, :], slotrow[0:1, tb * TBL:(tb + 1) * TBL], start=True, stop=True),
            waits=[sr_xl if tb == 0 else None, ("AC", sb_ac.get(tb - 2))], inc="PE")
        sb_ac[tb] = P.op("scalar", lambda st, tb=tb, pb=pb: st.copy(
            slot_bc[:, tb * TBL:(tb + 1) * TBL], BK(pb)),
            waits=[("PE", bcm)], inc="AC")
        bank_rd[pb] = ("AC", sb_ac[tb])
    P.barrier()

    last_scatter = None
    sc_fa = {}
    for e in range(NE):
        pg_dv = {}
        for tt in range(16):
            pg_dv[tt] = P.op("vector", lambda v, tt=tt, e=e: v.tensor_scalar(
                Pg[:, tt * CAP:(tt + 1) * CAP], iota512[:], sct2b[:, tt:tt + 1], -float(e * CAP),
                ALU.subtract, ALU.is_equal),
                waits=[("PE", last_scatter), ("SR", P.n("SR"))] if tt == 0 else [], inc="DV")
        pg_done = pg_dv[15]
        if XBAR & 4:
            P.barrier()
        ge_dv = {}
        for dh in range(2):
            mml = {}
            for dc in range(4):
                dch = dh * 4 + dc
                mmlast = None
                for tt in range(16):
                    wts = []
                    if tt == 0:
                        wts.append(("DV", pg_done))
                        wts.append(bank_rd.get(dc))
                    mmlast = P.op("tensor", lambda t, tt=tt, dch=dch, dc=dc: t.matmul(
                        BK(dc), X_tok[:, tt * KB + dch * 128: tt * KB + dch * 128 + 128],
                        Pg[:, tt * CAP:(tt + 1) * CAP],
                        start=(tt == 0), stop=(tt == 15)),
                        waits=wts, inc="PE")
                mml[dc] = mmlast
            for dc in range(4):
                dch = dh * 4 + dc
                ge_dv[dc] = P.op("vector", lambda v, dch=dch, dc=dc: v.tensor_copy(
                    xe[:, dch * CAP:(dch + 1) * CAP], BK(dc)),
                    waits=[("PE", mml[dc])], inc="DV")
                bank_rd[dc] = ("DV", ge_dv[dc])
        xe_done = P.n("DV")
        if XBAR & 8:
            P.barrier()
        if DBG and e == 0:
            P.op("sync", lambda s: s.dma_start(out=dbg_xe, in_=xe), waits=[("DV", xe_done)], inc="SO", k=16)
            P.op("sync", lambda s: s.dma_start(out=dbg_xtok, in_=X_tok), inc="SO", k=16)
            for _e2 in ENGINES:
                P.pending[_e2].append(("SO", P.n("SO")))
        dn_dv = {}
        for fh in range(2):
            gsil, um_dv, hdone = {}, {}, {}
            for fq in range(2):
                wgl = ("WG", P.op("gpsimd", lambda g, e=e, fh=fh, fq=fq: g.dma_start(
                    out=wg_sb[:].rearrange("p (c f) -> p c f", c=DCH),
                    in_=rg[e][:, fh * KB + fq * TBL: fh * KB + (fq + 1) * TBL].rearrange("(c p) f -> p c f", p=128)),
                    waits=[("PE", last_g)], inc="WG", k=16))
                wul = ("WU", P.op("gpsimd", lambda g, e=e, fh=fh, fq=fq: g.dma_start(
                    out=wu_sb[:].rearrange("p (c f) -> p c f", c=DCH),
                    in_=ru[e][:, fh * KB + fq * TBL: fh * KB + (fq + 1) * TBL].rearrange("(c p) f -> p c f", p=128)),
                    waits=[("PE", last_u)], inc="WU", k=16))
                if fq == 0:
                    wdl = ("WD", P.op("gpsimd", lambda g, e=e, fh=fh: g.dma_start(
                        out=wd_sb[:].rearrange("p (c dd) -> p c dd", c=8),
                        in_=rd[e][fh * KB:(fh + 1) * KB, :].rearrange("(c p) dd -> p c dd", p=128)),
                        waits=[("PE", last_d)], inc="WD", k=16))
                for fc in range(4):
                    fcl = fq * 4 + fc
                    pb = fc % 2
                    mmlast = None
                    for chn in range(DCH):
                        wts = []
                        if chn == 0:
                            wts.append(wgl)
                            wts.append(bank_rd.get(pb))
                            if fcl < 2:
                                wts.append(("PE", last_d))
                                wts.append(("DV", xe_done))
                        mmlast = P.op("tensor", lambda t, chn=chn, fcl=fcl, fc=fc, pb=pb: t.matmul(
                            BK(pb), wg_sb[:, chn * TBL + fc * 128: chn * TBL + fc * 128 + 128],
                            xe[:, chn * CAP:(chn + 1) * CAP],
                            start=(chn == 0), stop=(chn == DCH - 1)),
                            waits=wts, inc="PE")
                    last_g = mmlast
                    gsil[fcl] = P.op("scalar", lambda st, fcl=fcl, pb=pb: st.activation(
                        gs[:, fcl * TBL:(fcl + 1) * TBL], BK(pb), AF.Silu),
                        waits=[("PE", mmlast)], inc="AC")
                    bank_rd[pb] = ("AC", gsil[fcl])
                for fc in range(4):
                    fcl = fq * 4 + fc
                    pb = fc % 2
                    mmlast = None
                    for chn in range(DCH):
                        wts = []
                        if chn == 0:
                            wts.append(wul)
                            wts.append(bank_rd.get(pb))
                        mmlast = P.op("tensor", lambda t, chn=chn, fc=fc, pb=pb: t.matmul(
                            BK(pb), wu_sb[:, chn * TBL + fc * 128: chn * TBL + fc * 128 + 128],
                            xe[:, chn * CAP:(chn + 1) * CAP],
                            start=(chn == 0), stop=(chn == DCH - 1)),
                            waits=wts, inc="PE")
                    last_u = mmlast
                    uw = [("PE", mmlast), ("PL", hdone.get(fcl - 2) or P.n("PL"))]
                    um_dv[fcl] = P.op("vector", lambda v, fcl=fcl, pb=pb: v.tensor_copy(
                        um[fcl % 2][:], BK(pb)), waits=uw, inc="DV")
                    bank_rd[pb] = ("DV", um_dv[fcl])
                    hdone[fcl] = P.op("gpsimd", lambda g, fcl=fcl: g.tensor_mul(
                        gs[:, fcl * TBL:(fcl + 1) * TBL], gs[:, fcl * TBL:(fcl + 1) * TBL], um[fcl % 2][:]),
                        waits=[("DV", um_dv[fcl]), ("AC", gsil[fcl])], inc="PL")
            if XBAR & 16:
                P.barrier()
            for stile in range(4):
                for dblk in range(2):
                    gidx = fh * 8 + stile * 2 + dblk
                    pb = 2 + gidx % 2
                    mmlast = None
                    for fcl in range(8):
                        wts = []
                        if fcl == 0:
                            wts.append(wdl)
                            wts.append(("PL", hdone[7]))
                            wts.append(bank_rd.get(pb))
                        mmlast = P.op("tensor", lambda t, stile=stile, dblk=dblk, fcl=fcl, pb=pb: t.matmul(
                            BK(pb), gs[:, fcl * TBL + stile * 128: fcl * TBL + stile * 128 + 128],
                            wd_sb[:, fcl * KB + dblk * TBL: fcl * KB + (dblk + 1) * TBL],
                            start=(fcl == 0), stop=(fcl == 7)),
                            waits=wts, inc="PE")
                    last_d = mmlast
                    if fh == 0:
                        dn_dv[gidx] = P.op("vector", lambda v, stile=stile, dblk=dblk, pb=pb: v.tensor_copy(
                            dnb[:, stile * KB + dblk * TBL: stile * KB + (dblk + 1) * TBL], BK(pb)),
                            waits=[("PE", mmlast)], inc="DV")
                        bank_rd[pb] = ("DV", dn_dv[gidx])
                    else:
                        dn_dv[gidx] = P.op("vector", lambda v, stile=stile, dblk=dblk, pb=pb: v.tensor_add(
                            dnb[:, stile * KB + dblk * TBL: stile * KB + (dblk + 1) * TBL],
                            dnb[:, stile * KB + dblk * TBL: stile * KB + (dblk + 1) * TBL], BK(pb)),
                            waits=[("PE", mmlast)], inc="DV")
                        bank_rd[pb] = ("DV", dn_dv[gidx])
        dn_done = P.n("DV")
        if XBAR & 32:
            P.barrier()
        if DBG and e == 0:
            P.op("sync", lambda s: s.dma_start(out=dbg_dnb, in_=dnb), waits=[("DV", dn_done)], inc="SO", k=16)
            for _e2 in ENGINES:
                P.pending[_e2].append(("SO", P.n("SO")))
        if XBAR & 64:
            P.barrier()
        for tblk in range(4):
            pt_dv = {}
            for stile in range(4):
                pt_dv[stile] = P.op("vector", lambda v, stile=stile, tblk=tblk, e=e: v.tensor_scalar(
                    PTt[:, stile * TBL:(stile + 1) * TBL],
                    slot_bc[:, tblk * TBL:(tblk + 1) * TBL], iotacol[:],
                    float(e * CAP + stile * 128), ALU.subtract, ALU.is_equal),
                    waits=[("DV", dn_done)] if stile == 0 and tblk == 0 else [], inc="DV")
            if DBG and e == 0 and tblk == 0:
                P.op("sync", lambda s: s.dma_start(out=dbg_pt, in_=PTt), waits=[("DV", pt_dv[3])], inc="SO", k=16)
                P.op("sync", lambda s: s.dma_start(out=dbg_sbc, in_=slot_bc), inc="SO", k=16)
                for _e2 in ENGINES:
                    P.pending[_e2].append(("SO", P.n("SO")))
            for dch in range(DCH):
                pb = 4 + dch % 2
                mmlast = None
                for stile in range(4):
                    wts = []
                    if stile == 0:
                        wts.append(("DV", pt_dv[3]))
                        wts.append(bank_rd.get(pb))
                    mmlast = P.op("tensor", lambda t, stile=stile, dch=dch, tblk=tblk, pb=pb: t.matmul(
                        BK(pb), dnb[:, stile * KB + dch * 128: stile * KB + dch * 128 + 128],
                        PTt[:, stile * TBL:(stile + 1) * TBL],
                        start=(stile == 0), stop=(stile == 3)),
                        waits=wts, inc="PE")
                last_scatter = mmlast
                sc_fa[(tblk, dch)] = P.op("vector", lambda v, dch=dch, tblk=tblk, pb=pb: v.tensor_add(
                    faccv[:, dch * T + tblk * TBL: dch * T + (tblk + 1) * TBL],
                    faccv[:, dch * T + tblk * TBL: dch * T + (tblk + 1) * TBL], BK(pb)),
                    waits=[("PE", mmlast)], inc="DV")
                bank_rd[pb] = ("DV", sc_fa[(tblk, dch)])

    P.barrier()
    if DBG:
        P.op("sync", lambda s: s.dma_start(out=dbg_ffn, in_=faccv), inc="SO", k=16)
        for _e in ENGINES:
            P.pending[_e].append(("SO", P.n("SO")))
    ln_stats(faccv, T, 0, T, sqE, ssE, sqrE)
    rm4 = row_math(ssE, sqrE, ar2, br2, br2, T)
    fin_dv = {}
    for tb in range(4):
        pb = 4 + (tb % 2) * 2
        P.op("tensor", lambda t, tb=tb, pb=pb: t.matmul(
            BK(pb), ones_f[:], ar2[0:1, tb * TBL:(tb + 1) * TBL], start=True, stop=True),
            waits=[("DV", rm4) if tb == 0 else ("DV", fin_dv.get((tb - 2, DCH - 1)))], inc="PE")
        bcb = P.op("tensor", lambda t, tb=tb, pb=pb: t.matmul(
            BK(pb + 1), ones_f[:], br2[0:1, tb * TBL:(tb + 1) * TBL], start=True, stop=True), inc="PE")
        for och in range(DCH):
            i = tb * DCH + och
            sml = f"YL{i % 2}"
            ly = (sml, P.op("sync", lambda s, och=och, tb=tb, sl=i % 2: s.dma_start(
                out=y1tl[sl][:], in_=y1_h[och * 128:(och + 1) * 128, tb * TBL:(tb + 1) * TBL]),
                waits=[("DV", fin_dv.get((tb, och - 2)) or fin_dv.get((tb - 1, DCH - 2 + och)))], inc=sml, k=16))
            P.op("vector", lambda v, och=och, tb=tb, pb=pb, sl=i % 2: v.tensor_mul(
                sqE[sl][:], faccv[:, och * T + tb * TBL: och * T + (tb + 1) * TBL], BK(pb)),
                waits=[("PE", bcb)], inc="DV")
            P.op("vector", lambda v, pb=pb, sl=i % 2: v.tensor_add(sqE[sl][:], sqE[sl][:], BK(pb + 1)), inc="DV")
            fin_dv[(tb, och)] = P.op("vector", lambda v, och=och, tb=tb, sl=i % 2: v.tensor_add(
                faccv[:, och * T + tb * TBL: och * T + (tb + 1) * TBL], sqE[sl][:], y1tl[sl][:]),
                waits=[ly], inc="DV")
    for och in range(DCH):
        P.op("sync", lambda s, och=och: s.dma_start(
            out=yT[och * 128:(och + 1) * 128, :], in_=faccv[:, och * T:(och + 1) * T]),
            waits=[("DV", fin_dv[(3, och)])], inc="SO", k=16)

    P.op("sync", lambda s: s.nop(), waits=[("SO", P.n("SO"))])

    with nc.Block() as block:
        P.emit(block)
    return nc


_NC_CACHE = {}


def _get_nc():
    if "nc" not in _NC_CACHE:
        _NC_CACHE["nc"] = build_program()
    return _NC_CACHE["nc"]


def make_in_maps(L0_attn_x, L0_attn_W_qkv, L0_attn_W_out, L0_attn_b_out, L0_ffn_gate_w,
                 L0_ffn_shared_gate, L0_ffn_shared_up, L0_ffn_shared_down,
                 L0_ffn_r_gate, L0_ffn_r_up, L0_ffn_r_down):
    x = np.asarray(L0_attn_x)
    Wr = np.asarray(L0_attn_W_qkv).reshape(H, 3, HD, D)
    wq_h = np.ascontiguousarray(Wr[:, 0].reshape(H * HD, D).T)
    wk_h = np.ascontiguousarray(Wr[:, 1].reshape(H * HD, D).T)
    wv_h = np.ascontiguousarray(Wr[:, 2].reshape(H * HD, D).T)
    wo_h = np.ascontiguousarray(np.asarray(L0_attn_W_out).T)
    bo_h = np.ascontiguousarray(np.asarray(L0_attn_b_out).reshape(DCH, 128).T)
    gw_h = np.ascontiguousarray(np.asarray(L0_ffn_gate_w).T)
    sg_h = np.ascontiguousarray(np.asarray(L0_ffn_shared_gate).T)
    su_h = np.ascontiguousarray(np.asarray(L0_ffn_shared_up).T)
    sd_h = np.ascontiguousarray(np.asarray(L0_ffn_shared_down).T)
    rg_h = np.ascontiguousarray(np.transpose(np.asarray(L0_ffn_r_gate), (0, 2, 1)))
    ru_h = np.ascontiguousarray(np.transpose(np.asarray(L0_ffn_r_up), (0, 2, 1)))
    rd_h = np.ascontiguousarray(np.transpose(np.asarray(L0_ffn_r_down), (0, 2, 1)))
    in_maps = []
    for c in range(8):
        b, h = c // 2, c % 2
        xb = x[b]
        xcore = np.concatenate([xb[h * T:(h + 1) * T], xb[(1 - h) * T:(2 - h) * T]], axis=0)
        in_maps.append({
            "xT": np.ascontiguousarray(xcore.T),
            "wq": wq_h, "wk": wk_h, "wv": wv_h, "wo": wo_h, "bo": bo_h, "gw": gw_h,
            "sg": sg_h, "su": su_h, "sd": sd_h, "rg": rg_h, "ru": ru_h, "rd": rd_h,
        })
    return in_maps


def kernel(**inputs):
    in_maps = make_in_maps(**inputs)
    nc = _get_nc()
    res = run_bass_kernel_spmd(nc, in_maps, core_ids=list(range(8)))
    out = np.empty((4, S, D), np.float32)
    for c in range(8):
        b, h = c // 2, c % 2
        out[b, h * T:(h + 1) * T, :] = res.results[c]["yT"].T
    return out



# version 10
# speedup vs baseline: 1.2494x; 1.0105x over previous
import numpy as np
import concourse.bass as bass
import concourse.mybir as mybir
from concourse.bass_utils import run_bass_kernel_spmd

F32 = mybir.dt.float32
BF16 = mybir.dt.bfloat16
AF = mybir.ActivationFunctionType
ALU = mybir.AluOpType
AX = mybir.AxisListType

D, DCH = 1024, 8
CAP = 512
S = 4096
T = 2048
H, HD = 16, 64
FF = 2048
NE = 8
TBL = 512
EPS = 1e-5
ISC = 1.0 / 32.0
KB = 1024

ENGINES = ("tensor", "scalar", "vector", "gpsimd", "sync")
DBG = False
DBGKIND = "Internal"
XBAR = 3


class Planner:
    def __init__(self, nc, sems):
        self.nc = nc
        self.ops = {e: [] for e in ENGINES}
        self.sems = sems
        self.counts = {k: 0 for k in sems}
        self.pending = {e: [] for e in ENGINES}

    SELFSEM = {"vector": "DV", "scalar": "AC"}

    def op(self, eng, fn, waits=(), inc=None, k=1):
        w = self.pending[eng] + [x for x in waits if x is not None and x[1]]
        self.pending[eng] = []
        ss = self.SELFSEM.get(eng) or ("PL" if eng == "gpsimd" and inc == "PL" else None)
        if ss is not None and self.counts[ss] > 0:
            w.append((ss, self.counts[ss]))
        self.ops[eng].append((fn, w, inc, k))
        if inc is not None:
            self.counts[inc] += k
            return self.counts[inc]
        return None

    def n(self, sem):
        return self.counts[sem]

    def barrier(self):
        snap = [(s, c) for s, c in self.counts.items() if c > 0]
        for e in ENGINES:
            self.pending[e] = list(snap)

    def emit(self, block):
        for engname in ENGINES:
            def make_body(ops=self.ops[engname], sems=self.sems):
                def body(eng):
                    for fn, waits, inc, k in ops:
                        for s, v in waits:
                            eng.wait_ge(sems[s], v)
                        ins = fn(eng)
                        if inc is not None:
                            ins.then_inc(sems[inc], k)
                return body
            getattr(block, engname)(make_body())


class Arena:

    def __init__(self, master, origin, base, limit, tag):
        self.m, self.origin = master, origin
        self.off, self.limit, self.tag = base, limit, tag

    def al(self, name, shape, dtype):
        nbytes = int(np.prod(shape[1:])) * (2 if dtype == BF16 else 4)
        nbytes = (nbytes + 31) // 32 * 32
        o = self.off - self.origin
        view = self.m if dtype == F32 else self.m.bitcast(BF16)
        es = 4 if dtype == F32 else 2
        t = view[0:shape[0], o // es: o // es + int(np.prod(shape[1:]))]
        if len(shape) > 2:
            raise ValueError("arena views are 2-D")
        self.off += nbytes
        assert self.off <= self.limit, f"{self.tag}/{name}: off {self.off} > {self.limit}"
        return t


def build_program():
    nc = bass.Bass("TRN2", target_bir_lowering=False)
    xT = nc.dram_tensor("xT", [D, S], F32, kind="ExternalInput").ap()
    wq = nc.dram_tensor("wq", [D, D], F32, kind="ExternalInput").ap()
    wk = nc.dram_tensor("wk", [D, D], F32, kind="ExternalInput").ap()
    wv = nc.dram_tensor("wv", [D, D], F32, kind="ExternalInput").ap()
    wo = nc.dram_tensor("wo", [D, D], F32, kind="ExternalInput").ap()
    bo = nc.dram_tensor("bo", [128, DCH], F32, kind="ExternalInput").ap()
    gw = nc.dram_tensor("gw", [D, NE], F32, kind="ExternalInput").ap()
    sg = nc.dram_tensor("sg", [D, FF], F32, kind="ExternalInput").ap()
    su = nc.dram_tensor("su", [D, FF], F32, kind="ExternalInput").ap()
    sd = nc.dram_tensor("sd", [FF, D], F32, kind="ExternalInput").ap()
    rg = nc.dram_tensor("rg", [NE, D, FF], F32, kind="ExternalInput").ap()
    ru = nc.dram_tensor("ru", [NE, D, FF], F32, kind="ExternalInput").ap()
    rd = nc.dram_tensor("rd", [NE, FF, D], F32, kind="ExternalInput").ap()
    yT = nc.dram_tensor("yT", [D, T], F32, kind="ExternalOutput").ap()
    xh_h = nc.dram_tensor("xh_h", [D, S], BF16, kind="Internal").ap()
    y1_h = nc.dram_tensor("y1_h", [D, T], F32, kind="Internal").ap()
    if DBG:
        dbg_ot = nc.dram_tensor("dbg_ot", [128, DCH * T], BF16, kind=DBGKIND).ap()
        dbg_ares = nc.dram_tensor("dbg_ares", [128, DCH * T], F32, kind=DBGKIND).ap()
        dbg_x2 = nc.dram_tensor("dbg_x2", [128, DCH * T], BF16, kind=DBGKIND).ap()
        dbg_slot = nc.dram_tensor("dbg_slot", [1, T], F32, kind=DBGKIND).ap()
        dbg_sc = nc.dram_tensor("dbg_sc", [1, T], BF16, kind=DBGKIND).ap()
        dbg_ffn = nc.dram_tensor("dbg_ffn", [128, DCH * T], F32, kind=DBGKIND).ap()
        dbg_xe = nc.dram_tensor("dbg_xe", [128, DCH * CAP], BF16, kind=DBGKIND).ap()
        dbg_dnb = nc.dram_tensor("dbg_dnb", [128, 4 * KB], BF16, kind=DBGKIND).ap()
        dbg_xtok = nc.dram_tensor("dbg_xtok", [128, 16 * KB], BF16, kind=DBGKIND).ap()
        dbg_shr = nc.dram_tensor("dbg_shr", [128, DCH * T], F32, kind=DBGKIND).ap()
        dbg_pt = nc.dram_tensor("dbg_pt", [128, 4 * TBL], BF16, kind=DBGKIND).ap()
        dbg_sbc = nc.dram_tensor("dbg_sbc", [128, T], F32, kind=DBGKIND).ap()

    sems = {}
    for name in ("PE", "AC", "DV", "PL", "SO", "C0", "W0", "WO", "WG", "WU", "WD",
                 "MS", "SC", "Y1S", "C1", "XA0", "XA1", "XA2", "XA3", "XB0", "XB1",
                 "SH0", "SH1", "XT0", "XT1", "XF0", "XF1", "XG0", "XG1", "YL0", "YL1", "SR"):
        sems[name] = nc.semaphore(name).__enter__()
    P = Planner(nc, sems)

    GB = (nc.sbuf_base + 31) // 32 * 32
    TOP = nc.sbuf_top
    GSZ = 44 * KB
    GMST = nc.alloc_sbuf_tensor_at("GMST", [128, GSZ // 4], F32, offset=GB).ap()
    GA = Arena(GMST, GB, GB, GB + GSZ, "g")

    ones_b = GA.al("ones_b", [1, 128], BF16)
    ones_f = GA.al("ones_f", [1, 128], F32)
    col1_f = GA.al("col1_f", [128, 1], F32)
    identF = GA.al("identF", [128, 128], F32)
    gw_sb = GA.al("gw_sb", [128, DCH * NE], BF16)
    bo_sb = GA.al("bo_sb", [128, DCH], F32)
    iotaw = GA.al("iotaw", [128, NE], F32)
    epsr = GA.al("epsr", [1, 1], F32)
    ar2 = GA.al("ar2", [1, T], F32)
    br2 = GA.al("br2", [1, T], F32)
    scrow = GA.al("scrow", [1, T], BF16)
    identB = GA.al("identB", [128, 128], BF16)
    LS = GA.al("LS", [128, 128], BF16)
    ones2 = GA.al("ones2", [128, 128], BF16)
    ecaps = GA.al("ecaps", [128, NE], F32)
    iota512 = GA.al("iota512", [128, CAP], F32)
    iotacol = GA.al("iotacol", [128, 1], F32)
    mbf = GA.al("mbf", [128, 16 * NE], BF16)
    rkc = GA.al("rkc", [128, 8], F32)
    slotrow = GA.al("slotrow", [1, T], F32)
    sct2b = GA.al("sct2b", [128, 16], F32)
    P1 = (GA.off + 127) // 128 * 128
    NF32 = (TOP - P1) // 4 // 32 * 32
    MSC = nc.alloc_sbuf_tensor_at("MSC", [128, NF32], F32, offset=P1).ap()

    MB = MSC.bitcast(BF16)
    OT = MB[:, 0:DCH * T]
    x2b = MB[:, DCH * T: 2 * DCH * T]
    facc = MSC[:, 0:DCH * KB]

    PS = nc.alloc_psum_tensor("PS", [128, 4096], F32)
    BK = lambda b: PS[:, b * 512:(b + 1) * 512]

    P.op("gpsimd", lambda g: g.iota(iotaw[:], [[-1, NE]], base=NE, channel_multiplier=0,
                                    allow_small_or_imprecise_dtypes=True), inc="PL")
    P.op("gpsimd", lambda g: g.iota(identF[:], [[1, 128]], base=0, channel_multiplier=-1,
                                    allow_small_or_imprecise_dtypes=True), inc="PL")
    P.op("gpsimd", lambda g: g.iota(LS[:], [[1, 128]], base=0, channel_multiplier=-1,
                                    allow_small_or_imprecise_dtypes=True), inc="PL")
    P.op("gpsimd", lambda g: g.iota(ecaps[:], [[CAP, NE]], base=0, channel_multiplier=0,
                                    allow_small_or_imprecise_dtypes=True), inc="PL")
    P.op("gpsimd", lambda g: g.iota(iota512[:], [[1, CAP]], base=0, channel_multiplier=0,
                                    allow_small_or_imprecise_dtypes=True), inc="PL")
    P.op("gpsimd", lambda g: g.iota(iotacol[:], [[1, 1]], base=0, channel_multiplier=1,
                                    allow_small_or_imprecise_dtypes=True), inc="PL")
    ident_pl = P.n("PL")
    P.op("vector", lambda v: v.memset(ones_b[:], 1.0), inc="DV")
    P.op("vector", lambda v: v.memset(ones_f[:], 1.0), inc="DV")
    P.op("vector", lambda v: v.memset(col1_f[:], 1.0), inc="DV")
    P.op("vector", lambda v: v.memset(epsr[:], EPS), inc="DV")
    w_ident = P.op("vector", lambda v: v.tensor_single_scalar(identF[:], identF[:], 0.0, ALU.is_equal),
                   waits=[("PL", ident_pl)], inc="DV")
    P.op("vector", lambda v: v.tensor_copy(identB[:], identF[:]), inc="DV")
    P.op("vector", lambda v: v.tensor_single_scalar(LS[:], LS[:], 0.0, ALU.is_gt), inc="DV")
    P.op("vector", lambda v: v.memset(ones2[:], 1.0), inc="DV")
    init_dv = P.n("DV")
    P.barrier()
    P.op("sync", lambda s: s.dma_start(out=bo_sb[:], in_=bo), inc="C0", k=16)
    P.op("gpsimd", lambda g: g.dma_start(out=gw_sb[:].rearrange("p (c e) -> p c e", c=DCH),
                                         in_=gw.rearrange("(c p) e -> p c e", p=128)), inc="C1", k=16)

    def row_math(sumr, sqr, ar, br, tmp, width, c0=0, first_waits=None):
        sl = lambda r: r[0:1, c0:c0 + width]
        P.op("vector", lambda v: v.tensor_scalar_mul(sl(sumr), sl(sumr), 1.0 / D),
             waits=([("PE", P.n("PE"))] if first_waits is None else first_waits), inc="DV")
        P.op("vector", lambda v: v.tensor_scalar_mul(sl(sqr), sl(sqr), 1.0 / D), inc="DV")
        P.op("vector", lambda v: v.tensor_mul(sl(tmp), sl(sumr), sl(sumr)), inc="DV")
        r3 = P.op("vector", lambda v: v.tensor_sub(sl(sqr), sl(sqr), sl(tmp)), inc="DV")
        r4 = P.op("scalar", lambda s: s.activation(sl(tmp), sl(sqr), AF.Sqrt, bias=epsr[0:1, 0:1]),
                  waits=[("DV", r3)], inc="AC")
        P.op("vector", lambda v: v.reciprocal(sl(ar), sl(tmp)),
             waits=[("AC", r4)], inc="DV")
        return P.op("vector", lambda v: v.scalar_tensor_tensor(
            sl(br), sl(sumr), -1.0, sl(ar), ALU.mult, ALU.mult), inc="DV")

    A = Arena(MSC, P1, P1, TOP, "A")
    xfA = [A.al(f"xf{i}", [128, DCH * TBL], F32) for i in range(4)]
    sqA = [A.al(f"sq{i}", [128, TBL], F32) for i in range(2)]
    nscA = [A.al(f"nsc{i}", [128, TBL], F32) for i in range(2)]
    ssA = A.al("ssum", [1, S], F32)
    sqrA = A.al("ssq", [1, S], F32)
    a1r = A.al("a1r", [1, S], F32)
    b1r = A.al("b1r", [1, S], F32)
    xhA = [A.al(f"xh{i}", [128, DCH * TBL], BF16) for i in range(2)]

    NTBS = S // TBL
    mm2_id, cp_id, rm_id, nrm_id, st_id = {}, {}, {}, {}, {}
    for it in range(NTBS + 1):
        if it < NTBS:
            tb = it
            buf = xfA[tb % 4]
            sm = f"XA{tb % 4}"
            ltb = (sm, P.op("sync", lambda s, buf=buf, tb=tb: s.dma_start(
                out=buf[:].rearrange("p (c t) -> p c t", c=DCH),
                in_=xT[:, tb * TBL:(tb + 1) * TBL].rearrange("(c p) t -> p c t", p=128)),
                waits=[("DV", nrm_id.get(tb - 4))], inc=sm, k=16))
            pb = (tb % 2) * 2
            for chn in range(DCH):
                i = tb * DCH + chn
                sqb = sqA[i % 2]
                sq_ac = P.op("scalar", lambda s, buf=buf, chn=chn, sqb=sqb: s.activation(
                    sqb[:], buf[:, chn * TBL:(chn + 1) * TBL], AF.Square),
                    waits=[ltb, ("PE", mm2_id.get(i - 2))], inc="AC")
                P.op("tensor", lambda t, buf=buf, chn=chn, pb=pb: t.matmul(
                    BK(pb)[0:1, :], col1_f[:], buf[:, chn * TBL:(chn + 1) * TBL],
                    start=(chn == 0), stop=(chn == DCH - 1)),
                    waits=[ltb] + ([("DV", cp_id.get(tb - 2))] if chn == 0 else []),
                    inc="PE")
                mm2_id[i] = P.op("tensor", lambda t, sqb=sqb, pb=pb, chn=chn: t.matmul(
                    BK(pb + 1)[0:1, :], col1_f[:], sqb[:], start=(chn == 0), stop=(chn == DCH - 1)),
                    waits=[("AC", sq_ac)], inc="PE")
            P.op("vector", lambda v, tb=tb, pb=pb: v.tensor_copy(
                ssA[0:1, tb * TBL:(tb + 1) * TBL], BK(pb)[0:1, :]),
                waits=[("PE", mm2_id[tb * DCH + DCH - 1])], inc="DV")
            cp_id[tb] = P.op("vector", lambda v, tb=tb, pb=pb: v.tensor_copy(
                sqrA[0:1, tb * TBL:(tb + 1) * TBL], BK(pb + 1)[0:1, :]), inc="DV")
            rm_id[tb] = row_math(ssA, sqrA, a1r, b1r, b1r, TBL, c0=tb * TBL, first_waits=[])
        if it >= 1:
            tb = it - 1
            buf = xfA[tb % 4]
            pbb = 4 + (tb % 2) * 2
            P.op("tensor", lambda t, tb=tb, pbb=pbb: t.matmul(
                BK(pbb), ones_f[:], a1r[0:1, tb * TBL:(tb + 1) * TBL], start=True, stop=True),
                waits=[("DV", rm_id[tb]), ("DV", nrm_id.get(tb - 2))], inc="PE")
            bcb = P.op("tensor", lambda t, tb=tb, pbb=pbb: t.matmul(
                BK(pbb + 1), ones_f[:], b1r[0:1, tb * TBL:(tb + 1) * TBL], start=True, stop=True), inc="PE")
            xo = xhA[tb % 2]
            for chn in range(DCH):
                P.op("vector", lambda v, buf=buf, chn=chn, pbb=pbb, sl=chn % 2: v.tensor_mul(
                    nscA[sl][:], buf[:, chn * TBL:(chn + 1) * TBL], BK(pbb)),
                    waits=([("PE", bcb), st_id.get(tb - 2)] if chn == 0 else []), inc="DV")
                nrm_id[tb] = P.op("vector", lambda v, xo=xo, chn=chn, pbb=pbb, sl=chn % 2: v.tensor_add(
                    xo[:, chn * TBL:(chn + 1) * TBL], nscA[sl][:], BK(pbb + 1)), inc="DV")
            sms = f"SH{tb % 2}"
            st_id[tb] = (sms, P.op("gpsimd", lambda g, xo=xo, tb=tb: g.dma_start(
                out=xh_h[:, tb * TBL:(tb + 1) * TBL].rearrange("(c p) t -> p c t", p=128),
                in_=xo[:].rearrange("p (c t) -> p c t", c=DCH)),
                waits=[("DV", nrm_id[tb])], inc=sms, k=16))
    P.barrier()

    pref = None
    for hg in range(2):
        B = Arena(MSC, P1, P1 + 32 * KB, TOP, "B")
        Qc = B.al("Qc", [128, 4 * T], BF16)
        Kc = B.al("Kc", [128, 4 * S], BF16)
        Vt = B.al("Vt", [128, 32 * 520], BF16)
        wq_sb = B.al("wq", [128, DCH * TBL], BF16)
        wk_sb = B.al("wk", [128, DCH * TBL], BF16)
        wv_sb = B.al("wv", [128, DCH * TBL], BF16)
        xh_tb = [B.al(f"xh{i}", [128, DCH * TBL], BF16) for i in range(2)]
        At = [B.al(f"At{i}", [128, 1024], BF16) for i in range(3)]
        bc_sb = B.al("bc_sb", [64, TBL], F32)
        rsrow = B.al("rsrow", [1, TBL], F32)
        rsrow2 = B.al("rsrow2", [1, TBL], F32)

        if hg == 0:
            wls = {}
            for nm, wsrc, wdst in (("q", wq, wq_sb), ("k", wk, wk_sb), ("v", wv, wv_sb)):
                P.op("gpsimd", lambda g, wsrc=wsrc, wdst=wdst, hg=hg: g.dma_start(
                    out=wdst[:].rearrange("p (c o) -> p c o", c=DCH),
                    in_=wsrc[:, hg * TBL:(hg + 1) * TBL].rearrange("(c p) o -> p c o", p=128)),
                    inc="W0", k=16)
            for nm in ("q", "k", "v"):
                wls[nm] = ("W0", P.n("W0"))
        else:
            wls = pref["wls"]
        for tt in range(32):
            P.op("vector", lambda v, tt=tt: v.memset(
                Vt[:, tt * 520:(tt + 1) * 520].rearrange("p (h c) -> p h c", h=8)[:, :, 64:65], 1.0), inc="DV")
        vones_dv = P.n("DV")

        kq_ev, v_ev, pe_tb = [], [], {}
        for tb in range(NTBS):
            buf = xh_tb[tb % 2]
            smt = f"XT{tb % 2}"
            if hg == 1 and tb < 2:
                ltb = pref["xh"][tb]
            else:
                ltb = (smt, P.op("sync", lambda s, buf=buf, tb=tb: s.dma_start(
                    out=buf[:].rearrange("p (c t) -> p c t", c=DCH),
                    in_=xh_h[:, tb * TBL:(tb + 1) * TBL].rearrange("(c p) t -> p c t", p=128)),
                    waits=[("PE", pe_tb.get(tb - 2))], inc=smt, k=16))
            for kind in (("k", "q") if tb < 4 else ("k",)):
                wsb, dst, dstw = (wk_sb, Kc, S) if kind == "k" else (wq_sb, Qc, T)
                for och in range(4):
                    seq = len(kq_ev)
                    pb = seq % 2
                    mmlast = None
                    for chn in range(DCH):
                        w0 = ([ltb, wls[kind]] +
                              ([("DV", kq_ev[seq - 2])] if seq >= 2 else [])) if chn == 0 else []
                        mmlast = P.op("tensor", lambda t, wsb=wsb, chn=chn, och=och, buf=buf, pb=pb: t.matmul(
                            BK(pb), wsb[:, chn * TBL + och * 128: chn * TBL + och * 128 + 128],
                            buf[:, chn * TBL:(chn + 1) * TBL],
                            start=(chn == 0), stop=(chn == DCH - 1)),
                            waits=w0, inc="PE")
                    kq_ev.append(P.op("vector", lambda v, dst=dst, dstw=dstw, och=och, tb=tb, pb=pb: v.tensor_copy(
                        dst[:, och * dstw + tb * TBL: och * dstw + (tb + 1) * TBL], BK(pb)),
                        waits=[("PE", mmlast)], inc="DV"))
            for ttl in range(4):
                tt = tb * 4 + ttl
                sv = len(v_ev)
                pb = 2 + sv % 2
                mmlast = None
                for chn in range(DCH):
                    w0 = ([ltb, wls["v"]] +
                          ([("AC", v_ev[sv - 2])] if sv >= 2 else [])) if chn == 0 else []
                    mmlast = P.op("tensor", lambda t, chn=chn, buf=buf, ttl=ttl, pb=pb, wv_sb=wv_sb: t.matmul(
                        BK(pb), buf[:, chn * TBL + ttl * 128: chn * TBL + ttl * 128 + 128],
                        wv_sb[:, chn * TBL:(chn + 1) * TBL],
                        start=(chn == 0), stop=(chn == DCH - 1)),
                        waits=w0, inc="PE")
                v_ev.append(P.op("scalar", lambda s, tt=tt, pb=pb, Vt=Vt: s.copy(
                    Vt[:, tt * 520:(tt + 1) * 520].rearrange("p (h c) -> p h c", h=8)[:, :, 0:64],
                    BK(pb).rearrange("p (h c) -> p h c", h=8)),
                    waits=[("PE", mmlast)], inc="AC"))
            pe_tb[tb] = P.n("PE")
        b_dv, b_ac = P.n("DV"), P.n("AC")

        prev_exp = None
        prev_av = None
        prev_ot = None
        for hp in range(4):
            och = hp
            for qb in range(4):
                qk_id, ex_id, av_id = {}, {}, {}
                for kk in range(34):
                    if kk < 32:
                        kb, sl = kk, kk % 2
                        for hx in range(2):
                            rb = hx * 64
                            wts = []
                            if hx == 0:
                                if kb == 0 and hp == 0 and qb == 0:
                                    wts += [("DV", b_dv), ("AC", b_ac)]
                                if kb >= 2:
                                    wts.append(("AC", ex_id[kb - 2]))
                                elif prev_exp is not None:
                                    wts.append(("AC", prev_exp[kb]))
                            qk_id[(kb, hx)] = P.op("tensor", lambda t, och=och, rb=rb, kb=kb, hx=hx, qb=qb, sl=sl, Kc=Kc, Qc=Qc: t.matmul(
                                PS[:, sl * 1024 + hx * 512: sl * 1024 + (hx + 1) * 512],
                                Kc[rb:rb + 64, och * S + kb * 128: och * S + (kb + 1) * 128],
                                Qc[rb:rb + 64, och * T + qb * TBL: och * T + (qb + 1) * TBL],
                                start=True, stop=True, tile_position=(rb, 0)),
                                waits=wts, inc="PE")
                        ewts = [("PE", qk_id[(kb, 1)])]
                        if kb >= 3:
                            ewts.append(("PE", av_id[kb - 3]))
                        elif prev_av is not None:
                            ewts.append(("PE", prev_av))
                        ex_id[kb] = P.op("scalar", lambda s, sl=sl, a3=kb % 3, At=At: s.activation(
                            At[a3][:], PS[:, sl * 1024: (sl + 1) * 1024], AF.Exp, scale=ISC),
                            waits=ewts, inc="AC")
                    if kk >= 2:
                        kb = kk - 2
                        avlast = None
                        for hx in range(2):
                            hh = 2 * hp + hx
                            avw = [("AC", ex_id[kb])] if hx == 0 else []
                            if kb == 0 and hx == 0:
                                avw.append(("DV", vones_dv))
                                if prev_ot is not None:
                                    avw.append(("DV", prev_ot))
                            avlast = P.op("tensor", lambda t, hh=hh, kb=kb, hx=hx, a3=kb % 3, Vt=Vt, At=At: t.matmul(
                                PS[0:65, 2048 + hx * 512: 2048 + (hx + 1) * 512],
                                Vt[:, kb * 520 + hh * 65: kb * 520 + (hh + 1) * 65],
                                At[a3][:, hx * 512:(hx + 1) * 512],
                                start=(kb == 0), stop=(kb == 31)),
                                waits=avw, inc="PE")
                        av_id[kb] = avlast
                prev_exp = {0: ex_id[30], 1: ex_id[31]}
                prev_av = av_id[31]
                for hx in range(2):
                    h = hg * 8 + 2 * hp + hx
                    oc2, rb2 = h // 2, (h % 2) * 64
                    P.op("scalar", lambda s, hx=hx, rsrow=rsrow: s.activation(
                        rsrow[0:1, :], PS[64:65, 2048 + hx * 512: 2048 + (hx + 1) * 512], AF.Ln),
                        waits=[("PE", av_id[31])], inc="AC")
                    rc = P.op("scalar", lambda s, rsrow=rsrow, rsrow2=rsrow2: s.activation(
                        rsrow2[0:1, :], rsrow[0:1, :], AF.Exp, scale=-1.0), inc="AC")
                    bcm = P.op("tensor", lambda t, rsrow2=rsrow2: t.matmul(
                        PS[0:64, 3072:3584], ones_f[0:1, 0:64], rsrow2[0:1, :], start=True, stop=True),
                        waits=[("AC", rc)], inc="PE")
                    bce = P.op("scalar", lambda s, bc_sb=bc_sb: s.copy(bc_sb[:], PS[0:64, 3072:3584]),
                               waits=[("PE", bcm), ("DV", prev_ot)], inc="AC")
                    prev_ot = P.op("vector", lambda v, oc2=oc2, rb2=rb2, qb=qb, hx=hx, bc_sb=bc_sb: v.tensor_mul(
                        OT[rb2:rb2 + 64, oc2 * T + qb * TBL: oc2 * T + (qb + 1) * TBL],
                        PS[0:64, 2048 + hx * 512: 2048 + (hx + 1) * 512], bc_sb[:]),
                        waits=[("AC", bce)], inc="DV")
        if hg == 0:
            nwls = {}
            for nm, wsrc, wdst in (("q", wq, wq_sb), ("k", wk, wk_sb), ("v", wv, wv_sb)):
                P.op("gpsimd", lambda g, wsrc=wsrc, wdst=wdst: g.dma_start(
                    out=wdst[:].rearrange("p (c o) -> p c o", c=DCH),
                    in_=wsrc[:, TBL: 2 * TBL].rearrange("(c p) o -> p c o", p=128)),
                    waits=[("PE", pe_tb[NTBS - 1])], inc="W0", k=16)
            for nm in ("q", "k", "v"):
                nwls[nm] = ("W0", P.n("W0"))
            pxh = {}
            for tbp in range(2):
                bufp = xh_tb[tbp]
                smt = f"XT{tbp}"
                pxh[tbp] = (smt, P.op("sync", lambda s, bufp=bufp, tbp=tbp: s.dma_start(
                    out=bufp[:].rearrange("p (c t) -> p c t", c=DCH),
                    in_=xh_h[:, tbp * TBL:(tbp + 1) * TBL].rearrange("(c p) t -> p c t", p=128)),
                    waits=[("PE", pe_tb[NTBS - 2 + tbp])], inc=smt, k=16))
            pref = {"wls": nwls, "xh": pxh}
        P.barrier()

    if DBG:
        P.op("sync", lambda s: s.dma_start(out=dbg_ot, in_=OT[:]), inc="SO", k=16)

    Dz = Arena(MSC, P1, P1 + 64 * KB, TOP, "D")
    ares = Dz.al("ares", [128, DCH * T], F32)
    wo_sb = Dz.al("wo", [128, DCH * KB], BF16)
    xftl = [Dz.al(f"xf{i}", [128, TBL], F32) for i in range(2)]
    x1tl = [Dz.al(f"x1{i}", [128, TBL], BF16) for i in range(2)]
    sqD = [Dz.al(f"sq{i}", [128, TBL], F32) for i in range(2)]
    sq3 = [Dz.al(f"sq3{i}", [128, TBL], F32) for i in range(2)]
    ssD = MSC[0:1, 0:T]
    sqrD = MSC[0:1, 2 * KB: 2 * KB + T]
    lg_sb = Dz.al("lg", [128, 16 * NE], F32)
    mxt = Dz.al("mxt", [128, 16], F32)
    sct = Dz.al("sct", [128, 16], F32)
    mkt = Dz.al("mkt", [128, NE], F32)
    wkt = Dz.al("wkt", [128, NE], F32)
    sct2 = Dz.al("sct2", [16, 128], F32)
    sct3 = Dz.al("sct3", [16, 128], F32)

    wol = ("WO", P.op("gpsimd", lambda g: g.dma_start(
        out=wo_sb[:].rearrange("p (c o) -> p c o", c=DCH),
        in_=wo.rearrange("(c p) o -> p c o", p=128)), inc="WO", k=16))

    d1_dv = {}
    for och in range(DCH):
        for tb in range(4):
            i = och * 4 + tb
            pb = i % 2
            smg = f"XG{i % 2}"
            lx = (smg, P.op("sync", lambda s, och=och, tb=tb, sl=i % 2: s.dma_start(
                out=x1tl[sl][:], in_=xh_h[och * 128:(och + 1) * 128, tb * TBL:(tb + 1) * TBL]),
                waits=[("DV", d1_dv.get(i - 2))], inc=smg, k=16))
            mmlast = None
            for cin in range(DCH):
                wts = []
                if cin == 0:
                    wts.append(wol)
                    if i >= 2:
                        wts.append(("DV", d1_dv[i - 2]))
                mmlast = P.op("tensor", lambda t, och=och, tb=tb, cin=cin, pb=pb: t.matmul(
                    BK(pb), wo_sb[:, cin * KB + och * 128: cin * KB + och * 128 + 128],
                    OT[:, cin * T + tb * TBL: cin * T + (tb + 1) * TBL],
                    start=(cin == 0), stop=(cin == DCH - 1)),
                    waits=wts, inc="PE")
            d1_dv[i] = P.op("vector", lambda v, och=och, tb=tb, pb=pb, sl=i % 2: v.scalar_tensor_tensor(
                ares[:, och * T + tb * TBL: och * T + (tb + 1) * TBL],
                BK(pb), bo_sb[:, och:och + 1], x1tl[sl][:], ALU.add, ALU.add),
                waits=[("PE", mmlast), lx], inc="DV")

    def ln_stats(buf, cstride, colbase, width, sqr_ring, ssr, sqrr):
        ntb = width // TBL
        mm_l, cp_l = {}, {}
        for tb in range(ntb):
            for chn in range(DCH):
                i = tb * DCH + chn
                sqb = sqr_ring[i % 2]
                col = chn * cstride + colbase + tb * TBL
                sq_ac = P.op("scalar", lambda s, sqb=sqb, col=col, buf=buf: s.activation(
                    sqb[:], buf[:, col:col + TBL], AF.Square),
                    waits=[("PE", mm_l.get(i - 2))], inc="AC")
                pb = (tb % 2) * 2
                P.op("tensor", lambda t, buf=buf, col=col, pb=pb, chn=chn: t.matmul(
                    BK(pb)[0:1, :], col1_f[:], buf[:, col:col + TBL],
                    start=(chn == 0), stop=(chn == DCH - 1)),
                    waits=([("DV", cp_l.get(tb - 2))] if chn == 0 else []), inc="PE")
                mm_l[i] = P.op("tensor", lambda t, sqb=sqb, pb=pb, chn=chn: t.matmul(
                    BK(pb + 1)[0:1, :], col1_f[:], sqb[:],
                    start=(chn == 0), stop=(chn == DCH - 1)),
                    waits=[("AC", sq_ac)], inc="PE")
            pb = (tb % 2) * 2
            P.op("vector", lambda v, tb=tb, pb=pb: v.tensor_copy(
                ssr[0:1, tb * TBL:(tb + 1) * TBL], BK(pb)[0:1, :]),
                waits=[("PE", mm_l[tb * DCH + DCH - 1])], inc="DV")
            cp_l[tb] = P.op("vector", lambda v, tb=tb, pb=pb: v.tensor_copy(
                sqrr[0:1, tb * TBL:(tb + 1) * TBL], BK(pb + 1)[0:1, :]), inc="DV")

    P.barrier()
    ln_stats(ares, T, 0, T, sqD, ssD, sqrD)
    rm2 = row_math(ssD, sqrD, ar2, br2, br2, T)

    d4_dv, mm3_id, cp3_id = {}, {}, {}
    for tb in range(4):
        pb = 4 + (tb % 2) * 2
        P.op("tensor", lambda t, tb=tb, pb=pb: t.matmul(
            BK(pb), ones_f[:], ar2[0:1, tb * TBL:(tb + 1) * TBL], start=True, stop=True),
            waits=[("DV", rm2) if tb == 0 else ("DV", d4_dv.get((tb - 2, DCH - 1)))], inc="PE")
        bcb = P.op("tensor", lambda t, tb=tb, pb=pb: t.matmul(
            BK(pb + 1), ones_f[:], br2[0:1, tb * TBL:(tb + 1) * TBL], start=True, stop=True), inc="PE")
        for och in range(DCH):
            i = tb * DCH + och
            smf = f"XF{i % 2}"
            lx = (smf, P.op("sync", lambda s, och=och, tb=tb, sl=i % 2: s.dma_start(
                out=xftl[sl][:], in_=xT[och * 128:(och + 1) * 128, tb * TBL:(tb + 1) * TBL]),
                waits=[("DV", d4_dv.get((tb, och - 2)) or d4_dv.get((tb - 1, DCH - 2 + och)))], inc=smf, k=16))
            P.op("vector", lambda v, och=och, tb=tb, pb=pb, sl=i % 2: v.tensor_mul(
                sqD[sl][:], ares[:, och * T + tb * TBL: och * T + (tb + 1) * TBL], BK(pb)),
                waits=[("PE", bcb)], inc="DV")
            P.op("vector", lambda v, pb=pb, sl=i % 2: v.tensor_add(sqD[sl][:], sqD[sl][:], BK(pb + 1)), inc="DV")
            d4_dv[(tb, och)] = P.op("vector", lambda v, och=och, tb=tb, sl=i % 2: v.tensor_add(
                ares[:, och * T + tb * TBL: och * T + (tb + 1) * TBL], sqD[sl][:], xftl[sl][:]),
                waits=[lx], inc="DV")
            sq_ac = P.op("scalar", lambda s, och=och, tb=tb, sl=i % 2: s.activation(
                sq3[sl][:], ares[:, och * T + tb * TBL: och * T + (tb + 1) * TBL], AF.Square),
                waits=[("DV", d4_dv[(tb, och)]), ("PE", mm3_id.get(i - 2))], inc="AC")
            pbs = (tb % 2) * 2
            P.op("tensor", lambda t, och=och, tb=tb, pbs=pbs: t.matmul(
                BK(pbs)[0:1, :], col1_f[:], ares[:, och * T + tb * TBL: och * T + (tb + 1) * TBL],
                start=(och == 0), stop=(och == DCH - 1)),
                waits=[("DV", d4_dv[(tb, och)])] + ([("DV", cp3_id.get(tb - 2))] if och == 0 else []),
                inc="PE")
            mm3_id[i] = P.op("tensor", lambda t, sl=i % 2, pbs=pbs, och=och: t.matmul(
                BK(pbs + 1)[0:1, :], col1_f[:], sq3[sl][:], start=(och == 0), stop=(och == DCH - 1)),
                waits=[("AC", sq_ac)], inc="PE")
        pbs = (tb % 2) * 2
        P.op("vector", lambda v, tb=tb, pbs=pbs: v.tensor_copy(
            ssD[0:1, tb * TBL:(tb + 1) * TBL], BK(pbs)[0:1, :]),
            waits=[("PE", mm3_id[tb * DCH + DCH - 1])], inc="DV")
        cp3_id[tb] = P.op("vector", lambda v, tb=tb, pbs=pbs: v.tensor_copy(
            sqrD[0:1, tb * TBL:(tb + 1) * TBL], BK(pbs + 1)[0:1, :]), inc="DV")
    for och in range(DCH):
        P.op("sync", lambda s, och=och: s.dma_start(
            out=y1_h[och * 128:(och + 1) * 128, :], in_=ares[:, och * T:(och + 1) * T]),
            waits=[("DV", d4_dv[(3, och)])], inc="Y1S", k=16)

    rm3 = row_math(ssD, sqrD, ar2, br2, br2, T)

    d7_dv = {}
    for tb in range(4):
        pb = 4 + (tb % 2) * 2
        P.op("tensor", lambda t, tb=tb, pb=pb: t.matmul(
            BK(pb), ones_f[:], ar2[0:1, tb * TBL:(tb + 1) * TBL], start=True, stop=True),
            waits=[("DV", rm3) if tb == 0 else ("DV", d7_dv.get((tb - 2, DCH - 1)))], inc="PE")
        bcb = P.op("tensor", lambda t, tb=tb, pb=pb: t.matmul(
            BK(pb + 1), ones_f[:], br2[0:1, tb * TBL:(tb + 1) * TBL], start=True, stop=True), inc="PE")
        for och in range(DCH):
            i = tb * DCH + och
            P.op("vector", lambda v, och=och, tb=tb, pb=pb, sl=i % 2: v.tensor_mul(
                sqD[sl][:], ares[:, och * T + tb * TBL: och * T + (tb + 1) * TBL], BK(pb)),
                waits=[("PE", bcb)], inc="DV")
            d7_dv[(tb, och)] = P.op("vector", lambda v, och=och, tb=tb, pb=pb, sl=i % 2: v.tensor_add(
                x2b[:, och * T + tb * TBL: och * T + (tb + 1) * TBL], sqD[sl][:], BK(pb + 1)), inc="DV")
    x2_done = P.n("DV")

    P.barrier()
    rk_dv = {}
    for ttl in range(16):
        mmlast = None
        for chn in range(DCH):
            wts = []
            if chn == 0 and ttl >= 2:
                wts.append(("DV", rk_dv[ttl - 2]))
            mmlast = P.op("tensor", lambda t, ttl=ttl, chn=chn: t.matmul(
                PS[0:128, 2048 + (ttl % 2) * 512: 2048 + (ttl % 2) * 512 + NE],
                x2b[:, chn * T + ttl * 128: chn * T + (ttl + 1) * 128],
                gw_sb[:, chn * NE:(chn + 1) * NE],
                start=(chn == 0), stop=(chn == DCH - 1)),
                waits=wts, inc="PE")
        P.op("vector", lambda v, ttl=ttl: v.tensor_copy(
            lg_sb[:, ttl * NE:(ttl + 1) * NE], PS[0:128, 2048 + (ttl % 2) * 512: 2048 + (ttl % 2) * 512 + NE]),
            waits=[("PE", mmlast)], inc="DV")
        P.op("vector", lambda v, ttl=ttl: v.tensor_reduce(
            mxt[:, ttl:ttl + 1], lg_sb[:, ttl * NE:(ttl + 1) * NE], AX.X, ALU.max), inc="DV")
        P.op("vector", lambda v, ttl=ttl: v.tensor_single_scalar(
            mkt[:], lg_sb[:, ttl * NE:(ttl + 1) * NE], mxt[:, ttl:ttl + 1], ALU.is_equal), inc="DV")
        P.op("vector", lambda v: v.tensor_mul(wkt[:], mkt[:], iotaw[:]), inc="DV")
        P.op("vector", lambda v, ttl=ttl: v.tensor_reduce(sct[:, ttl:ttl + 1], wkt[:], AX.X, ALU.max), inc="DV")
        P.op("vector", lambda v, ttl=ttl: v.tensor_single_scalar(
            mkt[:], wkt[:], sct[:, ttl:ttl + 1], ALU.is_equal), inc="DV")
        mb_dv = P.op("vector", lambda v, ttl=ttl: v.tensor_copy(
            mbf[:, ttl * NE:(ttl + 1) * NE], mkt[:]), inc="DV")
        rlast = None
        for j in range(ttl + 1):
            lt = ones2 if j < ttl else LS
            rlast = P.op("tensor", lambda t, j=j, lt=lt, ttl=ttl: t.matmul(
                PS[0:128, 3584:3584 + NE], lt[:], mbf[:, j * NE:(j + 1) * NE],
                start=(j == 0), stop=(j == ttl)),
                waits=[("DV", mb_dv), ("DV", rk_dv.get(ttl - 1))] if j == 0 else [], inc="PE")
        P.op("vector", lambda v: v.tensor_mul(wkt[:], mkt[:], PS[0:128, 3584:3584 + NE]),
             waits=[("PE", rlast)], inc="DV")
        P.op("vector", lambda v: v.tensor_reduce(rkc[:, 0:1], wkt[:], AX.X, ALU.add), inc="DV")
        P.op("vector", lambda v: v.tensor_mul(wkt[:], mkt[:], ecaps[:]), inc="DV")
        P.op("vector", lambda v: v.tensor_reduce(rkc[:, 1:2], wkt[:], AX.X, ALU.add), inc="DV")
        P.op("vector", lambda v: v.tensor_add(rkc[:, 2:3], rkc[:, 0:1], rkc[:, 1:2]), inc="DV")
        P.op("vector", lambda v: v.tensor_single_scalar(rkc[:, 3:4], rkc[:, 0:1], float(CAP) - 0.5, ALU.is_ge), inc="DV")
        rk_dv[ttl] = P.op("vector", lambda v, ttl=ttl: v.scalar_tensor_tensor(
            sct2b[:, ttl:ttl + 1], rkc[:, 3:4], 8192.0, rkc[:, 2:3], ALU.mult, ALU.add), inc="DV")
    sg_ac = P.op("scalar", lambda s: s.activation(sct[:], mxt[:], AF.Sigmoid),
                 waits=[("DV", P.n("DV"))], inc="AC")
    trs = P.op("tensor", lambda t: t.transpose(PS[0:16, 3072:3200], sct[:], identF[:]),
               waits=[("AC", sg_ac), ("DV", w_ident)], inc="PE")
    sc_dv = P.op("vector", lambda v: v.tensor_copy(sct2[:], PS[0:16, 3072:3200]),
                 waits=[("PE", trs)], inc="DV")
    sc_xl = ("SC", P.op("gpsimd", lambda g: g.dma_start(out=scrow[0:1, :], in_=sct2[:]),
                 waits=[("DV", sc_dv)], inc="SC", k=16))
    trs2 = P.op("tensor", lambda t: t.transpose(PS[0:16, 3200:3328], sct2b[:], identF[:]),
                waits=[("DV", rk_dv[15])], inc="PE")
    sc3_dv = P.op("vector", lambda v: v.tensor_copy(sct3[:], PS[0:16, 3200:3328]),
                  waits=[("PE", trs2)], inc="DV")
    sr_xl = ("SR", P.op("sync", lambda s: s.dma_start(out=slotrow[0:1, :], in_=sct3[:]),
        waits=[("DV", sc3_dv)], inc="SR", k=16))
    if DBG:
        P.op("sync", lambda s: s.dma_start(out=dbg_x2, in_=x2b[:]), waits=[("DV", x2_done)], inc="SO", k=16)
        P.op("sync", lambda s: s.dma_start(out=dbg_slot, in_=slotrow[0:1, :]), waits=[sr_xl], inc="SO", k=16)
        P.op("sync", lambda s: s.dma_start(out=dbg_sc, in_=scrow[0:1, :]), waits=[sc_xl], inc="SO", k=16)
        P.op("sync", lambda s: s.dma_start(out=dbg_ares, in_=ares[:]), inc="SO", k=16)

    P.barrier()
    MB2 = MSC.bitcast(BF16)
    PSB = PS.bitcast(BF16)
    X_tok = MB2[:, 0:16 * KB]
    xe = MB2[:, 16 * KB:16 * KB + DCH * CAP]
    Pg = MB2[:, 20 * KB:20 * KB + 16 * CAP]
    dnb = MB2[:, 20 * KB:20 * KB + 4 * KB]
    PTt = MB2[:, 24 * KB:24 * KB + 4 * TBL]
    slot_bc = MSC[:, 14 * KB:14 * KB + T]
    ssE = MSC[0:1, 8 * KB:8 * KB + T]
    sqrE = MSC[0:1, 10 * KB:10 * KB + T]
    y1tl = [MSC[:, 12 * KB + i * TBL:12 * KB + (i + 1) * TBL] for i in range(2)]
    sqE = [MSC[:, 12 * KB + KB + i * TBL:12 * KB + KB + (i + 1) * TBL] for i in range(2)]
    faccv = MSC[:, 16 * KB:16 * KB + DCH * T]

    E = Arena(MSC, P1, P1 + 128 * KB, TOP, "E")
    wg_sb = E.al("wg", [128, DCH * TBL], BF16)
    wu_sb = E.al("wu", [128, DCH * TBL], BF16)
    wd_sb = E.al("wd", [128, 8 * KB], BF16)

    gs = E.al("gs", [128, 8 * TBL], BF16)
    um = [E.al(f"um{i}", [128, TBL], BF16) for i in range(2)]

    last_g = last_u = last_d = None
    acc_dv = {}
    bank_rd = {}

    npass = 0
    for fh in range(2):
        for fq in range(2):
            wgl = ("WG", P.op("gpsimd", lambda g, fh=fh, fq=fq: g.dma_start(
                out=wg_sb[:].rearrange("p (c f) -> p c f", c=DCH),
                in_=sg[:, fh * KB + fq * TBL: fh * KB + (fq + 1) * TBL].rearrange("(c p) f -> p c f", p=128)),
                waits=[("PE", last_g)], inc="WG", k=16))
            wul = ("WU", P.op("gpsimd", lambda g, fh=fh, fq=fq: g.dma_start(
                out=wu_sb[:].rearrange("p (c f) -> p c f", c=DCH),
                in_=su[:, fh * KB + fq * TBL: fh * KB + (fq + 1) * TBL].rearrange("(c p) f -> p c f", p=128)),
                waits=[("PE", last_u)], inc="WU", k=16))
            if fq == 0:
                wdl = ("WD", P.op("gpsimd", lambda g, fh=fh: g.dma_start(
                    out=wd_sb[:].rearrange("p (c dd) -> p c dd", c=8),
                    in_=sd[fh * KB:(fh + 1) * KB, :].rearrange("(c p) dd -> p c dd", p=128)),
                    waits=[("PE", last_d)], inc="WD", k=16))
            for tb in range(4):
                gsil, um_dv, hdone = {}, {}, {}
                for fc in range(4):
                    pb = fc % 2
                    mmlast = None
                    for chn in range(DCH):
                        wts = []
                        if chn == 0:
                            wts.append(wgl)
                            wts.append(bank_rd.get(pb))
                            if fc < 2:
                                wts.append(("PE", last_d))
                        mmlast = P.op("tensor", lambda t, chn=chn, fc=fc, fq=fq, tb=tb, pb=pb: t.matmul(
                            BK(pb), wg_sb[:, chn * TBL + fc * 128: chn * TBL + fc * 128 + 128],
                            x2b[:, chn * T + tb * TBL: chn * T + (tb + 1) * TBL],
                            start=(chn == 0), stop=(chn == DCH - 1)),
                            waits=wts, inc="PE")
                    last_g = mmlast
                    gsil[fc] = P.op("scalar", lambda st, fc=fc, pb=pb: st.activation(
                        gs[:, fc * TBL:(fc + 1) * TBL], BK(pb), AF.Silu),
                        waits=[("PE", mmlast)], inc="AC")
                    bank_rd[pb] = ("AC", gsil[fc])
                for fc in range(4):
                    pb = fc % 2
                    mmlast = None
                    for chn in range(DCH):
                        wts = []
                        if chn == 0:
                            wts.append(wul)
                            wts.append(bank_rd.get(pb))
                        mmlast = P.op("tensor", lambda t, chn=chn, fc=fc, tb=tb, pb=pb: t.matmul(
                            BK(pb), wu_sb[:, chn * TBL + fc * 128: chn * TBL + fc * 128 + 128],
                            x2b[:, chn * T + tb * TBL: chn * T + (tb + 1) * TBL],
                            start=(chn == 0), stop=(chn == DCH - 1)),
                            waits=wts, inc="PE")
                    last_u = mmlast
                    uw = [("PE", mmlast), ("PL", hdone.get(fc - 2) or P.n("PL"))]
                    um_dv[fc] = P.op("vector", lambda v, fc=fc, pb=pb: v.tensor_copy(
                        um[fc % 2][:], BK(pb)), waits=uw, inc="DV")
                    bank_rd[pb] = ("DV", um_dv[fc])
                    hdone[fc] = P.op("gpsimd", lambda g, fc=fc: g.tensor_mul(
                        gs[:, fc * TBL:(fc + 1) * TBL], gs[:, fc * TBL:(fc + 1) * TBL], um[fc % 2][:]),
                        waits=[("DV", um_dv[fc]), ("AC", gsil[fc])], inc="PL")
                for dch in range(DCH):
                    pb = 2 + dch % 2
                    mmlast = None
                    for fc in range(4):
                        wts = []
                        if fc == 0:
                            wts.append(wdl)
                            wts.append(("PL", hdone[3]))
                            wts.append(bank_rd.get(pb))
                        mmlast = P.op("tensor", lambda t, dch=dch, fc=fc, fq=fq, pb=pb: t.matmul(
                            BK(pb), wd_sb[:, (fq * 4 + fc) * KB + dch * 128: (fq * 4 + fc) * KB + dch * 128 + 128],
                            gs[:, fc * TBL:(fc + 1) * TBL],
                            start=(fc == 0), stop=(fc == 3)),
                            waits=wts, inc="PE")
                    last_d = mmlast
                    if npass == 0:
                        acc_dv[(tb, dch)] = P.op("vector", lambda v, dch=dch, tb=tb, pb=pb: v.tensor_copy(
                            faccv[:, dch * T + tb * TBL: dch * T + (tb + 1) * TBL], BK(pb)),
                            waits=[("PE", mmlast)], inc="DV")
                    else:
                        acc_dv[(tb, dch)] = P.op("vector", lambda v, dch=dch, tb=tb, pb=pb: v.tensor_add(
                            faccv[:, dch * T + tb * TBL: dch * T + (tb + 1) * TBL],
                            faccv[:, dch * T + tb * TBL: dch * T + (tb + 1) * TBL], BK(pb)),
                            waits=[("PE", mmlast)], inc="DV")
                    bank_rd[pb] = ("DV", acc_dv[(tb, dch)])
            npass += 1

    if XBAR & 1:
        P.barrier()
    if DBG:
        P.op("sync", lambda s: s.dma_start(out=dbg_shr, in_=faccv), waits=[("DV", P.n("DV"))], inc="SO", k=16)
        for _e2 in ENGINES:
            P.pending[_e2].append(("SO", P.n("SO")))
    shared_pe = P.n("PE")
    cv_dv = {}
    for tb in range(4):
        pb = 6 + tb % 2
        bcs = P.op("tensor", lambda t, tb=tb, pb=pb: t.matmul(
            BK(pb), ones_b[0:1, :], scrow[0:1, tb * TBL:(tb + 1) * TBL],
            start=True, stop=True),
            waits=([sc_xl, ("PE", shared_pe)] if tb == 0 else []) +
                  [bank_rd.get(pb)], inc="PE")
        for chn in range(DCH):
            cv_dv[(tb, chn)] = P.op("vector", lambda v, chn=chn, tb=tb, pb=pb: v.tensor_mul(
                x2b[:, chn * T + tb * TBL: chn * T + (tb + 1) * TBL],
                x2b[:, chn * T + tb * TBL: chn * T + (tb + 1) * TBL], BK(pb)),
                waits=[("PE", bcs)], inc="DV")
            bank_rd[pb] = ("DV", cv_dv[(tb, chn)])
    P.barrier()
    bank_rd = {}
    tr_dv = {}
    for tt in range(16):
        for dch in range(DCH):
            j = tt * DCH + dch
            sl = j % 2
            po = 6144 + 1024 * sl
            trp = P.op("tensor", lambda t, tt=tt, dch=dch, po=po: t.transpose(
                PSB[:, po:po + 128], x2b[:, dch * T + tt * 128: dch * T + (tt + 1) * 128], identB[:]),
                waits=[("DV", tr_dv.get(j - 2))], inc="PE")
            tr_dv[j] = P.op("vector", lambda v, tt=tt, dch=dch, po=po: v.tensor_copy(
                X_tok[:, tt * KB + dch * 128: tt * KB + dch * 128 + 128], PSB[:, po:po + 128]),
                waits=[("PE", trp)], inc="DV")
    if XBAR & 2:
        P.barrier()
    sb_ac = {}
    for tb in range(4):
        pb = 4 + tb % 2
        bcm = P.op("tensor", lambda t, tb=tb, pb=pb: t.matmul(
            BK(pb), ones_f[0:1, :], slotrow[0:1, tb * TBL:(tb + 1) * TBL], start=True, stop=True),
            waits=[sr_xl if tb == 0 else None, ("AC", sb_ac.get(tb - 2))], inc="PE")
        sb_ac[tb] = P.op("scalar", lambda st, tb=tb, pb=pb: st.copy(
            slot_bc[:, tb * TBL:(tb + 1) * TBL], BK(pb)),
            waits=[("PE", bcm)], inc="AC")
        bank_rd[pb] = ("AC", sb_ac[tb])
    P.barrier()

    last_scatter = None
    sc_fa = {}
    for e in range(NE):
        pg_dv = {}
        for tt in range(16):
            pg_dv[tt] = P.op("vector", lambda v, tt=tt, e=e: v.tensor_scalar(
                Pg[:, tt * CAP:(tt + 1) * CAP], iota512[:], sct2b[:, tt:tt + 1], -float(e * CAP),
                ALU.subtract, ALU.is_equal),
                waits=[("PE", last_scatter), ("SR", P.n("SR"))] if tt == 0 else [], inc="DV")
        pg_done = pg_dv[15]
        if XBAR & 4:
            P.barrier()
        ge_dv = {}
        for dh in range(2):
            mml = {}
            for dc in range(4):
                dch = dh * 4 + dc
                mmlast = None
                for tt in range(16):
                    wts = []
                    if tt == 0:
                        wts.append(("DV", pg_done))
                        wts.append(bank_rd.get(dc))
                    mmlast = P.op("tensor", lambda t, tt=tt, dch=dch, dc=dc: t.matmul(
                        BK(dc), X_tok[:, tt * KB + dch * 128: tt * KB + dch * 128 + 128],
                        Pg[:, tt * CAP:(tt + 1) * CAP],
                        start=(tt == 0), stop=(tt == 15)),
                        waits=wts, inc="PE")
                mml[dc] = mmlast
            for dc in range(4):
                dch = dh * 4 + dc
                ge_dv[dc] = P.op("vector", lambda v, dch=dch, dc=dc: v.tensor_copy(
                    xe[:, dch * CAP:(dch + 1) * CAP], BK(dc)),
                    waits=[("PE", mml[dc])], inc="DV")
                bank_rd[dc] = ("DV", ge_dv[dc])
        xe_done = P.n("DV")
        if XBAR & 8:
            P.barrier()
        if DBG and e == 0:
            P.op("sync", lambda s: s.dma_start(out=dbg_xe, in_=xe), waits=[("DV", xe_done)], inc="SO", k=16)
            P.op("sync", lambda s: s.dma_start(out=dbg_xtok, in_=X_tok), inc="SO", k=16)
            for _e2 in ENGINES:
                P.pending[_e2].append(("SO", P.n("SO")))
        dn_dv = {}
        for fh in range(2):
            gsil, um_dv, hdone = {}, {}, {}
            for fq in range(2):
                wgl = ("WG", P.op("gpsimd", lambda g, e=e, fh=fh, fq=fq: g.dma_start(
                    out=wg_sb[:].rearrange("p (c f) -> p c f", c=DCH),
                    in_=rg[e][:, fh * KB + fq * TBL: fh * KB + (fq + 1) * TBL].rearrange("(c p) f -> p c f", p=128)),
                    waits=[("PE", last_g)], inc="WG", k=16))
                wul = ("WU", P.op("gpsimd", lambda g, e=e, fh=fh, fq=fq: g.dma_start(
                    out=wu_sb[:].rearrange("p (c f) -> p c f", c=DCH),
                    in_=ru[e][:, fh * KB + fq * TBL: fh * KB + (fq + 1) * TBL].rearrange("(c p) f -> p c f", p=128)),
                    waits=[("PE", last_u)], inc="WU", k=16))
                if fq == 0:
                    wdl = ("WD", P.op("gpsimd", lambda g, e=e, fh=fh: g.dma_start(
                        out=wd_sb[:].rearrange("p (c dd) -> p c dd", c=8),
                        in_=rd[e][fh * KB:(fh + 1) * KB, :].rearrange("(c p) dd -> p c dd", p=128)),
                        waits=[("PE", last_d)], inc="WD", k=16))
                for fc in range(4):
                    fcl = fq * 4 + fc
                    pb = fc % 2
                    mmlast = None
                    for chn in range(DCH):
                        wts = []
                        if chn == 0:
                            wts.append(wgl)
                            wts.append(bank_rd.get(pb))
                            if fcl < 2:
                                wts.append(("PE", last_d))
                                wts.append(("DV", xe_done))
                        mmlast = P.op("tensor", lambda t, chn=chn, fcl=fcl, fc=fc, pb=pb: t.matmul(
                            BK(pb), wg_sb[:, chn * TBL + fc * 128: chn * TBL + fc * 128 + 128],
                            xe[:, chn * CAP:(chn + 1) * CAP],
                            start=(chn == 0), stop=(chn == DCH - 1)),
                            waits=wts, inc="PE")
                    last_g = mmlast
                    gsil[fcl] = P.op("scalar", lambda st, fcl=fcl, pb=pb: st.activation(
                        gs[:, fcl * TBL:(fcl + 1) * TBL], BK(pb), AF.Silu),
                        waits=[("PE", mmlast)], inc="AC")
                    bank_rd[pb] = ("AC", gsil[fcl])
                for fc in range(4):
                    fcl = fq * 4 + fc
                    pb = fc % 2
                    mmlast = None
                    for chn in range(DCH):
                        wts = []
                        if chn == 0:
                            wts.append(wul)
                            wts.append(bank_rd.get(pb))
                        mmlast = P.op("tensor", lambda t, chn=chn, fc=fc, pb=pb: t.matmul(
                            BK(pb), wu_sb[:, chn * TBL + fc * 128: chn * TBL + fc * 128 + 128],
                            xe[:, chn * CAP:(chn + 1) * CAP],
                            start=(chn == 0), stop=(chn == DCH - 1)),
                            waits=wts, inc="PE")
                    last_u = mmlast
                    uw = [("PE", mmlast), ("PL", hdone.get(fcl - 2) or P.n("PL"))]
                    um_dv[fcl] = P.op("vector", lambda v, fcl=fcl, pb=pb: v.tensor_copy(
                        um[fcl % 2][:], BK(pb)), waits=uw, inc="DV")
                    bank_rd[pb] = ("DV", um_dv[fcl])
                    hdone[fcl] = P.op("gpsimd", lambda g, fcl=fcl: g.tensor_mul(
                        gs[:, fcl * TBL:(fcl + 1) * TBL], gs[:, fcl * TBL:(fcl + 1) * TBL], um[fcl % 2][:]),
                        waits=[("DV", um_dv[fcl]), ("AC", gsil[fcl])], inc="PL")
            if XBAR & 16:
                P.barrier()
            for stile in range(4):
                for dblk in range(2):
                    gidx = fh * 8 + stile * 2 + dblk
                    pb = 2 + gidx % 2
                    mmlast = None
                    for fcl in range(8):
                        wts = []
                        if fcl == 0:
                            wts.append(wdl)
                            wts.append(("PL", hdone[7]))
                            wts.append(bank_rd.get(pb))
                        mmlast = P.op("tensor", lambda t, stile=stile, dblk=dblk, fcl=fcl, pb=pb: t.matmul(
                            BK(pb), gs[:, fcl * TBL + stile * 128: fcl * TBL + stile * 128 + 128],
                            wd_sb[:, fcl * KB + dblk * TBL: fcl * KB + (dblk + 1) * TBL],
                            start=(fcl == 0), stop=(fcl == 7)),
                            waits=wts, inc="PE")
                    last_d = mmlast
                    if fh == 0:
                        dn_dv[gidx] = P.op("vector", lambda v, stile=stile, dblk=dblk, pb=pb: v.tensor_copy(
                            dnb[:, stile * KB + dblk * TBL: stile * KB + (dblk + 1) * TBL], BK(pb)),
                            waits=[("PE", mmlast)], inc="DV")
                        bank_rd[pb] = ("DV", dn_dv[gidx])
                    else:
                        dn_dv[gidx] = P.op("vector", lambda v, stile=stile, dblk=dblk, pb=pb: v.tensor_add(
                            dnb[:, stile * KB + dblk * TBL: stile * KB + (dblk + 1) * TBL],
                            dnb[:, stile * KB + dblk * TBL: stile * KB + (dblk + 1) * TBL], BK(pb)),
                            waits=[("PE", mmlast)], inc="DV")
                        bank_rd[pb] = ("DV", dn_dv[gidx])
        dn_done = P.n("DV")
        if XBAR & 32:
            P.barrier()
        if DBG and e == 0:
            P.op("sync", lambda s: s.dma_start(out=dbg_dnb, in_=dnb), waits=[("DV", dn_done)], inc="SO", k=16)
            for _e2 in ENGINES:
                P.pending[_e2].append(("SO", P.n("SO")))
        if XBAR & 64:
            P.barrier()
        for tblk in range(4):
            pt_dv = {}
            for stile in range(4):
                pt_dv[stile] = P.op("vector", lambda v, stile=stile, tblk=tblk, e=e: v.tensor_scalar(
                    PTt[:, stile * TBL:(stile + 1) * TBL],
                    slot_bc[:, tblk * TBL:(tblk + 1) * TBL], iotacol[:],
                    float(e * CAP + stile * 128), ALU.subtract, ALU.is_equal),
                    waits=[("DV", dn_done)] if stile == 0 and tblk == 0 else [], inc="DV")
            if DBG and e == 0 and tblk == 0:
                P.op("sync", lambda s: s.dma_start(out=dbg_pt, in_=PTt), waits=[("DV", pt_dv[3])], inc="SO", k=16)
                P.op("sync", lambda s: s.dma_start(out=dbg_sbc, in_=slot_bc), inc="SO", k=16)
                for _e2 in ENGINES:
                    P.pending[_e2].append(("SO", P.n("SO")))
            for dch in range(DCH):
                pb = 4 + dch % 2
                mmlast = None
                for stile in range(4):
                    wts = []
                    if stile == 0:
                        wts.append(("DV", pt_dv[3]))
                        wts.append(bank_rd.get(pb))
                    mmlast = P.op("tensor", lambda t, stile=stile, dch=dch, tblk=tblk, pb=pb: t.matmul(
                        BK(pb), dnb[:, stile * KB + dch * 128: stile * KB + dch * 128 + 128],
                        PTt[:, stile * TBL:(stile + 1) * TBL],
                        start=(stile == 0), stop=(stile == 3)),
                        waits=wts, inc="PE")
                last_scatter = mmlast
                sc_fa[(tblk, dch)] = P.op("vector", lambda v, dch=dch, tblk=tblk, pb=pb: v.tensor_add(
                    faccv[:, dch * T + tblk * TBL: dch * T + (tblk + 1) * TBL],
                    faccv[:, dch * T + tblk * TBL: dch * T + (tblk + 1) * TBL], BK(pb)),
                    waits=[("PE", mmlast)], inc="DV")
                bank_rd[pb] = ("DV", sc_fa[(tblk, dch)])

    P.barrier()
    if DBG:
        P.op("sync", lambda s: s.dma_start(out=dbg_ffn, in_=faccv), inc="SO", k=16)
        for _e in ENGINES:
            P.pending[_e].append(("SO", P.n("SO")))
    ln_stats(faccv, T, 0, T, sqE, ssE, sqrE)
    rm4 = row_math(ssE, sqrE, ar2, br2, br2, T)
    fin_dv = {}
    for tb in range(4):
        pb = 4 + (tb % 2) * 2
        P.op("tensor", lambda t, tb=tb, pb=pb: t.matmul(
            BK(pb), ones_f[:], ar2[0:1, tb * TBL:(tb + 1) * TBL], start=True, stop=True),
            waits=[("DV", rm4) if tb == 0 else ("DV", fin_dv.get((tb - 2, DCH - 1)))], inc="PE")
        bcb = P.op("tensor", lambda t, tb=tb, pb=pb: t.matmul(
            BK(pb + 1), ones_f[:], br2[0:1, tb * TBL:(tb + 1) * TBL], start=True, stop=True), inc="PE")
        for och in range(DCH):
            i = tb * DCH + och
            sml = f"YL{i % 2}"
            ly = (sml, P.op("sync", lambda s, och=och, tb=tb, sl=i % 2: s.dma_start(
                out=y1tl[sl][:], in_=y1_h[och * 128:(och + 1) * 128, tb * TBL:(tb + 1) * TBL]),
                waits=[("DV", fin_dv.get((tb, och - 2)) or fin_dv.get((tb - 1, DCH - 2 + och)))], inc=sml, k=16))
            P.op("vector", lambda v, och=och, tb=tb, pb=pb, sl=i % 2: v.tensor_mul(
                sqE[sl][:], faccv[:, och * T + tb * TBL: och * T + (tb + 1) * TBL], BK(pb)),
                waits=[("PE", bcb)], inc="DV")
            P.op("vector", lambda v, pb=pb, sl=i % 2: v.tensor_add(sqE[sl][:], sqE[sl][:], BK(pb + 1)), inc="DV")
            fin_dv[(tb, och)] = P.op("vector", lambda v, och=och, tb=tb, sl=i % 2: v.tensor_add(
                faccv[:, och * T + tb * TBL: och * T + (tb + 1) * TBL], sqE[sl][:], y1tl[sl][:]),
                waits=[ly], inc="DV")
    for och in range(DCH):
        P.op("sync", lambda s, och=och: s.dma_start(
            out=yT[och * 128:(och + 1) * 128, :], in_=faccv[:, och * T:(och + 1) * T]),
            waits=[("DV", fin_dv[(3, och)])], inc="SO", k=16)

    P.op("sync", lambda s: s.nop(), waits=[("SO", P.n("SO"))])

    with nc.Block() as block:
        P.emit(block)
    return nc


_NC_CACHE = {}


def _get_nc():
    if "nc" not in _NC_CACHE:
        _NC_CACHE["nc"] = build_program()
    return _NC_CACHE["nc"]


def make_in_maps(L0_attn_x, L0_attn_W_qkv, L0_attn_W_out, L0_attn_b_out, L0_ffn_gate_w,
                 L0_ffn_shared_gate, L0_ffn_shared_up, L0_ffn_shared_down,
                 L0_ffn_r_gate, L0_ffn_r_up, L0_ffn_r_down):
    x = np.asarray(L0_attn_x)
    Wr = np.asarray(L0_attn_W_qkv).reshape(H, 3, HD, D)
    wq_h = np.ascontiguousarray(Wr[:, 0].reshape(H * HD, D).T)
    wk_h = np.ascontiguousarray(Wr[:, 1].reshape(H * HD, D).T)
    wv_h = np.ascontiguousarray(Wr[:, 2].reshape(H * HD, D).T)
    wo_h = np.ascontiguousarray(np.asarray(L0_attn_W_out).T)
    bo_h = np.ascontiguousarray(np.asarray(L0_attn_b_out).reshape(DCH, 128).T)
    gw_h = np.ascontiguousarray(np.asarray(L0_ffn_gate_w).T)
    sg_h = np.ascontiguousarray(np.asarray(L0_ffn_shared_gate).T)
    su_h = np.ascontiguousarray(np.asarray(L0_ffn_shared_up).T)
    sd_h = np.ascontiguousarray(np.asarray(L0_ffn_shared_down).T)
    rg_h = np.ascontiguousarray(np.transpose(np.asarray(L0_ffn_r_gate), (0, 2, 1)))
    ru_h = np.ascontiguousarray(np.transpose(np.asarray(L0_ffn_r_up), (0, 2, 1)))
    rd_h = np.ascontiguousarray(np.transpose(np.asarray(L0_ffn_r_down), (0, 2, 1)))
    in_maps = []
    for c in range(8):
        b, h = c // 2, c % 2
        xb = x[b]
        xcore = np.concatenate([xb[h * T:(h + 1) * T], xb[(1 - h) * T:(2 - h) * T]], axis=0)
        in_maps.append({
            "xT": np.ascontiguousarray(xcore.T),
            "wq": wq_h, "wk": wk_h, "wv": wv_h, "wo": wo_h, "bo": bo_h, "gw": gw_h,
            "sg": sg_h, "su": su_h, "sd": sd_h, "rg": rg_h, "ru": ru_h, "rd": rd_h,
        })
    return in_maps


def kernel(**inputs):
    in_maps = make_in_maps(**inputs)
    nc = _get_nc()
    res = run_bass_kernel_spmd(nc, in_maps, core_ids=list(range(8)))
    out = np.empty((4, S, D), np.float32)
    for c in range(8):
        b, h = c // 2, c % 2
        out[b, h * T:(h + 1) * T, :] = res.results[c]["yT"].T
    return out



# version 11
# speedup vs baseline: 1.2656x; 1.0130x over previous
import numpy as np
import concourse.bass as bass
import concourse.mybir as mybir
from concourse.bass_utils import run_bass_kernel_spmd

F32 = mybir.dt.float32
BF16 = mybir.dt.bfloat16
AF = mybir.ActivationFunctionType
ALU = mybir.AluOpType
AX = mybir.AxisListType

D, DCH = 1024, 8
CAP = 512
S = 4096
T = 2048
H, HD = 16, 64
FF = 2048
NE = 8
TBL = 512
EPS = 1e-5
ISC = 1.0 / 32.0
KB = 1024

ENGINES = ("tensor", "scalar", "vector", "gpsimd", "sync")
DBG = False
DBGKIND = "Internal"
XBAR = 3


class Planner:
    def __init__(self, nc, sems):
        self.nc = nc
        self.ops = {e: [] for e in ENGINES}
        self.sems = sems
        self.counts = {k: 0 for k in sems}
        self.pending = {e: [] for e in ENGINES}

    SELFSEM = {"vector": "DV", "scalar": "AC"}

    def op(self, eng, fn, waits=(), inc=None, k=1):
        w = self.pending[eng] + [x for x in waits if x is not None and x[1]]
        self.pending[eng] = []
        ss = self.SELFSEM.get(eng) or ("PL" if eng == "gpsimd" and inc == "PL" else None)
        if ss is not None and self.counts[ss] > 0:
            w.append((ss, self.counts[ss]))
        self.ops[eng].append((fn, w, inc, k))
        if inc is not None:
            self.counts[inc] += k
            return self.counts[inc]
        return None

    def n(self, sem):
        return self.counts[sem]

    def barrier(self):
        snap = [(s, c) for s, c in self.counts.items() if c > 0]
        for e in ENGINES:
            self.pending[e] = list(snap)

    def emit(self, block):
        for engname in ENGINES:
            def make_body(ops=self.ops[engname], sems=self.sems):
                def body(eng):
                    for fn, waits, inc, k in ops:
                        for s, v in waits:
                            eng.wait_ge(sems[s], v)
                        ins = fn(eng)
                        if inc is not None:
                            ins.then_inc(sems[inc], k)
                return body
            getattr(block, engname)(make_body())


class Arena:

    def __init__(self, master, origin, base, limit, tag):
        self.m, self.origin = master, origin
        self.off, self.limit, self.tag = base, limit, tag

    def al(self, name, shape, dtype):
        nbytes = int(np.prod(shape[1:])) * (2 if dtype == BF16 else 4)
        nbytes = (nbytes + 31) // 32 * 32
        o = self.off - self.origin
        view = self.m if dtype == F32 else self.m.bitcast(BF16)
        es = 4 if dtype == F32 else 2
        t = view[0:shape[0], o // es: o // es + int(np.prod(shape[1:]))]
        if len(shape) > 2:
            raise ValueError("arena views are 2-D")
        self.off += nbytes
        assert self.off <= self.limit, f"{self.tag}/{name}: off {self.off} > {self.limit}"
        return t


def build_program():
    nc = bass.Bass("TRN2", target_bir_lowering=False)
    xT = nc.dram_tensor("xT", [D, S], F32, kind="ExternalInput").ap()
    wq = nc.dram_tensor("wq", [D, D], F32, kind="ExternalInput").ap()
    wk = nc.dram_tensor("wk", [D, D], F32, kind="ExternalInput").ap()
    wv = nc.dram_tensor("wv", [D, D], F32, kind="ExternalInput").ap()
    wo = nc.dram_tensor("wo", [D, D], F32, kind="ExternalInput").ap()
    bo = nc.dram_tensor("bo", [128, DCH], F32, kind="ExternalInput").ap()
    gw = nc.dram_tensor("gw", [D, NE], F32, kind="ExternalInput").ap()
    sg = nc.dram_tensor("sg", [D, FF], F32, kind="ExternalInput").ap()
    su = nc.dram_tensor("su", [D, FF], F32, kind="ExternalInput").ap()
    sd = nc.dram_tensor("sd", [FF, D], F32, kind="ExternalInput").ap()
    rg = nc.dram_tensor("rg", [NE, D, FF], F32, kind="ExternalInput").ap()
    ru = nc.dram_tensor("ru", [NE, D, FF], F32, kind="ExternalInput").ap()
    rd = nc.dram_tensor("rd", [NE, FF, D], F32, kind="ExternalInput").ap()
    yT = nc.dram_tensor("yT", [D, T], F32, kind="ExternalOutput").ap()
    xh_h = nc.dram_tensor("xh_h", [D, S], BF16, kind="Internal").ap()
    y1_h = nc.dram_tensor("y1_h", [D, T], F32, kind="Internal").ap()
    if DBG:
        dbg_ot = nc.dram_tensor("dbg_ot", [128, DCH * T], BF16, kind=DBGKIND).ap()
        dbg_ares = nc.dram_tensor("dbg_ares", [128, DCH * T], F32, kind=DBGKIND).ap()
        dbg_x2 = nc.dram_tensor("dbg_x2", [128, DCH * T], BF16, kind=DBGKIND).ap()
        dbg_slot = nc.dram_tensor("dbg_slot", [1, T], F32, kind=DBGKIND).ap()
        dbg_sc = nc.dram_tensor("dbg_sc", [1, T], BF16, kind=DBGKIND).ap()
        dbg_ffn = nc.dram_tensor("dbg_ffn", [128, DCH * T], F32, kind=DBGKIND).ap()
        dbg_xe = nc.dram_tensor("dbg_xe", [128, DCH * CAP], BF16, kind=DBGKIND).ap()
        dbg_dnb = nc.dram_tensor("dbg_dnb", [128, 4 * KB], BF16, kind=DBGKIND).ap()
        dbg_xtok = nc.dram_tensor("dbg_xtok", [128, 16 * KB], BF16, kind=DBGKIND).ap()
        dbg_shr = nc.dram_tensor("dbg_shr", [128, DCH * T], F32, kind=DBGKIND).ap()
        dbg_pt = nc.dram_tensor("dbg_pt", [128, 4 * TBL], BF16, kind=DBGKIND).ap()
        dbg_sbc = nc.dram_tensor("dbg_sbc", [128, T], F32, kind=DBGKIND).ap()

    sems = {}
    for name in ("PE", "AC", "DV", "PL", "SO", "C0", "W0", "WO", "WG", "WU", "WD",
                 "MS", "SC", "Y1S", "C1", "XA0", "XA1", "XA2", "XA3", "XB0", "XB1",
                 "SH0", "SH1", "XT0", "XT1", "XF0", "XF1", "XG0", "XG1", "YL0", "YL1", "SR"):
        sems[name] = nc.semaphore(name).__enter__()
    P = Planner(nc, sems)

    GB = (nc.sbuf_base + 31) // 32 * 32
    TOP = nc.sbuf_top
    GSZ = 44 * KB
    GMST = nc.alloc_sbuf_tensor_at("GMST", [128, GSZ // 4], F32, offset=GB).ap()
    GA = Arena(GMST, GB, GB, GB + GSZ, "g")

    ones_b = GA.al("ones_b", [1, 128], BF16)
    ones_f = GA.al("ones_f", [1, 128], F32)
    col1_f = GA.al("col1_f", [128, 1], F32)
    identF = GA.al("identF", [128, 128], F32)
    gw_sb = GA.al("gw_sb", [128, DCH * NE], BF16)
    bo_sb = GA.al("bo_sb", [128, DCH], F32)
    iotaw = GA.al("iotaw", [128, NE], F32)
    epsr = GA.al("epsr", [1, 1], F32)
    ar2 = GA.al("ar2", [1, T], F32)
    br2 = GA.al("br2", [1, T], F32)
    scrow = GA.al("scrow", [1, T], BF16)
    identB = GA.al("identB", [128, 128], BF16)
    LS = GA.al("LS", [128, 128], BF16)
    ones2 = GA.al("ones2", [128, 128], BF16)
    ecaps = GA.al("ecaps", [128, NE], F32)
    iota512 = GA.al("iota512", [128, CAP], F32)
    iotacol = GA.al("iotacol", [128, 1], F32)
    mbf = GA.al("mbf", [128, 16 * NE], BF16)
    rkc = GA.al("rkc", [128, 8], F32)
    slotrow = GA.al("slotrow", [1, T], F32)
    sct2b = GA.al("sct2b", [128, 16], F32)
    P1 = (GA.off + 127) // 128 * 128
    NF32 = (TOP - P1) // 4 // 32 * 32
    MSC = nc.alloc_sbuf_tensor_at("MSC", [128, NF32], F32, offset=P1).ap()

    MB = MSC.bitcast(BF16)
    OT = MB[:, 0:DCH * T]
    x2b = MB[:, DCH * T: 2 * DCH * T]
    facc = MSC[:, 0:DCH * KB]

    PS = nc.alloc_psum_tensor("PS", [128, 4096], F32)
    BK = lambda b: PS[:, b * 512:(b + 1) * 512]

    P.op("gpsimd", lambda g: g.iota(iotaw[:], [[-1, NE]], base=NE, channel_multiplier=0,
                                    allow_small_or_imprecise_dtypes=True), inc="PL")
    P.op("gpsimd", lambda g: g.iota(identF[:], [[1, 128]], base=0, channel_multiplier=-1,
                                    allow_small_or_imprecise_dtypes=True), inc="PL")
    P.op("gpsimd", lambda g: g.iota(LS[:], [[1, 128]], base=0, channel_multiplier=-1,
                                    allow_small_or_imprecise_dtypes=True), inc="PL")
    P.op("gpsimd", lambda g: g.iota(ecaps[:], [[CAP, NE]], base=0, channel_multiplier=0,
                                    allow_small_or_imprecise_dtypes=True), inc="PL")
    P.op("gpsimd", lambda g: g.iota(iota512[:], [[1, CAP]], base=0, channel_multiplier=0,
                                    allow_small_or_imprecise_dtypes=True), inc="PL")
    P.op("gpsimd", lambda g: g.iota(iotacol[:], [[1, 1]], base=0, channel_multiplier=1,
                                    allow_small_or_imprecise_dtypes=True), inc="PL")
    ident_pl = P.n("PL")
    P.op("vector", lambda v: v.memset(ones_b[:], 1.0), inc="DV")
    P.op("vector", lambda v: v.memset(ones_f[:], 1.0), inc="DV")
    P.op("vector", lambda v: v.memset(col1_f[:], 1.0), inc="DV")
    P.op("vector", lambda v: v.memset(epsr[:], EPS), inc="DV")
    w_ident = P.op("vector", lambda v: v.tensor_single_scalar(identF[:], identF[:], 0.0, ALU.is_equal),
                   waits=[("PL", ident_pl)], inc="DV")
    P.op("vector", lambda v: v.tensor_copy(identB[:], identF[:]), inc="DV")
    P.op("vector", lambda v: v.tensor_single_scalar(LS[:], LS[:], 0.0, ALU.is_gt), inc="DV")
    P.op("vector", lambda v: v.memset(ones2[:], 1.0), inc="DV")
    init_dv = P.n("DV")
    P.barrier()
    P.op("sync", lambda s: s.dma_start(out=bo_sb[:], in_=bo), inc="C0", k=16)
    P.op("gpsimd", lambda g: g.dma_start(out=gw_sb[:].rearrange("p (c e) -> p c e", c=DCH),
                                         in_=gw.rearrange("(c p) e -> p c e", p=128)), inc="C1", k=16)

    def row_math(sumr, sqr, ar, br, tmp, width, c0=0, first_waits=None):
        sl = lambda r: r[0:1, c0:c0 + width]
        P.op("vector", lambda v: v.tensor_scalar_mul(sl(sumr), sl(sumr), 1.0 / D),
             waits=([("PE", P.n("PE"))] if first_waits is None else first_waits), inc="DV")
        P.op("vector", lambda v: v.tensor_scalar_mul(sl(sqr), sl(sqr), 1.0 / D), inc="DV")
        P.op("vector", lambda v: v.tensor_mul(sl(tmp), sl(sumr), sl(sumr)), inc="DV")
        r3 = P.op("vector", lambda v: v.tensor_sub(sl(sqr), sl(sqr), sl(tmp)), inc="DV")
        r4 = P.op("scalar", lambda s: s.activation(sl(tmp), sl(sqr), AF.Sqrt, bias=epsr[0:1, 0:1]),
                  waits=[("DV", r3)], inc="AC")
        P.op("vector", lambda v: v.reciprocal(sl(ar), sl(tmp)),
             waits=[("AC", r4)], inc="DV")
        return P.op("vector", lambda v: v.scalar_tensor_tensor(
            sl(br), sl(sumr), -1.0, sl(ar), ALU.mult, ALU.mult), inc="DV")

    A = Arena(MSC, P1, P1, TOP, "A")
    xfA = [A.al(f"xf{i}", [128, DCH * TBL], F32) for i in range(4)]
    sqA = [A.al(f"sq{i}", [128, TBL], F32) for i in range(2)]
    nscA = [A.al(f"nsc{i}", [128, TBL], F32) for i in range(2)]
    ssA = A.al("ssum", [1, S], F32)
    sqrA = A.al("ssq", [1, S], F32)
    a1r = A.al("a1r", [1, S], F32)
    b1r = A.al("b1r", [1, S], F32)
    xhA = [A.al(f"xh{i}", [128, DCH * TBL], BF16) for i in range(2)]

    NTBS = S // TBL
    mm2_id, cp_id, rm_id, nrm_id, st_id = {}, {}, {}, {}, {}
    for it in range(NTBS + 1):
        if it < NTBS:
            tb = it
            buf = xfA[tb % 4]
            sm = f"XA{tb % 4}"
            ltb = (sm, P.op("sync", lambda s, buf=buf, tb=tb: s.dma_start(
                out=buf[:].rearrange("p (c t) -> p c t", c=DCH),
                in_=xT[:, tb * TBL:(tb + 1) * TBL].rearrange("(c p) t -> p c t", p=128)),
                waits=[("DV", nrm_id.get(tb - 4))], inc=sm, k=16))
            pb = (tb % 2) * 2
            for chn in range(DCH):
                i = tb * DCH + chn
                sqb = sqA[i % 2]
                sq_ac = P.op("scalar", lambda s, buf=buf, chn=chn, sqb=sqb: s.activation(
                    sqb[:], buf[:, chn * TBL:(chn + 1) * TBL], AF.Square),
                    waits=[ltb, ("PE", mm2_id.get(i - 2))], inc="AC")
                P.op("tensor", lambda t, buf=buf, chn=chn, pb=pb: t.matmul(
                    BK(pb)[0:1, :], col1_f[:], buf[:, chn * TBL:(chn + 1) * TBL],
                    start=(chn == 0), stop=(chn == DCH - 1)),
                    waits=[ltb] + ([("DV", cp_id.get(tb - 2))] if chn == 0 else []),
                    inc="PE")
                mm2_id[i] = P.op("tensor", lambda t, sqb=sqb, pb=pb, chn=chn: t.matmul(
                    BK(pb + 1)[0:1, :], col1_f[:], sqb[:], start=(chn == 0), stop=(chn == DCH - 1)),
                    waits=[("AC", sq_ac)], inc="PE")
            P.op("vector", lambda v, tb=tb, pb=pb: v.tensor_copy(
                ssA[0:1, tb * TBL:(tb + 1) * TBL], BK(pb)[0:1, :]),
                waits=[("PE", mm2_id[tb * DCH + DCH - 1])], inc="DV")
            cp_id[tb] = P.op("vector", lambda v, tb=tb, pb=pb: v.tensor_copy(
                sqrA[0:1, tb * TBL:(tb + 1) * TBL], BK(pb + 1)[0:1, :]), inc="DV")
            rm_id[tb] = row_math(ssA, sqrA, a1r, b1r, b1r, TBL, c0=tb * TBL, first_waits=[])
        if it >= 1:
            tb = it - 1
            buf = xfA[tb % 4]
            pbb = 4 + (tb % 2) * 2
            P.op("tensor", lambda t, tb=tb, pbb=pbb: t.matmul(
                BK(pbb), ones_f[:], a1r[0:1, tb * TBL:(tb + 1) * TBL], start=True, stop=True),
                waits=[("DV", rm_id[tb]), ("DV", nrm_id.get(tb - 2))], inc="PE")
            bcb = P.op("tensor", lambda t, tb=tb, pbb=pbb: t.matmul(
                BK(pbb + 1), ones_f[:], b1r[0:1, tb * TBL:(tb + 1) * TBL], start=True, stop=True), inc="PE")
            xo = xhA[tb % 2]
            for chn in range(DCH):
                P.op("vector", lambda v, buf=buf, chn=chn, pbb=pbb, sl=chn % 2: v.tensor_mul(
                    nscA[sl][:], buf[:, chn * TBL:(chn + 1) * TBL], BK(pbb)),
                    waits=([("PE", bcb), st_id.get(tb - 2)] if chn == 0 else []), inc="DV")
                nrm_id[tb] = P.op("vector", lambda v, xo=xo, chn=chn, pbb=pbb, sl=chn % 2: v.tensor_add(
                    xo[:, chn * TBL:(chn + 1) * TBL], nscA[sl][:], BK(pbb + 1)), inc="DV")
            sms = f"SH{tb % 2}"
            st_id[tb] = (sms, P.op("gpsimd", lambda g, xo=xo, tb=tb: g.dma_start(
                out=xh_h[:, tb * TBL:(tb + 1) * TBL].rearrange("(c p) t -> p c t", p=128),
                in_=xo[:].rearrange("p (c t) -> p c t", c=DCH)),
                waits=[("DV", nrm_id[tb])], inc=sms, k=16))
    P.barrier()

    pref = None
    for hg in range(2):
        B = Arena(MSC, P1, P1 + 32 * KB, TOP, "B")
        Qc = B.al("Qc", [128, 4 * T], BF16)
        Kc = B.al("Kc", [128, 4 * S], BF16)
        Vt = B.al("Vt", [128, 32 * 520], BF16)
        wq_sb = B.al("wq", [128, DCH * TBL], BF16)
        wk_sb = B.al("wk", [128, DCH * TBL], BF16)
        wv_sb = B.al("wv", [128, DCH * TBL], BF16)
        xh_tb = [B.al(f"xh{i}", [128, DCH * TBL], BF16) for i in range(2)]
        At = [B.al(f"At{i}", [128, 1024], BF16) for i in range(3)]
        bc_sb = B.al("bc_sb", [64, TBL], F32)
        rsrow = B.al("rsrow", [1, TBL], F32)
        rsrow2 = B.al("rsrow2", [1, TBL], F32)

        if hg == 0:
            wls = {}
            for nm, wsrc, wdst in (("q", wq, wq_sb), ("k", wk, wk_sb), ("v", wv, wv_sb)):
                P.op("gpsimd", lambda g, wsrc=wsrc, wdst=wdst, hg=hg: g.dma_start(
                    out=wdst[:].rearrange("p (c o) -> p c o", c=DCH),
                    in_=wsrc[:, hg * TBL:(hg + 1) * TBL].rearrange("(c p) o -> p c o", p=128)),
                    inc="W0", k=16)
            for nm in ("q", "k", "v"):
                wls[nm] = ("W0", P.n("W0"))
        else:
            wls = pref["wls"]
        for tt in range(32):
            P.op("vector", lambda v, tt=tt: v.memset(
                Vt[:, tt * 520:(tt + 1) * 520].rearrange("p (h c) -> p h c", h=8)[:, :, 64:65], 1.0), inc="DV")
        vones_dv = P.n("DV")

        kq_ev, v_ev, pe_tb = [], [], {}
        for tb in range(NTBS):
            buf = xh_tb[tb % 2]
            smt = f"XT{tb % 2}"
            if hg == 1 and tb < 2:
                ltb = pref["xh"][tb]
            else:
                ltb = (smt, P.op("sync", lambda s, buf=buf, tb=tb: s.dma_start(
                    out=buf[:].rearrange("p (c t) -> p c t", c=DCH),
                    in_=xh_h[:, tb * TBL:(tb + 1) * TBL].rearrange("(c p) t -> p c t", p=128)),
                    waits=[("PE", pe_tb.get(tb - 2))], inc=smt, k=16))
            for kind in (("k", "q") if tb < 4 else ("k",)):
                wsb, dst, dstw = (wk_sb, Kc, S) if kind == "k" else (wq_sb, Qc, T)
                for och in range(4):
                    seq = len(kq_ev)
                    pb = seq % 2
                    mmlast = None
                    for chn in range(DCH):
                        w0 = ([ltb, wls[kind]] +
                              ([("DV", kq_ev[seq - 2])] if seq >= 2 else [])) if chn == 0 else []
                        mmlast = P.op("tensor", lambda t, wsb=wsb, chn=chn, och=och, buf=buf, pb=pb: t.matmul(
                            BK(pb), wsb[:, chn * TBL + och * 128: chn * TBL + och * 128 + 128],
                            buf[:, chn * TBL:(chn + 1) * TBL],
                            start=(chn == 0), stop=(chn == DCH - 1)),
                            waits=w0, inc="PE")
                    kq_ev.append(P.op("vector", lambda v, dst=dst, dstw=dstw, och=och, tb=tb, pb=pb: v.tensor_copy(
                        dst[:, och * dstw + tb * TBL: och * dstw + (tb + 1) * TBL], BK(pb)),
                        waits=[("PE", mmlast)], inc="DV"))
            for ttl in range(4):
                tt = tb * 4 + ttl
                sv = len(v_ev)
                pb = 2 + sv % 2
                mmlast = None
                for chn in range(DCH):
                    w0 = ([ltb, wls["v"]] +
                          ([("AC", v_ev[sv - 2])] if sv >= 2 else [])) if chn == 0 else []
                    mmlast = P.op("tensor", lambda t, chn=chn, buf=buf, ttl=ttl, pb=pb, wv_sb=wv_sb: t.matmul(
                        BK(pb), buf[:, chn * TBL + ttl * 128: chn * TBL + ttl * 128 + 128],
                        wv_sb[:, chn * TBL:(chn + 1) * TBL],
                        start=(chn == 0), stop=(chn == DCH - 1)),
                        waits=w0, inc="PE")
                v_ev.append(P.op("scalar", lambda s, tt=tt, pb=pb, Vt=Vt: s.copy(
                    Vt[:, tt * 520:(tt + 1) * 520].rearrange("p (h c) -> p h c", h=8)[:, :, 0:64],
                    BK(pb).rearrange("p (h c) -> p h c", h=8)),
                    waits=[("PE", mmlast)], inc="AC"))
            pe_tb[tb] = P.n("PE")
        b_dv, b_ac = P.n("DV"), P.n("AC")

        prev_exp = None
        prev_av = None
        prev_ot = None
        for hp in range(4):
            och = hp
            for qb in range(4):
                qk_id, ex_id, av_id = {}, {}, {}
                for kk in range(34):
                    if kk < 32:
                        kb, sl = kk, kk % 2
                        for hx in range(2):
                            rb = hx * 64
                            wts = []
                            if hx == 0:
                                if kb == 0 and hp == 0 and qb == 0:
                                    wts += [("DV", b_dv), ("AC", b_ac)]
                                if kb >= 2:
                                    wts.append(("AC", ex_id[kb - 2]))
                                elif prev_exp is not None:
                                    wts.append(("AC", prev_exp[kb]))
                            qk_id[(kb, hx)] = P.op("tensor", lambda t, och=och, rb=rb, kb=kb, hx=hx, qb=qb, sl=sl, Kc=Kc, Qc=Qc: t.matmul(
                                PS[:, sl * 1024 + hx * 512: sl * 1024 + (hx + 1) * 512],
                                Kc[rb:rb + 64, och * S + kb * 128: och * S + (kb + 1) * 128],
                                Qc[rb:rb + 64, och * T + qb * TBL: och * T + (qb + 1) * TBL],
                                start=True, stop=True, tile_position=(rb, 0)),
                                waits=wts, inc="PE")
                        ewts = [("PE", qk_id[(kb, 1)])]
                        if kb >= 3:
                            ewts.append(("PE", av_id[kb - 3]))
                        elif prev_av is not None:
                            ewts.append(("PE", prev_av))
                        ex_id[kb] = P.op("scalar", lambda s, sl=sl, a3=kb % 3, At=At: s.activation(
                            At[a3][:], PS[:, sl * 1024: (sl + 1) * 1024], AF.Exp, scale=ISC),
                            waits=ewts, inc="AC")
                    if kk >= 2:
                        kb = kk - 2
                        avlast = None
                        for hx in range(2):
                            hh = 2 * hp + hx
                            avw = [("AC", ex_id[kb])] if hx == 0 else []
                            if kb == 0 and hx == 0:
                                avw.append(("DV", vones_dv))
                                if prev_ot is not None:
                                    avw.append(("DV", prev_ot))
                            avlast = P.op("tensor", lambda t, hh=hh, kb=kb, hx=hx, a3=kb % 3, Vt=Vt, At=At: t.matmul(
                                PS[0:65, 2048 + hx * 512: 2048 + (hx + 1) * 512],
                                Vt[:, kb * 520 + hh * 65: kb * 520 + (hh + 1) * 65],
                                At[a3][:, hx * 512:(hx + 1) * 512],
                                start=(kb == 0), stop=(kb == 31)),
                                waits=avw, inc="PE")
                        av_id[kb] = avlast
                prev_exp = {0: ex_id[30], 1: ex_id[31]}
                prev_av = av_id[31]
                for hx in range(2):
                    h = hg * 8 + 2 * hp + hx
                    oc2, rb2 = h // 2, (h % 2) * 64
                    P.op("scalar", lambda s, hx=hx, rsrow=rsrow: s.activation(
                        rsrow[0:1, :], PS[64:65, 2048 + hx * 512: 2048 + (hx + 1) * 512], AF.Ln),
                        waits=[("PE", av_id[31])], inc="AC")
                    rc = P.op("scalar", lambda s, rsrow=rsrow, rsrow2=rsrow2: s.activation(
                        rsrow2[0:1, :], rsrow[0:1, :], AF.Exp, scale=-1.0), inc="AC")
                    bcm = P.op("tensor", lambda t, rsrow2=rsrow2: t.matmul(
                        PS[0:64, 3072:3584], ones_f[0:1, 0:64], rsrow2[0:1, :], start=True, stop=True),
                        waits=[("AC", rc)], inc="PE")
                    bce = P.op("scalar", lambda s, bc_sb=bc_sb: s.copy(bc_sb[:], PS[0:64, 3072:3584]),
                               waits=[("PE", bcm), ("DV", prev_ot)], inc="AC")
                    prev_ot = P.op("vector", lambda v, oc2=oc2, rb2=rb2, qb=qb, hx=hx, bc_sb=bc_sb: v.tensor_mul(
                        OT[rb2:rb2 + 64, oc2 * T + qb * TBL: oc2 * T + (qb + 1) * TBL],
                        PS[0:64, 2048 + hx * 512: 2048 + (hx + 1) * 512], bc_sb[:]),
                        waits=[("AC", bce)], inc="DV")
        if hg == 0:
            nwls = {}
            for nm, wsrc, wdst in (("q", wq, wq_sb), ("k", wk, wk_sb), ("v", wv, wv_sb)):
                P.op("gpsimd", lambda g, wsrc=wsrc, wdst=wdst: g.dma_start(
                    out=wdst[:].rearrange("p (c o) -> p c o", c=DCH),
                    in_=wsrc[:, TBL: 2 * TBL].rearrange("(c p) o -> p c o", p=128)),
                    waits=[("PE", pe_tb[NTBS - 1])], inc="W0", k=16)
            for nm in ("q", "k", "v"):
                nwls[nm] = ("W0", P.n("W0"))
            pxh = {}
            for tbp in range(2):
                bufp = xh_tb[tbp]
                smt = f"XT{tbp}"
                pxh[tbp] = (smt, P.op("sync", lambda s, bufp=bufp, tbp=tbp: s.dma_start(
                    out=bufp[:].rearrange("p (c t) -> p c t", c=DCH),
                    in_=xh_h[:, tbp * TBL:(tbp + 1) * TBL].rearrange("(c p) t -> p c t", p=128)),
                    waits=[("PE", pe_tb[NTBS - 2 + tbp])], inc=smt, k=16))
            pref = {"wls": nwls, "xh": pxh}
        P.barrier()

    if DBG:
        P.op("sync", lambda s: s.dma_start(out=dbg_ot, in_=OT[:]), inc="SO", k=16)

    Dz = Arena(MSC, P1, P1 + 64 * KB, TOP, "D")
    ares = Dz.al("ares", [128, DCH * T], F32)
    wo_sb = Dz.al("wo", [128, DCH * KB], BF16)
    xftl = [Dz.al(f"xf{i}", [128, TBL], F32) for i in range(2)]
    x1tl = [Dz.al(f"x1{i}", [128, TBL], BF16) for i in range(2)]
    sqD = [Dz.al(f"sq{i}", [128, TBL], F32) for i in range(2)]
    sq3 = [Dz.al(f"sq3{i}", [128, TBL], F32) for i in range(2)]
    ssD = MSC[0:1, 0:T]
    sqrD = MSC[0:1, 2 * KB: 2 * KB + T]
    lg_sb = Dz.al("lg", [128, 16 * NE], F32)
    mxt = Dz.al("mxt", [128, 16], F32)
    sct = Dz.al("sct", [128, 16], F32)
    mkt = Dz.al("mkt", [128, NE], F32)
    wkt = Dz.al("wkt", [128, NE], F32)
    sct2 = Dz.al("sct2", [16, 128], F32)
    sct3 = Dz.al("sct3", [16, 128], F32)

    wol = ("WO", P.op("gpsimd", lambda g: g.dma_start(
        out=wo_sb[:].rearrange("p (c o) -> p c o", c=DCH),
        in_=wo.rearrange("(c p) o -> p c o", p=128)), inc="WO", k=16))

    d1_dv = {}
    for och in range(DCH):
        for tb in range(4):
            i = och * 4 + tb
            pb = i % 2
            smg = f"XG{i % 2}"
            lx = (smg, P.op("sync", lambda s, och=och, tb=tb, sl=i % 2: s.dma_start(
                out=x1tl[sl][:], in_=xh_h[och * 128:(och + 1) * 128, tb * TBL:(tb + 1) * TBL]),
                waits=[("DV", d1_dv.get(i - 2))], inc=smg, k=16))
            mmlast = None
            for cin in range(DCH):
                wts = []
                if cin == 0:
                    wts.append(wol)
                    if i >= 2:
                        wts.append(("DV", d1_dv[i - 2]))
                mmlast = P.op("tensor", lambda t, och=och, tb=tb, cin=cin, pb=pb: t.matmul(
                    BK(pb), wo_sb[:, cin * KB + och * 128: cin * KB + och * 128 + 128],
                    OT[:, cin * T + tb * TBL: cin * T + (tb + 1) * TBL],
                    start=(cin == 0), stop=(cin == DCH - 1)),
                    waits=wts, inc="PE")
            d1_dv[i] = P.op("vector", lambda v, och=och, tb=tb, pb=pb, sl=i % 2: v.scalar_tensor_tensor(
                ares[:, och * T + tb * TBL: och * T + (tb + 1) * TBL],
                BK(pb), bo_sb[:, och:och + 1], x1tl[sl][:], ALU.add, ALU.add),
                waits=[("PE", mmlast), lx], inc="DV")

    def ln_stats(buf, cstride, colbase, width, sqr_ring, ssr, sqrr):
        ntb = width // TBL
        mm_l, cp_l = {}, {}
        for tb in range(ntb):
            for chn in range(DCH):
                i = tb * DCH + chn
                sqb = sqr_ring[i % 2]
                col = chn * cstride + colbase + tb * TBL
                sq_ac = P.op("scalar", lambda s, sqb=sqb, col=col, buf=buf: s.activation(
                    sqb[:], buf[:, col:col + TBL], AF.Square),
                    waits=[("PE", mm_l.get(i - 2))], inc="AC")
                pb = (tb % 2) * 2
                P.op("tensor", lambda t, buf=buf, col=col, pb=pb, chn=chn: t.matmul(
                    BK(pb)[0:1, :], col1_f[:], buf[:, col:col + TBL],
                    start=(chn == 0), stop=(chn == DCH - 1)),
                    waits=([("DV", cp_l.get(tb - 2))] if chn == 0 else []), inc="PE")
                mm_l[i] = P.op("tensor", lambda t, sqb=sqb, pb=pb, chn=chn: t.matmul(
                    BK(pb + 1)[0:1, :], col1_f[:], sqb[:],
                    start=(chn == 0), stop=(chn == DCH - 1)),
                    waits=[("AC", sq_ac)], inc="PE")
            pb = (tb % 2) * 2
            P.op("vector", lambda v, tb=tb, pb=pb: v.tensor_copy(
                ssr[0:1, tb * TBL:(tb + 1) * TBL], BK(pb)[0:1, :]),
                waits=[("PE", mm_l[tb * DCH + DCH - 1])], inc="DV")
            cp_l[tb] = P.op("vector", lambda v, tb=tb, pb=pb: v.tensor_copy(
                sqrr[0:1, tb * TBL:(tb + 1) * TBL], BK(pb + 1)[0:1, :]), inc="DV")

    P.barrier()
    ln_stats(ares, T, 0, T, sqD, ssD, sqrD)
    rm2 = row_math(ssD, sqrD, ar2, br2, br2, T)

    d4_dv, mm3_id, cp3_id = {}, {}, {}
    for tb in range(4):
        pb = 4 + (tb % 2) * 2
        P.op("tensor", lambda t, tb=tb, pb=pb: t.matmul(
            BK(pb), ones_f[:], ar2[0:1, tb * TBL:(tb + 1) * TBL], start=True, stop=True),
            waits=[("DV", rm2) if tb == 0 else ("DV", d4_dv.get((tb - 2, DCH - 1)))], inc="PE")
        bcb = P.op("tensor", lambda t, tb=tb, pb=pb: t.matmul(
            BK(pb + 1), ones_f[:], br2[0:1, tb * TBL:(tb + 1) * TBL], start=True, stop=True), inc="PE")
        for och in range(DCH):
            i = tb * DCH + och
            smf = f"XF{i % 2}"
            lx = (smf, P.op("sync", lambda s, och=och, tb=tb, sl=i % 2: s.dma_start(
                out=xftl[sl][:], in_=xT[och * 128:(och + 1) * 128, tb * TBL:(tb + 1) * TBL]),
                waits=[("DV", d4_dv.get((tb, och - 2)) or d4_dv.get((tb - 1, DCH - 2 + och)))], inc=smf, k=16))
            P.op("vector", lambda v, och=och, tb=tb, pb=pb, sl=i % 2: v.tensor_mul(
                sqD[sl][:], ares[:, och * T + tb * TBL: och * T + (tb + 1) * TBL], BK(pb)),
                waits=[("PE", bcb)], inc="DV")
            P.op("vector", lambda v, pb=pb, sl=i % 2: v.tensor_add(sqD[sl][:], sqD[sl][:], BK(pb + 1)), inc="DV")
            d4_dv[(tb, och)] = P.op("vector", lambda v, och=och, tb=tb, sl=i % 2: v.tensor_add(
                ares[:, och * T + tb * TBL: och * T + (tb + 1) * TBL], sqD[sl][:], xftl[sl][:]),
                waits=[lx], inc="DV")
            sq_ac = P.op("scalar", lambda s, och=och, tb=tb, sl=i % 2: s.activation(
                sq3[sl][:], ares[:, och * T + tb * TBL: och * T + (tb + 1) * TBL], AF.Square),
                waits=[("DV", d4_dv[(tb, och)]), ("PE", mm3_id.get(i - 2))], inc="AC")
            pbs = (tb % 2) * 2
            P.op("tensor", lambda t, och=och, tb=tb, pbs=pbs: t.matmul(
                BK(pbs)[0:1, :], col1_f[:], ares[:, och * T + tb * TBL: och * T + (tb + 1) * TBL],
                start=(och == 0), stop=(och == DCH - 1)),
                waits=[("DV", d4_dv[(tb, och)])] + ([("DV", cp3_id.get(tb - 2))] if och == 0 else []),
                inc="PE")
            mm3_id[i] = P.op("tensor", lambda t, sl=i % 2, pbs=pbs, och=och: t.matmul(
                BK(pbs + 1)[0:1, :], col1_f[:], sq3[sl][:], start=(och == 0), stop=(och == DCH - 1)),
                waits=[("AC", sq_ac)], inc="PE")
        pbs = (tb % 2) * 2
        P.op("vector", lambda v, tb=tb, pbs=pbs: v.tensor_copy(
            ssD[0:1, tb * TBL:(tb + 1) * TBL], BK(pbs)[0:1, :]),
            waits=[("PE", mm3_id[tb * DCH + DCH - 1])], inc="DV")
        cp3_id[tb] = P.op("vector", lambda v, tb=tb, pbs=pbs: v.tensor_copy(
            sqrD[0:1, tb * TBL:(tb + 1) * TBL], BK(pbs + 1)[0:1, :]), inc="DV")
    for och in range(DCH):
        P.op("sync", lambda s, och=och: s.dma_start(
            out=y1_h[och * 128:(och + 1) * 128, :], in_=ares[:, och * T:(och + 1) * T]),
            waits=[("DV", d4_dv[(3, och)])], inc="Y1S", k=16)

    rm3 = row_math(ssD, sqrD, ar2, br2, br2, T)

    d7_dv = {}
    for tb in range(4):
        pb = 4 + (tb % 2) * 2
        P.op("tensor", lambda t, tb=tb, pb=pb: t.matmul(
            BK(pb), ones_f[:], ar2[0:1, tb * TBL:(tb + 1) * TBL], start=True, stop=True),
            waits=[("DV", rm3) if tb == 0 else ("DV", d7_dv.get((tb - 2, DCH - 1)))], inc="PE")
        bcb = P.op("tensor", lambda t, tb=tb, pb=pb: t.matmul(
            BK(pb + 1), ones_f[:], br2[0:1, tb * TBL:(tb + 1) * TBL], start=True, stop=True), inc="PE")
        for och in range(DCH):
            i = tb * DCH + och
            P.op("vector", lambda v, och=och, tb=tb, pb=pb, sl=i % 2: v.tensor_mul(
                sqD[sl][:], ares[:, och * T + tb * TBL: och * T + (tb + 1) * TBL], BK(pb)),
                waits=[("PE", bcb)], inc="DV")
            d7_dv[(tb, och)] = P.op("vector", lambda v, och=och, tb=tb, pb=pb, sl=i % 2: v.tensor_add(
                x2b[:, och * T + tb * TBL: och * T + (tb + 1) * TBL], sqD[sl][:], BK(pb + 1)), inc="DV")
    x2_done = P.n("DV")

    P.barrier()
    rk_dv = {}
    for ttl in range(16):
        mmlast = None
        for chn in range(DCH):
            wts = []
            if chn == 0 and ttl >= 2:
                wts.append(("DV", rk_dv[ttl - 2]))
            mmlast = P.op("tensor", lambda t, ttl=ttl, chn=chn: t.matmul(
                PS[0:128, 2048 + (ttl % 2) * 512: 2048 + (ttl % 2) * 512 + NE],
                x2b[:, chn * T + ttl * 128: chn * T + (ttl + 1) * 128],
                gw_sb[:, chn * NE:(chn + 1) * NE],
                start=(chn == 0), stop=(chn == DCH - 1)),
                waits=wts, inc="PE")
        P.op("vector", lambda v, ttl=ttl: v.tensor_copy(
            lg_sb[:, ttl * NE:(ttl + 1) * NE], PS[0:128, 2048 + (ttl % 2) * 512: 2048 + (ttl % 2) * 512 + NE]),
            waits=[("PE", mmlast)], inc="DV")
        P.op("vector", lambda v, ttl=ttl: v.tensor_reduce(
            mxt[:, ttl:ttl + 1], lg_sb[:, ttl * NE:(ttl + 1) * NE], AX.X, ALU.max), inc="DV")
        P.op("vector", lambda v, ttl=ttl: v.tensor_single_scalar(
            mkt[:], lg_sb[:, ttl * NE:(ttl + 1) * NE], mxt[:, ttl:ttl + 1], ALU.is_equal), inc="DV")
        P.op("vector", lambda v: v.tensor_mul(wkt[:], mkt[:], iotaw[:]), inc="DV")
        P.op("vector", lambda v, ttl=ttl: v.tensor_reduce(sct[:, ttl:ttl + 1], wkt[:], AX.X, ALU.max), inc="DV")
        P.op("vector", lambda v, ttl=ttl: v.tensor_single_scalar(
            mkt[:], wkt[:], sct[:, ttl:ttl + 1], ALU.is_equal), inc="DV")
        mb_dv = P.op("vector", lambda v, ttl=ttl: v.tensor_copy(
            mbf[:, ttl * NE:(ttl + 1) * NE], mkt[:]), inc="DV")
        rlast = None
        for j in range(ttl + 1):
            lt = ones2 if j < ttl else LS
            rlast = P.op("tensor", lambda t, j=j, lt=lt, ttl=ttl: t.matmul(
                PS[0:128, 3584:3584 + NE], lt[:], mbf[:, j * NE:(j + 1) * NE],
                start=(j == 0), stop=(j == ttl)),
                waits=[("DV", mb_dv), ("DV", rk_dv.get(ttl - 1))] if j == 0 else [], inc="PE")
        P.op("vector", lambda v: v.tensor_mul(wkt[:], mkt[:], PS[0:128, 3584:3584 + NE]),
             waits=[("PE", rlast)], inc="DV")
        P.op("vector", lambda v: v.tensor_reduce(rkc[:, 0:1], wkt[:], AX.X, ALU.add), inc="DV")
        P.op("vector", lambda v: v.tensor_mul(wkt[:], mkt[:], ecaps[:]), inc="DV")
        P.op("vector", lambda v: v.tensor_reduce(rkc[:, 1:2], wkt[:], AX.X, ALU.add), inc="DV")
        P.op("vector", lambda v: v.tensor_add(rkc[:, 2:3], rkc[:, 0:1], rkc[:, 1:2]), inc="DV")
        P.op("vector", lambda v: v.tensor_single_scalar(rkc[:, 3:4], rkc[:, 0:1], float(CAP) - 0.5, ALU.is_ge), inc="DV")
        rk_dv[ttl] = P.op("vector", lambda v, ttl=ttl: v.scalar_tensor_tensor(
            sct2b[:, ttl:ttl + 1], rkc[:, 3:4], 8192.0, rkc[:, 2:3], ALU.mult, ALU.add), inc="DV")
    sg_ac = P.op("scalar", lambda s: s.activation(sct[:], mxt[:], AF.Sigmoid),
                 waits=[("DV", P.n("DV"))], inc="AC")
    trs = P.op("tensor", lambda t: t.transpose(PS[0:16, 3072:3200], sct[:], identF[:]),
               waits=[("AC", sg_ac), ("DV", w_ident)], inc="PE")
    sc_dv = P.op("vector", lambda v: v.tensor_copy(sct2[:], PS[0:16, 3072:3200]),
                 waits=[("PE", trs)], inc="DV")
    sc_xl = ("SC", P.op("gpsimd", lambda g: g.dma_start(out=scrow[0:1, :], in_=sct2[:]),
                 waits=[("DV", sc_dv)], inc="SC", k=16))
    trs2 = P.op("tensor", lambda t: t.transpose(PS[0:16, 3200:3328], sct2b[:], identF[:]),
                waits=[("DV", rk_dv[15])], inc="PE")
    sc3_dv = P.op("vector", lambda v: v.tensor_copy(sct3[:], PS[0:16, 3200:3328]),
                  waits=[("PE", trs2)], inc="DV")
    sr_xl = ("SR", P.op("sync", lambda s: s.dma_start(out=slotrow[0:1, :], in_=sct3[:]),
        waits=[("DV", sc3_dv)], inc="SR", k=16))
    if DBG:
        P.op("sync", lambda s: s.dma_start(out=dbg_x2, in_=x2b[:]), waits=[("DV", x2_done)], inc="SO", k=16)
        P.op("sync", lambda s: s.dma_start(out=dbg_slot, in_=slotrow[0:1, :]), waits=[sr_xl], inc="SO", k=16)
        P.op("sync", lambda s: s.dma_start(out=dbg_sc, in_=scrow[0:1, :]), waits=[sc_xl], inc="SO", k=16)
        P.op("sync", lambda s: s.dma_start(out=dbg_ares, in_=ares[:]), inc="SO", k=16)

    P.barrier()
    MB2 = MSC.bitcast(BF16)
    PSB = PS.bitcast(BF16)
    X_tok = MB2[:, 0:16 * KB]
    xe = MB2[:, 16 * KB:16 * KB + DCH * CAP]
    Pg = MB2[:, 20 * KB:20 * KB + 16 * CAP]
    dnb = MB2[:, 20 * KB:20 * KB + 4 * KB]
    PTt = MB2[:, 24 * KB:24 * KB + 4 * TBL]
    slot_bc = MSC[:, 14 * KB:14 * KB + T]
    ssE = MSC[0:1, 8 * KB:8 * KB + T]
    sqrE = MSC[0:1, 10 * KB:10 * KB + T]
    y1tl = [MSC[:, 12 * KB + i * TBL:12 * KB + (i + 1) * TBL] for i in range(2)]
    sqE = [MSC[:, 12 * KB + KB + i * TBL:12 * KB + KB + (i + 1) * TBL] for i in range(2)]
    faccv = MSC[:, 16 * KB:16 * KB + DCH * T]

    E = Arena(MSC, P1, P1 + 128 * KB, TOP, "E")
    wg_sb = E.al("wg", [128, DCH * TBL], BF16)
    wu_sb = E.al("wu", [128, DCH * TBL], BF16)
    wd_sb = E.al("wd", [128, 8 * KB], BF16)

    gs = E.al("gs", [128, 8 * TBL], BF16)
    um = [E.al(f"um{i}", [128, TBL], BF16) for i in range(2)]

    last_g = last_u = last_d = None
    acc_dv = {}
    bank_rd = {}

    npass = 0
    for fh in range(2):
        for fq in range(2):
            wgl = ("WG", P.op("gpsimd", lambda g, fh=fh, fq=fq: g.dma_start(
                out=wg_sb[:].rearrange("p (c f) -> p c f", c=DCH),
                in_=sg[:, fh * KB + fq * TBL: fh * KB + (fq + 1) * TBL].rearrange("(c p) f -> p c f", p=128)),
                waits=[("PE", last_g)], inc="WG", k=16))
            wul = ("WU", P.op("gpsimd", lambda g, fh=fh, fq=fq: g.dma_start(
                out=wu_sb[:].rearrange("p (c f) -> p c f", c=DCH),
                in_=su[:, fh * KB + fq * TBL: fh * KB + (fq + 1) * TBL].rearrange("(c p) f -> p c f", p=128)),
                waits=[("PE", last_u)], inc="WU", k=16))
            if fq == 0:
                wdl = ("WD", P.op("gpsimd", lambda g, fh=fh: g.dma_start(
                    out=wd_sb[:].rearrange("p (c dd) -> p c dd", c=8),
                    in_=sd[fh * KB:(fh + 1) * KB, :].rearrange("(c p) dd -> p c dd", p=128)),
                    waits=[("PE", last_d)], inc="WD", k=16))
            for tb in range(4):
                gsil, um_dv, hdone = {}, {}, {}
                for fc in range(4):
                    pb = fc % 2
                    mmlast = None
                    for chn in range(DCH):
                        wts = []
                        if chn == 0:
                            wts.append(wgl)
                            wts.append(bank_rd.get(pb))
                            if fc < 2:
                                wts.append(("PE", last_d))
                        mmlast = P.op("tensor", lambda t, chn=chn, fc=fc, fq=fq, tb=tb, pb=pb: t.matmul(
                            BK(pb), wg_sb[:, chn * TBL + fc * 128: chn * TBL + fc * 128 + 128],
                            x2b[:, chn * T + tb * TBL: chn * T + (tb + 1) * TBL],
                            start=(chn == 0), stop=(chn == DCH - 1)),
                            waits=wts, inc="PE")
                    last_g = mmlast
                    gsil[fc] = P.op("scalar", lambda st, fc=fc, pb=pb: st.activation(
                        gs[:, fc * TBL:(fc + 1) * TBL], BK(pb), AF.Silu),
                        waits=[("PE", mmlast)], inc="AC")
                    bank_rd[pb] = ("AC", gsil[fc])
                for fc in range(4):
                    pb = fc % 2
                    mmlast = None
                    for chn in range(DCH):
                        wts = []
                        if chn == 0:
                            wts.append(wul)
                            wts.append(bank_rd.get(pb))
                        mmlast = P.op("tensor", lambda t, chn=chn, fc=fc, tb=tb, pb=pb: t.matmul(
                            BK(pb), wu_sb[:, chn * TBL + fc * 128: chn * TBL + fc * 128 + 128],
                            x2b[:, chn * T + tb * TBL: chn * T + (tb + 1) * TBL],
                            start=(chn == 0), stop=(chn == DCH - 1)),
                            waits=wts, inc="PE")
                    last_u = mmlast
                    uw = [("PE", mmlast), ("PL", hdone.get(fc - 2) or P.n("PL"))]
                    um_dv[fc] = P.op("vector", lambda v, fc=fc, pb=pb: v.tensor_copy(
                        um[fc % 2][:], BK(pb)), waits=uw, inc="DV")
                    bank_rd[pb] = ("DV", um_dv[fc])
                    hdone[fc] = P.op("gpsimd", lambda g, fc=fc: g.tensor_mul(
                        gs[:, fc * TBL:(fc + 1) * TBL], gs[:, fc * TBL:(fc + 1) * TBL], um[fc % 2][:]),
                        waits=[("DV", um_dv[fc]), ("AC", gsil[fc])], inc="PL")
                for dch in range(DCH):
                    pb = 2 + dch % 2
                    mmlast = None
                    for fc in range(4):
                        wts = []
                        if fc == 0:
                            wts.append(wdl)
                            wts.append(("PL", hdone[3]))
                            wts.append(bank_rd.get(pb))
                        mmlast = P.op("tensor", lambda t, dch=dch, fc=fc, fq=fq, pb=pb: t.matmul(
                            BK(pb), wd_sb[:, (fq * 4 + fc) * KB + dch * 128: (fq * 4 + fc) * KB + dch * 128 + 128],
                            gs[:, fc * TBL:(fc + 1) * TBL],
                            start=(fc == 0), stop=(fc == 3)),
                            waits=wts, inc="PE")
                    last_d = mmlast
                    if npass == 0:
                        acc_dv[(tb, dch)] = P.op("vector", lambda v, dch=dch, tb=tb, pb=pb: v.tensor_copy(
                            faccv[:, dch * T + tb * TBL: dch * T + (tb + 1) * TBL], BK(pb)),
                            waits=[("PE", mmlast)], inc="DV")
                    else:
                        acc_dv[(tb, dch)] = P.op("vector", lambda v, dch=dch, tb=tb, pb=pb: v.tensor_add(
                            faccv[:, dch * T + tb * TBL: dch * T + (tb + 1) * TBL],
                            faccv[:, dch * T + tb * TBL: dch * T + (tb + 1) * TBL], BK(pb)),
                            waits=[("PE", mmlast)], inc="DV")
                    bank_rd[pb] = ("DV", acc_dv[(tb, dch)])
            npass += 1

    if XBAR & 1:
        P.barrier()
    if DBG:
        P.op("sync", lambda s: s.dma_start(out=dbg_shr, in_=faccv), waits=[("DV", P.n("DV"))], inc="SO", k=16)
        for _e2 in ENGINES:
            P.pending[_e2].append(("SO", P.n("SO")))
    shared_pe = P.n("PE")
    cv_dv = {}
    for tb in range(4):
        pb = 6 + tb % 2
        bcs = P.op("tensor", lambda t, tb=tb, pb=pb: t.matmul(
            BK(pb), ones_b[0:1, :], scrow[0:1, tb * TBL:(tb + 1) * TBL],
            start=True, stop=True),
            waits=([sc_xl, ("PE", shared_pe)] if tb == 0 else []) +
                  [bank_rd.get(pb)], inc="PE")
        for chn in range(DCH):
            cv_dv[(tb, chn)] = P.op("vector", lambda v, chn=chn, tb=tb, pb=pb: v.tensor_mul(
                x2b[:, chn * T + tb * TBL: chn * T + (tb + 1) * TBL],
                x2b[:, chn * T + tb * TBL: chn * T + (tb + 1) * TBL], BK(pb)),
                waits=[("PE", bcs)], inc="DV")
            bank_rd[pb] = ("DV", cv_dv[(tb, chn)])
    P.barrier()
    bank_rd = {}
    tr_dv = {}
    for tt in range(16):
        for dch in range(DCH):
            j = tt * DCH + dch
            sl = j % 2
            po = 6144 + 1024 * sl
            trp = P.op("tensor", lambda t, tt=tt, dch=dch, po=po: t.transpose(
                PSB[:, po:po + 128], x2b[:, dch * T + tt * 128: dch * T + (tt + 1) * 128], identB[:]),
                waits=[("DV", tr_dv.get(j - 2))], inc="PE")
            tr_dv[j] = P.op("vector", lambda v, tt=tt, dch=dch, po=po: v.tensor_copy(
                X_tok[:, tt * KB + dch * 128: tt * KB + dch * 128 + 128], PSB[:, po:po + 128]),
                waits=[("PE", trp)], inc="DV")
    if XBAR & 2:
        P.barrier()
    sb_ac = {}
    for tb in range(4):
        pb = 4 + tb % 2
        bcm = P.op("tensor", lambda t, tb=tb, pb=pb: t.matmul(
            BK(pb), ones_f[0:1, :], slotrow[0:1, tb * TBL:(tb + 1) * TBL], start=True, stop=True),
            waits=[sr_xl if tb == 0 else None, ("AC", sb_ac.get(tb - 2))], inc="PE")
        sb_ac[tb] = P.op("scalar", lambda st, tb=tb, pb=pb: st.copy(
            slot_bc[:, tb * TBL:(tb + 1) * TBL], BK(pb)),
            waits=[("PE", bcm)], inc="AC")
        bank_rd[pb] = ("AC", sb_ac[tb])
    P.barrier()

    last_scatter = None
    sc_fa = {}
    for e in range(NE):
        pg_dv = {}
        for tt in range(16):
            pg_dv[tt] = P.op("vector", lambda v, tt=tt, e=e: v.tensor_scalar(
                Pg[:, tt * CAP:(tt + 1) * CAP], iota512[:], sct2b[:, tt:tt + 1], -float(e * CAP),
                ALU.subtract, ALU.is_equal),
                waits=[("PE", last_scatter), ("SR", P.n("SR"))] if tt == 0 else [], inc="DV")
        pg_done = pg_dv[15]
        if XBAR & 4:
            P.barrier()
        ge_dv = {}
        for dh in range(2):
            mml = {}
            for dc in range(4):
                dch = dh * 4 + dc
                mmlast = None
                for tt in range(16):
                    wts = []
                    if dh == 0 and dc == 0:
                        wts.append(("DV", pg_dv[tt]))
                    elif tt == 0:
                        wts.append(("DV", pg_done))
                    if tt == 0:
                        wts.append(bank_rd.get(dc))
                    mmlast = P.op("tensor", lambda t, tt=tt, dch=dch, dc=dc: t.matmul(
                        BK(dc), X_tok[:, tt * KB + dch * 128: tt * KB + dch * 128 + 128],
                        Pg[:, tt * CAP:(tt + 1) * CAP],
                        start=(tt == 0), stop=(tt == 15)),
                        waits=wts, inc="PE")
                mml[dc] = mmlast
            for dc in range(4):
                dch = dh * 4 + dc
                ge_dv[dc] = P.op("vector", lambda v, dch=dch, dc=dc: v.tensor_copy(
                    xe[:, dch * CAP:(dch + 1) * CAP], BK(dc)),
                    waits=[("PE", mml[dc])], inc="DV")
                bank_rd[dc] = ("DV", ge_dv[dc])
        gather_end = mml[3]
        xe_done = P.n("DV")
        if XBAR & 8:
            P.barrier()
        if DBG and e == 0:
            P.op("sync", lambda s: s.dma_start(out=dbg_xe, in_=xe), waits=[("DV", xe_done)], inc="SO", k=16)
            P.op("sync", lambda s: s.dma_start(out=dbg_xtok, in_=X_tok), inc="SO", k=16)
            for _e2 in ENGINES:
                P.pending[_e2].append(("SO", P.n("SO")))
        pt0_dv = {}
        for stile in range(4):
            pt0_dv[stile] = P.op("vector", lambda v, stile=stile, e=e: v.tensor_scalar(
                PTt[:, stile * TBL:(stile + 1) * TBL],
                slot_bc[:, 0:TBL], iotacol[:],
                float(e * CAP + stile * 128), ALU.subtract, ALU.is_equal),
                waits=[("PE", gather_end)] if stile == 0 else [], inc="DV")
        dn_dv = {}
        for fh in range(2):
            gsil, um_dv, hdone = {}, {}, {}
            for fq in range(2):
                wgl = ("WG", P.op("gpsimd", lambda g, e=e, fh=fh, fq=fq: g.dma_start(
                    out=wg_sb[:].rearrange("p (c f) -> p c f", c=DCH),
                    in_=rg[e][:, fh * KB + fq * TBL: fh * KB + (fq + 1) * TBL].rearrange("(c p) f -> p c f", p=128)),
                    waits=[("PE", last_g)], inc="WG", k=16))
                wul = ("WU", P.op("gpsimd", lambda g, e=e, fh=fh, fq=fq: g.dma_start(
                    out=wu_sb[:].rearrange("p (c f) -> p c f", c=DCH),
                    in_=ru[e][:, fh * KB + fq * TBL: fh * KB + (fq + 1) * TBL].rearrange("(c p) f -> p c f", p=128)),
                    waits=[("PE", last_u)], inc="WU", k=16))
                if fq == 0:
                    wdl = ("WD", P.op("gpsimd", lambda g, e=e, fh=fh: g.dma_start(
                        out=wd_sb[:].rearrange("p (c dd) -> p c dd", c=8),
                        in_=rd[e][fh * KB:(fh + 1) * KB, :].rearrange("(c p) dd -> p c dd", p=128)),
                        waits=[("PE", last_d)], inc="WD", k=16))
                for fc in range(4):
                    fcl = fq * 4 + fc
                    pb = fc % 2
                    mmlast = None
                    for chn in range(DCH):
                        wts = []
                        if chn == 0:
                            wts.append(wgl)
                            wts.append(bank_rd.get(pb))
                            if fcl < 2:
                                wts.append(("PE", last_d))
                                wts.append(("DV", xe_done))
                        mmlast = P.op("tensor", lambda t, chn=chn, fcl=fcl, fc=fc, pb=pb: t.matmul(
                            BK(pb), wg_sb[:, chn * TBL + fc * 128: chn * TBL + fc * 128 + 128],
                            xe[:, chn * CAP:(chn + 1) * CAP],
                            start=(chn == 0), stop=(chn == DCH - 1)),
                            waits=wts, inc="PE")
                    last_g = mmlast
                    gsil[fcl] = P.op("scalar", lambda st, fcl=fcl, pb=pb: st.activation(
                        gs[:, fcl * TBL:(fcl + 1) * TBL], BK(pb), AF.Silu),
                        waits=[("PE", mmlast)], inc="AC")
                    bank_rd[pb] = ("AC", gsil[fcl])
                for fc in range(4):
                    fcl = fq * 4 + fc
                    pb = fc % 2
                    mmlast = None
                    for chn in range(DCH):
                        wts = []
                        if chn == 0:
                            wts.append(wul)
                            wts.append(bank_rd.get(pb))
                        mmlast = P.op("tensor", lambda t, chn=chn, fc=fc, pb=pb: t.matmul(
                            BK(pb), wu_sb[:, chn * TBL + fc * 128: chn * TBL + fc * 128 + 128],
                            xe[:, chn * CAP:(chn + 1) * CAP],
                            start=(chn == 0), stop=(chn == DCH - 1)),
                            waits=wts, inc="PE")
                    last_u = mmlast
                    uw = [("PE", mmlast), ("PL", hdone.get(fcl - 2) or P.n("PL"))]
                    um_dv[fcl] = P.op("vector", lambda v, fcl=fcl, pb=pb: v.tensor_copy(
                        um[fcl % 2][:], BK(pb)), waits=uw, inc="DV")
                    bank_rd[pb] = ("DV", um_dv[fcl])
                    hdone[fcl] = P.op("gpsimd", lambda g, fcl=fcl: g.tensor_mul(
                        gs[:, fcl * TBL:(fcl + 1) * TBL], gs[:, fcl * TBL:(fcl + 1) * TBL], um[fcl % 2][:]),
                        waits=[("DV", um_dv[fcl]), ("AC", gsil[fcl])], inc="PL")
            if XBAR & 16:
                P.barrier()
            for stile in range(4):
                for dblk in range(2):
                    gidx = fh * 8 + stile * 2 + dblk
                    pb = 2 + gidx % 2
                    mmlast = None
                    for fcl in range(8):
                        wts = []
                        if fcl == 0:
                            wts.append(wdl)
                            wts.append(("PL", hdone[7]))
                            wts.append(bank_rd.get(pb))
                        mmlast = P.op("tensor", lambda t, stile=stile, dblk=dblk, fcl=fcl, pb=pb: t.matmul(
                            BK(pb), gs[:, fcl * TBL + stile * 128: fcl * TBL + stile * 128 + 128],
                            wd_sb[:, fcl * KB + dblk * TBL: fcl * KB + (dblk + 1) * TBL],
                            start=(fcl == 0), stop=(fcl == 7)),
                            waits=wts, inc="PE")
                    last_d = mmlast
                    if fh == 0:
                        dn_dv[gidx] = P.op("vector", lambda v, stile=stile, dblk=dblk, pb=pb: v.tensor_copy(
                            dnb[:, stile * KB + dblk * TBL: stile * KB + (dblk + 1) * TBL], BK(pb)),
                            waits=[("PE", mmlast)], inc="DV")
                        bank_rd[pb] = ("DV", dn_dv[gidx])
                    else:
                        dn_dv[gidx] = P.op("vector", lambda v, stile=stile, dblk=dblk, pb=pb: v.tensor_add(
                            dnb[:, stile * KB + dblk * TBL: stile * KB + (dblk + 1) * TBL],
                            dnb[:, stile * KB + dblk * TBL: stile * KB + (dblk + 1) * TBL], BK(pb)),
                            waits=[("PE", mmlast)], inc="DV")
                        bank_rd[pb] = ("DV", dn_dv[gidx])
        dn_done = P.n("DV")
        if XBAR & 32:
            P.barrier()
        if DBG and e == 0:
            P.op("sync", lambda s: s.dma_start(out=dbg_dnb, in_=dnb), waits=[("DV", dn_done)], inc="SO", k=16)
            for _e2 in ENGINES:
                P.pending[_e2].append(("SO", P.n("SO")))
        if XBAR & 64:
            P.barrier()
        for tblk in range(4):
            if tblk == 0:
                pt_dv = pt0_dv
            else:
                pt_dv = {}
                for stile in range(4):
                    pt_dv[stile] = P.op("vector", lambda v, stile=stile, tblk=tblk, e=e: v.tensor_scalar(
                        PTt[:, stile * TBL:(stile + 1) * TBL],
                        slot_bc[:, tblk * TBL:(tblk + 1) * TBL], iotacol[:],
                        float(e * CAP + stile * 128), ALU.subtract, ALU.is_equal),
                        inc="DV")
            if DBG and e == 0 and tblk == 0:
                P.op("sync", lambda s: s.dma_start(out=dbg_pt, in_=PTt), waits=[("DV", pt_dv[3])], inc="SO", k=16)
                P.op("sync", lambda s: s.dma_start(out=dbg_sbc, in_=slot_bc), inc="SO", k=16)
                for _e2 in ENGINES:
                    P.pending[_e2].append(("SO", P.n("SO")))
            for dch in range(DCH):
                pb = 4 + dch % 2
                mmlast = None
                for stile in range(4):
                    wts = []
                    if stile == 0:
                        wts.append(("DV", pt_dv[3] if tblk else dn_done))
                        wts.append(bank_rd.get(pb))
                    mmlast = P.op("tensor", lambda t, stile=stile, dch=dch, tblk=tblk, pb=pb: t.matmul(
                        BK(pb), dnb[:, stile * KB + dch * 128: stile * KB + dch * 128 + 128],
                        PTt[:, stile * TBL:(stile + 1) * TBL],
                        start=(stile == 0), stop=(stile == 3)),
                        waits=wts, inc="PE")
                last_scatter = mmlast
                sc_fa[(tblk, dch)] = P.op("vector", lambda v, dch=dch, tblk=tblk, pb=pb: v.tensor_add(
                    faccv[:, dch * T + tblk * TBL: dch * T + (tblk + 1) * TBL],
                    faccv[:, dch * T + tblk * TBL: dch * T + (tblk + 1) * TBL], BK(pb)),
                    waits=[("PE", mmlast)], inc="DV")
                bank_rd[pb] = ("DV", sc_fa[(tblk, dch)])

    P.barrier()
    if DBG:
        P.op("sync", lambda s: s.dma_start(out=dbg_ffn, in_=faccv), inc="SO", k=16)
        for _e in ENGINES:
            P.pending[_e].append(("SO", P.n("SO")))
    ln_stats(faccv, T, 0, T, sqE, ssE, sqrE)
    rm4 = row_math(ssE, sqrE, ar2, br2, br2, T)
    fin_dv = {}
    for tb in range(4):
        pb = 4 + (tb % 2) * 2
        P.op("tensor", lambda t, tb=tb, pb=pb: t.matmul(
            BK(pb), ones_f[:], ar2[0:1, tb * TBL:(tb + 1) * TBL], start=True, stop=True),
            waits=[("DV", rm4) if tb == 0 else ("DV", fin_dv.get((tb - 2, DCH - 1)))], inc="PE")
        bcb = P.op("tensor", lambda t, tb=tb, pb=pb: t.matmul(
            BK(pb + 1), ones_f[:], br2[0:1, tb * TBL:(tb + 1) * TBL], start=True, stop=True), inc="PE")
        for och in range(DCH):
            i = tb * DCH + och
            sml = f"YL{i % 2}"
            ly = (sml, P.op("sync", lambda s, och=och, tb=tb, sl=i % 2: s.dma_start(
                out=y1tl[sl][:], in_=y1_h[och * 128:(och + 1) * 128, tb * TBL:(tb + 1) * TBL]),
                waits=[("DV", fin_dv.get((tb, och - 2)) or fin_dv.get((tb - 1, DCH - 2 + och)))], inc=sml, k=16))
            P.op("vector", lambda v, och=och, tb=tb, pb=pb, sl=i % 2: v.tensor_mul(
                sqE[sl][:], faccv[:, och * T + tb * TBL: och * T + (tb + 1) * TBL], BK(pb)),
                waits=[("PE", bcb)], inc="DV")
            P.op("vector", lambda v, pb=pb, sl=i % 2: v.tensor_add(sqE[sl][:], sqE[sl][:], BK(pb + 1)), inc="DV")
            fin_dv[(tb, och)] = P.op("vector", lambda v, och=och, tb=tb, sl=i % 2: v.tensor_add(
                faccv[:, och * T + tb * TBL: och * T + (tb + 1) * TBL], sqE[sl][:], y1tl[sl][:]),
                waits=[ly], inc="DV")
    for och in range(DCH):
        P.op("sync", lambda s, och=och: s.dma_start(
            out=yT[och * 128:(och + 1) * 128, :], in_=faccv[:, och * T:(och + 1) * T]),
            waits=[("DV", fin_dv[(3, och)])], inc="SO", k=16)

    P.op("sync", lambda s: s.nop(), waits=[("SO", P.n("SO"))])

    with nc.Block() as block:
        P.emit(block)
    return nc


_NC_CACHE = {}


def _get_nc():
    if "nc" not in _NC_CACHE:
        _NC_CACHE["nc"] = build_program()
    return _NC_CACHE["nc"]


def make_in_maps(L0_attn_x, L0_attn_W_qkv, L0_attn_W_out, L0_attn_b_out, L0_ffn_gate_w,
                 L0_ffn_shared_gate, L0_ffn_shared_up, L0_ffn_shared_down,
                 L0_ffn_r_gate, L0_ffn_r_up, L0_ffn_r_down):
    x = np.asarray(L0_attn_x)
    Wr = np.asarray(L0_attn_W_qkv).reshape(H, 3, HD, D)
    wq_h = np.ascontiguousarray(Wr[:, 0].reshape(H * HD, D).T)
    wk_h = np.ascontiguousarray(Wr[:, 1].reshape(H * HD, D).T)
    wv_h = np.ascontiguousarray(Wr[:, 2].reshape(H * HD, D).T)
    wo_h = np.ascontiguousarray(np.asarray(L0_attn_W_out).T)
    bo_h = np.ascontiguousarray(np.asarray(L0_attn_b_out).reshape(DCH, 128).T)
    gw_h = np.ascontiguousarray(np.asarray(L0_ffn_gate_w).T)
    sg_h = np.ascontiguousarray(np.asarray(L0_ffn_shared_gate).T)
    su_h = np.ascontiguousarray(np.asarray(L0_ffn_shared_up).T)
    sd_h = np.ascontiguousarray(np.asarray(L0_ffn_shared_down).T)
    rg_h = np.ascontiguousarray(np.transpose(np.asarray(L0_ffn_r_gate), (0, 2, 1)))
    ru_h = np.ascontiguousarray(np.transpose(np.asarray(L0_ffn_r_up), (0, 2, 1)))
    rd_h = np.ascontiguousarray(np.transpose(np.asarray(L0_ffn_r_down), (0, 2, 1)))
    in_maps = []
    for c in range(8):
        b, h = c // 2, c % 2
        xb = x[b]
        xcore = np.concatenate([xb[h * T:(h + 1) * T], xb[(1 - h) * T:(2 - h) * T]], axis=0)
        in_maps.append({
            "xT": np.ascontiguousarray(xcore.T),
            "wq": wq_h, "wk": wk_h, "wv": wv_h, "wo": wo_h, "bo": bo_h, "gw": gw_h,
            "sg": sg_h, "su": su_h, "sd": sd_h, "rg": rg_h, "ru": ru_h, "rd": rd_h,
        })
    return in_maps


def kernel(**inputs):
    in_maps = make_in_maps(**inputs)
    nc = _get_nc()
    res = run_bass_kernel_spmd(nc, in_maps, core_ids=list(range(8)))
    out = np.empty((4, S, D), np.float32)
    for c in range(8):
        b, h = c // 2, c % 2
        out[b, h * T:(h + 1) * T, :] = res.results[c]["yT"].T
    return out



# version 12
# speedup vs baseline: 1.3124x; 1.0369x over previous
import numpy as np
import concourse.bass as bass
import concourse.mybir as mybir
from concourse.bass_utils import run_bass_kernel_spmd

F32 = mybir.dt.float32
BF16 = mybir.dt.bfloat16
AF = mybir.ActivationFunctionType
ALU = mybir.AluOpType
AX = mybir.AxisListType

D, DCH = 1024, 8
CAP = 512
S = 4096
T = 2048
H, HD = 16, 64
FF = 2048
NE = 8
TBL = 512
EPS = 1e-5
ISC = 1.0 / 32.0
KB = 1024

ENGINES = ("tensor", "scalar", "vector", "gpsimd", "sync")
DBG = False
DBGKIND = "Internal"
XBAR = 3


class Planner:
    def __init__(self, nc, sems):
        self.nc = nc
        self.ops = {e: [] for e in ENGINES}
        self.sems = sems
        self.counts = {k: 0 for k in sems}
        self.pending = {e: [] for e in ENGINES}

    SELFSEM = {"vector": "DV", "scalar": "AC"}

    def op(self, eng, fn, waits=(), inc=None, k=1):
        w = self.pending[eng] + [x for x in waits if x is not None and x[1]]
        self.pending[eng] = []
        ss = self.SELFSEM.get(eng) or ("PL" if eng == "gpsimd" and inc == "PL" else None)
        if ss is not None and self.counts[ss] > 0:
            w.append((ss, self.counts[ss]))
        self.ops[eng].append((fn, w, inc, k))
        if inc is not None:
            self.counts[inc] += k
            return self.counts[inc]
        return None

    def n(self, sem):
        return self.counts[sem]

    def barrier(self):
        snap = [(s, c) for s, c in self.counts.items() if c > 0]
        for e in ENGINES:
            self.pending[e] = list(snap)

    def emit(self, block):
        for engname in ENGINES:
            def make_body(ops=self.ops[engname], sems=self.sems):
                def body(eng):
                    for fn, waits, inc, k in ops:
                        for s, v in waits:
                            eng.wait_ge(sems[s], v)
                        ins = fn(eng)
                        if inc is not None:
                            ins.then_inc(sems[inc], k)
                return body
            getattr(block, engname)(make_body())


class Arena:

    def __init__(self, master, origin, base, limit, tag):
        self.m, self.origin = master, origin
        self.off, self.limit, self.tag = base, limit, tag

    def al(self, name, shape, dtype):
        nbytes = int(np.prod(shape[1:])) * (2 if dtype == BF16 else 4)
        nbytes = (nbytes + 31) // 32 * 32
        o = self.off - self.origin
        view = self.m if dtype == F32 else self.m.bitcast(BF16)
        es = 4 if dtype == F32 else 2
        t = view[0:shape[0], o // es: o // es + int(np.prod(shape[1:]))]
        if len(shape) > 2:
            raise ValueError("arena views are 2-D")
        self.off += nbytes
        assert self.off <= self.limit, f"{self.tag}/{name}: off {self.off} > {self.limit}"
        return t


def build_program():
    nc = bass.Bass("TRN2", target_bir_lowering=False)
    xT = nc.dram_tensor("xT", [D, S], F32, kind="ExternalInput").ap()
    wq = nc.dram_tensor("wq", [D, D], F32, kind="ExternalInput").ap()
    wk = nc.dram_tensor("wk", [D, D], F32, kind="ExternalInput").ap()
    wv = nc.dram_tensor("wv", [D, D], F32, kind="ExternalInput").ap()
    wo = nc.dram_tensor("wo", [D, D], F32, kind="ExternalInput").ap()
    bo = nc.dram_tensor("bo", [128, DCH], F32, kind="ExternalInput").ap()
    gw = nc.dram_tensor("gw", [D, NE], F32, kind="ExternalInput").ap()
    sg = nc.dram_tensor("sg", [D, FF], F32, kind="ExternalInput").ap()
    su = nc.dram_tensor("su", [D, FF], F32, kind="ExternalInput").ap()
    sd = nc.dram_tensor("sd", [FF, D], F32, kind="ExternalInput").ap()
    rg = nc.dram_tensor("rg", [NE, D, FF], F32, kind="ExternalInput").ap()
    ru = nc.dram_tensor("ru", [NE, D, FF], F32, kind="ExternalInput").ap()
    rd = nc.dram_tensor("rd", [NE, FF, D], F32, kind="ExternalInput").ap()
    yT = nc.dram_tensor("yT", [D, T], F32, kind="ExternalOutput").ap()
    xh_h = nc.dram_tensor("xh_h", [D, S], BF16, kind="Internal").ap()
    y1_h = nc.dram_tensor("y1_h", [D, T], F32, kind="Internal").ap()
    if DBG:
        dbg_ot = nc.dram_tensor("dbg_ot", [128, DCH * T], BF16, kind=DBGKIND).ap()
        dbg_ares = nc.dram_tensor("dbg_ares", [128, DCH * T], F32, kind=DBGKIND).ap()
        dbg_x2 = nc.dram_tensor("dbg_x2", [128, DCH * T], BF16, kind=DBGKIND).ap()
        dbg_slot = nc.dram_tensor("dbg_slot", [1, T], F32, kind=DBGKIND).ap()
        dbg_sc = nc.dram_tensor("dbg_sc", [1, T], BF16, kind=DBGKIND).ap()
        dbg_ffn = nc.dram_tensor("dbg_ffn", [128, DCH * T], F32, kind=DBGKIND).ap()
        dbg_xe = nc.dram_tensor("dbg_xe", [128, DCH * CAP], BF16, kind=DBGKIND).ap()
        dbg_dnb = nc.dram_tensor("dbg_dnb", [128, 4 * KB], BF16, kind=DBGKIND).ap()
        dbg_xtok = nc.dram_tensor("dbg_xtok", [128, 16 * KB], BF16, kind=DBGKIND).ap()
        dbg_shr = nc.dram_tensor("dbg_shr", [128, DCH * T], F32, kind=DBGKIND).ap()
        dbg_pt = nc.dram_tensor("dbg_pt", [128, 4 * TBL], BF16, kind=DBGKIND).ap()
        dbg_sbc = nc.dram_tensor("dbg_sbc", [128, T], F32, kind=DBGKIND).ap()

    sems = {}
    for name in ("PE", "AC", "DV", "PL", "SO", "C0", "W0", "WO", "WG", "WU", "WD",
                 "MS", "SC", "Y1S", "C1", "XA0", "XA1", "XA2", "XA3", "XB0", "XB1",
                 "SH0", "SH1", "XT0", "XT1", "XF0", "XF1", "XG0", "XG1", "YL0", "YL1", "SR"):
        sems[name] = nc.semaphore(name).__enter__()
    P = Planner(nc, sems)

    GB = (nc.sbuf_base + 31) // 32 * 32
    TOP = nc.sbuf_top
    GSZ = 44 * KB
    GMST = nc.alloc_sbuf_tensor_at("GMST", [128, GSZ // 4], F32, offset=GB).ap()
    GA = Arena(GMST, GB, GB, GB + GSZ, "g")

    ones_b = GA.al("ones_b", [1, 128], BF16)
    ones_f = GA.al("ones_f", [1, 128], F32)
    col1_f = GA.al("col1_f", [128, 1], F32)
    identF = GA.al("identF", [128, 128], F32)
    gw_sb = GA.al("gw_sb", [128, DCH * NE], BF16)
    bo_sb = GA.al("bo_sb", [128, DCH], F32)
    iotaw = GA.al("iotaw", [128, NE], F32)
    epsr = GA.al("epsr", [1, 1], F32)
    ar2 = GA.al("ar2", [1, T], F32)
    br2 = GA.al("br2", [1, T], F32)
    scrow = GA.al("scrow", [1, T], BF16)
    identB = GA.al("identB", [128, 128], BF16)
    LS = GA.al("LS", [128, 128], BF16)
    ones2 = GA.al("ones2", [128, 128], BF16)
    ecaps = GA.al("ecaps", [128, NE], F32)
    iota512 = GA.al("iota512", [128, CAP], F32)
    iotacol = GA.al("iotacol", [128, 1], F32)
    mbf = GA.al("mbf", [128, 16 * NE], BF16)
    rkc = GA.al("rkc", [128, 8], F32)
    slotrow = GA.al("slotrow", [1, T], F32)
    sct2b = GA.al("sct2b", [128, 16], F32)
    P1 = (GA.off + 127) // 128 * 128
    NF32 = (TOP - P1) // 4 // 32 * 32
    MSC = nc.alloc_sbuf_tensor_at("MSC", [128, NF32], F32, offset=P1).ap()

    MB = MSC.bitcast(BF16)
    OT = MB[:, 0:DCH * T]
    x2b = MB[:, DCH * T: 2 * DCH * T]
    facc = MSC[:, 0:DCH * KB]

    PS = nc.alloc_psum_tensor("PS", [128, 4096], F32)
    BK = lambda b: PS[:, b * 512:(b + 1) * 512]

    P.op("gpsimd", lambda g: g.iota(iotaw[:], [[-1, NE]], base=NE, channel_multiplier=0,
                                    allow_small_or_imprecise_dtypes=True), inc="PL")
    P.op("gpsimd", lambda g: g.iota(identF[:], [[1, 128]], base=0, channel_multiplier=-1,
                                    allow_small_or_imprecise_dtypes=True), inc="PL")
    P.op("gpsimd", lambda g: g.iota(LS[:], [[1, 128]], base=0, channel_multiplier=-1,
                                    allow_small_or_imprecise_dtypes=True), inc="PL")
    P.op("gpsimd", lambda g: g.iota(ecaps[:], [[CAP, NE]], base=0, channel_multiplier=0,
                                    allow_small_or_imprecise_dtypes=True), inc="PL")
    P.op("gpsimd", lambda g: g.iota(iota512[:], [[1, CAP]], base=0, channel_multiplier=0,
                                    allow_small_or_imprecise_dtypes=True), inc="PL")
    P.op("gpsimd", lambda g: g.iota(iotacol[:], [[1, 1]], base=0, channel_multiplier=1,
                                    allow_small_or_imprecise_dtypes=True), inc="PL")
    ident_pl = P.n("PL")
    P.op("vector", lambda v: v.memset(ones_b[:], 1.0), inc="DV")
    P.op("vector", lambda v: v.memset(ones_f[:], 1.0), inc="DV")
    P.op("vector", lambda v: v.memset(col1_f[:], 1.0), inc="DV")
    P.op("vector", lambda v: v.memset(epsr[:], EPS), inc="DV")
    w_ident = P.op("vector", lambda v: v.tensor_single_scalar(identF[:], identF[:], 0.0, ALU.is_equal),
                   waits=[("PL", ident_pl)], inc="DV")
    P.op("vector", lambda v: v.tensor_copy(identB[:], identF[:]), inc="DV")
    P.op("vector", lambda v: v.tensor_single_scalar(LS[:], LS[:], 0.0, ALU.is_gt), inc="DV")
    P.op("vector", lambda v: v.memset(ones2[:], 1.0), inc="DV")
    init_dv = P.n("DV")
    P.barrier()
    P.op("sync", lambda s: s.dma_start(out=bo_sb[:], in_=bo), inc="C0", k=16)
    P.op("gpsimd", lambda g: g.dma_start(out=gw_sb[:].rearrange("p (c e) -> p c e", c=DCH),
                                         in_=gw.rearrange("(c p) e -> p c e", p=128)), inc="C1", k=16)

    def row_math(sumr, sqr, ar, br, tmp, width, c0=0, first_waits=None):
        sl = lambda r: r[0:1, c0:c0 + width]
        P.op("vector", lambda v: v.tensor_scalar_mul(sl(sumr), sl(sumr), 1.0 / D),
             waits=([("PE", P.n("PE"))] if first_waits is None else first_waits), inc="DV")
        P.op("vector", lambda v: v.tensor_scalar_mul(sl(sqr), sl(sqr), 1.0 / D), inc="DV")
        P.op("vector", lambda v: v.tensor_mul(sl(tmp), sl(sumr), sl(sumr)), inc="DV")
        r3 = P.op("vector", lambda v: v.tensor_sub(sl(sqr), sl(sqr), sl(tmp)), inc="DV")
        r4 = P.op("scalar", lambda s: s.activation(sl(tmp), sl(sqr), AF.Sqrt, bias=epsr[0:1, 0:1]),
                  waits=[("DV", r3)], inc="AC")
        P.op("vector", lambda v: v.reciprocal(sl(ar), sl(tmp)),
             waits=[("AC", r4)], inc="DV")
        return P.op("vector", lambda v: v.scalar_tensor_tensor(
            sl(br), sl(sumr), -1.0, sl(ar), ALU.mult, ALU.mult), inc="DV")

    A = Arena(MSC, P1, P1, TOP, "A")
    xfA = [A.al(f"xf{i}", [128, DCH * TBL], F32) for i in range(4)]
    sqA = [A.al(f"sq{i}", [128, TBL], F32) for i in range(2)]
    nscA = [A.al(f"nsc{i}", [128, TBL], F32) for i in range(2)]
    ssA = A.al("ssum", [1, S], F32)
    sqrA = A.al("ssq", [1, S], F32)
    a1r = A.al("a1r", [1, S], F32)
    b1r = A.al("b1r", [1, S], F32)
    xhA = [A.al(f"xh{i}", [128, DCH * TBL], BF16) for i in range(2)]

    NTBS = S // TBL
    mm2_id, cp_id, rm_id, nrm_id, st_id = {}, {}, {}, {}, {}
    for it in range(NTBS + 1):
        if it < NTBS:
            tb = it
            buf = xfA[tb % 4]
            sm = f"XA{tb % 4}"
            ltb = (sm, P.op("sync", lambda s, buf=buf, tb=tb: s.dma_start(
                out=buf[:].rearrange("p (c t) -> p c t", c=DCH),
                in_=xT[:, tb * TBL:(tb + 1) * TBL].rearrange("(c p) t -> p c t", p=128)),
                waits=[("DV", nrm_id.get(tb - 4))], inc=sm, k=16))
            pb = (tb % 2) * 2
            for chn in range(DCH):
                i = tb * DCH + chn
                sqb = sqA[i % 2]
                sq_ac = P.op("scalar", lambda s, buf=buf, chn=chn, sqb=sqb: s.activation(
                    sqb[:], buf[:, chn * TBL:(chn + 1) * TBL], AF.Square),
                    waits=[ltb, ("PE", mm2_id.get(i - 2))], inc="AC")
                P.op("tensor", lambda t, buf=buf, chn=chn, pb=pb: t.matmul(
                    BK(pb)[0:1, :], col1_f[:], buf[:, chn * TBL:(chn + 1) * TBL],
                    start=(chn == 0), stop=(chn == DCH - 1)),
                    waits=[ltb] + ([("DV", cp_id.get(tb - 2))] if chn == 0 else []),
                    inc="PE")
                mm2_id[i] = P.op("tensor", lambda t, sqb=sqb, pb=pb, chn=chn: t.matmul(
                    BK(pb + 1)[0:1, :], col1_f[:], sqb[:], start=(chn == 0), stop=(chn == DCH - 1)),
                    waits=[("AC", sq_ac)], inc="PE")
            P.op("vector", lambda v, tb=tb, pb=pb: v.tensor_copy(
                ssA[0:1, tb * TBL:(tb + 1) * TBL], BK(pb)[0:1, :]),
                waits=[("PE", mm2_id[tb * DCH + DCH - 1])], inc="DV")
            cp_id[tb] = P.op("vector", lambda v, tb=tb, pb=pb: v.tensor_copy(
                sqrA[0:1, tb * TBL:(tb + 1) * TBL], BK(pb + 1)[0:1, :]), inc="DV")
            rm_id[tb] = row_math(ssA, sqrA, a1r, b1r, b1r, TBL, c0=tb * TBL, first_waits=[])
        if it >= 1:
            tb = it - 1
            buf = xfA[tb % 4]
            pbb = 4 + (tb % 2) * 2
            P.op("tensor", lambda t, tb=tb, pbb=pbb: t.matmul(
                BK(pbb), ones_f[:], a1r[0:1, tb * TBL:(tb + 1) * TBL], start=True, stop=True),
                waits=[("DV", rm_id[tb]), ("DV", nrm_id.get(tb - 2))], inc="PE")
            bcb = P.op("tensor", lambda t, tb=tb, pbb=pbb: t.matmul(
                BK(pbb + 1), ones_f[:], b1r[0:1, tb * TBL:(tb + 1) * TBL], start=True, stop=True), inc="PE")
            xo = xhA[tb % 2]
            for chn in range(DCH):
                P.op("vector", lambda v, buf=buf, chn=chn, pbb=pbb, sl=chn % 2: v.tensor_mul(
                    nscA[sl][:], buf[:, chn * TBL:(chn + 1) * TBL], BK(pbb)),
                    waits=([("PE", bcb), st_id.get(tb - 2)] if chn == 0 else []), inc="DV")
                nrm_id[tb] = P.op("vector", lambda v, xo=xo, chn=chn, pbb=pbb, sl=chn % 2: v.tensor_add(
                    xo[:, chn * TBL:(chn + 1) * TBL], nscA[sl][:], BK(pbb + 1)), inc="DV")
            sms = f"SH{tb % 2}"
            st_id[tb] = (sms, P.op("gpsimd", lambda g, xo=xo, tb=tb: g.dma_start(
                out=xh_h[:, tb * TBL:(tb + 1) * TBL].rearrange("(c p) t -> p c t", p=128),
                in_=xo[:].rearrange("p (c t) -> p c t", c=DCH)),
                waits=[("DV", nrm_id[tb])], inc=sms, k=16))
    P.barrier()

    pref = None
    for hg in range(2):
        B = Arena(MSC, P1, P1 + 32 * KB, TOP, "B")
        Qc = B.al("Qc", [128, 4 * T], BF16)
        Kc = B.al("Kc", [128, 4 * S], BF16)
        Vt = B.al("Vt", [128, 32 * 520], BF16)
        wq_sb = B.al("wq", [128, DCH * TBL], BF16)
        wk_sb = B.al("wk", [128, DCH * TBL], BF16)
        wv_sb = B.al("wv", [128, DCH * TBL], BF16)
        xh_tb = [B.al(f"xh{i}", [128, DCH * TBL], BF16) for i in range(2)]
        At = [B.al(f"At{i}", [128, 1024], BF16) for i in range(3)]
        bc_sb = B.al("bc_sb", [64, TBL], F32)
        rsrow = B.al("rsrow", [1, TBL], F32)
        rsrow2 = B.al("rsrow2", [1, TBL], F32)

        if hg == 0:
            wls = {}
            for nm, wsrc, wdst in (("q", wq, wq_sb), ("k", wk, wk_sb), ("v", wv, wv_sb)):
                P.op("gpsimd", lambda g, wsrc=wsrc, wdst=wdst, hg=hg: g.dma_start(
                    out=wdst[:].rearrange("p (c o) -> p c o", c=DCH),
                    in_=wsrc[:, hg * TBL:(hg + 1) * TBL].rearrange("(c p) o -> p c o", p=128)),
                    inc="W0", k=16)
            for nm in ("q", "k", "v"):
                wls[nm] = ("W0", P.n("W0"))
        else:
            wls = pref["wls"]
        for tt in range(32):
            P.op("vector", lambda v, tt=tt: v.memset(
                Vt[:, tt * 520:(tt + 1) * 520].rearrange("p (h c) -> p h c", h=8)[:, :, 64:65], 1.0), inc="DV")
        vones_dv = P.n("DV")

        kq_ev, v_ev, pe_tb = [], [], {}
        for tb in range(NTBS):
            buf = xh_tb[tb % 2]
            smt = f"XT{tb % 2}"
            if hg == 1 and tb < 2:
                ltb = pref["xh"][tb]
            else:
                ltb = (smt, P.op("sync", lambda s, buf=buf, tb=tb: s.dma_start(
                    out=buf[:].rearrange("p (c t) -> p c t", c=DCH),
                    in_=xh_h[:, tb * TBL:(tb + 1) * TBL].rearrange("(c p) t -> p c t", p=128)),
                    waits=[("PE", pe_tb.get(tb - 2))], inc=smt, k=16))
            for kind in (("k", "q") if tb < 4 else ("k",)):
                wsb, dst, dstw = (wk_sb, Kc, S) if kind == "k" else (wq_sb, Qc, T)
                for och in range(4):
                    seq = len(kq_ev)
                    pb = seq % 2
                    mmlast = None
                    for chn in range(DCH):
                        w0 = ([ltb, wls[kind]] +
                              ([("DV", kq_ev[seq - 2])] if seq >= 2 else [])) if chn == 0 else []
                        mmlast = P.op("tensor", lambda t, wsb=wsb, chn=chn, och=och, buf=buf, pb=pb: t.matmul(
                            BK(pb), wsb[:, chn * TBL + och * 128: chn * TBL + och * 128 + 128],
                            buf[:, chn * TBL:(chn + 1) * TBL],
                            start=(chn == 0), stop=(chn == DCH - 1)),
                            waits=w0, inc="PE")
                    kq_ev.append(P.op("vector", lambda v, dst=dst, dstw=dstw, och=och, tb=tb, pb=pb: v.tensor_copy(
                        dst[:, och * dstw + tb * TBL: och * dstw + (tb + 1) * TBL], BK(pb)),
                        waits=[("PE", mmlast)], inc="DV"))
            for ttl in range(4):
                tt = tb * 4 + ttl
                sv = len(v_ev)
                pb = 2 + sv % 2
                mmlast = None
                for chn in range(DCH):
                    w0 = ([ltb, wls["v"]] +
                          ([("AC", v_ev[sv - 2])] if sv >= 2 else [])) if chn == 0 else []
                    mmlast = P.op("tensor", lambda t, chn=chn, buf=buf, ttl=ttl, pb=pb, wv_sb=wv_sb: t.matmul(
                        BK(pb), buf[:, chn * TBL + ttl * 128: chn * TBL + ttl * 128 + 128],
                        wv_sb[:, chn * TBL:(chn + 1) * TBL],
                        start=(chn == 0), stop=(chn == DCH - 1)),
                        waits=w0, inc="PE")
                v_ev.append(P.op("scalar", lambda s, tt=tt, pb=pb, Vt=Vt: s.copy(
                    Vt[:, tt * 520:(tt + 1) * 520].rearrange("p (h c) -> p h c", h=8)[:, :, 0:64],
                    BK(pb).rearrange("p (h c) -> p h c", h=8)),
                    waits=[("PE", mmlast)], inc="AC"))
            pe_tb[tb] = P.n("PE")
        b_dv, b_ac = P.n("DV"), P.n("AC")

        prev_exp = None
        prev_av = None
        prev_ot = None
        for hp in range(4):
            och = hp
            for qb in range(4):
                qk_id, ex_id, av_id = {}, {}, {}
                for kk in range(34):
                    if kk < 32:
                        kb, sl = kk, kk % 2
                        for hx in range(2):
                            rb = hx * 64
                            wts = []
                            if hx == 0:
                                if kb == 0 and hp == 0 and qb == 0:
                                    wts += [("DV", b_dv), ("AC", b_ac)]
                                if kb >= 2:
                                    wts.append(("AC", ex_id[kb - 2]))
                                elif prev_exp is not None:
                                    wts.append(("AC", prev_exp[kb]))
                            qk_id[(kb, hx)] = P.op("tensor", lambda t, och=och, rb=rb, kb=kb, hx=hx, qb=qb, sl=sl, Kc=Kc, Qc=Qc: t.matmul(
                                PS[:, sl * 1024 + hx * 512: sl * 1024 + (hx + 1) * 512],
                                Kc[rb:rb + 64, och * S + kb * 128: och * S + (kb + 1) * 128],
                                Qc[rb:rb + 64, och * T + qb * TBL: och * T + (qb + 1) * TBL],
                                start=True, stop=True, tile_position=(rb, 0)),
                                waits=wts, inc="PE")
                        ewts = [("PE", qk_id[(kb, 1)])]
                        if kb >= 3:
                            ewts.append(("PE", av_id[kb - 3]))
                        elif prev_av is not None:
                            ewts.append(("PE", prev_av))
                        ex_id[kb] = P.op("scalar", lambda s, sl=sl, a3=kb % 3, At=At: s.activation(
                            At[a3][:], PS[:, sl * 1024: (sl + 1) * 1024], AF.Exp, scale=ISC),
                            waits=ewts, inc="AC")
                    if kk >= 2:
                        kb = kk - 2
                        avlast = None
                        for hx in range(2):
                            hh = 2 * hp + hx
                            avw = [("AC", ex_id[kb])] if hx == 0 else []
                            if kb == 0 and hx == 0:
                                avw.append(("DV", vones_dv))
                                if prev_ot is not None:
                                    avw.append(("DV", prev_ot))
                            avlast = P.op("tensor", lambda t, hh=hh, kb=kb, hx=hx, a3=kb % 3, Vt=Vt, At=At: t.matmul(
                                PS[0:65, 2048 + hx * 512: 2048 + (hx + 1) * 512],
                                Vt[:, kb * 520 + hh * 65: kb * 520 + (hh + 1) * 65],
                                At[a3][:, hx * 512:(hx + 1) * 512],
                                start=(kb == 0), stop=(kb == 31)),
                                waits=avw, inc="PE")
                        av_id[kb] = avlast
                prev_exp = {0: ex_id[30], 1: ex_id[31]}
                prev_av = av_id[31]
                for hx in range(2):
                    h = hg * 8 + 2 * hp + hx
                    oc2, rb2 = h // 2, (h % 2) * 64
                    P.op("scalar", lambda s, hx=hx, rsrow=rsrow: s.activation(
                        rsrow[0:1, :], PS[64:65, 2048 + hx * 512: 2048 + (hx + 1) * 512], AF.Ln),
                        waits=[("PE", av_id[31])], inc="AC")
                    rc = P.op("scalar", lambda s, rsrow=rsrow, rsrow2=rsrow2: s.activation(
                        rsrow2[0:1, :], rsrow[0:1, :], AF.Exp, scale=-1.0), inc="AC")
                    bcm = P.op("tensor", lambda t, rsrow2=rsrow2: t.matmul(
                        PS[0:64, 3072:3584], ones_f[0:1, 0:64], rsrow2[0:1, :], start=True, stop=True),
                        waits=[("AC", rc)], inc="PE")
                    bce = P.op("scalar", lambda s, bc_sb=bc_sb: s.copy(bc_sb[:], PS[0:64, 3072:3584]),
                               waits=[("PE", bcm), ("DV", prev_ot)], inc="AC")
                    prev_ot = P.op("vector", lambda v, oc2=oc2, rb2=rb2, qb=qb, hx=hx, bc_sb=bc_sb: v.tensor_mul(
                        OT[rb2:rb2 + 64, oc2 * T + qb * TBL: oc2 * T + (qb + 1) * TBL],
                        PS[0:64, 2048 + hx * 512: 2048 + (hx + 1) * 512], bc_sb[:]),
                        waits=[("AC", bce)], inc="DV")
        if hg == 0:
            nwls = {}
            for nm, wsrc, wdst in (("q", wq, wq_sb), ("k", wk, wk_sb), ("v", wv, wv_sb)):
                P.op("gpsimd", lambda g, wsrc=wsrc, wdst=wdst: g.dma_start(
                    out=wdst[:].rearrange("p (c o) -> p c o", c=DCH),
                    in_=wsrc[:, TBL: 2 * TBL].rearrange("(c p) o -> p c o", p=128)),
                    waits=[("PE", pe_tb[NTBS - 1])], inc="W0", k=16)
            for nm in ("q", "k", "v"):
                nwls[nm] = ("W0", P.n("W0"))
            pxh = {}
            for tbp in range(2):
                bufp = xh_tb[tbp]
                smt = f"XT{tbp}"
                pxh[tbp] = (smt, P.op("sync", lambda s, bufp=bufp, tbp=tbp: s.dma_start(
                    out=bufp[:].rearrange("p (c t) -> p c t", c=DCH),
                    in_=xh_h[:, tbp * TBL:(tbp + 1) * TBL].rearrange("(c p) t -> p c t", p=128)),
                    waits=[("PE", pe_tb[NTBS - 2 + tbp])], inc=smt, k=16))
            pref = {"wls": nwls, "xh": pxh}
        P.barrier()

    if DBG:
        P.op("sync", lambda s: s.dma_start(out=dbg_ot, in_=OT[:]), inc="SO", k=16)

    Dz = Arena(MSC, P1, P1 + 64 * KB, TOP, "D")
    ares = Dz.al("ares", [128, DCH * T], F32)
    wo_sb = Dz.al("wo", [128, DCH * KB], BF16)
    xftl = [Dz.al(f"xf{i}", [128, TBL], F32) for i in range(2)]
    x1tl = [Dz.al(f"x1{i}", [128, TBL], BF16) for i in range(2)]
    sqD = [Dz.al(f"sq{i}", [128, TBL], F32) for i in range(2)]
    sq3 = [Dz.al(f"sq3{i}", [128, TBL], F32) for i in range(2)]
    ssD = MSC[0:1, 0:T]
    sqrD = MSC[0:1, 2 * KB: 2 * KB + T]
    lg_sb = Dz.al("lg", [128, 16 * NE], F32)
    mxt = Dz.al("mxt", [128, 16], F32)
    sct = Dz.al("sct", [128, 16], F32)
    mkt = Dz.al("mkt", [128, NE], F32)
    wkt = Dz.al("wkt", [128, NE], F32)
    sct2 = Dz.al("sct2", [16, 128], F32)
    sct3 = Dz.al("sct3", [16, 128], F32)

    wol = ("WO", P.op("gpsimd", lambda g: g.dma_start(
        out=wo_sb[:].rearrange("p (c o) -> p c o", c=DCH),
        in_=wo.rearrange("(c p) o -> p c o", p=128)), inc="WO", k=16))

    d1_dv = {}
    for och in range(DCH):
        for tb in range(4):
            i = och * 4 + tb
            pb = i % 2
            smg = f"XG{i % 2}"
            lx = (smg, P.op("sync", lambda s, och=och, tb=tb, sl=i % 2: s.dma_start(
                out=x1tl[sl][:], in_=xh_h[och * 128:(och + 1) * 128, tb * TBL:(tb + 1) * TBL]),
                waits=[("DV", d1_dv.get(i - 2))], inc=smg, k=16))
            mmlast = None
            for cin in range(DCH):
                wts = []
                if cin == 0:
                    wts.append(wol)
                    if i >= 2:
                        wts.append(("DV", d1_dv[i - 2]))
                mmlast = P.op("tensor", lambda t, och=och, tb=tb, cin=cin, pb=pb: t.matmul(
                    BK(pb), wo_sb[:, cin * KB + och * 128: cin * KB + och * 128 + 128],
                    OT[:, cin * T + tb * TBL: cin * T + (tb + 1) * TBL],
                    start=(cin == 0), stop=(cin == DCH - 1)),
                    waits=wts, inc="PE")
            d1_dv[i] = P.op("vector", lambda v, och=och, tb=tb, pb=pb, sl=i % 2: v.scalar_tensor_tensor(
                ares[:, och * T + tb * TBL: och * T + (tb + 1) * TBL],
                BK(pb), bo_sb[:, och:och + 1], x1tl[sl][:], ALU.add, ALU.add),
                waits=[("PE", mmlast), lx], inc="DV")

    def ln_stats(buf, cstride, colbase, width, sqr_ring, ssr, sqrr):
        ntb = width // TBL
        mm_l, cp_l = {}, {}
        for tb in range(ntb):
            for chn in range(DCH):
                i = tb * DCH + chn
                sqb = sqr_ring[i % 2]
                col = chn * cstride + colbase + tb * TBL
                sq_ac = P.op("scalar", lambda s, sqb=sqb, col=col, buf=buf: s.activation(
                    sqb[:], buf[:, col:col + TBL], AF.Square),
                    waits=[("PE", mm_l.get(i - 2))], inc="AC")
                pb = (tb % 2) * 2
                P.op("tensor", lambda t, buf=buf, col=col, pb=pb, chn=chn: t.matmul(
                    BK(pb)[0:1, :], col1_f[:], buf[:, col:col + TBL],
                    start=(chn == 0), stop=(chn == DCH - 1)),
                    waits=([("DV", cp_l.get(tb - 2))] if chn == 0 else []), inc="PE")
                mm_l[i] = P.op("tensor", lambda t, sqb=sqb, pb=pb, chn=chn: t.matmul(
                    BK(pb + 1)[0:1, :], col1_f[:], sqb[:],
                    start=(chn == 0), stop=(chn == DCH - 1)),
                    waits=[("AC", sq_ac)], inc="PE")
            pb = (tb % 2) * 2
            P.op("vector", lambda v, tb=tb, pb=pb: v.tensor_copy(
                ssr[0:1, tb * TBL:(tb + 1) * TBL], BK(pb)[0:1, :]),
                waits=[("PE", mm_l[tb * DCH + DCH - 1])], inc="DV")
            cp_l[tb] = P.op("vector", lambda v, tb=tb, pb=pb: v.tensor_copy(
                sqrr[0:1, tb * TBL:(tb + 1) * TBL], BK(pb + 1)[0:1, :]), inc="DV")

    P.barrier()
    ln_stats(ares, T, 0, T, sqD, ssD, sqrD)
    rm2 = row_math(ssD, sqrD, ar2, br2, br2, T)

    d4_dv, mm3_id, cp3_id = {}, {}, {}
    for tb in range(4):
        pb = 4 + (tb % 2) * 2
        P.op("tensor", lambda t, tb=tb, pb=pb: t.matmul(
            BK(pb), ones_f[:], ar2[0:1, tb * TBL:(tb + 1) * TBL], start=True, stop=True),
            waits=[("DV", rm2) if tb == 0 else ("DV", d4_dv.get((tb - 2, DCH - 1)))], inc="PE")
        bcb = P.op("tensor", lambda t, tb=tb, pb=pb: t.matmul(
            BK(pb + 1), ones_f[:], br2[0:1, tb * TBL:(tb + 1) * TBL], start=True, stop=True), inc="PE")
        for och in range(DCH):
            i = tb * DCH + och
            smf = f"XF{i % 2}"
            lx = (smf, P.op("sync", lambda s, och=och, tb=tb, sl=i % 2: s.dma_start(
                out=xftl[sl][:], in_=xT[och * 128:(och + 1) * 128, tb * TBL:(tb + 1) * TBL]),
                waits=[("DV", d4_dv.get((tb, och - 2)) or d4_dv.get((tb - 1, DCH - 2 + och)))], inc=smf, k=16))
            P.op("vector", lambda v, och=och, tb=tb, pb=pb, sl=i % 2: v.tensor_mul(
                sqD[sl][:], ares[:, och * T + tb * TBL: och * T + (tb + 1) * TBL], BK(pb)),
                waits=[("PE", bcb)], inc="DV")
            P.op("vector", lambda v, pb=pb, sl=i % 2: v.tensor_add(sqD[sl][:], sqD[sl][:], BK(pb + 1)), inc="DV")
            d4_dv[(tb, och)] = P.op("vector", lambda v, och=och, tb=tb, sl=i % 2: v.tensor_add(
                ares[:, och * T + tb * TBL: och * T + (tb + 1) * TBL], sqD[sl][:], xftl[sl][:]),
                waits=[lx], inc="DV")
            sq_ac = P.op("scalar", lambda s, och=och, tb=tb, sl=i % 2: s.activation(
                sq3[sl][:], ares[:, och * T + tb * TBL: och * T + (tb + 1) * TBL], AF.Square),
                waits=[("DV", d4_dv[(tb, och)]), ("PE", mm3_id.get(i - 2))], inc="AC")
            pbs = (tb % 2) * 2
            P.op("tensor", lambda t, och=och, tb=tb, pbs=pbs: t.matmul(
                BK(pbs)[0:1, :], col1_f[:], ares[:, och * T + tb * TBL: och * T + (tb + 1) * TBL],
                start=(och == 0), stop=(och == DCH - 1)),
                waits=[("DV", d4_dv[(tb, och)])] + ([("DV", cp3_id.get(tb - 2))] if och == 0 else []),
                inc="PE")
            mm3_id[i] = P.op("tensor", lambda t, sl=i % 2, pbs=pbs, och=och: t.matmul(
                BK(pbs + 1)[0:1, :], col1_f[:], sq3[sl][:], start=(och == 0), stop=(och == DCH - 1)),
                waits=[("AC", sq_ac)], inc="PE")
        pbs = (tb % 2) * 2
        P.op("vector", lambda v, tb=tb, pbs=pbs: v.tensor_copy(
            ssD[0:1, tb * TBL:(tb + 1) * TBL], BK(pbs)[0:1, :]),
            waits=[("PE", mm3_id[tb * DCH + DCH - 1])], inc="DV")
        cp3_id[tb] = P.op("vector", lambda v, tb=tb, pbs=pbs: v.tensor_copy(
            sqrD[0:1, tb * TBL:(tb + 1) * TBL], BK(pbs + 1)[0:1, :]), inc="DV")
    for och in range(DCH):
        P.op("sync", lambda s, och=och: s.dma_start(
            out=y1_h[och * 128:(och + 1) * 128, :], in_=ares[:, och * T:(och + 1) * T]),
            waits=[("DV", d4_dv[(3, och)])], inc="Y1S", k=16)

    rm3 = row_math(ssD, sqrD, ar2, br2, br2, T)

    d7_dv = {}
    for tb in range(4):
        pb = 4 + (tb % 2) * 2
        P.op("tensor", lambda t, tb=tb, pb=pb: t.matmul(
            BK(pb), ones_f[:], ar2[0:1, tb * TBL:(tb + 1) * TBL], start=True, stop=True),
            waits=[("DV", rm3) if tb == 0 else ("DV", d7_dv.get((tb - 2, DCH - 1)))], inc="PE")
        bcb = P.op("tensor", lambda t, tb=tb, pb=pb: t.matmul(
            BK(pb + 1), ones_f[:], br2[0:1, tb * TBL:(tb + 1) * TBL], start=True, stop=True), inc="PE")
        for och in range(DCH):
            i = tb * DCH + och
            P.op("vector", lambda v, och=och, tb=tb, pb=pb, sl=i % 2: v.tensor_mul(
                sqD[sl][:], ares[:, och * T + tb * TBL: och * T + (tb + 1) * TBL], BK(pb)),
                waits=[("PE", bcb)], inc="DV")
            d7_dv[(tb, och)] = P.op("vector", lambda v, och=och, tb=tb, pb=pb, sl=i % 2: v.tensor_add(
                x2b[:, och * T + tb * TBL: och * T + (tb + 1) * TBL], sqD[sl][:], BK(pb + 1)), inc="DV")
    x2_done = P.n("DV")

    P.barrier()
    rk_dv = {}
    for ttl in range(16):
        mmlast = None
        for chn in range(DCH):
            wts = []
            if chn == 0 and ttl >= 2:
                wts.append(("DV", rk_dv[ttl - 2]))
            mmlast = P.op("tensor", lambda t, ttl=ttl, chn=chn: t.matmul(
                PS[0:128, 2048 + (ttl % 2) * 512: 2048 + (ttl % 2) * 512 + NE],
                x2b[:, chn * T + ttl * 128: chn * T + (ttl + 1) * 128],
                gw_sb[:, chn * NE:(chn + 1) * NE],
                start=(chn == 0), stop=(chn == DCH - 1)),
                waits=wts, inc="PE")
        P.op("vector", lambda v, ttl=ttl: v.tensor_copy(
            lg_sb[:, ttl * NE:(ttl + 1) * NE], PS[0:128, 2048 + (ttl % 2) * 512: 2048 + (ttl % 2) * 512 + NE]),
            waits=[("PE", mmlast)], inc="DV")
        P.op("vector", lambda v, ttl=ttl: v.tensor_reduce(
            mxt[:, ttl:ttl + 1], lg_sb[:, ttl * NE:(ttl + 1) * NE], AX.X, ALU.max), inc="DV")
        P.op("vector", lambda v, ttl=ttl: v.tensor_single_scalar(
            mkt[:], lg_sb[:, ttl * NE:(ttl + 1) * NE], mxt[:, ttl:ttl + 1], ALU.is_equal), inc="DV")
        P.op("vector", lambda v: v.tensor_mul(wkt[:], mkt[:], iotaw[:]), inc="DV")
        P.op("vector", lambda v, ttl=ttl: v.tensor_reduce(sct[:, ttl:ttl + 1], wkt[:], AX.X, ALU.max), inc="DV")
        P.op("vector", lambda v, ttl=ttl: v.tensor_single_scalar(
            mkt[:], wkt[:], sct[:, ttl:ttl + 1], ALU.is_equal), inc="DV")
        mb_dv = P.op("vector", lambda v, ttl=ttl: v.tensor_copy(
            mbf[:, ttl * NE:(ttl + 1) * NE], mkt[:]), inc="DV")
        rlast = None
        for j in range(ttl + 1):
            lt = ones2 if j < ttl else LS
            rlast = P.op("tensor", lambda t, j=j, lt=lt, ttl=ttl: t.matmul(
                PS[0:128, 3584:3584 + NE], lt[:], mbf[:, j * NE:(j + 1) * NE],
                start=(j == 0), stop=(j == ttl)),
                waits=[("DV", mb_dv), ("DV", rk_dv.get(ttl - 1))] if j == 0 else [], inc="PE")
        P.op("vector", lambda v: v.tensor_mul(wkt[:], mkt[:], PS[0:128, 3584:3584 + NE]),
             waits=[("PE", rlast)], inc="DV")
        P.op("vector", lambda v: v.tensor_reduce(rkc[:, 0:1], wkt[:], AX.X, ALU.add), inc="DV")
        P.op("vector", lambda v: v.tensor_mul(wkt[:], mkt[:], ecaps[:]), inc="DV")
        P.op("vector", lambda v: v.tensor_reduce(rkc[:, 1:2], wkt[:], AX.X, ALU.add), inc="DV")
        P.op("vector", lambda v: v.tensor_add(rkc[:, 2:3], rkc[:, 0:1], rkc[:, 1:2]), inc="DV")
        P.op("vector", lambda v: v.tensor_single_scalar(rkc[:, 3:4], rkc[:, 0:1], float(CAP) - 0.5, ALU.is_ge), inc="DV")
        rk_dv[ttl] = P.op("vector", lambda v, ttl=ttl: v.scalar_tensor_tensor(
            sct2b[:, ttl:ttl + 1], rkc[:, 3:4], 8192.0, rkc[:, 2:3], ALU.mult, ALU.add), inc="DV")
    sg_ac = P.op("scalar", lambda s: s.activation(sct[:], mxt[:], AF.Sigmoid),
                 waits=[("DV", P.n("DV"))], inc="AC")
    trs = P.op("tensor", lambda t: t.transpose(PS[0:16, 3072:3200], sct[:], identF[:]),
               waits=[("AC", sg_ac), ("DV", w_ident)], inc="PE")
    sc_dv = P.op("vector", lambda v: v.tensor_copy(sct2[:], PS[0:16, 3072:3200]),
                 waits=[("PE", trs)], inc="DV")
    sc_xl = ("SC", P.op("gpsimd", lambda g: g.dma_start(out=scrow[0:1, :], in_=sct2[:]),
                 waits=[("DV", sc_dv)], inc="SC", k=16))
    trs2 = P.op("tensor", lambda t: t.transpose(PS[0:16, 3200:3328], sct2b[:], identF[:]),
                waits=[("DV", rk_dv[15])], inc="PE")
    sc3_dv = P.op("vector", lambda v: v.tensor_copy(sct3[:], PS[0:16, 3200:3328]),
                  waits=[("PE", trs2)], inc="DV")
    sr_xl = ("SR", P.op("sync", lambda s: s.dma_start(out=slotrow[0:1, :], in_=sct3[:]),
        waits=[("DV", sc3_dv)], inc="SR", k=16))
    if DBG:
        P.op("sync", lambda s: s.dma_start(out=dbg_x2, in_=x2b[:]), waits=[("DV", x2_done)], inc="SO", k=16)
        P.op("sync", lambda s: s.dma_start(out=dbg_slot, in_=slotrow[0:1, :]), waits=[sr_xl], inc="SO", k=16)
        P.op("sync", lambda s: s.dma_start(out=dbg_sc, in_=scrow[0:1, :]), waits=[sc_xl], inc="SO", k=16)
        P.op("sync", lambda s: s.dma_start(out=dbg_ares, in_=ares[:]), inc="SO", k=16)

    P.barrier()
    MB2 = MSC.bitcast(BF16)
    PSB = PS.bitcast(BF16)
    X_tok = MB2[:, 0:16 * KB]
    xe = MB2[:, 16 * KB:16 * KB + DCH * CAP]
    Pg = MB2[:, 20 * KB:20 * KB + 16 * CAP]
    dnb = MB2[:, 20 * KB:20 * KB + 4 * KB]
    PTt = MB2[:, 24 * KB:24 * KB + 4 * TBL]
    slot_bc = MSC[:, 14 * KB:14 * KB + T]
    ssE = MSC[0:1, 8 * KB:8 * KB + T]
    sqrE = MSC[0:1, 10 * KB:10 * KB + T]
    y1tl = [MSC[:, 12 * KB + i * TBL:12 * KB + (i + 1) * TBL] for i in range(2)]
    sqE = [MSC[:, 12 * KB + KB + i * TBL:12 * KB + KB + (i + 1) * TBL] for i in range(2)]
    faccv = MSC[:, 16 * KB:16 * KB + DCH * T]

    E = Arena(MSC, P1, P1 + 128 * KB, TOP, "E")
    wg_sb = E.al("wg", [128, DCH * TBL], BF16)
    wu_sb = E.al("wu", [128, DCH * TBL], BF16)
    wd_sb = E.al("wd", [128, 8 * KB], BF16)

    gs = E.al("gs", [128, 8 * TBL], BF16)
    um = [E.al(f"um{i}", [128, TBL], BF16) for i in range(2)]

    last_g = last_u = last_d = None
    acc_dv = {}
    bank_rd = {}

    npass = 0
    for fh in range(2):
        for fq in range(2):
            wgl = ("WG", P.op("gpsimd", lambda g, fh=fh, fq=fq: g.dma_start(
                out=wg_sb[:].rearrange("p (c f) -> p c f", c=DCH),
                in_=sg[:, fh * KB + fq * TBL: fh * KB + (fq + 1) * TBL].rearrange("(c p) f -> p c f", p=128)),
                waits=[("PE", last_g)], inc="WG", k=16))
            wul = ("WU", P.op("gpsimd", lambda g, fh=fh, fq=fq: g.dma_start(
                out=wu_sb[:].rearrange("p (c f) -> p c f", c=DCH),
                in_=su[:, fh * KB + fq * TBL: fh * KB + (fq + 1) * TBL].rearrange("(c p) f -> p c f", p=128)),
                waits=[("PE", last_u)], inc="WU", k=16))
            if fq == 0:
                wdl = ("WD", P.op("gpsimd", lambda g, fh=fh: g.dma_start(
                    out=wd_sb[:].rearrange("p (c dd) -> p c dd", c=8),
                    in_=sd[fh * KB:(fh + 1) * KB, :].rearrange("(c p) dd -> p c dd", p=128)),
                    waits=[("PE", last_d)], inc="WD", k=16))
            for tb in range(4):
                gsil, um_dv, hdone = {}, {}, {}
                for fc in range(4):
                    pb = fc % 2
                    mmlast = None
                    for chn in range(DCH):
                        wts = []
                        if chn == 0:
                            wts.append(wgl)
                            wts.append(bank_rd.get(pb))
                            if fc < 2:
                                wts.append(("PE", last_d))
                        mmlast = P.op("tensor", lambda t, chn=chn, fc=fc, fq=fq, tb=tb, pb=pb: t.matmul(
                            BK(pb), wg_sb[:, chn * TBL + fc * 128: chn * TBL + fc * 128 + 128],
                            x2b[:, chn * T + tb * TBL: chn * T + (tb + 1) * TBL],
                            start=(chn == 0), stop=(chn == DCH - 1)),
                            waits=wts, inc="PE")
                    last_g = mmlast
                    gsil[fc] = P.op("scalar", lambda st, fc=fc, pb=pb: st.activation(
                        gs[:, fc * TBL:(fc + 1) * TBL], BK(pb), AF.Silu),
                        waits=[("PE", mmlast)], inc="AC")
                    bank_rd[pb] = ("AC", gsil[fc])
                for fc in range(4):
                    pb = fc % 2
                    mmlast = None
                    for chn in range(DCH):
                        wts = []
                        if chn == 0:
                            wts.append(wul)
                            wts.append(bank_rd.get(pb))
                        mmlast = P.op("tensor", lambda t, chn=chn, fc=fc, tb=tb, pb=pb: t.matmul(
                            BK(pb), wu_sb[:, chn * TBL + fc * 128: chn * TBL + fc * 128 + 128],
                            x2b[:, chn * T + tb * TBL: chn * T + (tb + 1) * TBL],
                            start=(chn == 0), stop=(chn == DCH - 1)),
                            waits=wts, inc="PE")
                    last_u = mmlast
                    um_dv[fc] = P.op("vector", lambda v, fc=fc, pb=pb: v.tensor_copy(
                        um[fc % 2][:], BK(pb)), waits=[("PE", mmlast)], inc="DV")
                    bank_rd[pb] = ("DV", um_dv[fc])
                    hdone[fc] = P.op("vector", lambda v, fc=fc: v.tensor_mul(
                        gs[:, fc * TBL:(fc + 1) * TBL], gs[:, fc * TBL:(fc + 1) * TBL], um[fc % 2][:]),
                        waits=[("AC", gsil[fc])], inc="DV")
                for dch in range(DCH):
                    pb = 2 + dch % 2
                    mmlast = None
                    for fc in range(4):
                        wts = []
                        if fc == 0:
                            wts.append(wdl)
                            wts.append(("DV", hdone[3]))
                            wts.append(bank_rd.get(pb))
                        mmlast = P.op("tensor", lambda t, dch=dch, fc=fc, fq=fq, pb=pb: t.matmul(
                            BK(pb), wd_sb[:, (fq * 4 + fc) * KB + dch * 128: (fq * 4 + fc) * KB + dch * 128 + 128],
                            gs[:, fc * TBL:(fc + 1) * TBL],
                            start=(fc == 0), stop=(fc == 3)),
                            waits=wts, inc="PE")
                    last_d = mmlast
                    if npass == 0:
                        acc_dv[(tb, dch)] = P.op("vector", lambda v, dch=dch, tb=tb, pb=pb: v.tensor_copy(
                            faccv[:, dch * T + tb * TBL: dch * T + (tb + 1) * TBL], BK(pb)),
                            waits=[("PE", mmlast)], inc="DV")
                    else:
                        acc_dv[(tb, dch)] = P.op("vector", lambda v, dch=dch, tb=tb, pb=pb: v.tensor_add(
                            faccv[:, dch * T + tb * TBL: dch * T + (tb + 1) * TBL],
                            faccv[:, dch * T + tb * TBL: dch * T + (tb + 1) * TBL], BK(pb)),
                            waits=[("PE", mmlast)], inc="DV")
                    bank_rd[pb] = ("DV", acc_dv[(tb, dch)])
            npass += 1

    if XBAR & 1:
        P.barrier()
    if DBG:
        P.op("sync", lambda s: s.dma_start(out=dbg_shr, in_=faccv), waits=[("DV", P.n("DV"))], inc="SO", k=16)
        for _e2 in ENGINES:
            P.pending[_e2].append(("SO", P.n("SO")))
    shared_pe = P.n("PE")
    cv_dv = {}
    for tb in range(4):
        pb = 6 + tb % 2
        bcs = P.op("tensor", lambda t, tb=tb, pb=pb: t.matmul(
            BK(pb), ones_b[0:1, :], scrow[0:1, tb * TBL:(tb + 1) * TBL],
            start=True, stop=True),
            waits=([sc_xl, ("PE", shared_pe)] if tb == 0 else []) +
                  [bank_rd.get(pb)], inc="PE")
        for chn in range(DCH):
            cv_dv[(tb, chn)] = P.op("vector", lambda v, chn=chn, tb=tb, pb=pb: v.tensor_mul(
                x2b[:, chn * T + tb * TBL: chn * T + (tb + 1) * TBL],
                x2b[:, chn * T + tb * TBL: chn * T + (tb + 1) * TBL], BK(pb)),
                waits=[("PE", bcs)], inc="DV")
            bank_rd[pb] = ("DV", cv_dv[(tb, chn)])
    P.barrier()
    bank_rd = {}
    tr_dv = {}
    for tt in range(16):
        for dch in range(DCH):
            j = tt * DCH + dch
            sl = j % 2
            po = 6144 + 1024 * sl
            trp = P.op("tensor", lambda t, tt=tt, dch=dch, po=po: t.transpose(
                PSB[:, po:po + 128], x2b[:, dch * T + tt * 128: dch * T + (tt + 1) * 128], identB[:]),
                waits=[("DV", tr_dv.get(j - 2))], inc="PE")
            tr_dv[j] = P.op("vector", lambda v, tt=tt, dch=dch, po=po: v.tensor_copy(
                X_tok[:, tt * KB + dch * 128: tt * KB + dch * 128 + 128], PSB[:, po:po + 128]),
                waits=[("PE", trp)], inc="DV")
    if XBAR & 2:
        P.barrier()
    sb_ac = {}
    for tb in range(4):
        pb = 4 + tb % 2
        bcm = P.op("tensor", lambda t, tb=tb, pb=pb: t.matmul(
            BK(pb), ones_f[0:1, :], slotrow[0:1, tb * TBL:(tb + 1) * TBL], start=True, stop=True),
            waits=[sr_xl if tb == 0 else None, ("AC", sb_ac.get(tb - 2))], inc="PE")
        sb_ac[tb] = P.op("scalar", lambda st, tb=tb, pb=pb: st.copy(
            slot_bc[:, tb * TBL:(tb + 1) * TBL], BK(pb)),
            waits=[("PE", bcm)], inc="AC")
        bank_rd[pb] = ("AC", sb_ac[tb])
    P.barrier()

    last_scatter = None
    sc_fa = {}
    for e in range(NE):
        pg_dv = {}
        for tt in range(16):
            pg_dv[tt] = P.op("vector", lambda v, tt=tt, e=e: v.tensor_scalar(
                Pg[:, tt * CAP:(tt + 1) * CAP], iota512[:], sct2b[:, tt:tt + 1], -float(e * CAP),
                ALU.subtract, ALU.is_equal),
                waits=[("PE", last_scatter), ("SR", P.n("SR"))] if tt == 0 else [], inc="DV")
        pg_done = pg_dv[15]
        if XBAR & 4:
            P.barrier()
        ge_dv = {}
        for dh in range(2):
            mml = {}
            for dc in range(4):
                dch = dh * 4 + dc
                mmlast = None
                for tt in range(16):
                    wts = []
                    if dh == 0 and dc == 0:
                        wts.append(("DV", pg_dv[tt]))
                    elif tt == 0:
                        wts.append(("DV", pg_done))
                    if tt == 0:
                        wts.append(bank_rd.get(dc))
                    mmlast = P.op("tensor", lambda t, tt=tt, dch=dch, dc=dc: t.matmul(
                        BK(dc), X_tok[:, tt * KB + dch * 128: tt * KB + dch * 128 + 128],
                        Pg[:, tt * CAP:(tt + 1) * CAP],
                        start=(tt == 0), stop=(tt == 15)),
                        waits=wts, inc="PE")
                mml[dc] = mmlast
            for dc in range(4):
                dch = dh * 4 + dc
                ge_dv[dc] = P.op("vector", lambda v, dch=dch, dc=dc: v.tensor_copy(
                    xe[:, dch * CAP:(dch + 1) * CAP], BK(dc)),
                    waits=[("PE", mml[dc])], inc="DV")
                bank_rd[dc] = ("DV", ge_dv[dc])
        gather_end = mml[3]
        xe_done = P.n("DV")
        if XBAR & 8:
            P.barrier()
        if DBG and e == 0:
            P.op("sync", lambda s: s.dma_start(out=dbg_xe, in_=xe), waits=[("DV", xe_done)], inc="SO", k=16)
            P.op("sync", lambda s: s.dma_start(out=dbg_xtok, in_=X_tok), inc="SO", k=16)
            for _e2 in ENGINES:
                P.pending[_e2].append(("SO", P.n("SO")))
        pt0_dv = {}
        for stile in range(4):
            pt0_dv[stile] = P.op("vector", lambda v, stile=stile, e=e: v.tensor_scalar(
                PTt[:, stile * TBL:(stile + 1) * TBL],
                slot_bc[:, 0:TBL], iotacol[:],
                float(e * CAP + stile * 128), ALU.subtract, ALU.is_equal),
                waits=[("PE", gather_end)] if stile == 0 else [], inc="DV")
        dn_dv = {}
        for fh in range(2):
            gsil, um_dv, hdone = {}, {}, {}
            for fq in range(2):
                wgl = ("WG", P.op("gpsimd", lambda g, e=e, fh=fh, fq=fq: g.dma_start(
                    out=wg_sb[:].rearrange("p (c f) -> p c f", c=DCH),
                    in_=rg[e][:, fh * KB + fq * TBL: fh * KB + (fq + 1) * TBL].rearrange("(c p) f -> p c f", p=128)),
                    waits=[("PE", last_g)], inc="WG", k=16))
                wul = ("WU", P.op("gpsimd", lambda g, e=e, fh=fh, fq=fq: g.dma_start(
                    out=wu_sb[:].rearrange("p (c f) -> p c f", c=DCH),
                    in_=ru[e][:, fh * KB + fq * TBL: fh * KB + (fq + 1) * TBL].rearrange("(c p) f -> p c f", p=128)),
                    waits=[("PE", last_u)], inc="WU", k=16))
                if fq == 0:
                    wdl = ("WD", P.op("gpsimd", lambda g, e=e, fh=fh: g.dma_start(
                        out=wd_sb[:].rearrange("p (c dd) -> p c dd", c=8),
                        in_=rd[e][fh * KB:(fh + 1) * KB, :].rearrange("(c p) dd -> p c dd", p=128)),
                        waits=[("PE", last_d)], inc="WD", k=16))
                for fc in range(4):
                    fcl = fq * 4 + fc
                    pb = fc % 2
                    mmlast = None
                    for chn in range(DCH):
                        wts = []
                        if chn == 0:
                            wts.append(wgl)
                            wts.append(bank_rd.get(pb))
                            if fcl < 2:
                                wts.append(("PE", last_d))
                                wts.append(("DV", xe_done))
                        mmlast = P.op("tensor", lambda t, chn=chn, fcl=fcl, fc=fc, pb=pb: t.matmul(
                            BK(pb), wg_sb[:, chn * TBL + fc * 128: chn * TBL + fc * 128 + 128],
                            xe[:, chn * CAP:(chn + 1) * CAP],
                            start=(chn == 0), stop=(chn == DCH - 1)),
                            waits=wts, inc="PE")
                    last_g = mmlast
                    gsil[fcl] = P.op("scalar", lambda st, fcl=fcl, pb=pb: st.activation(
                        gs[:, fcl * TBL:(fcl + 1) * TBL], BK(pb), AF.Silu),
                        waits=[("PE", mmlast)], inc="AC")
                    bank_rd[pb] = ("AC", gsil[fcl])
                for fc in range(4):
                    fcl = fq * 4 + fc
                    pb = fc % 2
                    mmlast = None
                    for chn in range(DCH):
                        wts = []
                        if chn == 0:
                            wts.append(wul)
                            wts.append(bank_rd.get(pb))
                        mmlast = P.op("tensor", lambda t, chn=chn, fc=fc, pb=pb: t.matmul(
                            BK(pb), wu_sb[:, chn * TBL + fc * 128: chn * TBL + fc * 128 + 128],
                            xe[:, chn * CAP:(chn + 1) * CAP],
                            start=(chn == 0), stop=(chn == DCH - 1)),
                            waits=wts, inc="PE")
                    last_u = mmlast
                    um_dv[fcl] = P.op("vector", lambda v, fcl=fcl, pb=pb: v.tensor_copy(
                        um[fcl % 2][:], BK(pb)), waits=[("PE", mmlast)], inc="DV")
                    bank_rd[pb] = ("DV", um_dv[fcl])
                    hdone[fcl] = P.op("vector", lambda v, fcl=fcl: v.tensor_mul(
                        gs[:, fcl * TBL:(fcl + 1) * TBL], gs[:, fcl * TBL:(fcl + 1) * TBL], um[fcl % 2][:]),
                        waits=[("AC", gsil[fcl])], inc="DV")
            if XBAR & 16:
                P.barrier()
            for stile in range(4):
                for dblk in range(2):
                    gidx = fh * 8 + stile * 2 + dblk
                    pb = 2 + gidx % 2
                    mmlast = None
                    for fcl in range(8):
                        wts = []
                        if fcl == 0:
                            wts.append(wdl)
                            wts.append(("DV", hdone[7]))
                            wts.append(bank_rd.get(pb))
                        mmlast = P.op("tensor", lambda t, stile=stile, dblk=dblk, fcl=fcl, pb=pb: t.matmul(
                            BK(pb), gs[:, fcl * TBL + stile * 128: fcl * TBL + stile * 128 + 128],
                            wd_sb[:, fcl * KB + dblk * TBL: fcl * KB + (dblk + 1) * TBL],
                            start=(fcl == 0), stop=(fcl == 7)),
                            waits=wts, inc="PE")
                    last_d = mmlast
                    if fh == 0:
                        dn_dv[gidx] = P.op("vector", lambda v, stile=stile, dblk=dblk, pb=pb: v.tensor_copy(
                            dnb[:, stile * KB + dblk * TBL: stile * KB + (dblk + 1) * TBL], BK(pb)),
                            waits=[("PE", mmlast)], inc="DV")
                        bank_rd[pb] = ("DV", dn_dv[gidx])
                    else:
                        dn_dv[gidx] = P.op("vector", lambda v, stile=stile, dblk=dblk, pb=pb: v.tensor_add(
                            dnb[:, stile * KB + dblk * TBL: stile * KB + (dblk + 1) * TBL],
                            dnb[:, stile * KB + dblk * TBL: stile * KB + (dblk + 1) * TBL], BK(pb)),
                            waits=[("PE", mmlast)], inc="DV")
                        bank_rd[pb] = ("DV", dn_dv[gidx])
        dn_done = P.n("DV")
        if XBAR & 32:
            P.barrier()
        if DBG and e == 0:
            P.op("sync", lambda s: s.dma_start(out=dbg_dnb, in_=dnb), waits=[("DV", dn_done)], inc="SO", k=16)
            for _e2 in ENGINES:
                P.pending[_e2].append(("SO", P.n("SO")))
        if XBAR & 64:
            P.barrier()
        for tblk in range(4):
            if tblk == 0:
                pt_dv = pt0_dv
            else:
                pt_dv = {}
                for stile in range(4):
                    pt_dv[stile] = P.op("vector", lambda v, stile=stile, tblk=tblk, e=e: v.tensor_scalar(
                        PTt[:, stile * TBL:(stile + 1) * TBL],
                        slot_bc[:, tblk * TBL:(tblk + 1) * TBL], iotacol[:],
                        float(e * CAP + stile * 128), ALU.subtract, ALU.is_equal),
                        inc="DV")
            if DBG and e == 0 and tblk == 0:
                P.op("sync", lambda s: s.dma_start(out=dbg_pt, in_=PTt), waits=[("DV", pt_dv[3])], inc="SO", k=16)
                P.op("sync", lambda s: s.dma_start(out=dbg_sbc, in_=slot_bc), inc="SO", k=16)
                for _e2 in ENGINES:
                    P.pending[_e2].append(("SO", P.n("SO")))
            for dch in range(DCH):
                pb = 4 + dch % 2
                mmlast = None
                for stile in range(4):
                    wts = []
                    if stile == 0:
                        wts.append(("DV", pt_dv[3] if tblk else dn_done))
                        wts.append(bank_rd.get(pb))
                    mmlast = P.op("tensor", lambda t, stile=stile, dch=dch, tblk=tblk, pb=pb: t.matmul(
                        BK(pb), dnb[:, stile * KB + dch * 128: stile * KB + dch * 128 + 128],
                        PTt[:, stile * TBL:(stile + 1) * TBL],
                        start=(stile == 0), stop=(stile == 3)),
                        waits=wts, inc="PE")
                last_scatter = mmlast
                sc_fa[(tblk, dch)] = P.op("vector", lambda v, dch=dch, tblk=tblk, pb=pb: v.tensor_add(
                    faccv[:, dch * T + tblk * TBL: dch * T + (tblk + 1) * TBL],
                    faccv[:, dch * T + tblk * TBL: dch * T + (tblk + 1) * TBL], BK(pb)),
                    waits=[("PE", mmlast)], inc="DV")
                bank_rd[pb] = ("DV", sc_fa[(tblk, dch)])

    P.barrier()
    if DBG:
        P.op("sync", lambda s: s.dma_start(out=dbg_ffn, in_=faccv), inc="SO", k=16)
        for _e in ENGINES:
            P.pending[_e].append(("SO", P.n("SO")))
    ln_stats(faccv, T, 0, T, sqE, ssE, sqrE)
    rm4 = row_math(ssE, sqrE, ar2, br2, br2, T)
    fin_dv = {}
    for tb in range(4):
        pb = 4 + (tb % 2) * 2
        P.op("tensor", lambda t, tb=tb, pb=pb: t.matmul(
            BK(pb), ones_f[:], ar2[0:1, tb * TBL:(tb + 1) * TBL], start=True, stop=True),
            waits=[("DV", rm4) if tb == 0 else ("DV", fin_dv.get((tb - 2, DCH - 1)))], inc="PE")
        bcb = P.op("tensor", lambda t, tb=tb, pb=pb: t.matmul(
            BK(pb + 1), ones_f[:], br2[0:1, tb * TBL:(tb + 1) * TBL], start=True, stop=True), inc="PE")
        for och in range(DCH):
            i = tb * DCH + och
            sml = f"YL{i % 2}"
            ly = (sml, P.op("sync", lambda s, och=och, tb=tb, sl=i % 2: s.dma_start(
                out=y1tl[sl][:], in_=y1_h[och * 128:(och + 1) * 128, tb * TBL:(tb + 1) * TBL]),
                waits=[("DV", fin_dv.get((tb, och - 2)) or fin_dv.get((tb - 1, DCH - 2 + och)))], inc=sml, k=16))
            P.op("vector", lambda v, och=och, tb=tb, pb=pb, sl=i % 2: v.tensor_mul(
                sqE[sl][:], faccv[:, och * T + tb * TBL: och * T + (tb + 1) * TBL], BK(pb)),
                waits=[("PE", bcb)], inc="DV")
            P.op("vector", lambda v, pb=pb, sl=i % 2: v.tensor_add(sqE[sl][:], sqE[sl][:], BK(pb + 1)), inc="DV")
            fin_dv[(tb, och)] = P.op("vector", lambda v, och=och, tb=tb, sl=i % 2: v.tensor_add(
                faccv[:, och * T + tb * TBL: och * T + (tb + 1) * TBL], sqE[sl][:], y1tl[sl][:]),
                waits=[ly], inc="DV")
    for och in range(DCH):
        P.op("sync", lambda s, och=och: s.dma_start(
            out=yT[och * 128:(och + 1) * 128, :], in_=faccv[:, och * T:(och + 1) * T]),
            waits=[("DV", fin_dv[(3, och)])], inc="SO", k=16)

    P.op("sync", lambda s: s.nop(), waits=[("SO", P.n("SO"))])

    with nc.Block() as block:
        P.emit(block)
    return nc


_NC_CACHE = {}


def _get_nc():
    if "nc" not in _NC_CACHE:
        _NC_CACHE["nc"] = build_program()
    return _NC_CACHE["nc"]


def make_in_maps(L0_attn_x, L0_attn_W_qkv, L0_attn_W_out, L0_attn_b_out, L0_ffn_gate_w,
                 L0_ffn_shared_gate, L0_ffn_shared_up, L0_ffn_shared_down,
                 L0_ffn_r_gate, L0_ffn_r_up, L0_ffn_r_down):
    x = np.asarray(L0_attn_x)
    Wr = np.asarray(L0_attn_W_qkv).reshape(H, 3, HD, D)
    wq_h = np.ascontiguousarray(Wr[:, 0].reshape(H * HD, D).T)
    wk_h = np.ascontiguousarray(Wr[:, 1].reshape(H * HD, D).T)
    wv_h = np.ascontiguousarray(Wr[:, 2].reshape(H * HD, D).T)
    wo_h = np.ascontiguousarray(np.asarray(L0_attn_W_out).T)
    bo_h = np.ascontiguousarray(np.asarray(L0_attn_b_out).reshape(DCH, 128).T)
    gw_h = np.ascontiguousarray(np.asarray(L0_ffn_gate_w).T)
    sg_h = np.ascontiguousarray(np.asarray(L0_ffn_shared_gate).T)
    su_h = np.ascontiguousarray(np.asarray(L0_ffn_shared_up).T)
    sd_h = np.ascontiguousarray(np.asarray(L0_ffn_shared_down).T)
    rg_h = np.ascontiguousarray(np.transpose(np.asarray(L0_ffn_r_gate), (0, 2, 1)))
    ru_h = np.ascontiguousarray(np.transpose(np.asarray(L0_ffn_r_up), (0, 2, 1)))
    rd_h = np.ascontiguousarray(np.transpose(np.asarray(L0_ffn_r_down), (0, 2, 1)))
    in_maps = []
    for c in range(8):
        b, h = c // 2, c % 2
        xb = x[b]
        xcore = np.concatenate([xb[h * T:(h + 1) * T], xb[(1 - h) * T:(2 - h) * T]], axis=0)
        in_maps.append({
            "xT": np.ascontiguousarray(xcore.T),
            "wq": wq_h, "wk": wk_h, "wv": wv_h, "wo": wo_h, "bo": bo_h, "gw": gw_h,
            "sg": sg_h, "su": su_h, "sd": sd_h, "rg": rg_h, "ru": ru_h, "rd": rd_h,
        })
    return in_maps


def kernel(**inputs):
    in_maps = make_in_maps(**inputs)
    nc = _get_nc()
    res = run_bass_kernel_spmd(nc, in_maps, core_ids=list(range(8)))
    out = np.empty((4, S, D), np.float32)
    for c in range(8):
        b, h = c // 2, c % 2
        out[b, h * T:(h + 1) * T, :] = res.results[c]["yT"].T
    return out



# version 13 (final)
# speedup vs baseline: 1.3189x; 1.0050x over previous
import numpy as np
import concourse.bass as bass
import concourse.mybir as mybir
from concourse.bass_utils import run_bass_kernel_spmd

F32 = mybir.dt.float32
BF16 = mybir.dt.bfloat16
AF = mybir.ActivationFunctionType
ALU = mybir.AluOpType
AX = mybir.AxisListType

D, DCH = 1024, 8
CAP = 512
S = 4096
T = 2048
H, HD = 16, 64
FF = 2048
NE = 8
TBL = 512
EPS = 1e-5
ISC = 1.0 / 32.0
KB = 1024

ENGINES = ("tensor", "scalar", "vector", "gpsimd", "sync")
DBG = False
DBGKIND = "Internal"
XBAR = 3


class Planner:
    def __init__(self, nc, sems):
        self.nc = nc
        self.ops = {e: [] for e in ENGINES}
        self.sems = sems
        self.counts = {k: 0 for k in sems}
        self.pending = {e: [] for e in ENGINES}

    SELFSEM = {"vector": "DV", "scalar": "AC"}

    def op(self, eng, fn, waits=(), inc=None, k=1):
        w = self.pending[eng] + [x for x in waits if x is not None and x[1]]
        self.pending[eng] = []
        ss = self.SELFSEM.get(eng) or ("PL" if eng == "gpsimd" and inc == "PL" else None)
        if ss is not None and self.counts[ss] > 0:
            w.append((ss, self.counts[ss]))
        self.ops[eng].append((fn, w, inc, k))
        if inc is not None:
            self.counts[inc] += k
            return self.counts[inc]
        return None

    def n(self, sem):
        return self.counts[sem]

    def barrier(self):
        snap = [(s, c) for s, c in self.counts.items() if c > 0]
        for e in ENGINES:
            self.pending[e] = list(snap)

    def emit(self, block):
        for engname in ENGINES:
            def make_body(ops=self.ops[engname], sems=self.sems):
                def body(eng):
                    for fn, waits, inc, k in ops:
                        for s, v in waits:
                            eng.wait_ge(sems[s], v)
                        ins = fn(eng)
                        if inc is not None:
                            ins.then_inc(sems[inc], k)
                return body
            getattr(block, engname)(make_body())


class Arena:

    def __init__(self, master, origin, base, limit, tag):
        self.m, self.origin = master, origin
        self.off, self.limit, self.tag = base, limit, tag

    def al(self, name, shape, dtype):
        nbytes = int(np.prod(shape[1:])) * (2 if dtype == BF16 else 4)
        nbytes = (nbytes + 31) // 32 * 32
        o = self.off - self.origin
        view = self.m if dtype == F32 else self.m.bitcast(BF16)
        es = 4 if dtype == F32 else 2
        t = view[0:shape[0], o // es: o // es + int(np.prod(shape[1:]))]
        if len(shape) > 2:
            raise ValueError("arena views are 2-D")
        self.off += nbytes
        assert self.off <= self.limit, f"{self.tag}/{name}: off {self.off} > {self.limit}"
        return t


def build_program():
    nc = bass.Bass("TRN2", target_bir_lowering=False)
    xT = nc.dram_tensor("xT", [D, S], F32, kind="ExternalInput").ap()
    wq = nc.dram_tensor("wq", [D, D], F32, kind="ExternalInput").ap()
    wk = nc.dram_tensor("wk", [D, D], F32, kind="ExternalInput").ap()
    wv = nc.dram_tensor("wv", [D, D], F32, kind="ExternalInput").ap()
    wo = nc.dram_tensor("wo", [D, D], F32, kind="ExternalInput").ap()
    bo = nc.dram_tensor("bo", [128, DCH], F32, kind="ExternalInput").ap()
    gw = nc.dram_tensor("gw", [D, NE], F32, kind="ExternalInput").ap()
    sg = nc.dram_tensor("sg", [D, FF], F32, kind="ExternalInput").ap()
    su = nc.dram_tensor("su", [D, FF], F32, kind="ExternalInput").ap()
    sd = nc.dram_tensor("sd", [FF, D], F32, kind="ExternalInput").ap()
    rg = nc.dram_tensor("rg", [NE, D, FF], F32, kind="ExternalInput").ap()
    ru = nc.dram_tensor("ru", [NE, D, FF], F32, kind="ExternalInput").ap()
    rd = nc.dram_tensor("rd", [NE, FF, D], F32, kind="ExternalInput").ap()
    yT = nc.dram_tensor("yT", [D, T], F32, kind="ExternalOutput").ap()
    xh_h = nc.dram_tensor("xh_h", [D, S], BF16, kind="Internal").ap()
    y1_h = nc.dram_tensor("y1_h", [D, T], F32, kind="Internal").ap()
    if DBG:
        dbg_ot = nc.dram_tensor("dbg_ot", [128, DCH * T], BF16, kind=DBGKIND).ap()
        dbg_ares = nc.dram_tensor("dbg_ares", [128, DCH * T], F32, kind=DBGKIND).ap()
        dbg_x2 = nc.dram_tensor("dbg_x2", [128, DCH * T], BF16, kind=DBGKIND).ap()
        dbg_slot = nc.dram_tensor("dbg_slot", [1, T], F32, kind=DBGKIND).ap()
        dbg_sc = nc.dram_tensor("dbg_sc", [1, T], BF16, kind=DBGKIND).ap()
        dbg_ffn = nc.dram_tensor("dbg_ffn", [128, DCH * T], F32, kind=DBGKIND).ap()
        dbg_xe = nc.dram_tensor("dbg_xe", [128, DCH * CAP], BF16, kind=DBGKIND).ap()
        dbg_dnb = nc.dram_tensor("dbg_dnb", [128, 4 * KB], BF16, kind=DBGKIND).ap()
        dbg_xtok = nc.dram_tensor("dbg_xtok", [128, 16 * KB], BF16, kind=DBGKIND).ap()
        dbg_shr = nc.dram_tensor("dbg_shr", [128, DCH * T], F32, kind=DBGKIND).ap()
        dbg_pt = nc.dram_tensor("dbg_pt", [128, 4 * TBL], BF16, kind=DBGKIND).ap()
        dbg_sbc = nc.dram_tensor("dbg_sbc", [128, T], F32, kind=DBGKIND).ap()

    sems = {}
    for name in ("PE", "AC", "DV", "PL", "SO", "C0", "W0", "WO", "WG", "WU", "WD",
                 "MS", "SC", "Y1S", "C1", "XA0", "XA1", "XA2", "XA3", "XB0", "XB1",
                 "SH0", "SH1", "XT0", "XT1", "XF0", "XF1", "XG0", "XG1", "YL0", "YL1", "SR"):
        sems[name] = nc.semaphore(name).__enter__()
    P = Planner(nc, sems)

    GB = (nc.sbuf_base + 31) // 32 * 32
    TOP = nc.sbuf_top
    GSZ = 44 * KB
    GMST = nc.alloc_sbuf_tensor_at("GMST", [128, GSZ // 4], F32, offset=GB).ap()
    GA = Arena(GMST, GB, GB, GB + GSZ, "g")

    ones_b = GA.al("ones_b", [1, 128], BF16)
    ones_f = GA.al("ones_f", [1, 128], F32)
    col1_f = GA.al("col1_f", [128, 1], F32)
    identF = GA.al("identF", [128, 128], F32)
    gw_sb = GA.al("gw_sb", [128, DCH * NE], BF16)
    bo_sb = GA.al("bo_sb", [128, DCH], F32)
    iotaw = GA.al("iotaw", [128, NE], F32)
    epsr = GA.al("epsr", [1, 1], F32)
    ar2 = GA.al("ar2", [1, T], F32)
    br2 = GA.al("br2", [1, T], F32)
    scrow = GA.al("scrow", [1, T], BF16)
    identB = GA.al("identB", [128, 128], BF16)
    LS = GA.al("LS", [128, 128], BF16)
    ones2 = GA.al("ones2", [128, 128], BF16)
    ecaps = GA.al("ecaps", [128, NE], F32)
    iota512 = GA.al("iota512", [128, CAP], F32)
    iotacol = GA.al("iotacol", [128, 1], F32)
    mbf = GA.al("mbf", [128, 16 * NE], BF16)
    rkc = GA.al("rkc", [128, 8], F32)
    slotrow = GA.al("slotrow", [1, T], F32)
    sct2b = GA.al("sct2b", [128, 16], F32)
    P1 = (GA.off + 127) // 128 * 128
    NF32 = (TOP - P1) // 4 // 32 * 32
    MSC = nc.alloc_sbuf_tensor_at("MSC", [128, NF32], F32, offset=P1).ap()

    MB = MSC.bitcast(BF16)
    OT = MB[:, 0:DCH * T]
    x2b = MB[:, DCH * T: 2 * DCH * T]
    facc = MSC[:, 0:DCH * KB]

    PS = nc.alloc_psum_tensor("PS", [128, 4096], F32)
    BK = lambda b: PS[:, b * 512:(b + 1) * 512]

    P.op("gpsimd", lambda g: g.iota(iotaw[:], [[-1, NE]], base=NE, channel_multiplier=0,
                                    allow_small_or_imprecise_dtypes=True), inc="PL")
    P.op("gpsimd", lambda g: g.iota(identF[:], [[1, 128]], base=0, channel_multiplier=-1,
                                    allow_small_or_imprecise_dtypes=True), inc="PL")
    P.op("gpsimd", lambda g: g.iota(LS[:], [[1, 128]], base=0, channel_multiplier=-1,
                                    allow_small_or_imprecise_dtypes=True), inc="PL")
    P.op("gpsimd", lambda g: g.iota(ecaps[:], [[CAP, NE]], base=0, channel_multiplier=0,
                                    allow_small_or_imprecise_dtypes=True), inc="PL")
    P.op("gpsimd", lambda g: g.iota(iota512[:], [[1, CAP]], base=0, channel_multiplier=0,
                                    allow_small_or_imprecise_dtypes=True), inc="PL")
    P.op("gpsimd", lambda g: g.iota(iotacol[:], [[1, 1]], base=0, channel_multiplier=1,
                                    allow_small_or_imprecise_dtypes=True), inc="PL")
    ident_pl = P.n("PL")
    P.op("vector", lambda v: v.memset(ones_b[:], 1.0), inc="DV")
    P.op("vector", lambda v: v.memset(ones_f[:], 1.0), inc="DV")
    P.op("vector", lambda v: v.memset(col1_f[:], 1.0), inc="DV")
    P.op("vector", lambda v: v.memset(epsr[:], EPS), inc="DV")
    w_ident = P.op("vector", lambda v: v.tensor_single_scalar(identF[:], identF[:], 0.0, ALU.is_equal),
                   waits=[("PL", ident_pl)], inc="DV")
    P.op("vector", lambda v: v.tensor_copy(identB[:], identF[:]), inc="DV")
    P.op("vector", lambda v: v.tensor_single_scalar(LS[:], LS[:], 0.0, ALU.is_gt), inc="DV")
    P.op("vector", lambda v: v.memset(ones2[:], 1.0), inc="DV")
    init_dv = P.n("DV")
    P.barrier()
    P.op("sync", lambda s: s.dma_start(out=bo_sb[:], in_=bo), inc="C0", k=16)
    P.op("gpsimd", lambda g: g.dma_start(out=gw_sb[:].rearrange("p (c e) -> p c e", c=DCH),
                                         in_=gw.rearrange("(c p) e -> p c e", p=128)), inc="C1", k=16)

    def row_math(sumr, sqr, ar, br, tmp, width, c0=0, first_waits=None):
        sl = lambda r: r[0:1, c0:c0 + width]
        P.op("vector", lambda v: v.tensor_scalar_mul(sl(sumr), sl(sumr), 1.0 / D),
             waits=([("PE", P.n("PE"))] if first_waits is None else first_waits), inc="DV")
        P.op("vector", lambda v: v.tensor_scalar_mul(sl(sqr), sl(sqr), 1.0 / D), inc="DV")
        P.op("vector", lambda v: v.tensor_mul(sl(tmp), sl(sumr), sl(sumr)), inc="DV")
        r3 = P.op("vector", lambda v: v.tensor_sub(sl(sqr), sl(sqr), sl(tmp)), inc="DV")
        r4 = P.op("scalar", lambda s: s.activation(sl(tmp), sl(sqr), AF.Sqrt, bias=epsr[0:1, 0:1]),
                  waits=[("DV", r3)], inc="AC")
        P.op("vector", lambda v: v.reciprocal(sl(ar), sl(tmp)),
             waits=[("AC", r4)], inc="DV")
        return P.op("vector", lambda v: v.scalar_tensor_tensor(
            sl(br), sl(sumr), -1.0, sl(ar), ALU.mult, ALU.mult), inc="DV")

    A = Arena(MSC, P1, P1, TOP, "A")
    xfA = [A.al(f"xf{i}", [128, DCH * TBL], F32) for i in range(4)]
    sqA = [A.al(f"sq{i}", [128, TBL], F32) for i in range(2)]
    nscA = [A.al(f"nsc{i}", [128, TBL], F32) for i in range(2)]
    ssA = A.al("ssum", [1, S], F32)
    sqrA = A.al("ssq", [1, S], F32)
    a1r = A.al("a1r", [1, S], F32)
    b1r = A.al("b1r", [1, S], F32)
    xhA = [A.al(f"xh{i}", [128, DCH * TBL], BF16) for i in range(2)]

    NTBS = S // TBL
    mm2_id, cp_id, rm_id, nrm_id, st_id = {}, {}, {}, {}, {}
    for it in range(NTBS + 1):
        if it < NTBS:
            tb = it
            buf = xfA[tb % 4]
            sm = f"XA{tb % 4}"
            ltb = (sm, P.op("sync", lambda s, buf=buf, tb=tb: s.dma_start(
                out=buf[:].rearrange("p (c t) -> p c t", c=DCH),
                in_=xT[:, tb * TBL:(tb + 1) * TBL].rearrange("(c p) t -> p c t", p=128)),
                waits=[("DV", nrm_id.get(tb - 4))], inc=sm, k=16))
            pb = (tb % 2) * 2
            for chn in range(DCH):
                i = tb * DCH + chn
                sqb = sqA[i % 2]
                sq_ac = P.op("scalar", lambda s, buf=buf, chn=chn, sqb=sqb: s.activation(
                    sqb[:], buf[:, chn * TBL:(chn + 1) * TBL], AF.Square),
                    waits=[ltb, ("PE", mm2_id.get(i - 2))], inc="AC")
                P.op("tensor", lambda t, buf=buf, chn=chn, pb=pb: t.matmul(
                    BK(pb)[0:1, :], col1_f[:], buf[:, chn * TBL:(chn + 1) * TBL],
                    start=(chn == 0), stop=(chn == DCH - 1)),
                    waits=[ltb] + ([("DV", cp_id.get(tb - 2))] if chn == 0 else []),
                    inc="PE")
                mm2_id[i] = P.op("tensor", lambda t, sqb=sqb, pb=pb, chn=chn: t.matmul(
                    BK(pb + 1)[0:1, :], col1_f[:], sqb[:], start=(chn == 0), stop=(chn == DCH - 1)),
                    waits=[("AC", sq_ac)], inc="PE")
            P.op("vector", lambda v, tb=tb, pb=pb: v.tensor_copy(
                ssA[0:1, tb * TBL:(tb + 1) * TBL], BK(pb)[0:1, :]),
                waits=[("PE", mm2_id[tb * DCH + DCH - 1])], inc="DV")
            cp_id[tb] = P.op("vector", lambda v, tb=tb, pb=pb: v.tensor_copy(
                sqrA[0:1, tb * TBL:(tb + 1) * TBL], BK(pb + 1)[0:1, :]), inc="DV")
            rm_id[tb] = row_math(ssA, sqrA, a1r, b1r, b1r, TBL, c0=tb * TBL, first_waits=[])
        if it >= 1:
            tb = it - 1
            buf = xfA[tb % 4]
            pbb = 4 + (tb % 2) * 2
            P.op("tensor", lambda t, tb=tb, pbb=pbb: t.matmul(
                BK(pbb), ones_f[:], a1r[0:1, tb * TBL:(tb + 1) * TBL], start=True, stop=True),
                waits=[("DV", rm_id[tb]), ("DV", nrm_id.get(tb - 2))], inc="PE")
            bcb = P.op("tensor", lambda t, tb=tb, pbb=pbb: t.matmul(
                BK(pbb + 1), ones_f[:], b1r[0:1, tb * TBL:(tb + 1) * TBL], start=True, stop=True), inc="PE")
            xo = xhA[tb % 2]
            for chn in range(DCH):
                P.op("vector", lambda v, buf=buf, chn=chn, pbb=pbb, sl=chn % 2: v.tensor_mul(
                    nscA[sl][:], buf[:, chn * TBL:(chn + 1) * TBL], BK(pbb)),
                    waits=([("PE", bcb), st_id.get(tb - 2)] if chn == 0 else []), inc="DV")
                nrm_id[tb] = P.op("vector", lambda v, xo=xo, chn=chn, pbb=pbb, sl=chn % 2: v.tensor_add(
                    xo[:, chn * TBL:(chn + 1) * TBL], nscA[sl][:], BK(pbb + 1)), inc="DV")
            sms = f"SH{tb % 2}"
            st_id[tb] = (sms, P.op("gpsimd", lambda g, xo=xo, tb=tb: g.dma_start(
                out=xh_h[:, tb * TBL:(tb + 1) * TBL].rearrange("(c p) t -> p c t", p=128),
                in_=xo[:].rearrange("p (c t) -> p c t", c=DCH)),
                waits=[("DV", nrm_id[tb])], inc=sms, k=16))
    P.barrier()

    pref = None
    for hg in range(2):
        B = Arena(MSC, P1, P1 + 32 * KB, TOP, "B")
        Qc = B.al("Qc", [128, 4 * T], BF16)
        Kc = B.al("Kc", [128, 4 * S], BF16)
        Vt = B.al("Vt", [128, 32 * 520], BF16)
        wq_sb = B.al("wq", [128, DCH * TBL], BF16)
        wk_sb = B.al("wk", [128, DCH * TBL], BF16)
        wv_sb = B.al("wv", [128, DCH * TBL], BF16)
        xh_tb = [B.al(f"xh{i}", [128, DCH * TBL], BF16) for i in range(2)]
        At = [B.al(f"At{i}", [128, 1024], BF16) for i in range(3)]
        bc_sb = B.al("bc_sb", [64, TBL], F32)
        rsrow = B.al("rsrow", [1, TBL], F32)
        rsrow2 = B.al("rsrow2", [1, TBL], F32)

        if hg == 0:
            wls = {}
            for nm, wsrc, wdst in (("q", wq, wq_sb), ("k", wk, wk_sb), ("v", wv, wv_sb)):
                P.op("gpsimd", lambda g, wsrc=wsrc, wdst=wdst, hg=hg: g.dma_start(
                    out=wdst[:].rearrange("p (c o) -> p c o", c=DCH),
                    in_=wsrc[:, hg * TBL:(hg + 1) * TBL].rearrange("(c p) o -> p c o", p=128)),
                    inc="W0", k=16)
            for nm in ("q", "k", "v"):
                wls[nm] = ("W0", P.n("W0"))
        else:
            wls = pref["wls"]
        for tt in range(32):
            P.op("vector", lambda v, tt=tt: v.memset(
                Vt[:, tt * 520:(tt + 1) * 520].rearrange("p (h c) -> p h c", h=8)[:, :, 64:65], 1.0), inc="DV")
        vones_dv = P.n("DV")

        kq_ev, v_ev, pe_tb = [], [], {}
        for tb in range(NTBS):
            buf = xh_tb[tb % 2]
            smt = f"XT{tb % 2}"
            if hg == 1 and tb < 2:
                ltb = pref["xh"][tb]
            else:
                ltb = (smt, P.op("sync", lambda s, buf=buf, tb=tb: s.dma_start(
                    out=buf[:].rearrange("p (c t) -> p c t", c=DCH),
                    in_=xh_h[:, tb * TBL:(tb + 1) * TBL].rearrange("(c p) t -> p c t", p=128)),
                    waits=[("PE", pe_tb.get(tb - 2))], inc=smt, k=16))
            for kind in (("k", "q") if tb < 4 else ("k",)):
                wsb, dst, dstw = (wk_sb, Kc, S) if kind == "k" else (wq_sb, Qc, T)
                for och in range(4):
                    seq = len(kq_ev)
                    pb = seq % 2
                    mmlast = None
                    for chn in range(DCH):
                        w0 = ([ltb, wls[kind]] +
                              ([("DV", kq_ev[seq - 2])] if seq >= 2 else [])) if chn == 0 else []
                        mmlast = P.op("tensor", lambda t, wsb=wsb, chn=chn, och=och, buf=buf, pb=pb: t.matmul(
                            BK(pb), wsb[:, chn * TBL + och * 128: chn * TBL + och * 128 + 128],
                            buf[:, chn * TBL:(chn + 1) * TBL],
                            start=(chn == 0), stop=(chn == DCH - 1)),
                            waits=w0, inc="PE")
                    kq_ev.append(P.op("vector", lambda v, dst=dst, dstw=dstw, och=och, tb=tb, pb=pb: v.tensor_copy(
                        dst[:, och * dstw + tb * TBL: och * dstw + (tb + 1) * TBL], BK(pb)),
                        waits=[("PE", mmlast)], inc="DV"))
            for ttl in range(4):
                tt = tb * 4 + ttl
                sv = len(v_ev)
                pb = 2 + sv % 2
                mmlast = None
                for chn in range(DCH):
                    w0 = ([ltb, wls["v"]] +
                          ([("AC", v_ev[sv - 2])] if sv >= 2 else [])) if chn == 0 else []
                    mmlast = P.op("tensor", lambda t, chn=chn, buf=buf, ttl=ttl, pb=pb, wv_sb=wv_sb: t.matmul(
                        BK(pb), buf[:, chn * TBL + ttl * 128: chn * TBL + ttl * 128 + 128],
                        wv_sb[:, chn * TBL:(chn + 1) * TBL],
                        start=(chn == 0), stop=(chn == DCH - 1)),
                        waits=w0, inc="PE")
                v_ev.append(P.op("scalar", lambda s, tt=tt, pb=pb, Vt=Vt: s.copy(
                    Vt[:, tt * 520:(tt + 1) * 520].rearrange("p (h c) -> p h c", h=8)[:, :, 0:64],
                    BK(pb).rearrange("p (h c) -> p h c", h=8)),
                    waits=[("PE", mmlast)], inc="AC"))
            pe_tb[tb] = P.n("PE")
        b_dv, b_ac = P.n("DV"), P.n("AC")

        prev_exp = None
        prev_av = None
        prev_ot = None
        for hp in range(4):
            och = hp
            for qb in range(4):
                qk_id, ex_id, av_id = {}, {}, {}
                for kk in range(34):
                    if kk < 32:
                        kb, sl = kk, kk % 2
                        for hx in range(2):
                            rb = hx * 64
                            wts = []
                            if hx == 0:
                                if kb == 0 and hp == 0 and qb == 0:
                                    wts += [("DV", b_dv), ("AC", b_ac)]
                                if kb >= 2:
                                    wts.append(("AC", ex_id[kb - 2]))
                                elif prev_exp is not None:
                                    wts.append(("AC", prev_exp[kb]))
                            qk_id[(kb, hx)] = P.op("tensor", lambda t, och=och, rb=rb, kb=kb, hx=hx, qb=qb, sl=sl, Kc=Kc, Qc=Qc: t.matmul(
                                PS[:, sl * 1024 + hx * 512: sl * 1024 + (hx + 1) * 512],
                                Kc[rb:rb + 64, och * S + kb * 128: och * S + (kb + 1) * 128],
                                Qc[rb:rb + 64, och * T + qb * TBL: och * T + (qb + 1) * TBL],
                                start=True, stop=True, tile_position=(rb, 0)),
                                waits=wts, inc="PE")
                        ewts = [("PE", qk_id[(kb, 1)])]
                        if kb >= 3:
                            ewts.append(("PE", av_id[kb - 3]))
                        elif prev_av is not None:
                            ewts.append(("PE", prev_av))
                        ex_id[kb] = P.op("scalar", lambda s, sl=sl, a3=kb % 3, At=At: s.activation(
                            At[a3][:], PS[:, sl * 1024: (sl + 1) * 1024], AF.Exp, scale=ISC),
                            waits=ewts, inc="AC")
                    if kk >= 2:
                        kb = kk - 2
                        avlast = None
                        for hx in range(2):
                            hh = 2 * hp + hx
                            avw = [("AC", ex_id[kb])] if hx == 0 else []
                            if kb == 0 and hx == 0:
                                avw.append(("DV", vones_dv))
                                if prev_ot is not None:
                                    avw.append(("DV", prev_ot))
                            avlast = P.op("tensor", lambda t, hh=hh, kb=kb, hx=hx, a3=kb % 3, Vt=Vt, At=At: t.matmul(
                                PS[0:65, 2048 + hx * 512: 2048 + (hx + 1) * 512],
                                Vt[:, kb * 520 + hh * 65: kb * 520 + (hh + 1) * 65],
                                At[a3][:, hx * 512:(hx + 1) * 512],
                                start=(kb == 0), stop=(kb == 31)),
                                waits=avw, inc="PE")
                        av_id[kb] = avlast
                prev_exp = {0: ex_id[30], 1: ex_id[31]}
                prev_av = av_id[31]
                for hx in range(2):
                    h = hg * 8 + 2 * hp + hx
                    oc2, rb2 = h // 2, (h % 2) * 64
                    P.op("scalar", lambda s, hx=hx, rsrow=rsrow: s.activation(
                        rsrow[0:1, :], PS[64:65, 2048 + hx * 512: 2048 + (hx + 1) * 512], AF.Ln),
                        waits=[("PE", av_id[31])], inc="AC")
                    rc = P.op("scalar", lambda s, rsrow=rsrow, rsrow2=rsrow2: s.activation(
                        rsrow2[0:1, :], rsrow[0:1, :], AF.Exp, scale=-1.0), inc="AC")
                    bcm = P.op("tensor", lambda t, rsrow2=rsrow2: t.matmul(
                        PS[0:64, 3072:3584], ones_f[0:1, 0:64], rsrow2[0:1, :], start=True, stop=True),
                        waits=[("AC", rc)], inc="PE")
                    bce = P.op("scalar", lambda s, bc_sb=bc_sb: s.copy(bc_sb[:], PS[0:64, 3072:3584]),
                               waits=[("PE", bcm), ("DV", prev_ot)], inc="AC")
                    prev_ot = P.op("vector", lambda v, oc2=oc2, rb2=rb2, qb=qb, hx=hx, bc_sb=bc_sb: v.tensor_mul(
                        OT[rb2:rb2 + 64, oc2 * T + qb * TBL: oc2 * T + (qb + 1) * TBL],
                        PS[0:64, 2048 + hx * 512: 2048 + (hx + 1) * 512], bc_sb[:]),
                        waits=[("AC", bce)], inc="DV")
        if hg == 0:
            nwls = {}
            for nm, wsrc, wdst in (("q", wq, wq_sb), ("k", wk, wk_sb), ("v", wv, wv_sb)):
                P.op("gpsimd", lambda g, wsrc=wsrc, wdst=wdst: g.dma_start(
                    out=wdst[:].rearrange("p (c o) -> p c o", c=DCH),
                    in_=wsrc[:, TBL: 2 * TBL].rearrange("(c p) o -> p c o", p=128)),
                    waits=[("PE", pe_tb[NTBS - 1])], inc="W0", k=16)
            for nm in ("q", "k", "v"):
                nwls[nm] = ("W0", P.n("W0"))
            pxh = {}
            for tbp in range(2):
                bufp = xh_tb[tbp]
                smt = f"XT{tbp}"
                pxh[tbp] = (smt, P.op("sync", lambda s, bufp=bufp, tbp=tbp: s.dma_start(
                    out=bufp[:].rearrange("p (c t) -> p c t", c=DCH),
                    in_=xh_h[:, tbp * TBL:(tbp + 1) * TBL].rearrange("(c p) t -> p c t", p=128)),
                    waits=[("PE", pe_tb[NTBS - 2 + tbp])], inc=smt, k=16))
            pref = {"wls": nwls, "xh": pxh}
        P.barrier()

    if DBG:
        P.op("sync", lambda s: s.dma_start(out=dbg_ot, in_=OT[:]), inc="SO", k=16)

    Dz = Arena(MSC, P1, P1 + 64 * KB, TOP, "D")
    ares = Dz.al("ares", [128, DCH * T], F32)
    wo_sb = Dz.al("wo", [128, DCH * KB], BF16)
    xftl = [Dz.al(f"xf{i}", [128, TBL], F32) for i in range(2)]
    x1tl = [Dz.al(f"x1{i}", [128, TBL], BF16) for i in range(2)]
    sqD = [Dz.al(f"sq{i}", [128, TBL], F32) for i in range(2)]
    sq3 = [Dz.al(f"sq3{i}", [128, TBL], F32) for i in range(2)]
    ssD = MSC[0:1, 0:T]
    sqrD = MSC[0:1, 2 * KB: 2 * KB + T]
    lg_sb = Dz.al("lg", [128, 16 * NE], F32)
    mxt = Dz.al("mxt", [128, 16], F32)
    sct = Dz.al("sct", [128, 16], F32)
    mkt = Dz.al("mkt", [128, NE], F32)
    wkt = Dz.al("wkt", [128, NE], F32)
    sct2 = Dz.al("sct2", [16, 128], F32)
    sct3 = Dz.al("sct3", [16, 128], F32)

    wol = ("WO", P.op("gpsimd", lambda g: g.dma_start(
        out=wo_sb[:].rearrange("p (c o) -> p c o", c=DCH),
        in_=wo.rearrange("(c p) o -> p c o", p=128)), inc="WO", k=16))

    d1_dv = {}
    for och in range(DCH):
        for tb in range(4):
            i = och * 4 + tb
            pb = i % 2
            smg = f"XG{i % 2}"
            lx = (smg, P.op("sync", lambda s, och=och, tb=tb, sl=i % 2: s.dma_start(
                out=x1tl[sl][:], in_=xh_h[och * 128:(och + 1) * 128, tb * TBL:(tb + 1) * TBL]),
                waits=[("DV", d1_dv.get(i - 2))], inc=smg, k=16))
            mmlast = None
            for cin in range(DCH):
                wts = []
                if cin == 0:
                    wts.append(wol)
                    if i >= 2:
                        wts.append(("DV", d1_dv[i - 2]))
                mmlast = P.op("tensor", lambda t, och=och, tb=tb, cin=cin, pb=pb: t.matmul(
                    BK(pb), wo_sb[:, cin * KB + och * 128: cin * KB + och * 128 + 128],
                    OT[:, cin * T + tb * TBL: cin * T + (tb + 1) * TBL],
                    start=(cin == 0), stop=(cin == DCH - 1)),
                    waits=wts, inc="PE")
            d1_dv[i] = P.op("vector", lambda v, och=och, tb=tb, pb=pb, sl=i % 2: v.scalar_tensor_tensor(
                ares[:, och * T + tb * TBL: och * T + (tb + 1) * TBL],
                BK(pb), bo_sb[:, och:och + 1], x1tl[sl][:], ALU.add, ALU.add),
                waits=[("PE", mmlast), lx], inc="DV")

    def ln_stats(buf, cstride, colbase, width, sqr_ring, ssr, sqrr):
        ntb = width // TBL
        mm_l, cp_l = {}, {}
        for tb in range(ntb):
            for chn in range(DCH):
                i = tb * DCH + chn
                sqb = sqr_ring[i % 2]
                col = chn * cstride + colbase + tb * TBL
                sq_ac = P.op("scalar", lambda s, sqb=sqb, col=col, buf=buf: s.activation(
                    sqb[:], buf[:, col:col + TBL], AF.Square),
                    waits=[("PE", mm_l.get(i - 2))], inc="AC")
                pb = (tb % 2) * 2
                P.op("tensor", lambda t, buf=buf, col=col, pb=pb, chn=chn: t.matmul(
                    BK(pb)[0:1, :], col1_f[:], buf[:, col:col + TBL],
                    start=(chn == 0), stop=(chn == DCH - 1)),
                    waits=([("DV", cp_l.get(tb - 2))] if chn == 0 else []), inc="PE")
                mm_l[i] = P.op("tensor", lambda t, sqb=sqb, pb=pb, chn=chn: t.matmul(
                    BK(pb + 1)[0:1, :], col1_f[:], sqb[:],
                    start=(chn == 0), stop=(chn == DCH - 1)),
                    waits=[("AC", sq_ac)], inc="PE")
            pb = (tb % 2) * 2
            P.op("vector", lambda v, tb=tb, pb=pb: v.tensor_copy(
                ssr[0:1, tb * TBL:(tb + 1) * TBL], BK(pb)[0:1, :]),
                waits=[("PE", mm_l[tb * DCH + DCH - 1])], inc="DV")
            cp_l[tb] = P.op("vector", lambda v, tb=tb, pb=pb: v.tensor_copy(
                sqrr[0:1, tb * TBL:(tb + 1) * TBL], BK(pb + 1)[0:1, :]), inc="DV")

    P.barrier()
    ln_stats(ares, T, 0, T, sqD, ssD, sqrD)
    rm2 = row_math(ssD, sqrD, ar2, br2, br2, T)

    d4_dv, mm3_id, cp3_id = {}, {}, {}
    for tb in range(4):
        pb = 4 + (tb % 2) * 2
        P.op("tensor", lambda t, tb=tb, pb=pb: t.matmul(
            BK(pb), ones_f[:], ar2[0:1, tb * TBL:(tb + 1) * TBL], start=True, stop=True),
            waits=[("DV", rm2) if tb == 0 else ("DV", d4_dv.get((tb - 2, DCH - 1)))], inc="PE")
        bcb = P.op("tensor", lambda t, tb=tb, pb=pb: t.matmul(
            BK(pb + 1), ones_f[:], br2[0:1, tb * TBL:(tb + 1) * TBL], start=True, stop=True), inc="PE")
        for och in range(DCH):
            i = tb * DCH + och
            smf = f"XF{i % 2}"
            lx = (smf, P.op("sync", lambda s, och=och, tb=tb, sl=i % 2: s.dma_start(
                out=xftl[sl][:], in_=xT[och * 128:(och + 1) * 128, tb * TBL:(tb + 1) * TBL]),
                waits=[("DV", d4_dv.get((tb, och - 2)) or d4_dv.get((tb - 1, DCH - 2 + och)))], inc=smf, k=16))
            P.op("vector", lambda v, och=och, tb=tb, pb=pb, sl=i % 2: v.tensor_mul(
                sqD[sl][:], ares[:, och * T + tb * TBL: och * T + (tb + 1) * TBL], BK(pb)),
                waits=[("PE", bcb)], inc="DV")
            P.op("vector", lambda v, pb=pb, sl=i % 2: v.tensor_add(sqD[sl][:], sqD[sl][:], BK(pb + 1)), inc="DV")
            d4_dv[(tb, och)] = P.op("vector", lambda v, och=och, tb=tb, sl=i % 2: v.tensor_add(
                ares[:, och * T + tb * TBL: och * T + (tb + 1) * TBL], sqD[sl][:], xftl[sl][:]),
                waits=[lx], inc="DV")
            sq_ac = P.op("scalar", lambda s, och=och, tb=tb, sl=i % 2: s.activation(
                sq3[sl][:], ares[:, och * T + tb * TBL: och * T + (tb + 1) * TBL], AF.Square),
                waits=[("DV", d4_dv[(tb, och)]), ("PE", mm3_id.get(i - 2))], inc="AC")
            pbs = (tb % 2) * 2
            P.op("tensor", lambda t, och=och, tb=tb, pbs=pbs: t.matmul(
                BK(pbs)[0:1, :], col1_f[:], ares[:, och * T + tb * TBL: och * T + (tb + 1) * TBL],
                start=(och == 0), stop=(och == DCH - 1)),
                waits=[("DV", d4_dv[(tb, och)])] + ([("DV", cp3_id.get(tb - 2))] if och == 0 else []),
                inc="PE")
            mm3_id[i] = P.op("tensor", lambda t, sl=i % 2, pbs=pbs, och=och: t.matmul(
                BK(pbs + 1)[0:1, :], col1_f[:], sq3[sl][:], start=(och == 0), stop=(och == DCH - 1)),
                waits=[("AC", sq_ac)], inc="PE")
        pbs = (tb % 2) * 2
        P.op("vector", lambda v, tb=tb, pbs=pbs: v.tensor_copy(
            ssD[0:1, tb * TBL:(tb + 1) * TBL], BK(pbs)[0:1, :]),
            waits=[("PE", mm3_id[tb * DCH + DCH - 1])], inc="DV")
        cp3_id[tb] = P.op("vector", lambda v, tb=tb, pbs=pbs: v.tensor_copy(
            sqrD[0:1, tb * TBL:(tb + 1) * TBL], BK(pbs + 1)[0:1, :]), inc="DV")
    for och in range(DCH):
        P.op("sync", lambda s, och=och: s.dma_start(
            out=y1_h[och * 128:(och + 1) * 128, :], in_=ares[:, och * T:(och + 1) * T]),
            waits=[("DV", d4_dv[(3, och)])], inc="Y1S", k=16)

    rm3 = row_math(ssD, sqrD, ar2, br2, br2, T)

    d7_dv = {}
    for tb in range(4):
        pb = 4 + (tb % 2) * 2
        P.op("tensor", lambda t, tb=tb, pb=pb: t.matmul(
            BK(pb), ones_f[:], ar2[0:1, tb * TBL:(tb + 1) * TBL], start=True, stop=True),
            waits=[("DV", rm3) if tb == 0 else ("DV", d7_dv.get((tb - 2, DCH - 1)))], inc="PE")
        bcb = P.op("tensor", lambda t, tb=tb, pb=pb: t.matmul(
            BK(pb + 1), ones_f[:], br2[0:1, tb * TBL:(tb + 1) * TBL], start=True, stop=True), inc="PE")
        for och in range(DCH):
            i = tb * DCH + och
            P.op("vector", lambda v, och=och, tb=tb, pb=pb, sl=i % 2: v.tensor_mul(
                sqD[sl][:], ares[:, och * T + tb * TBL: och * T + (tb + 1) * TBL], BK(pb)),
                waits=[("PE", bcb)], inc="DV")
            d7_dv[(tb, och)] = P.op("vector", lambda v, och=och, tb=tb, pb=pb, sl=i % 2: v.tensor_add(
                x2b[:, och * T + tb * TBL: och * T + (tb + 1) * TBL], sqD[sl][:], BK(pb + 1)), inc="DV")
    x2_done = P.n("DV")

    P.barrier()
    rk_dv = {}
    for ttl in range(16):
        mmlast = None
        for chn in range(DCH):
            wts = []
            if chn == 0 and ttl >= 2:
                wts.append(("DV", rk_dv[ttl - 2]))
            mmlast = P.op("tensor", lambda t, ttl=ttl, chn=chn: t.matmul(
                PS[0:128, 2048 + (ttl % 2) * 512: 2048 + (ttl % 2) * 512 + NE],
                x2b[:, chn * T + ttl * 128: chn * T + (ttl + 1) * 128],
                gw_sb[:, chn * NE:(chn + 1) * NE],
                start=(chn == 0), stop=(chn == DCH - 1)),
                waits=wts, inc="PE")
        P.op("vector", lambda v, ttl=ttl: v.tensor_copy(
            lg_sb[:, ttl * NE:(ttl + 1) * NE], PS[0:128, 2048 + (ttl % 2) * 512: 2048 + (ttl % 2) * 512 + NE]),
            waits=[("PE", mmlast)], inc="DV")
        P.op("vector", lambda v, ttl=ttl: v.tensor_reduce(
            mxt[:, ttl:ttl + 1], lg_sb[:, ttl * NE:(ttl + 1) * NE], AX.X, ALU.max), inc="DV")
        P.op("vector", lambda v, ttl=ttl: v.tensor_single_scalar(
            mkt[:], lg_sb[:, ttl * NE:(ttl + 1) * NE], mxt[:, ttl:ttl + 1], ALU.is_equal), inc="DV")
        P.op("vector", lambda v: v.tensor_mul(wkt[:], mkt[:], iotaw[:]), inc="DV")
        P.op("vector", lambda v, ttl=ttl: v.tensor_reduce(sct[:, ttl:ttl + 1], wkt[:], AX.X, ALU.max), inc="DV")
        P.op("vector", lambda v, ttl=ttl: v.tensor_single_scalar(
            mkt[:], wkt[:], sct[:, ttl:ttl + 1], ALU.is_equal), inc="DV")
        mb_dv = P.op("vector", lambda v, ttl=ttl: v.tensor_copy(
            mbf[:, ttl * NE:(ttl + 1) * NE], mkt[:]), inc="DV")
        rlast = None
        for j in range(ttl + 1):
            lt = ones2 if j < ttl else LS
            rlast = P.op("tensor", lambda t, j=j, lt=lt, ttl=ttl: t.matmul(
                PS[0:128, 3584:3584 + NE], lt[:], mbf[:, j * NE:(j + 1) * NE],
                start=(j == 0), stop=(j == ttl)),
                waits=[("DV", mb_dv), ("DV", rk_dv.get(ttl - 1))] if j == 0 else [], inc="PE")
        P.op("vector", lambda v: v.tensor_mul(wkt[:], mkt[:], PS[0:128, 3584:3584 + NE]),
             waits=[("PE", rlast)], inc="DV")
        P.op("vector", lambda v: v.tensor_reduce(rkc[:, 0:1], wkt[:], AX.X, ALU.add), inc="DV")
        P.op("vector", lambda v: v.tensor_mul(wkt[:], mkt[:], ecaps[:]), inc="DV")
        P.op("vector", lambda v: v.tensor_reduce(rkc[:, 1:2], wkt[:], AX.X, ALU.add), inc="DV")
        P.op("vector", lambda v: v.tensor_add(rkc[:, 2:3], rkc[:, 0:1], rkc[:, 1:2]), inc="DV")
        P.op("vector", lambda v: v.tensor_single_scalar(rkc[:, 3:4], rkc[:, 0:1], float(CAP) - 0.5, ALU.is_ge), inc="DV")
        rk_dv[ttl] = P.op("vector", lambda v, ttl=ttl: v.scalar_tensor_tensor(
            sct2b[:, ttl:ttl + 1], rkc[:, 3:4], 8192.0, rkc[:, 2:3], ALU.mult, ALU.add), inc="DV")
    sg_ac = P.op("scalar", lambda s: s.activation(sct[:], mxt[:], AF.Sigmoid),
                 waits=[("DV", P.n("DV"))], inc="AC")
    trs = P.op("tensor", lambda t: t.transpose(PS[0:16, 3072:3200], sct[:], identF[:]),
               waits=[("AC", sg_ac), ("DV", w_ident)], inc="PE")
    sc_dv = P.op("vector", lambda v: v.tensor_copy(sct2[:], PS[0:16, 3072:3200]),
                 waits=[("PE", trs)], inc="DV")
    sc_xl = ("SC", P.op("gpsimd", lambda g: g.dma_start(out=scrow[0:1, :], in_=sct2[:]),
                 waits=[("DV", sc_dv)], inc="SC", k=16))
    trs2 = P.op("tensor", lambda t: t.transpose(PS[0:16, 3200:3328], sct2b[:], identF[:]),
                waits=[("DV", rk_dv[15])], inc="PE")
    sc3_dv = P.op("vector", lambda v: v.tensor_copy(sct3[:], PS[0:16, 3200:3328]),
                  waits=[("PE", trs2)], inc="DV")
    sr_xl = ("SR", P.op("sync", lambda s: s.dma_start(out=slotrow[0:1, :], in_=sct3[:]),
        waits=[("DV", sc3_dv)], inc="SR", k=16))
    if DBG:
        P.op("sync", lambda s: s.dma_start(out=dbg_x2, in_=x2b[:]), waits=[("DV", x2_done)], inc="SO", k=16)
        P.op("sync", lambda s: s.dma_start(out=dbg_slot, in_=slotrow[0:1, :]), waits=[sr_xl], inc="SO", k=16)
        P.op("sync", lambda s: s.dma_start(out=dbg_sc, in_=scrow[0:1, :]), waits=[sc_xl], inc="SO", k=16)
        P.op("sync", lambda s: s.dma_start(out=dbg_ares, in_=ares[:]), inc="SO", k=16)

    P.barrier()
    MB2 = MSC.bitcast(BF16)
    PSB = PS.bitcast(BF16)
    X_tok = MB2[:, 0:16 * KB]
    xe = MB2[:, 16 * KB:16 * KB + DCH * CAP]
    Pg = MB2[:, 20 * KB:20 * KB + 16 * CAP]
    dnb = MB2[:, 20 * KB:20 * KB + 4 * KB]
    PTt = MB2[:, 24 * KB:24 * KB + 4 * TBL]
    slot_bc = MSC[:, 14 * KB:14 * KB + T]
    ssE = MSC[0:1, 8 * KB:8 * KB + T]
    sqrE = MSC[0:1, 10 * KB:10 * KB + T]
    y1tl = [MSC[:, 12 * KB + i * TBL:12 * KB + (i + 1) * TBL] for i in range(2)]
    sqE = [MSC[:, 12 * KB + KB + i * TBL:12 * KB + KB + (i + 1) * TBL] for i in range(2)]
    faccv = MSC[:, 16 * KB:16 * KB + DCH * T]

    E = Arena(MSC, P1, P1 + 128 * KB, TOP, "E")
    wg_sb = E.al("wg", [128, DCH * TBL], BF16)
    wu_sb = E.al("wu", [128, DCH * TBL], BF16)
    wd_sb = E.al("wd", [128, 8 * KB], BF16)

    gs = E.al("gs", [128, 8 * TBL], BF16)
    um = [E.al(f"um{i}", [128, TBL], BF16) for i in range(2)]

    last_g = last_u = last_d = None
    acc_dv = {}
    bank_rd = {}

    npass = 0
    for fh in range(2):
        for fq in range(2):
            wgl = ("WG", P.op("gpsimd", lambda g, fh=fh, fq=fq: g.dma_start(
                out=wg_sb[:].rearrange("p (c f) -> p c f", c=DCH),
                in_=sg[:, fh * KB + fq * TBL: fh * KB + (fq + 1) * TBL].rearrange("(c p) f -> p c f", p=128)),
                waits=[("PE", last_g)], inc="WG", k=16))
            wul = ("WU", P.op("gpsimd", lambda g, fh=fh, fq=fq: g.dma_start(
                out=wu_sb[:].rearrange("p (c f) -> p c f", c=DCH),
                in_=su[:, fh * KB + fq * TBL: fh * KB + (fq + 1) * TBL].rearrange("(c p) f -> p c f", p=128)),
                waits=[("PE", last_u)], inc="WU", k=16))
            if fq == 0:
                wdl = ("WD", P.op("gpsimd", lambda g, fh=fh: g.dma_start(
                    out=wd_sb[:].rearrange("p (c dd) -> p c dd", c=8),
                    in_=sd[fh * KB:(fh + 1) * KB, :].rearrange("(c p) dd -> p c dd", p=128)),
                    waits=[("PE", last_d)], inc="WD", k=16))
            for tb in range(4):
                gsil, um_dv, hdone = {}, {}, {}
                for fc in range(4):
                    pb = fc % 2
                    mmlast = None
                    for chn in range(DCH):
                        wts = []
                        if chn == 0:
                            wts.append(wgl)
                            wts.append(bank_rd.get(pb))
                            if fc < 2:
                                wts.append(("PE", last_d))
                        mmlast = P.op("tensor", lambda t, chn=chn, fc=fc, fq=fq, tb=tb, pb=pb: t.matmul(
                            BK(pb), wg_sb[:, chn * TBL + fc * 128: chn * TBL + fc * 128 + 128],
                            x2b[:, chn * T + tb * TBL: chn * T + (tb + 1) * TBL],
                            start=(chn == 0), stop=(chn == DCH - 1)),
                            waits=wts, inc="PE")
                    last_g = mmlast
                    gsil[fc] = P.op("scalar", lambda st, fc=fc, pb=pb: st.activation(
                        gs[:, fc * TBL:(fc + 1) * TBL], BK(pb), AF.Silu),
                        waits=[("PE", mmlast)], inc="AC")
                    bank_rd[pb] = ("AC", gsil[fc])
                for fc in range(4):
                    pb = fc % 2
                    mmlast = None
                    for chn in range(DCH):
                        wts = []
                        if chn == 0:
                            wts.append(wul)
                            wts.append(bank_rd.get(pb))
                        mmlast = P.op("tensor", lambda t, chn=chn, fc=fc, tb=tb, pb=pb: t.matmul(
                            BK(pb), wu_sb[:, chn * TBL + fc * 128: chn * TBL + fc * 128 + 128],
                            x2b[:, chn * T + tb * TBL: chn * T + (tb + 1) * TBL],
                            start=(chn == 0), stop=(chn == DCH - 1)),
                            waits=wts, inc="PE")
                    last_u = mmlast
                    um_dv[fc] = P.op("vector", lambda v, fc=fc, pb=pb: v.tensor_copy(
                        um[fc % 2][:], BK(pb)), waits=[("PE", mmlast)], inc="DV")
                    bank_rd[pb] = ("DV", um_dv[fc])
                    hdone[fc] = P.op("vector", lambda v, fc=fc: v.tensor_mul(
                        gs[:, fc * TBL:(fc + 1) * TBL], gs[:, fc * TBL:(fc + 1) * TBL], um[fc % 2][:]),
                        waits=[("AC", gsil[fc])], inc="DV")
                for dch in range(DCH):
                    pb = 2 + dch % 2
                    mmlast = None
                    for fc in range(4):
                        wts = []
                        if fc == 0:
                            wts.append(wdl)
                            wts.append(("DV", hdone[3]))
                            wts.append(bank_rd.get(pb))
                        mmlast = P.op("tensor", lambda t, dch=dch, fc=fc, fq=fq, pb=pb: t.matmul(
                            BK(pb), wd_sb[:, (fq * 4 + fc) * KB + dch * 128: (fq * 4 + fc) * KB + dch * 128 + 128],
                            gs[:, fc * TBL:(fc + 1) * TBL],
                            start=(fc == 0), stop=(fc == 3)),
                            waits=wts, inc="PE")
                    last_d = mmlast
                    if npass == 0:
                        acc_dv[(tb, dch)] = P.op("vector", lambda v, dch=dch, tb=tb, pb=pb: v.tensor_copy(
                            faccv[:, dch * T + tb * TBL: dch * T + (tb + 1) * TBL], BK(pb)),
                            waits=[("PE", mmlast)], inc="DV")
                    else:
                        acc_dv[(tb, dch)] = P.op("vector", lambda v, dch=dch, tb=tb, pb=pb: v.tensor_add(
                            faccv[:, dch * T + tb * TBL: dch * T + (tb + 1) * TBL],
                            faccv[:, dch * T + tb * TBL: dch * T + (tb + 1) * TBL], BK(pb)),
                            waits=[("PE", mmlast)], inc="DV")
                    bank_rd[pb] = ("DV", acc_dv[(tb, dch)])
            npass += 1

    if XBAR & 1:
        P.barrier()
    if DBG:
        P.op("sync", lambda s: s.dma_start(out=dbg_shr, in_=faccv), waits=[("DV", P.n("DV"))], inc="SO", k=16)
        for _e2 in ENGINES:
            P.pending[_e2].append(("SO", P.n("SO")))
    shared_pe = P.n("PE")
    cv_dv = {}
    for tb in range(4):
        pb = 6 + tb % 2
        bcs = P.op("tensor", lambda t, tb=tb, pb=pb: t.matmul(
            BK(pb), ones_b[0:1, :], scrow[0:1, tb * TBL:(tb + 1) * TBL],
            start=True, stop=True),
            waits=([sc_xl, ("PE", shared_pe)] if tb == 0 else []) +
                  [bank_rd.get(pb)], inc="PE")
        for chn in range(DCH):
            cv_dv[(tb, chn)] = P.op("vector", lambda v, chn=chn, tb=tb, pb=pb: v.tensor_mul(
                x2b[:, chn * T + tb * TBL: chn * T + (tb + 1) * TBL],
                x2b[:, chn * T + tb * TBL: chn * T + (tb + 1) * TBL], BK(pb)),
                waits=[("PE", bcs)], inc="DV")
            bank_rd[pb] = ("DV", cv_dv[(tb, chn)])
    P.barrier()
    bank_rd = {}
    tr_dv = {}
    for tt in range(16):
        for dch in range(DCH):
            j = tt * DCH + dch
            sl = j % 2
            po = 6144 + 1024 * sl
            trp = P.op("tensor", lambda t, tt=tt, dch=dch, po=po: t.transpose(
                PSB[:, po:po + 128], x2b[:, dch * T + tt * 128: dch * T + (tt + 1) * 128], identB[:]),
                waits=[("DV", tr_dv.get(j - 2))], inc="PE")
            tr_dv[j] = P.op("vector", lambda v, tt=tt, dch=dch, po=po: v.tensor_copy(
                X_tok[:, tt * KB + dch * 128: tt * KB + dch * 128 + 128], PSB[:, po:po + 128]),
                waits=[("PE", trp)], inc="DV")
    if XBAR & 2:
        P.barrier()
    sb_ac = {}
    for tb in range(4):
        pb = 4 + tb % 2
        bcm = P.op("tensor", lambda t, tb=tb, pb=pb: t.matmul(
            BK(pb), ones_f[0:1, :], slotrow[0:1, tb * TBL:(tb + 1) * TBL], start=True, stop=True),
            waits=[sr_xl if tb == 0 else None, ("AC", sb_ac.get(tb - 2))], inc="PE")
        sb_ac[tb] = P.op("scalar", lambda st, tb=tb, pb=pb: st.copy(
            slot_bc[:, tb * TBL:(tb + 1) * TBL], BK(pb)),
            waits=[("PE", bcm)], inc="AC")
        bank_rd[pb] = ("AC", sb_ac[tb])
    P.barrier()

    last_scatter = None
    sc_fa = {}
    for e in range(NE):
        pg_dv = {}
        for tt in range(16):
            pg_dv[tt] = P.op("vector", lambda v, tt=tt, e=e: v.tensor_scalar(
                Pg[:, tt * CAP:(tt + 1) * CAP], iota512[:], sct2b[:, tt:tt + 1], -float(e * CAP),
                ALU.subtract, ALU.is_equal),
                waits=[("PE", last_scatter), ("SR", P.n("SR"))] if tt == 0 else [], inc="DV")
        pg_done = pg_dv[15]
        if XBAR & 4:
            P.barrier()
        ge_dv = {}
        for dh in range(2):
            mml = {}
            for dc in range(4):
                dch = dh * 4 + dc
                mmlast = None
                for tt in range(16):
                    wts = []
                    if dh == 0 and dc == 0:
                        wts.append(("DV", pg_dv[tt]))
                    elif tt == 0:
                        wts.append(("DV", pg_done))
                    if tt == 0:
                        wts.append(bank_rd.get(dc))
                    mmlast = P.op("tensor", lambda t, tt=tt, dch=dch, dc=dc: t.matmul(
                        BK(dc), X_tok[:, tt * KB + dch * 128: tt * KB + dch * 128 + 128],
                        Pg[:, tt * CAP:(tt + 1) * CAP],
                        start=(tt == 0), stop=(tt == 15)),
                        waits=wts, inc="PE")
                mml[dc] = mmlast
            for dc in range(4):
                dch = dh * 4 + dc
                ge_dv[dc] = P.op("vector", lambda v, dch=dch, dc=dc: v.tensor_copy(
                    xe[:, dch * CAP:(dch + 1) * CAP], BK(dc)),
                    waits=[("PE", mml[dc])], inc="DV")
                bank_rd[dc] = ("DV", ge_dv[dc])
        gather_end = mml[3]
        xe_done = P.n("DV")
        if XBAR & 8:
            P.barrier()
        if DBG and e == 0:
            P.op("sync", lambda s: s.dma_start(out=dbg_xe, in_=xe), waits=[("DV", xe_done)], inc="SO", k=16)
            P.op("sync", lambda s: s.dma_start(out=dbg_xtok, in_=X_tok), inc="SO", k=16)
            for _e2 in ENGINES:
                P.pending[_e2].append(("SO", P.n("SO")))
        pt0_dv = {}
        for stile in range(4):
            pt0_dv[stile] = P.op("vector", lambda v, stile=stile, e=e: v.tensor_scalar(
                PTt[:, stile * TBL:(stile + 1) * TBL],
                slot_bc[:, 0:TBL], iotacol[:],
                float(e * CAP + stile * 128), ALU.subtract, ALU.is_equal),
                waits=[("PE", gather_end)] if stile == 0 else [], inc="DV")
        dn_dv = {}
        for fh in range(2):
            gsil, um_dv, hdone = {}, {}, {}
            for fq in range(2):
                wgl = ("WG", P.op("gpsimd", lambda g, e=e, fh=fh, fq=fq: g.dma_start(
                    out=wg_sb[:].rearrange("p (c f) -> p c f", c=DCH),
                    in_=rg[e][:, fh * KB + fq * TBL: fh * KB + (fq + 1) * TBL].rearrange("(c p) f -> p c f", p=128)),
                    waits=[("PE", last_g)], inc="WG", k=16))
                wul = ("WU", P.op("gpsimd", lambda g, e=e, fh=fh, fq=fq: g.dma_start(
                    out=wu_sb[:].rearrange("p (c f) -> p c f", c=DCH),
                    in_=ru[e][:, fh * KB + fq * TBL: fh * KB + (fq + 1) * TBL].rearrange("(c p) f -> p c f", p=128)),
                    waits=[("PE", last_u)], inc="WU", k=16))
                if fq == 0:
                    wdl = ("WD", P.op("gpsimd", lambda g, e=e, fh=fh: g.dma_start(
                        out=wd_sb[:].rearrange("p (c dd) -> p c dd", c=8),
                        in_=rd[e][fh * KB:(fh + 1) * KB, :].rearrange("(c p) dd -> p c dd", p=128)),
                        waits=[("PE", last_d)], inc="WD", k=16))
                for fc in range(4):
                    fcl = fq * 4 + fc
                    pb = fc % 2
                    mmlast = None
                    for chn in range(DCH):
                        wts = []
                        if chn == 0:
                            wts.append(wgl)
                            wts.append(bank_rd.get(pb))
                            if fcl < 2:
                                wts.append(("PE", last_d))
                                wts.append(("DV", xe_done))
                        mmlast = P.op("tensor", lambda t, chn=chn, fcl=fcl, fc=fc, pb=pb: t.matmul(
                            BK(pb), wg_sb[:, chn * TBL + fc * 128: chn * TBL + fc * 128 + 128],
                            xe[:, chn * CAP:(chn + 1) * CAP],
                            start=(chn == 0), stop=(chn == DCH - 1)),
                            waits=wts, inc="PE")
                    last_g = mmlast
                    gsil[fcl] = P.op("scalar", lambda st, fcl=fcl, pb=pb: st.activation(
                        gs[:, fcl * TBL:(fcl + 1) * TBL], BK(pb), AF.Silu),
                        waits=[("PE", mmlast)], inc="AC")
                    bank_rd[pb] = ("AC", gsil[fcl])
                for fc in range(4):
                    fcl = fq * 4 + fc
                    pb = fc % 2
                    mmlast = None
                    for chn in range(DCH):
                        wts = []
                        if chn == 0:
                            wts.append(wul)
                            wts.append(bank_rd.get(pb))
                        mmlast = P.op("tensor", lambda t, chn=chn, fc=fc, pb=pb: t.matmul(
                            BK(pb), wu_sb[:, chn * TBL + fc * 128: chn * TBL + fc * 128 + 128],
                            xe[:, chn * CAP:(chn + 1) * CAP],
                            start=(chn == 0), stop=(chn == DCH - 1)),
                            waits=wts, inc="PE")
                    last_u = mmlast
                    um_dv[fcl] = P.op("vector", lambda v, fcl=fcl, pb=pb: v.tensor_copy(
                        um[fcl % 2][:], BK(pb)), waits=[("PE", mmlast)], inc="DV")
                    bank_rd[pb] = ("DV", um_dv[fcl])
                    hdone[fcl] = P.op("vector", lambda v, fcl=fcl: v.tensor_mul(
                        gs[:, fcl * TBL:(fcl + 1) * TBL], gs[:, fcl * TBL:(fcl + 1) * TBL], um[fcl % 2][:]),
                        waits=[("AC", gsil[fcl])], inc="DV")
            if XBAR & 16:
                P.barrier()
            for stile in range(4):
                for dblk in range(2):
                    gidx = fh * 8 + stile * 2 + dblk
                    pb = 2 + gidx % 2
                    mmlast = None
                    for fcl in range(8):
                        wts = []
                        if fcl == 0:
                            wts.append(wdl)
                            wts.append(("DV", hdone[7]))
                            wts.append(bank_rd.get(pb))
                        mmlast = P.op("tensor", lambda t, stile=stile, dblk=dblk, fcl=fcl, pb=pb: t.matmul(
                            BK(pb), gs[:, fcl * TBL + stile * 128: fcl * TBL + stile * 128 + 128],
                            wd_sb[:, fcl * KB + dblk * TBL: fcl * KB + (dblk + 1) * TBL],
                            start=(fcl == 0), stop=(fcl == 7)),
                            waits=wts, inc="PE")
                    last_d = mmlast
                    if fh == 0:
                        dn_dv[gidx] = P.op("vector", lambda v, stile=stile, dblk=dblk, pb=pb: v.tensor_copy(
                            dnb[:, stile * KB + dblk * TBL: stile * KB + (dblk + 1) * TBL], BK(pb)),
                            waits=[("PE", mmlast)], inc="DV")
                        bank_rd[pb] = ("DV", dn_dv[gidx])
                    else:
                        dn_dv[gidx] = P.op("vector", lambda v, stile=stile, dblk=dblk, pb=pb: v.tensor_add(
                            dnb[:, stile * KB + dblk * TBL: stile * KB + (dblk + 1) * TBL],
                            dnb[:, stile * KB + dblk * TBL: stile * KB + (dblk + 1) * TBL], BK(pb)),
                            waits=[("PE", mmlast)], inc="DV")
                        bank_rd[pb] = ("DV", dn_dv[gidx])
        dn_done = P.n("DV")
        if XBAR & 32:
            P.barrier()
        if DBG and e == 0:
            P.op("sync", lambda s: s.dma_start(out=dbg_dnb, in_=dnb), waits=[("DV", dn_done)], inc="SO", k=16)
            for _e2 in ENGINES:
                P.pending[_e2].append(("SO", P.n("SO")))
        if XBAR & 64:
            P.barrier()
        for tblk in range(4):
            if tblk == 0:
                pt_dv = pt0_dv
            else:
                pt_dv = {}
                for stile in range(4):
                    pt_dv[stile] = P.op("vector", lambda v, stile=stile, tblk=tblk, e=e: v.tensor_scalar(
                        PTt[:, stile * TBL:(stile + 1) * TBL],
                        slot_bc[:, tblk * TBL:(tblk + 1) * TBL], iotacol[:],
                        float(e * CAP + stile * 128), ALU.subtract, ALU.is_equal),
                        inc="DV")
            if DBG and e == 0 and tblk == 0:
                P.op("sync", lambda s: s.dma_start(out=dbg_pt, in_=PTt), waits=[("DV", pt_dv[3])], inc="SO", k=16)
                P.op("sync", lambda s: s.dma_start(out=dbg_sbc, in_=slot_bc), inc="SO", k=16)
                for _e2 in ENGINES:
                    P.pending[_e2].append(("SO", P.n("SO")))
            for dch in range(DCH):
                pb = 4 + dch % 2
                mmlast = None
                for stile in range(4):
                    wts = []
                    if stile == 0:
                        wts.append(("DV", pt_dv[3] if tblk else dn_done))
                        wts.append(bank_rd.get(pb))
                    mmlast = P.op("tensor", lambda t, stile=stile, dch=dch, tblk=tblk, pb=pb: t.matmul(
                        BK(pb), dnb[:, stile * KB + dch * 128: stile * KB + dch * 128 + 128],
                        PTt[:, stile * TBL:(stile + 1) * TBL],
                        start=(stile == 0), stop=(stile == 3)),
                        waits=wts, inc="PE")
                last_scatter = mmlast
                sc_fa[(tblk, dch)] = P.op("vector", lambda v, dch=dch, tblk=tblk, pb=pb: v.tensor_add(
                    faccv[:, dch * T + tblk * TBL: dch * T + (tblk + 1) * TBL],
                    faccv[:, dch * T + tblk * TBL: dch * T + (tblk + 1) * TBL], BK(pb)),
                    waits=[("PE", mmlast)], inc="DV")
                bank_rd[pb] = ("DV", sc_fa[(tblk, dch)])

    P.barrier()
    if DBG:
        P.op("sync", lambda s: s.dma_start(out=dbg_ffn, in_=faccv), inc="SO", k=16)
        for _e in ENGINES:
            P.pending[_e].append(("SO", P.n("SO")))
    sqE4 = [MSC[:, i * TBL:(i + 1) * TBL] for i in range(2)]
    fin_dv, mm4_id, cp4_id = {}, {}, {}
    for tb in range(4):
        pbs = (tb % 2) * 2
        for och in range(DCH):
            i = tb * DCH + och
            sq_ac = P.op("scalar", lambda s, och=och, tb=tb, sl=i % 2: s.activation(
                sqE4[sl][:], faccv[:, och * T + tb * TBL: och * T + (tb + 1) * TBL], AF.Square),
                waits=[("PE", mm4_id.get(i - 2))], inc="AC")
            P.op("tensor", lambda t, och=och, tb=tb, pbs=pbs: t.matmul(
                BK(pbs)[0:1, :], col1_f[:], faccv[:, och * T + tb * TBL: och * T + (tb + 1) * TBL],
                start=(och == 0), stop=(och == DCH - 1)),
                waits=([("DV", cp4_id.get(tb - 2))] if och == 0 else []), inc="PE")
            mm4_id[i] = P.op("tensor", lambda t, sl=i % 2, pbs=pbs, och=och: t.matmul(
                BK(pbs + 1)[0:1, :], col1_f[:], sqE4[sl][:], start=(och == 0), stop=(och == DCH - 1)),
                waits=[("AC", sq_ac)], inc="PE")
        P.op("vector", lambda v, tb=tb, pbs=pbs: v.tensor_copy(
            ssE[0:1, tb * TBL:(tb + 1) * TBL], BK(pbs)[0:1, :]),
            waits=[("PE", mm4_id[tb * DCH + DCH - 1])], inc="DV")
        cp4_id[tb] = P.op("vector", lambda v, tb=tb, pbs=pbs: v.tensor_copy(
            sqrE[0:1, tb * TBL:(tb + 1) * TBL], BK(pbs + 1)[0:1, :]), inc="DV")
        rm4tb = row_math(ssE, sqrE, ar2, br2, br2, TBL, c0=tb * TBL, first_waits=[])
        pb = 4 + (tb % 2) * 2
        P.op("tensor", lambda t, tb=tb, pb=pb: t.matmul(
            BK(pb), ones_f[:], ar2[0:1, tb * TBL:(tb + 1) * TBL], start=True, stop=True),
            waits=[("DV", rm4tb), ("DV", fin_dv.get((tb - 2, DCH - 1)))], inc="PE")
        bcb = P.op("tensor", lambda t, tb=tb, pb=pb: t.matmul(
            BK(pb + 1), ones_f[:], br2[0:1, tb * TBL:(tb + 1) * TBL], start=True, stop=True), inc="PE")
        for och in range(DCH):
            i = tb * DCH + och
            sml = f"YL{i % 2}"
            ly = (sml, P.op("sync", lambda s, och=och, tb=tb, sl=i % 2: s.dma_start(
                out=y1tl[sl][:], in_=y1_h[och * 128:(och + 1) * 128, tb * TBL:(tb + 1) * TBL]),
                waits=[("DV", fin_dv.get((tb, och - 2)) or fin_dv.get((tb - 1, DCH - 2 + och)))], inc=sml, k=16))
            P.op("vector", lambda v, och=och, tb=tb, pb=pb, sl=i % 2: v.tensor_mul(
                sqE[sl][:], faccv[:, och * T + tb * TBL: och * T + (tb + 1) * TBL], BK(pb)),
                waits=[("PE", bcb)], inc="DV")
            P.op("vector", lambda v, pb=pb, sl=i % 2: v.tensor_add(sqE[sl][:], sqE[sl][:], BK(pb + 1)), inc="DV")
            fin_dv[(tb, och)] = P.op("vector", lambda v, och=och, tb=tb, sl=i % 2: v.tensor_add(
                faccv[:, och * T + tb * TBL: och * T + (tb + 1) * TBL], sqE[sl][:], y1tl[sl][:]),
                waits=[ly], inc="DV")
    for och in range(DCH):
        P.op("sync", lambda s, och=och: s.dma_start(
            out=yT[och * 128:(och + 1) * 128, :], in_=faccv[:, och * T:(och + 1) * T]),
            waits=[("DV", fin_dv[(3, och)])], inc="SO", k=16)

    P.op("sync", lambda s: s.nop(), waits=[("SO", P.n("SO"))])

    with nc.Block() as block:
        P.emit(block)
    return nc


_NC_CACHE = {}


def _get_nc():
    if "nc" not in _NC_CACHE:
        _NC_CACHE["nc"] = build_program()
    return _NC_CACHE["nc"]


def make_in_maps(L0_attn_x, L0_attn_W_qkv, L0_attn_W_out, L0_attn_b_out, L0_ffn_gate_w,
                 L0_ffn_shared_gate, L0_ffn_shared_up, L0_ffn_shared_down,
                 L0_ffn_r_gate, L0_ffn_r_up, L0_ffn_r_down):
    x = np.asarray(L0_attn_x)
    Wr = np.asarray(L0_attn_W_qkv).reshape(H, 3, HD, D)
    wq_h = np.ascontiguousarray(Wr[:, 0].reshape(H * HD, D).T)
    wk_h = np.ascontiguousarray(Wr[:, 1].reshape(H * HD, D).T)
    wv_h = np.ascontiguousarray(Wr[:, 2].reshape(H * HD, D).T)
    wo_h = np.ascontiguousarray(np.asarray(L0_attn_W_out).T)
    bo_h = np.ascontiguousarray(np.asarray(L0_attn_b_out).reshape(DCH, 128).T)
    gw_h = np.ascontiguousarray(np.asarray(L0_ffn_gate_w).T)
    sg_h = np.ascontiguousarray(np.asarray(L0_ffn_shared_gate).T)
    su_h = np.ascontiguousarray(np.asarray(L0_ffn_shared_up).T)
    sd_h = np.ascontiguousarray(np.asarray(L0_ffn_shared_down).T)
    rg_h = np.ascontiguousarray(np.transpose(np.asarray(L0_ffn_r_gate), (0, 2, 1)))
    ru_h = np.ascontiguousarray(np.transpose(np.asarray(L0_ffn_r_up), (0, 2, 1)))
    rd_h = np.ascontiguousarray(np.transpose(np.asarray(L0_ffn_r_down), (0, 2, 1)))
    in_maps = []
    for c in range(8):
        b, h = c // 2, c % 2
        xb = x[b]
        xcore = np.concatenate([xb[h * T:(h + 1) * T], xb[(1 - h) * T:(2 - h) * T]], axis=0)
        in_maps.append({
            "xT": np.ascontiguousarray(xcore.T),
            "wq": wq_h, "wk": wk_h, "wv": wv_h, "wo": wo_h, "bo": bo_h, "gw": gw_h,
            "sg": sg_h, "su": su_h, "sd": sd_h, "rg": rg_h, "ru": ru_h, "rd": rd_h,
        })
    return in_maps


def kernel(**inputs):
    in_maps = make_in_maps(**inputs)
    nc = _get_nc()
    res = run_bass_kernel_spmd(nc, in_maps, core_ids=list(range(8)))
    out = np.empty((4, S, D), np.float32)
    for c in range(8):
        b, h = c // 2, c % 2
        out[b, h * T:(h + 1) * T, :] = res.results[c]["yT"].T
    return out

